# version 03
# baseline (speedup 1.0000x reference)
import contextlib
import numpy as np
import concourse.bass as bass
import concourse.mybir as mybir
from concourse.bass_utils import run_bass_kernel_spmd

F32 = mybir.dt.float32
BF16 = mybir.dt.bfloat16
I32 = mybir.dt.int32
AF = mybir.ActivationFunctionType
ALU = mybir.AluOpType
AX = mybir.AxisListType

NCORE = 8
T_LOC = 2048
NT = T_LOC // 128
D = 1024
ND = D // 128
LN_EPS = 1e-5
RMAGIC = 12582912.0


class Ctx:
    def __init__(self, nc, es):
        self.nc = nc
        self.es = es
        self._sems = {}
        self._sb = {}

    def sem(self, name):
        if name not in self._sems:
            h = self.es.enter_context(self.nc.semaphore(name))
            self._sems[name] = [h, 0]
        return self._sems[name]

    def inc(self, instr, name, n=1):
        s = self.sem(name)
        s[1] += n
        instr.then_inc(s[0], n)
        return s[1]

    def h(self, name):
        return self.sem(name)[0]

    def val(self, name):
        return self.sem(name)[1]

    def setval(self, name, v):
        self.sem(name)[1] = v

    def sig(self, eng, name, n=1):
        sm = self.sem(name)
        sm[1] += n
        eng.sem_inc(sm[0], n)
        return sm[1]

    def wait(self, eng, name, val=None):
        s = self.sem(name)
        eng.wait_ge(s[0], s[1] if val is None else val)

    def sb(self, name, shape, dtype):
        if name not in self._sb:
            self._sb[name] = self.es.enter_context(self.nc.sbuf_tensor(name, shape, dtype))
        return self._sb[name]

    def ps(self, name, shape, dtype=F32):
        if name not in self._sb:
            self._sb[name] = self.es.enter_context(self.nc.psum_tensor(name, shape, dtype))
        return self._sb[name]


_EDGE_OPS = {
    "tensor_tensor", "tensor_add", "tensor_sub", "tensor_mul", "tensor_max",
    "tensor_scalar", "tensor_single_scalar", "tensor_scalar_add", "tensor_scalar_sub",
    "tensor_scalar_mul", "tensor_scalar_max", "tensor_scalar_min", "tensor_relu",
    "tensor_copy", "tensor_reduce", "reduce_sum", "reduce_max", "reciprocal",
    "bn_stats", "bn_aggr", "select", "copy_predicated", "memset", "memzero",
    "activation", "add", "mul", "copy", "sqrt", "square", "sign", "iota",
    "tensor_tensor_scan", "scalar_tensor_tensor", "transpose", "stream_shuffle",
}


class SafeEng:

    def __init__(self, ctx, e, nm):
        self._ctx, self._e, self._nm = ctx, e, nm

    def __getattr__(self, attr):
        f = getattr(self._e, attr)
        if attr not in _EDGE_OPS:
            return f
        ctx, e, sem = self._ctx, self._e, "se_" + self._nm

        def wrapped(*a, **k):
            ins = f(*a, **k)
            n = ctx.inc(ins, sem, 1)
            e.wait_ge(ctx.h(sem), n)
            return ins
        return wrapped


def phase(nc, name=None, _ctx=None, **bodies):
    with nc.Block(name, no_gpsimd_drain=True) as blk:
        for engname, fn in bodies.items():
            if _ctx is not None and engname in ("vector", "scalar", "gpsimd"):
                getattr(blk, engname)(lambda e, fn=fn, n=engname: fn(SafeEng(_ctx, e, n)))
            else:
                getattr(blk, engname)(fn)


def bcast_ap(dram_handle, n, parts=128):
    return bass.AP(dram_handle, 0, [[0, parts], [1, n]])


def row_ap(dram_handle, n):
    return bass.AP(dram_handle, 0, [[0, 1], [1, n]])


def rstd_ops(e_s, var_ap, tmp_ap, out_ap, epsc):
    e_s.activation(tmp_ap, var_ap, AF.Ln, bias=epsc)
    return e_s.activation(out_ap, tmp_ap, AF.Exp, scale=-0.5)


def ln_stats(e_v, src_ap, width, st6, st2, mcol, vcol, inc=None, ctx=None):
    nk = (width + 511) // 512
    for k in range(nk):
        w0, w1 = 512 * k, min(512 * (k + 1), width)
        e_v.bn_stats(st6[:, 6 * k:6 * (k + 1)], src_ap[:, w0:w1])
    e_v.bn_aggr(st2[:], st6[:, 0:6 * nk])
    e_v.tensor_copy(mcol, st2[:, 0:1])
    ins = e_v.tensor_copy(vcol, st2[:, 1:2])
    if inc is not None:
        ctx.sig(e_v, inc, 1)
    return ins


def emit_weight_quant(ctx, wT, rows, cols, wq, mwc, tag, stg, wi):
    import os
    """Ternary-quantize host-transposed weight wT ([rows, cols] f32 DRAM) into
    wq (SBUF bf16 [128, (rows//128)*cols], chunk r at offset r*cols).
    mwc (SBUF f32 [128, 2]): col0 = clip(mean|w|,1e-5); col1 = 1/col0. Two DMA passes.
    stg: f32 scratch [128, >=cols]; wi: i32 scratch [128, >=cols] (borrowed)."""
    nc = ctx.nc
    nch = rows // 128
    acc = ctx.sb(f"wacc_{tag}", [128, 33], F32)
    onescol = ctx.sb("onescolf", [128, 1], F32)
    onesrow = ctx.sb("onesrowf", [1, 128], F32)
    sca = ctx.sb("wsca", [1, 2], F32)
    psm = ctx.ps("ps_pm1_0", [128, 512], F32)

    ld, vq = f"wld_{tag}", f"wvq_{tag}"

    def sync(e):
        base_vq = ctx.val(vq)
        for r in range(nch):
            if r >= 1:
                e.wait_ge(ctx.h(vq), base_vq + r)
            ins = e.dma_start(out=stg[:, 0:cols], in_=wT[r * 128:(r + 1) * 128, :])
            ctx.inc(ins, ld, 16)

    def vector(e):
        e.memset(onescol[:], 1.0)
        e.memset(onesrow[:], 1.0)
        base = ctx.val(ld) - 16 * nch
        for r in range(nch):
            e.wait_ge(ctx.h(ld), base + 16 * (r + 1))
            e.reduce_sum(acc[:, r:r + 1], stg[:, 0:cols],
                         axis=AX.X, apply_absolute_value=True)
            ctx.sig(e, vq, 1)
        e.reduce_sum(acc[:, 32:33], acc[:, 0:nch], axis=AX.X)
    phase(nc, f"wqa_{tag}", _ctx=ctx, sync=sync, vector=vector)
    if int(os.environ.get("WQ_STAGE", "9")) < 2 and tag.endswith("w2"):
        return

    def tensor(e):
        ins = e.matmul(psm[0:1, 0:1], acc[:, 32:33], onescol[:], start=True, stop=True)
        ctx.inc(ins, vq, 1)

    def vector2(e):
        ctx.wait(e, vq)
        e.tensor_scalar(sca[0:1, 0:1], psm[0:1, 0:1], 1.0 / (rows * cols), 1e-5,
                        ALU.mult, ALU.max)
        ctx.sig(e, vq, 1)
    phase(nc, f"wqb_{tag}", _ctx=ctx, tensor=tensor, vector=vector2)
    if int(os.environ.get("WQ_STAGE", "9")) < 3 and tag.endswith("w2"):
        return

    def tensor2(e):
        ins = e.matmul(psm[:, 0:1], onesrow[:], sca[0:1, 0:1], start=True, stop=True)
        ctx.inc(ins, vq, 1)

    def vector3(e):
        ctx.wait(e, vq)
        e.tensor_copy(mwc[:, 0:1], psm[:, 0:1])
        e.reciprocal(mwc[:, 1:2], mwc[:, 0:1])
    phase(nc, f"wqc_{tag}", _ctx=ctx, tensor=tensor2, vector=vector3)
    if int(os.environ.get("WQ_STAGE", "9")) < 4 and tag.endswith("w2"):
        return

    def sync2(e):
        base_vq = ctx.val(vq)
        for r in range(nch):
            if r >= 1:
                e.wait_ge(ctx.h(vq), base_vq + r)
            ins = e.dma_start(out=stg[:, 0:cols], in_=wT[r * 128:(r + 1) * 128, :])
            ctx.inc(ins, ld, 16)

    def vector4(e):
        base_ld = ctx.val(ld) - 16 * nch
        for r in range(nch):
            e.wait_ge(ctx.h(ld), base_ld + 16 * (r + 1))
            e.tensor_scalar(stg[:, 0:cols], stg[:, 0:cols], mwc[:, 1:2], RMAGIC,
                            ALU.mult, ALU.add)
            e.tensor_scalar(stg[:, 0:cols], stg[:, 0:cols], -RMAGIC, 1.0,
                            ALU.add, ALU.min)
            e.tensor_scalar_max(wq[:, r * cols:(r + 1) * cols], stg[:, 0:cols], -1.0)
            ctx.sig(e, vq, 1)
    phase(nc, f"wqd_{tag}", _ctx=ctx, sync=sync2, vector=vector4)


import os
L2F_STAGE = int(os.environ.get("L2F_STAGE", "9"))


def emit_l2f(ctx, IN, tag="l2f"):
    nc = ctx.nc
    FF = 4096
    NF = FF // 128

    wqbuf = ctx.sb("wqbuf", [128, 32768], BF16)
    mwc1 = ctx.sb(f"{tag}_mwc1", [128, 2], F32)
    mwc2 = ctx.sb(f"{tag}_mwc2", [128, 2], F32)
    g1b = ctx.sb(f"{tag}_g1b", [128, D], BF16)
    b1lb = ctx.sb(f"{tag}_b1lb", [128, D], BF16)
    g2b = ctx.sb(f"{tag}_g2b", [128, FF], BF16)
    b2lb = ctx.sb(f"{tag}_b2lb", [128, FF], BF16)
    b1row = ctx.sb(f"{tag}_b1row", [1, FF], BF16)
    b2row = ctx.sb(f"{tag}_b2row", [1, D], BF16)
    ident = ctx.sb("identb", [128, 128], BF16)
    epsc = ctx.sb("epscol", [128, 1], F32)

    xt = ctx.sb("g_xt", [128, 2 * D], F32)
    xq1t = ctx.sb(f"{tag}_xq1t", [128, D], BF16)
    xq1f = ctx.sb(f"{tag}_xq1f", [128, D], BF16)
    ht = ctx.sb(f"{tag}_ht", [128, 2 * FF], BF16)
    wstg = ctx.sb("wstg", [128, FF], F32)
    hbt = ctx.sb(f"{tag}_hbt", [128, 2 * FF], BF16)
    xq2t = ctx.sb(f"{tag}_xq2t", [128, FF], BF16)
    xq2f = ctx.sb(f"{tag}_xq2f", [128, FF], BF16)
    un2 = ht[:, 0:FF]
    ut = wstg[:, 0:D]
    ct = ctx.sb("g_ct", [128, 2 * D], F32)
    ot = ctx.sb("g_ot", [128, 2 * D], F32)
    sc = ctx.sb("g_sc", [128, 16], F32)
    st6 = ctx.sb("g_st6", [128, 48], F32)
    st2 = ctx.sb("g_st2", [128, 2], F32)
    r1c = ctx.sb("g_r1c", [128, 4], F32)
    rs1 = ctx.sb(f"{tag}_rs1", [128, 1], F32)
    rs2 = ctx.sb(f"{tag}_rs2", [128, 1], F32)
    iv1c = ctx.sb(f"{tag}_iv1c", [128, 1], BF16)
    iv2c = ctx.sb(f"{tag}_iv2c", [128, 1], BF16)
    iv1r = ctx.sb(f"{tag}_iv1r", [1, 128], BF16)
    iv2r = ctx.sb(f"{tag}_iv2r", [1, 128], BF16)

    pm1 = [ctx.ps(f"ps_pm1_{i}", [128, 512], F32) for i in range(4)]
    pm2 = [ctx.ps(f"ps_pm2_{i}", [128, 512], F32) for i in range(2)]
    ptr = [ctx.ps(f"ps_ptr_{i}", [128, 128], BF16) for i in range(2)]

    x_in, x_out, h_dram = IN["x"], IN["x_out"], IN["h_spill"]

    def syncp(e):
        ins = e.dma_start(out=ident[:], in_=IN["identC"])
        ctx.inc(ins, "ld", 16)

    def gpp(e):
        e.memset(epsc[:], LN_EPS)
        for dst, src in ((g1b, IN["ln1_g"]), (b1lb, IN["ln1_b"]),
                         (g2b, IN["ln2_g"]), (b2lb, IN["ln2_b"])):
            ins = e.dma_start(out=dst[:], in_=src)
            ctx.inc(ins, "ldg", 16)
        for dst, src in ((b1row, IN["b1"]), (b2row, IN["b2"])):
            ins = e.dma_start(out=dst[:], in_=src)
            ctx.inc(ins, "ldg", 16)
        ctx.wait(e, "ldg")
        ctx.wait(e, "ld")
    phase(nc, f"{tag}_pro", _ctx=ctx, sync=syncp, gpsimd=gpp)

    if L2F_STAGE < 1:
        return
    emit_weight_quant(ctx, IN["w1T"], D, FF, wqbuf, mwc1, tag + "w1", wstg, None)
    if L2F_STAGE < 2:
        return

    BA = {k: ctx.val(k) for k in ("ld16", "va", "sr", "q", "tp", "ev", "m1", "hg", "hs")}

    def syncA(e):
        for t in range(NT):
            if t >= 2:
                e.wait_ge(ctx.h("va"), BA["va"] + 2 * (t - 2) + 2)
            e.dma_start(out=xt[:, (t % 2) * D:((t % 2) + 1) * D],
                        in_=x_in[t * 128:(t + 1) * 128, :]).then_inc(ctx.h("ld16"), 16)

    def gpA(e):
        for t in range(NT):
            e.wait_ge(ctx.h("hg"), BA["hg"] + 8 * (t + 1))
            e.dma_start(out=h_dram[t * 128:(t + 1) * 128, :],
                        in_=ht[:, (t % 2) * FF:((t % 2) + 1) * FF]).then_inc(ctx.h("hs"), 16)
        e.wait_ge(ctx.h("hs"), BA["hs"] + 16 * NT)

    def vectorA(e):
        for t in range(NT):
            TP, EV = BA["tp"] + 9 * t, BA["ev"] + 9 * t
            xs = xt[:, (t % 2) * D:((t % 2) + 1) * D]
            e.wait_ge(ctx.h("ld16"), BA["ld16"] + 16 * (t + 1))
            ln_stats(e, xs, D, st6, st2, sc[:, 0:1], sc[:, 1:2], "va", ctx)
            e.wait_ge(ctx.h("sr"), BA["sr"] + 2 * t + 1)
            e.tensor_scalar(ut[:], xs, sc[:, 0:1], r1c[:, 0:1], ALU.subtract, ALU.mult)
            ln_stats(e, ut[:], D, st6, st2, sc[:, 2:3], sc[:, 3:4], "va", ctx)
            e.wait_ge(ctx.h("sr"), BA["sr"] + 2 * t + 2)
            e.tensor_scalar(ut[:], ut[:], sc[:, 2:3], r1c[:, 1:2], ALU.subtract, ALU.mult)
            e.tensor_mul(ut[:], ut[:], g1b[:])
            e.tensor_add(ut[:], ut[:], b1lb[:])
            e.reduce_max(sc[:, 4:5], ut[:], axis=AX.X, apply_absolute_value=True)
            e.tensor_scalar_max(sc[:, 4:5], sc[:, 4:5], 1e-5)
            e.reciprocal(sc[:, 5:6], sc[:, 4:5])
            e.tensor_scalar(ut[:], ut[:], sc[:, 5:6], 127.0, ALU.mult, ALU.mult)
            e.tensor_scalar(xq1t[:], ut[:], RMAGIC, -RMAGIC, ALU.add, ALU.add)
            e.tensor_scalar(rs1[:], sc[:, 4:5], 1.0 / 127.0, mwc1[:, 0:1], ALU.mult, ALU.mult)
            e.tensor_scalar(iv1c[:], sc[:, 5:6], 127.0, mwc1[:, 1:2],
                            ALU.mult, ALU.mult)
            ctx.sig(e, "q", 1)
            e.wait_ge(ctx.h("tp"), TP + 1)
            e.tensor_copy(iv1r[:], ptr[0][0:1, 0:128])
            ctx.sig(e, "ev", 1)
            for j in range(8):
                e.wait_ge(ctx.h("tp"), TP + 2 + j)
                e.tensor_copy(xq1f[:, j * 128:(j + 1) * 128], ptr[(1 + j) % 2][:])
                ctx.sig(e, "ev", 1)

    def scalarA(e):
        for t in range(NT):
            e.wait_ge(ctx.h("va"), BA["va"] + 2 * t + 1)
            rstd_ops(e, sc[:, 1:2], sc[:, 12:13], r1c[:, 0:1], epsc[:])
            ctx.sig(e, "sr", 1)
            e.wait_ge(ctx.h("va"), BA["va"] + 2 * t + 2)
            rstd_ops(e, sc[:, 3:4], sc[:, 12:13], r1c[:, 1:2], epsc[:])
            ctx.sig(e, "sr", 1)
            hs = ht[:, (t % 2) * FF:((t % 2) + 1) * FF]
            if t >= 2:
                e.wait_ge(ctx.h("hs"), BA["hs"] + 16 * (t - 1))
            for g in range(8):
                e.wait_ge(ctx.h("m1"), BA["m1"] + 8 * t + g + 1)
                e.activation(hs[:, 512 * g:512 * (g + 1)], pm1[(8 * t + g) % 4][:],
                             AF.Gelu, scale=rs1[:])
                ctx.sig(e, "hg", 1)

    def tensorA(e):
        for t in range(NT):
            TP, EV = BA["tp"] + 9 * t, BA["ev"] + 9 * t
            e.wait_ge(ctx.h("q"), BA["q"] + t + 1)
            e.wait_ge(ctx.h("ev"), EV)
            e.transpose(ptr[0][0:1, 0:128], iv1c[:], ident[:]).then_inc(ctx.h("tp"), 1)
            for j in range(8):
                i = 1 + j
                if i >= 2:
                    e.wait_ge(ctx.h("ev"), EV + i - 1)
                e.transpose(ptr[(1 + j) % 2][:], xq1t[:, j * 128:(j + 1) * 128],
                            ident[:]).then_inc(ctx.h("tp"), 1)
            e.wait_ge(ctx.h("ev"), EV + 9)
            for g in range(8):
                G = 8 * t + g
                if G >= 4:
                    e.wait_ge(ctx.h("hg"), BA["hg"] + G - 3)
                for j in range(8):
                    e.matmul(pm1[G % 4][:], xq1f[:, j * 128:(j + 1) * 128],
                             wqbuf[:, j * FF + g * 512: j * FF + (g + 1) * 512],
                             start=(j == 0), stop=False)
                e.matmul(pm1[G % 4][:], iv1r[:], b1row[0:1, g * 512:(g + 1) * 512],
                         start=False, stop=True).then_inc(ctx.h("m1"), 1)

    phase(nc, f"{tag}_A", _ctx=ctx, sync=syncA, gpsimd=gpA, vector=vectorA, scalar=scalarA, tensor=tensorA)
    ctx.setval("ld16", BA["ld16"] + 16 * NT)
    ctx.setval("va", BA["va"] + 2 * NT)
    ctx.setval("sr", BA["sr"] + 2 * NT)
    ctx.setval("q", BA["q"] + NT)
    ctx.setval("tp", BA["tp"] + 9 * NT)
    ctx.setval("ev", BA["ev"] + 9 * NT)
    ctx.setval("m1", BA["m1"] + 8 * NT)
    ctx.setval("hg", BA["hg"] + 8 * NT)
    ctx.setval("hs", BA["hs"] + 16 * NT)

    if L2F_STAGE < 3:
        return
    emit_weight_quant(ctx, IN["w2T"], FF, D, wqbuf, mwc2, tag + "w2", wstg, None)
    if L2F_STAGE < 4:
        return

    BB = {k: ctx.val(k) for k in ("ld16", "hb", "va", "sr", "q", "tp", "ev",
                                  "m2", "e2", "gl", "st", "v2g", "g2v")}

    def syncB(e):
        for t in range(NT):
            if t >= 2:
                e.wait_ge(ctx.h("v2g"), BB["v2g"] + t - 1)
            e.dma_start(out=hbt[:, (t % 2) * FF:((t % 2) + 1) * FF],
                        in_=h_dram[t * 128:(t + 1) * 128, :]).then_inc(ctx.h("hb"), 16)
            if t >= 2:
                e.wait_ge(ctx.h("gl"), BB["gl"] + t - 1)
            e.dma_start(out=xt[:, (t % 2) * D:((t % 2) + 1) * D],
                        in_=x_in[t * 128:(t + 1) * 128, :]).then_inc(ctx.h("ld16"), 16)

    def gpB(e):
        for t in range(NT):
            e.wait_ge(ctx.h("v2g"), BB["v2g"] + t + 1)
            e.tensor_mul(un2[:], un2[:], g2b[:])
            e.tensor_add(un2[:], un2[:], b2lb[:])
            ctx.sig(e, "g2v", 1)
            e.wait_ge(ctx.h("gl"), BB["gl"] + t + 1)
            e.dma_start(out=x_out[t * 128:(t + 1) * 128, :],
                        in_=ot[:, (t % 2) * D:((t % 2) + 1) * D]).then_inc(ctx.h("st"), 16)
        e.wait_ge(ctx.h("st"), BB["st"] + 16 * NT)

    def vectorB(e):
        for t in range(NT):
            TP, EV = BB["tp"] + 33 * t, BB["ev"] + 33 * t
            hbs = hbt[:, (t % 2) * FF:((t % 2) + 1) * FF]
            e.wait_ge(ctx.h("hb"), BB["hb"] + 16 * (t + 1))
            ln_stats(e, hbs, FF, st6, st2, sc[:, 6:7], sc[:, 7:8], "va", ctx)
            e.wait_ge(ctx.h("sr"), BB["sr"] + 2 * t + 1)
            e.tensor_scalar(un2[:], hbs, sc[:, 6:7], r1c[:, 2:3],
                            ALU.subtract, ALU.mult)
            ctx.sig(e, "v2g", 1)
            e.wait_ge(ctx.h("g2v"), BB["g2v"] + t + 1)
            e.reduce_max(sc[:, 8:9], un2[:], axis=AX.X, apply_absolute_value=True)
            e.tensor_scalar_max(sc[:, 8:9], sc[:, 8:9], 1e-5)
            e.reciprocal(sc[:, 9:10], sc[:, 8:9])
            e.tensor_scalar(wstg[:], un2[:], sc[:, 9:10], 127.0, ALU.mult, ALU.mult)
            e.tensor_scalar(xq2t[:], wstg[:], RMAGIC, -RMAGIC, ALU.add, ALU.add)
            e.tensor_scalar(rs2[:], sc[:, 8:9], 1.0 / 127.0, mwc2[:, 0:1], ALU.mult, ALU.mult)
            e.tensor_scalar(iv2c[:], sc[:, 9:10], 127.0, mwc2[:, 1:2],
                            ALU.mult, ALU.mult)
            ctx.sig(e, "q", 1)
            e.wait_ge(ctx.h("tp"), TP + 1)
            e.tensor_copy(iv2r[:], ptr[0][0:1, 0:128])
            ctx.sig(e, "ev", 1)
            for j in range(32):
                e.wait_ge(ctx.h("tp"), TP + 2 + j)
                e.tensor_copy(xq2f[:, j * 128:(j + 1) * 128], ptr[(1 + j) % 2][:])
                ctx.sig(e, "ev", 1)
            cs = ct[:, (t % 2) * D:((t % 2) + 1) * D]
            xs = xt[:, (t % 2) * D:((t % 2) + 1) * D]
            e.wait_ge(ctx.h("e2"), BB["e2"] + t + 1)
            e.wait_ge(ctx.h("ld16"), BB["ld16"] + 16 * (t + 1))
            ln_stats(e, cs, D, st6, st2, sc[:, 10:11], sc[:, 11:12], "va", ctx)
            e.wait_ge(ctx.h("sr"), BB["sr"] + 2 * t + 2)
            os_ = ot[:, (t % 2) * D:((t % 2) + 1) * D]
            if t >= 2:
                e.wait_ge(ctx.h("st"), BB["st"] + 16 * (t - 1))
            e.tensor_scalar(os_, cs, sc[:, 10:11], r1c[:, 3:4], ALU.subtract, ALU.mult)
            e.tensor_add(os_, os_, xs)
            ctx.sig(e, "gl", 1)

    def scalarB(e):
        for t in range(NT):
            e.wait_ge(ctx.h("va"), BB["va"] + 2 * t + 1)
            rstd_ops(e, sc[:, 7:8], sc[:, 12:13], r1c[:, 2:3], epsc[:])
            ctx.sig(e, "sr", 1)
            cs = ct[:, (t % 2) * D:((t % 2) + 1) * D]
            for dg in range(2):
                e.wait_ge(ctx.h("m2"), BB["m2"] + 2 * t + dg + 1)
                e.activation(cs[:, 512 * dg:512 * (dg + 1)], pm2[dg][:],
                             AF.Identity, scale=rs2[:])
                if dg == 1:
                    ctx.sig(e, "e2", 1)
            e.wait_ge(ctx.h("va"), BB["va"] + 2 * t + 2)
            rstd_ops(e, sc[:, 11:12], sc[:, 12:13], r1c[:, 3:4], epsc[:])
            ctx.sig(e, "sr", 1)

    def tensorB(e):
        for t in range(NT):
            TP, EV = BB["tp"] + 33 * t, BB["ev"] + 33 * t
            e.wait_ge(ctx.h("q"), BB["q"] + t + 1)
            e.wait_ge(ctx.h("ev"), EV)
            e.transpose(ptr[0][0:1, 0:128], iv2c[:], ident[:]).then_inc(ctx.h("tp"), 1)
            for j in range(32):
                i = 1 + j
                if i >= 2:
                    e.wait_ge(ctx.h("ev"), EV + i - 1)
                e.transpose(ptr[(1 + j) % 2][:], xq2t[:, j * 128:(j + 1) * 128],
                            ident[:]).then_inc(ctx.h("tp"), 1)
            e.wait_ge(ctx.h("ev"), EV + 33)
            if t >= 1:
                e.wait_ge(ctx.h("e2"), BB["e2"] + t)
            for dg in range(2):
                for j in range(32):
                    e.matmul(pm2[dg][:], xq2f[:, j * 128:(j + 1) * 128],
                             wqbuf[:, j * D + dg * 512: j * D + (dg + 1) * 512],
                             start=(j == 0), stop=False)
                e.matmul(pm2[dg][:], iv2r[:], b2row[0:1, dg * 512:(dg + 1) * 512],
                         start=False, stop=True).then_inc(ctx.h("m2"), 1)

    phase(nc, f"{tag}_B", _ctx=ctx, sync=syncB, gpsimd=gpB, vector=vectorB, scalar=scalarB, tensor=tensorB)
    ctx.setval("ld16", BB["ld16"] + 16 * NT)
    ctx.setval("hb", BB["hb"] + 16 * NT)
    ctx.setval("va", BB["va"] + 2 * NT)
    ctx.setval("sr", BB["sr"] + 2 * NT)
    ctx.setval("q", BB["q"] + NT)
    ctx.setval("tp", BB["tp"] + 33 * NT)
    ctx.setval("ev", BB["ev"] + 33 * NT)
    ctx.setval("m2", BB["m2"] + 2 * NT)
    ctx.setval("e2", BB["e2"] + NT)
    ctx.setval("gl", BB["gl"] + NT)
    ctx.setval("st", BB["st"] + 16 * NT)
    ctx.setval("v2g", BB["v2g"] + NT)
    ctx.setval("g2v", BB["g2v"] + NT)



def emit_l3f(ctx, IN, tag="l3f"):
    nc = ctx.nc
    E = 8

    ew = ctx.sb(f"{tag}_ew", [128, E * 8 * D], BF16)
    ident = ctx.sb("identb", [128, 128], BF16)
    epsc = ctx.sb("epscol", [128, 1], F32)
    gws = ctx.sb(f"{tag}_gw", [128, 8 * E], BF16)
    gbrow = ctx.sb(f"{tag}_gb", [1, E], BF16)
    onesrb = ctx.sb(f"{tag}_ones", [1, 128], BF16)
    ebs = ctx.sb(f"{tag}_eb", [E, D], BF16)

    xt = ctx.sb("g_xt", [128, 2 * D], F32)
    xn = ctx.sb(f"{tag}_xn", [128, D], F32)
    xnb = ctx.sb(f"{tag}_xnb", [128, D], BF16)
    xnf = ctx.sb(f"{tag}_xnf", [128, D], BF16)
    acc = ctx.sb(f"{tag}_acc", [128, 2 * D], F32)
    ot = ctx.sb("g_ot", [128, 2 * D], F32)
    sc = ctx.sb("g_sc", [128, 16], F32)
    st6 = ctx.sb("g_st6", [128, 48], F32)
    st2 = ctx.sb("g_st2", [128, 2], F32)
    r1c = ctx.sb("g_r1c", [128, 4], F32)
    gE = ctx.sb(f"{tag}_gE", [128, E], F32)
    wts = ctx.sb(f"{tag}_wts", [128, E], F32)
    wtsb = ctx.sb(f"{tag}_wtsb", [128, E], BF16)
    wtsf = ctx.sb(f"{tag}_wtsf", [8, 128], BF16)

    pg = ctx.ps("ps_pm1_0", [128, 512], F32)
    pb = [ctx.ps(f"ps_pm2_{i}", [128, 512], F32) for i in range(2)]
    pe = [ctx.ps(f"ps_pm1_{i}", [128, 512], F32) for i in (1, 2, 3)]
    ptr = [ctx.ps(f"ps_ptr_{i}", [128, 128], BF16) for i in range(2)]

    def syncp(e):
        ins = e.dma_start(out=ident[:], in_=IN["identC"])
        ctx.inc(ins, "ld", 16)

    def gpp(e):
        e.memset(epsc[:], LN_EPS)
        e.memset(onesrb[:], 1.0)
        ins = e.dma_start(out=gbrow[:], in_=IN["gate_b"])
        ctx.inc(ins, "ldg", 16)
        ins = e.dma_start(out=ew[:].rearrange("p (c n) -> p c n", n=D),
                          in_=IN["expert_w"].rearrange("(c p) n -> p c n", p=128))
        ctx.inc(ins, "ldg", 16)
        ins = e.dma_start(out=gws[:].rearrange("p (c n) -> p c n", n=E),
                          in_=IN["gate_w"].rearrange("(c p) n -> p c n", p=128))
        ctx.inc(ins, "ldg", 16)
        ins = e.dma_start(out=ebs[:], in_=IN["expert_b"])
        ctx.inc(ins, "ldg", 16)
        ctx.wait(e, "ldg")
        ctx.wait(e, "ld")
    phase(nc, f"{tag}_pro", _ctx=ctx, sync=syncp, gpsimd=gpp)

    B = {k: ctx.val(k) for k in ("ld16", "va", "sr", "q", "tp", "ev", "mg",
                                 "sx", "m2", "me", "ea", "gl", "st")}
    x_in, x_out = IN["x"], IN["x_out"]

    def sync(e):
        for t in range(NT):
            if t >= 2:
                e.wait_ge(ctx.h("gl"), B["gl"] + t - 1)
            e.dma_start(out=xt[:, (t % 2) * D:((t % 2) + 1) * D],
                        in_=x_in[t * 128:(t + 1) * 128, :]).then_inc(ctx.h("ld16"), 16)

    def gp(e):
        for t in range(NT):
            e.wait_ge(ctx.h("gl"), B["gl"] + t + 1)
            e.dma_start(out=x_out[t * 128:(t + 1) * 128, :],
                        in_=ot[:, (t % 2) * D:((t % 2) + 1) * D]).then_inc(ctx.h("st"), 16)
        e.wait_ge(ctx.h("st"), B["st"] + 16 * NT)

    def vector(e):
        for t in range(NT):
            xs = xt[:, (t % 2) * D:((t % 2) + 1) * D]
            e.wait_ge(ctx.h("ld16"), B["ld16"] + 16 * (t + 1))
            ln_stats(e, xs, D, st6, st2, sc[:, 0:1], sc[:, 1:2], "va", ctx)
            e.wait_ge(ctx.h("sr"), B["sr"] + 2 * t + 1)
            e.tensor_scalar(xn[:], xs, sc[:, 0:1], r1c[:, 0:1], ALU.subtract, ALU.mult)
            e.tensor_copy(xnb[:], xn[:])
            ctx.sig(e, "q", 1)
            for j in range(8):
                e.wait_ge(ctx.h("tp"), B["tp"] + 9 * t + j + 1)
                e.tensor_copy(xnf[:, j * 128:(j + 1) * 128], ptr[j % 2][:])
                ctx.sig(e, "ev", 1)
            e.wait_ge(ctx.h("mg"), B["mg"] + t + 1)
            e.reduce_max(sc[:, 2:3], pg[:, 0:E], axis=AX.X)
            e.tensor_scalar_mul(sc[:, 3:4], sc[:, 2:3], -1.0)
            ctx.sig(e, "va", 1)
            e.wait_ge(ctx.h("sx"), B["sx"] + t + 1)
            e.reduce_sum(sc[:, 4:5], gE[:, 0:E], axis=AX.X)
            e.tensor_scalar_mul(sc[:, 5:6], sc[:, 4:5], 0.1)
            e.tensor_scalar(wts[:], gE[:], sc[:, 5:6], None, ALU.is_ge)
            e.tensor_mul(wts[:], wts[:], gE[:])
            e.reduce_sum(sc[:, 6:7], wts[:], axis=AX.X)
            e.reciprocal(sc[:, 7:8], sc[:, 6:7])
            e.tensor_scalar_mul(wts[:], wts[:], sc[:, 7:8])
            e.tensor_copy(wtsb[:], wts[:])
            ctx.sig(e, "q", 1)
            e.wait_ge(ctx.h("tp"), B["tp"] + 9 * t + 9)
            e.tensor_copy(wtsf[:], ptr[0][0:8, 0:128])
            ctx.sig(e, "ev", 1)
            ac = acc[:, (t % 2) * D:((t % 2) + 1) * D]
            e.wait_ge(ctx.h("m2"), B["m2"] + 2 * (t + 1))
            e.tensor_copy(ac[:, 0:512], pb[0][:])
            e.tensor_copy(ac[:, 512:1024], pb[1][:])
            ctx.sig(e, "ea", 1)
            for ei in range(E):
                e.wait_ge(ctx.h("me"), B["me"] + 16 * t + 2 * ei + 2)
                e.scalar_tensor_tensor(ac[:, 0:512], pe[0][:], wts[:, ei:ei + 1],
                                       ac[:, 0:512], ALU.mult, ALU.add)
                e.scalar_tensor_tensor(ac[:, 512:1024], pe[1][:], wts[:, ei:ei + 1],
                                       ac[:, 512:1024], ALU.mult, ALU.add)
                ctx.sig(e, "ea", 2)
            ln_stats(e, ac, D, st6, st2, sc[:, 10:11], sc[:, 11:12], "va", ctx)
            e.wait_ge(ctx.h("sr"), B["sr"] + 2 * t + 2)
            os_ = ot[:, (t % 2) * D:((t % 2) + 1) * D]
            if t >= 2:
                e.wait_ge(ctx.h("st"), B["st"] + 16 * (t - 1))
            e.tensor_scalar(os_, ac, sc[:, 10:11], r1c[:, 3:4], ALU.subtract, ALU.mult)
            e.tensor_add(os_, os_, xs)
            ctx.sig(e, "gl", 1)

    def scalar(e):
        for t in range(NT):
            e.wait_ge(ctx.h("va"), B["va"] + 3 * t + 1)
            rstd_ops(e, sc[:, 1:2], sc[:, 12:13], r1c[:, 0:1], epsc[:])
            ctx.sig(e, "sr", 1)
            e.wait_ge(ctx.h("va"), B["va"] + 3 * t + 2)
            e.activation(gE[:, 0:E], pg[:, 0:E], AF.Exp, bias=sc[:, 3:4])
            ctx.sig(e, "sx", 1)
            e.wait_ge(ctx.h("va"), B["va"] + 3 * t + 3)
            rstd_ops(e, sc[:, 11:12], sc[:, 12:13], r1c[:, 3:4], epsc[:])
            ctx.sig(e, "sr", 1)

    def tensor(e):
        for t in range(NT):
            TP, EV = B["tp"] + 9 * t, B["ev"] + 9 * t
            e.wait_ge(ctx.h("q"), B["q"] + 2 * t + 1)
            e.wait_ge(ctx.h("ev"), EV)
            for j in range(8):
                if j >= 2:
                    e.wait_ge(ctx.h("ev"), EV + j - 1)
                e.transpose(ptr[j % 2][:], xnb[:, j * 128:(j + 1) * 128],
                            ident[:]).then_inc(ctx.h("tp"), 1)
            e.wait_ge(ctx.h("ev"), EV + 8)
            if t >= 1:
                e.wait_ge(ctx.h("sx"), B["sx"] + t)
            for j in range(8):
                e.matmul(pg[:, 0:E], xnf[:, j * 128:(j + 1) * 128], gws[:, j * E:(j + 1) * E],
                         start=(j == 0), stop=False)
            mm = e.matmul(pg[:, 0:E], onesrb[:], gbrow[:], start=False, stop=True)
            mm.then_inc(ctx.h("mg"), 1)
            e.wait_ge(ctx.h("q"), B["q"] + 2 * t + 2)
            e.transpose(ptr[0][0:8, 0:128], wtsb[:, 0:E], ident[:]).then_inc(ctx.h("tp"), 1)
            e.wait_ge(ctx.h("ev"), EV + 9)
            if t >= 1:
                e.wait_ge(ctx.h("ea"), B["ea"] + 17 * t)
            for dg in range(2):
                e.matmul(pb[dg][:], wtsf[0:8, :], ebs[0:8, dg * 512:(dg + 1) * 512],
                         start=True, stop=True).then_inc(ctx.h("m2"), 1)
            for ei in range(E):
                if ei >= 1:
                    e.wait_ge(ctx.h("ea"), B["ea"] + 17 * t + 1 + 2 * ei)
                for dg in range(2):
                    pdst = pe[0] if dg == 0 else pe[1]
                    for j in range(8):
                        mm = e.matmul(pdst[:], xnf[:, j * 128:(j + 1) * 128],
                                 ew[:, (ei * 8 + j) * D + dg * 512:(ei * 8 + j) * D + (dg + 1) * 512],
                                 start=(j == 0), stop=(j == 7))
                    mm.then_inc(ctx.h("me"), 1)

    phase(nc, f"{tag}_main", _ctx=ctx, sync=sync, gpsimd=gp, vector=vector, scalar=scalar, tensor=tensor)
    for k, r in dict(ld16=16, va=3, sr=2, q=2, tp=9, ev=9, mg=1, sx=1, m2=2,
                     me=16, ea=17, gl=1, st=16).items():
        ctx.setval(k, B[k] + r * NT)


def build_l3f_program():
    es = contextlib.ExitStack()
    nc = mk_bass()
    IN = {}
    IN["x"] = nc.dram_tensor("x", [T_LOC, D], F32, kind="ExternalInput").ap()
    IN["gate_w"] = nc.dram_tensor("gate_w", [D, 8], F32, kind="ExternalInput").ap()
    IN["gate_b"] = nc.dram_tensor("gate_b", [1, 8], F32, kind="ExternalInput").ap()
    IN["expert_w"] = nc.dram_tensor("expert_w", [8 * D, D], F32, kind="ExternalInput").ap()
    IN["expert_b"] = nc.dram_tensor("expert_b", [8, D], F32, kind="ExternalInput").ap()
    IN["identC"] = nc.dram_tensor("identC", [128, 128], BF16, kind="ExternalInput").ap()
    IN["x_out"] = nc.dram_tensor("x_out", [T_LOC, D], F32, kind="ExternalOutput").ap()
    ctx = Ctx(nc, es)
    emit_l3f(ctx, IN)
    es.close()
    return nc


def l3f_in_maps(x_full, ins):
    import ml_dtypes
    identC = np.eye(128, dtype=ml_dtypes.bfloat16)
    ew = np.ascontiguousarray(ins["L3_ffn_expert_w"].astype(np.float32).reshape(8 * D, D))
    maps = []
    for c in range(NCORE):
        b, h = c // 2, c % 2
        shard = np.ascontiguousarray(x_full[b, h * T_LOC:(h + 1) * T_LOC, :].astype(np.float32))
        maps.append(dict(
            x=shard, gate_w=np.ascontiguousarray(ins["L3_ffn_gate_w"].astype(np.float32)),
            gate_b=row2d(ins["L3_ffn_gate_b"]), expert_w=ew,
            expert_b=np.ascontiguousarray(ins["L3_ffn_expert_b"].astype(np.float32)),
            identC=identC,
        ))
    return maps



def emit_l0f(ctx, IN, tag="l0f"):
    nc = ctx.nc
    FF = 4096
    E, R = 8, 4

    wqbuf = ctx.sb("wqbuf", [128, 32768], BF16)
    ident = ctx.sb("identb", [128, 128], BF16)
    epsc = ctx.sb("epscol", [128, 1], F32)
    gws = ctx.sb(f"{tag}_gw", [128, 8 * E], BF16)
    gbrow = ctx.sb(f"{tag}_gb", [1, E], BF16)
    onesrb = ctx.sb(f"{tag}_ones", [1, 128], BF16)
    a2 = ctx.sb(f"{tag}_a2", [128, 8 * 32], BF16)
    bcat = ctx.sb(f"{tag}_bcat", [32, FF], BF16)
    bi_row = ctx.sb(f"{tag}_birow", [1, FF], BF16)
    bo_row = ctx.sb(f"{tag}_borow", [1, D], BF16)

    xt = ctx.sb("g_xt", [128, 2 * D], F32)
    xn = ctx.sb(f"{tag}_xn", [128, D], F32)
    xnb = ctx.sb(f"{tag}_xnb", [128, D], BF16)
    xnf = ctx.sb(f"{tag}_xnf", [128, D], BF16)
    mrep = ctx.sb(f"{tag}_mrep", [128, 64], F32)
    cm = ctx.sb(f"{tag}_cm", [128, 32], BF16)
    cmf = ctx.sb(f"{tag}_cmf", [32, 128], BF16)
    inter = ctx.sb(f"{tag}_inter", [128, 2 * FF], BF16)
    intf = ctx.sb(f"{tag}_intf", [128, FF], BF16)
    ct = ctx.sb("g_ct", [128, 2 * D], F32)
    ot = ctx.sb("g_ot", [128, 2 * D], F32)
    sc = ctx.sb("g_sc", [128, 16], F32)
    st6 = ctx.sb("g_st6", [128, 48], F32)
    st2 = ctx.sb("g_st2", [128, 2], F32)
    r1c = ctx.sb("g_r1c", [128, 4], F32)

    pm1 = [ctx.ps(f"ps_pm1_{i}", [128, 512], F32) for i in range(4)]
    pm2 = [ctx.ps(f"ps_pm2_{i}", [128, 512], F32) for i in range(2)]
    ptr = [ctx.ps(f"ps_ptr_{i}", [128, 128], BF16) for i in range(2)]

    x_in, x_out, h_dram = IN["x"], IN["x_out"], IN["h_spill"]

    def syncp(e):
        ins = e.dma_start(out=ident[:], in_=IN["identC"])
        ctx.inc(ins, "ld", 16)

    def gpp(e):
        e.memset(epsc[:], LN_EPS)
        e.memset(onesrb[:], 1.0)
        for r in range(8):
            ins = e.dma_start(out=wqbuf[:, r * FF:(r + 1) * FF],
                              in_=IN["wiT"][r * 128:(r + 1) * 128, :])
            ctx.inc(ins, "ldg", 16)
        for dst, src in ((gbrow, IN["gate_b"]), (bi_row, IN["wi_b"]), (bo_row, IN["wo_b"]),
                         (bcat, IN["Bcat"])):
            ins = e.dma_start(out=dst[:], in_=src)
            ctx.inc(ins, "ldg", 16)
        ins = e.dma_start(out=gws[:].rearrange("p (c n) -> p c n", n=E),
                          in_=IN["gateWT"].rearrange("(c p) n -> p c n", p=128))
        ctx.inc(ins, "ldg", 16)
        ins = e.dma_start(out=a2[:].rearrange("p (c n) -> p c n", n=32),
                          in_=IN["AcatT"].rearrange("(c p) n -> p c n", p=128))
        ctx.inc(ins, "ldg", 16)
        ctx.wait(e, "ldg")
        ctx.wait(e, "ld")
    phase(nc, f"{tag}_pro", _ctx=ctx, sync=syncp, gpsimd=gpp)
    if int(os.environ.get("L0F_STAGE", "9")) < 2:
        return

    BA = {k: ctx.val(k) for k in ("ld16", "va", "sr", "q", "tp", "ev", "mg",
                                  "sx", "mc", "m1", "hg", "hs")}

    def syncA(e):
        for t in range(NT):
            if t >= 2:
                e.wait_ge(ctx.h("q"), BA["q"] + 2 * (t - 2) + 1)
            e.dma_start(out=xt[:, (t % 2) * D:((t % 2) + 1) * D],
                        in_=x_in[t * 128:(t + 1) * 128, :]).then_inc(ctx.h("ld16"), 16)

    def gpA(e):
        for t in range(NT):
            e.wait_ge(ctx.h("hg"), BA["hg"] + 8 * (t + 1))
            e.dma_start(out=h_dram[t * 128:(t + 1) * 128, :],
                        in_=inter[:, (t % 2) * FF:((t % 2) + 1) * FF]).then_inc(ctx.h("hs"), 16)
        e.wait_ge(ctx.h("hs"), BA["hs"] + 16 * NT)

    def vectorA(e):
        for t in range(NT):
            TP, EV = BA["tp"] + 9 * t, BA["ev"] + 9 * t
            xs = xt[:, (t % 2) * D:((t % 2) + 1) * D]
            e.wait_ge(ctx.h("ld16"), BA["ld16"] + 16 * (t + 1))
            ln_stats(e, xs, D, st6, st2, sc[:, 0:1], sc[:, 1:2], "va", ctx)
            e.wait_ge(ctx.h("sr"), BA["sr"] + t + 1)
            e.tensor_scalar(xn[:], xs, sc[:, 0:1], r1c[:, 0:1], ALU.subtract, ALU.mult)
            e.tensor_copy(xnb[:], xn[:])
            ctx.sig(e, "q", 1)
            for j in range(8):
                e.wait_ge(ctx.h("tp"), TP + j + 1)
                e.tensor_copy(xnf[:, j * 128:(j + 1) * 128], ptr[j % 2][:])
                ctx.sig(e, "ev", 1)
            e.wait_ge(ctx.h("mg"), BA["mg"] + t + 1)
            m3 = os.environ.get("L0F_M3", "c") if os.environ.get("L0F_NOGATE", "0") in ("0", "3") else ""
            if m3 in ("a", "b", "c"):
                e.tensor_copy(mrep[:, 8:16], pm1[3][:, 0:E])
            if m3 in ("b", "c"):
                e.reduce_max(sc[:, 2:3], mrep[:, 8:16], axis=AX.X)
            if m3 == "c":
                e.tensor_scalar(mrep[:, 0:E], mrep[:, 8:16], sc[:, 2:3], None, ALU.is_ge)
            else:
                e.memset(mrep[:, 0:E], 1.0)
            ctx.sig(e, "va", 1)
            e.wait_ge(ctx.h("sx"), BA["sx"] + t + 1)
            e.wait_ge(ctx.h("mc"), BA["mc"] + t + 1)
            if os.environ.get("L0F_NOGATE", "0") == "0":
                e.tensor_copy(mrep[:, 0:32], pm2[0][:, 0:32])
                e.tensor_mul(mrep[:, 32:64], mrep[:, 32:64], mrep[:, 0:32])
                e.tensor_copy(cm[:], mrep[:, 32:64])
            else:
                e.memset(cm[:], 0.0)
            ctx.sig(e, "q", 1)
            e.wait_ge(ctx.h("tp"), TP + 9)
            e.tensor_copy(cmf[:], ptr[1][0:32, 0:128])
            ctx.sig(e, "ev", 1)

    def scalarA(e):
        for t in range(NT):
            e.wait_ge(ctx.h("va"), BA["va"] + 2 * t + 1)
            rstd_ops(e, sc[:, 1:2], sc[:, 12:13], r1c[:, 0:1], epsc[:])
            ctx.sig(e, "sr", 1)
            e.wait_ge(ctx.h("va"), BA["va"] + 2 * t + 2)
            if os.environ.get("L0F_NOGATE", "0") in ("0", "4"):
                src = bass.AP(ctx._sb[f"{tag}_mrep"], 0, [[64, 128], [1, E], [0, R]])
                e.activation(mrep[:, 32:64], src, AF.Identity)
            else:
                e.copy(sc[:, 14:15], sc[:, 0:1])
            ctx.sig(e, "sx", 1)
            hs = inter[:, (t % 2) * FF:((t % 2) + 1) * FF]
            if t >= 2:
                e.wait_ge(ctx.h("hs"), BA["hs"] + 16 * (t - 1))
            for g in range(8):
                e.wait_ge(ctx.h("m1"), BA["m1"] + 8 * t + g + 1)
                e.activation(hs[:, 512 * g:512 * (g + 1)], pm1[(8 * t + g) % 3][:], AF.Relu)
                ctx.sig(e, "hg", 1)

    def tensorA(e):
        for t in range(NT):
            TP, EV = BA["tp"] + 9 * t, BA["ev"] + 9 * t
            e.wait_ge(ctx.h("q"), BA["q"] + 2 * t + 1)
            e.wait_ge(ctx.h("ev"), EV)
            for j in range(8):
                if j >= 2:
                    e.wait_ge(ctx.h("ev"), EV + j - 1)
                e.transpose(ptr[j % 2][:], xnb[:, j * 128:(j + 1) * 128],
                            ident[:]).then_inc(ctx.h("tp"), 1)
            e.wait_ge(ctx.h("ev"), EV + 8)
            if t >= 1:
                e.wait_ge(ctx.h("q"), BA["q"] + 2 * t)
            if os.environ.get("L0F_NOGATE", "0") in ("0", "2", "3", "4"):
                for j in range(8):
                    e.matmul(pm1[3][:, 0:E], xnf[:, j * 128:(j + 1) * 128],
                             gws[:, j * E:(j + 1) * E], start=(j == 0), stop=False)
                e.matmul(pm1[3][:, 0:E], onesrb[:], gbrow[:], start=False, stop=True
                         ).then_inc(ctx.h("mg"), 1)
                for j in range(8):
                    mm = e.matmul(pm2[0][:, 0:32], xnf[:, j * 128:(j + 1) * 128],
                                  a2[:, j * 32:(j + 1) * 32], start=(j == 0), stop=(j == 7))
                mm.then_inc(ctx.h("mc"), 1)
            else:
                e.matmul(pm1[3][:, 0:E], onesrb[:], gbrow[:], start=True, stop=True
                         ).then_inc(ctx.h("mg"), 1)
                e.matmul(pm2[0][:, 0:32], onesrb[:], bcat[0:1, 0:32],
                         start=True, stop=True).then_inc(ctx.h("mc"), 1)
            e.wait_ge(ctx.h("q"), BA["q"] + 2 * t + 2)
            e.transpose(ptr[1][0:32, 0:128], cm[:, 0:32], ident[:]).then_inc(ctx.h("tp"), 1)
            e.wait_ge(ctx.h("ev"), EV + 9)
            for g in range(8):
                G = 8 * t + g
                if G >= 3:
                    e.wait_ge(ctx.h("hg"), BA["hg"] + G - 2)
                for j in range(8):
                    e.matmul(pm1[G % 3][:], xnf[:, j * 128:(j + 1) * 128],
                             wqbuf[:, j * FF + g * 512: j * FF + (g + 1) * 512],
                             start=(j == 0), stop=False)
                e.matmul(pm1[G % 3][:], cmf[:], bcat[0:32, g * 512:(g + 1) * 512],
                         start=False, stop=False)
                e.matmul(pm1[G % 3][:], onesrb[:], bi_row[0:1, g * 512:(g + 1) * 512],
                         start=False, stop=True).then_inc(ctx.h("m1"), 1)

    phase(nc, f"{tag}_A", _ctx=ctx, sync=syncA, gpsimd=gpA, vector=vectorA,
          scalar=scalarA, tensor=tensorA)
    for k, r in dict(ld16=16, va=2, sr=1, q=2, tp=9, ev=9, mg=1, sx=1, mc=1,
                     m1=8, hg=8, hs=16).items():
        ctx.setval(k, BA[k] + r * NT)

    if int(os.environ.get("L0F_STAGE", "9")) < 3:
        return
    def syncw(e):
        for r in range(32):
            ins = e.dma_start(out=wqbuf[:, r * D:(r + 1) * D],
                              in_=IN["woT"][r * 128:(r + 1) * 128, :])
            ctx.inc(ins, "ldg", 16)
        ctx.wait(e, "ldg")
    phase(nc, f"{tag}_wo", _ctx=ctx, gpsimd=syncw)

    BB = {k: ctx.val(k) for k in ("ld16", "hb", "va", "sr", "tp", "ev",
                                  "m2", "e2", "gl", "st")}

    def syncB(e):
        for t in range(NT):
            if t >= 2:
                e.wait_ge(ctx.h("tp"), BB["tp"] + 32 * (t - 1))
            e.dma_start(out=inter[:, (t % 2) * FF:((t % 2) + 1) * FF],
                        in_=h_dram[t * 128:(t + 1) * 128, :]).then_inc(ctx.h("hb"), 16)
            if t >= 2:
                e.wait_ge(ctx.h("gl"), BB["gl"] + t - 1)
            e.dma_start(out=xt[:, (t % 2) * D:((t % 2) + 1) * D],
                        in_=x_in[t * 128:(t + 1) * 128, :]).then_inc(ctx.h("ld16"), 16)

    def gpB(e):
        for t in range(NT):
            e.wait_ge(ctx.h("gl"), BB["gl"] + t + 1)
            e.dma_start(out=x_out[t * 128:(t + 1) * 128, :],
                        in_=ot[:, (t % 2) * D:((t % 2) + 1) * D]).then_inc(ctx.h("st"), 16)
        e.wait_ge(ctx.h("st"), BB["st"] + 16 * NT)

    def vectorB(e):
        for t in range(NT):
            TP, EV = BB["tp"] + 32 * t, BB["ev"] + 32 * t
            hbs = inter[:, (t % 2) * FF:((t % 2) + 1) * FF]
            for j in range(32):
                e.wait_ge(ctx.h("tp"), TP + j + 1)
                e.tensor_copy(intf[:, j * 128:(j + 1) * 128], ptr[j % 2][:])
                ctx.sig(e, "ev", 1)
            cs = ct[:, (t % 2) * D:((t % 2) + 1) * D]
            xs = xt[:, (t % 2) * D:((t % 2) + 1) * D]
            e.wait_ge(ctx.h("e2"), BB["e2"] + t + 1)
            e.wait_ge(ctx.h("ld16"), BB["ld16"] + 16 * (t + 1))
            ln_stats(e, cs, D, st6, st2, sc[:, 10:11], sc[:, 11:12], "va", ctx)
            e.wait_ge(ctx.h("sr"), BB["sr"] + t + 1)
            os_ = ot[:, (t % 2) * D:((t % 2) + 1) * D]
            if t >= 2:
                e.wait_ge(ctx.h("st"), BB["st"] + 16 * (t - 1))
            e.tensor_scalar(os_, cs, sc[:, 10:11], r1c[:, 3:4], ALU.subtract, ALU.mult)
            e.tensor_add(os_, os_, xs)
            ctx.sig(e, "gl", 1)

    def scalarB(e):
        for t in range(NT):
            cs = ct[:, (t % 2) * D:((t % 2) + 1) * D]
            for dg in range(2):
                e.wait_ge(ctx.h("m2"), BB["m2"] + 2 * t + dg + 1)
                e.activation(cs[:, 512 * dg:512 * (dg + 1)], pm2[dg][:], AF.Identity)
                if dg == 1:
                    ctx.sig(e, "e2", 1)
            e.wait_ge(ctx.h("va"), BB["va"] + t + 1)
            rstd_ops(e, sc[:, 11:12], sc[:, 12:13], r1c[:, 3:4], epsc[:])
            ctx.sig(e, "sr", 1)

    def tensorB(e):
        for t in range(NT):
            TP, EV = BB["tp"] + 32 * t, BB["ev"] + 32 * t
            e.wait_ge(ctx.h("hb"), BB["hb"] + 16 * (t + 1))
            e.wait_ge(ctx.h("ev"), EV)
            for j in range(32):
                if j >= 2:
                    e.wait_ge(ctx.h("ev"), EV + j - 1)
                e.transpose(ptr[j % 2][:], hbs_ap(inter, t, j),
                            ident[:]).then_inc(ctx.h("tp"), 1)
            e.wait_ge(ctx.h("ev"), EV + 32)
            if t >= 1:
                e.wait_ge(ctx.h("e2"), BB["e2"] + t)
            for dg in range(2):
                for j in range(32):
                    e.matmul(pm2[dg][:], intf[:, j * 128:(j + 1) * 128],
                             wqbuf[:, j * D + dg * 512: j * D + (dg + 1) * 512],
                             start=(j == 0), stop=False)
                e.matmul(pm2[dg][:], onesrb[:], bo_row[0:1, dg * 512:(dg + 1) * 512],
                         start=False, stop=True).then_inc(ctx.h("m2"), 1)

    phase(nc, f"{tag}_B", _ctx=ctx, sync=syncB, gpsimd=gpB, vector=vectorB,
          scalar=scalarB, tensor=tensorB)
    for k, r in dict(ld16=16, hb=16, va=1, sr=1, tp=32, ev=32, m2=2,
                     e2=1, gl=1, st=16).items():
        ctx.setval(k, BB[k] + r * NT)


def hbs_ap(inter, t, j):
    FF = 4096
    return inter[:, (t % 2) * FF + j * 128:(t % 2) * FF + (j + 1) * 128]


def build_l0f_program():
    es = contextlib.ExitStack()
    nc = mk_bass()
    IN = {}
    IN["x"] = nc.dram_tensor("x", [T_LOC, D], F32, kind="ExternalInput").ap()
    IN["wiT"] = nc.dram_tensor("wiT", [D, 4096], F32, kind="ExternalInput").ap()
    IN["woT"] = nc.dram_tensor("woT", [4096, D], F32, kind="ExternalInput").ap()
    IN["gateWT"] = nc.dram_tensor("gateWT", [D, 8], F32, kind="ExternalInput").ap()
    IN["gate_b"] = nc.dram_tensor("gate_b", [1, 8], F32, kind="ExternalInput").ap()
    IN["wi_b"] = nc.dram_tensor("wi_b", [1, 4096], F32, kind="ExternalInput").ap()
    IN["wo_b"] = nc.dram_tensor("wo_b", [1, D], F32, kind="ExternalInput").ap()
    IN["AcatT"] = nc.dram_tensor("AcatT", [D, 32], F32, kind="ExternalInput").ap()
    IN["Bcat"] = nc.dram_tensor("Bcat", [32, 4096], F32, kind="ExternalInput").ap()
    IN["identC"] = nc.dram_tensor("identC", [128, 128], BF16, kind="ExternalInput").ap()
    IN["x_out"] = nc.dram_tensor("x_out", [T_LOC, D], F32, kind="ExternalOutput").ap()
    IN["h_spill"] = nc.dram_tensor("h_spill", [T_LOC, 4096], BF16).ap()
    ctx = Ctx(nc, es)
    emit_l0f(ctx, IN)
    es.close()
    return nc


def l0f_in_maps(x_full, ins):
    import ml_dtypes
    identC = np.eye(128, dtype=ml_dtypes.bfloat16)
    f32 = lambda a: np.ascontiguousarray(np.asarray(a, dtype=np.float32))
    wiT = f32(ins["L0_ffn_wi_W"].T)
    woT = f32(ins["L0_ffn_wo_W"].T)
    gateWT = f32(ins["L0_ffn_gate_W"].T)
    AcatT = f32(ins["L0_ffn_lora_A"].reshape(32, D).T)
    Bcat = f32(ins["L0_ffn_lora_B"].transpose(0, 2, 1).reshape(32, 4096))
    maps = []
    for c in range(NCORE):
        b, h = c // 2, c % 2
        shard = f32(x_full[b, h * T_LOC:(h + 1) * T_LOC, :])
        maps.append(dict(
            x=shard, wiT=wiT, woT=woT, gateWT=gateWT,
            gate_b=row2d(ins["L0_ffn_gate_b"]), wi_b=row2d(ins["L0_ffn_wi_b"]),
            wo_b=row2d(ins["L0_ffn_wo_b"]), AcatT=AcatT, Bcat=Bcat, identC=identC,
        ))
    return maps




def emit_l1f(ctx, IN, tag="l1f"):
    nc = ctx.nc
    FF = 4096

    wqbuf = ctx.sb("wqbuf", [128, 32768], BF16)
    ident = ctx.sb("identb", [128, 128], BF16)
    epsc = ctx.sb("epscol", [128, 1], F32)
    mwc1 = ctx.sb(f"{tag}_mwc1", [128, 2], F32)
    mwc2 = ctx.sb(f"{tag}_mwc2", [128, 2], F32)
    w0b = ctx.sb(f"{tag}_w0b", [128, FF], BF16)
    w1b = ctx.sb(f"{tag}_w1b", [128, FF], BF16)
    w2b = ctx.sb(f"{tag}_w2b", [128, FF], BF16)
    cbb = ctx.sb(f"{tag}_cbb", [128, FF], BF16)
    zrow = ctx.sb(f"{tag}_zrow", [1, 4096], BF16)
    halomL = ctx.sb(f"{tag}_hL", [128, 1], F32)
    halomR = ctx.sb(f"{tag}_hR", [128, 1], F32)
    meta = ctx.sb("metai", [1, 48], I32)

    xt = ctx.sb("g_xt", [128, 2 * D], F32)
    wstg = ctx.sb("wstg", [128, FF], F32)
    ut = wstg[:, 0:D]
    xq1t = ctx.sb(f"{tag}_xq1t", [128, D], BF16)
    xq1f = ctx.sb(f"{tag}_xq1f", [128, D], BF16)
    ht = ctx.sb(f"{tag}_ht", [128, 2 * FF], BF16)
    h0 = ctx.sb(f"{tag}_h0", [128, FF], BF16)
    hm1 = ctx.sb(f"{tag}_hm1", [128, FF], BF16)
    hp1 = ctx.sb(f"{tag}_hp1", [128, FF], BF16)
    ys = ctx.sb(f"{tag}_ys", [128, FF], BF16)
    xq2f = ctx.sb(f"{tag}_xq2f", [128, FF], BF16)
    ct = ctx.sb("g_ct", [128, 2 * D], F32)
    ot = ctx.sb("g_ot", [128, 2 * D], F32)
    sc = ctx.sb("g_sc", [128, 16], F32)
    st6 = ctx.sb("g_st6", [128, 48], F32)
    st2 = ctx.sb("g_st2", [128, 2], F32)
    r1c = ctx.sb("g_r1c", [128, 4], F32)
    rs1 = ctx.sb(f"{tag}_rs1", [128, 1], F32)
    rs2 = ctx.sb(f"{tag}_rs2", [128, 1], F32)

    pm1 = [ctx.ps(f"ps_pm1_{i}", [128, 512], F32) for i in range(4)]
    pm2 = [ctx.ps(f"ps_pm2_{i}", [128, 512], F32) for i in range(2)]
    ptr = [ctx.ps(f"ps_ptr_{i}", [128, 128], BF16) for i in range(2)]

    x_in, x_out = IN["x"], IN["x_out"]
    h_sh, flags_t = IN["h_sh"], IN["flags"]

    def syncp(e):
        ins = e.dma_start(out=ident[:], in_=IN["identC"])
        ctx.inc(ins, "ld", 16)

    def gpp(e):
        e.memset(epsc[:], LN_EPS)
        e.memset(zrow[:], 0.0)
        ins = e.dma_start(out=IN["h_sh"][0:1, :], in_=zrow[:])
        ctx.inc(ins, "ldg", 16)
        ins = e.dma_start(out=IN["h_sh"][2 * T_LOC + 1:2 * T_LOC + 2, :], in_=zrow[:])
        ctx.inc(ins, "ldg", 16)
        for dst, src in ((w0b, IN["w0r"]), (w1b, IN["w1r"]), (w2b, IN["w2r"]),
                         (cbb, IN["cbr"])):
            ins = e.dma_start(out=dst[:], in_=src)
            ctx.inc(ins, "ldg", 16)
        for dst, src in ((halomL, IN["halomL"]), (halomR, IN["halomR"]),
                         (meta, IN["meta"])):
            ins = e.dma_start(out=dst[:], in_=src)
            ctx.inc(ins, "ldg", 16)
        ctx.wait(e, "ldg")
        ctx.wait(e, "ld")
    phase(nc, f"{tag}_pro", _ctx=ctx, sync=syncp, gpsimd=gpp)

    emit_weight_quant(ctx, IN["w1T"], D, FF, wqbuf, mwc1, tag + "w1", wstg, None)

    BA = {k: ctx.val(k) for k in ("ld16", "va", "sr", "q", "tp", "ev", "m1", "hg", "hs")}

    def syncA(e):
        for t in range(NT):
            if t >= 2:
                e.wait_ge(ctx.h("q"), BA["q"] + t - 1)
            e.dma_start(out=xt[:, (t % 2) * D:((t % 2) + 1) * D],
                        in_=x_in[t * 128:(t + 1) * 128, :]).then_inc(ctx.h("ld16"), 16)

    def gpA(e):
        with e.register("parA") as par:
            e.reg_load(par, meta[0:1, 0:1])
            for branch, off in ((0, 1), (1, 1 + T_LOC)):
                guard = e.If_eq(par, 0) if branch == 0 else e.Else()
                with guard:
                    for t in range(NT):
                        e.wait_ge(ctx.h("hg"), BA["hg"] + 8 * (t + 1))
                        e.dma_start(out=h_sh[off + t * 128:off + (t + 1) * 128, :],
                                    in_=ht[:, (t % 2) * FF:((t % 2) + 1) * FF]
                                    ).then_inc(ctx.h("hs"), 16)
        e.wait_ge(ctx.h("hs"), BA["hs"] + 16 * NT)

    def vectorA(e):
        for t in range(NT):
            TP, EV = BA["tp"] + 8 * t, BA["ev"] + 8 * t
            xs = xt[:, (t % 2) * D:((t % 2) + 1) * D]
            e.wait_ge(ctx.h("ld16"), BA["ld16"] + 16 * (t + 1))
            ln_stats(e, xs, D, st6, st2, sc[:, 0:1], sc[:, 1:2], "va", ctx)
            e.wait_ge(ctx.h("sr"), BA["sr"] + t + 1)
            e.tensor_scalar(ut[:], xs, sc[:, 0:1], r1c[:, 0:1], ALU.subtract, ALU.mult)
            e.reduce_max(sc[:, 4:5], ut[:], axis=AX.X, apply_absolute_value=True)
            e.tensor_scalar_max(sc[:, 4:5], sc[:, 4:5], 1e-5)
            e.reciprocal(sc[:, 5:6], sc[:, 4:5])
            e.tensor_scalar(ut[:], ut[:], sc[:, 5:6], 127.0, ALU.mult, ALU.mult)
            e.tensor_scalar(xq1t[:], ut[:], RMAGIC, -RMAGIC, ALU.add, ALU.add)
            e.tensor_scalar(rs1[:], sc[:, 4:5], 1.0 / 127.0, mwc1[:, 0:1], ALU.mult, ALU.mult)
            ctx.sig(e, "q", 1)
            for j in range(8):
                e.wait_ge(ctx.h("tp"), TP + 1 + j)
                e.tensor_copy(xq1f[:, j * 128:(j + 1) * 128], ptr[j % 2][:])
                ctx.sig(e, "ev", 1)

    def scalarA(e):
        for t in range(NT):
            e.wait_ge(ctx.h("va"), BA["va"] + t + 1)
            rstd_ops(e, sc[:, 1:2], sc[:, 12:13], r1c[:, 0:1], epsc[:])
            ctx.sig(e, "sr", 1)
            hs = ht[:, (t % 2) * FF:((t % 2) + 1) * FF]
            if t >= 2:
                e.wait_ge(ctx.h("hs"), BA["hs"] + 16 * (t - 1))
            for g in range(8):
                e.wait_ge(ctx.h("m1"), BA["m1"] + 8 * t + g + 1)
                e.activation(hs[:, 512 * g:512 * (g + 1)], pm1[(8 * t + g) % 4][:],
                             AF.Identity, scale=rs1[:])
                ctx.sig(e, "hg", 1)

    def tensorA(e):
        for t in range(NT):
            TP, EV = BA["tp"] + 8 * t, BA["ev"] + 8 * t
            e.wait_ge(ctx.h("q"), BA["q"] + t + 1)
            e.wait_ge(ctx.h("ev"), EV)
            for j in range(8):
                if j >= 2:
                    e.wait_ge(ctx.h("ev"), EV + j - 1)
                e.transpose(ptr[j % 2][:], xq1t[:, j * 128:(j + 1) * 128],
                            ident[:]).then_inc(ctx.h("tp"), 1)
            e.wait_ge(ctx.h("ev"), EV + 8)
            for g in range(8):
                G = 8 * t + g
                if G >= 4:
                    e.wait_ge(ctx.h("hg"), BA["hg"] + G - 3)
                for j in range(8):
                    mm = e.matmul(pm1[G % 4][:], xq1f[:, j * 128:(j + 1) * 128],
                                  wqbuf[:, j * FF + g * 512: j * FF + (g + 1) * 512],
                                  start=(j == 0), stop=(j == 7))
                mm.then_inc(ctx.h("m1"), 1)

    phase(nc, f"{tag}_A", _ctx=ctx, sync=syncA, gpsimd=gpA, vector=vectorA,
          scalar=scalarA, tensor=tensorA)
    for k, r in dict(ld16=16, va=1, sr=1, q=1, tp=8, ev=8, m1=8, hg=8, hs=16).items():
        ctx.setval(k, BA[k] + r * NT)

    emit_pair_sync(ctx, tag, IN["flag_k"], flags_t, meta)

    emit_weight_quant(ctx, IN["w2T"], FF, D, wqbuf, mwc2, tag + "w2", wstg, None)

    BB = {k: ctx.val(k) for k in ("ld16", "hb", "va", "sr", "q", "tp", "ev",
                                  "m2", "e2", "gl", "st", "v2g")}

    def syncB(e):
        with e.register("parB") as par:
            e.reg_load(par, meta[0:1, 0:1])
            for branch, off in ((0, 1), (1, 1 + T_LOC)):
                guard = e.If_eq(par, 0) if branch == 0 else e.Else()
                with guard:
                    for t in range(NT):
                        if t >= 1:
                            e.wait_ge(ctx.h("v2g"), BB["v2g"] + t)
                        r0 = off + t * 128
                        e.dma_start(out=hm1[:], in_=h_sh[r0 - 1:r0 + 127, :]).then_inc(ctx.h("hb"), 16)
                        e.dma_start(out=h0[:], in_=h_sh[r0:r0 + 128, :]).then_inc(ctx.h("hb"), 16)
                        e.dma_start(out=hp1[:], in_=h_sh[r0 + 1:r0 + 129, :]).then_inc(ctx.h("hb"), 16)
                        if t >= 2:
                            e.wait_ge(ctx.h("gl"), BB["gl"] + t - 1)
                        e.dma_start(out=xt[:, (t % 2) * D:((t % 2) + 1) * D],
                                    in_=x_in[t * 128:(t + 1) * 128, :]).then_inc(ctx.h("ld16"), 16)

    def gpB(e):
        for t in range(NT):
            e.wait_ge(ctx.h("gl"), BB["gl"] + t + 1)
            e.dma_start(out=x_out[t * 128:(t + 1) * 128, :],
                        in_=ot[:, (t % 2) * D:((t % 2) + 1) * D]).then_inc(ctx.h("st"), 16)
        e.wait_ge(ctx.h("st"), BB["st"] + 16 * NT)

    def vectorB(e):
        for t in range(NT):
            TP, EV = BB["tp"] + 32 * t, BB["ev"] + 32 * t
            e.wait_ge(ctx.h("hb"), BB["hb"] + 48 * (t + 1))
            if t >= 1:
                e.wait_ge(ctx.h("sr"), BB["sr"] + 2 * t - 1)
                e.wait_ge(ctx.h("tp"), BB["tp"] + 32 * t)
            e.tensor_tensor(wstg[:], hm1[:], w0b[:], ALU.mult)
            if t == 0:
                e.tensor_scalar_mul(wstg[:], wstg[:], halomL[:])
            e.tensor_tensor(ys[:], h0[:], w1b[:], ALU.mult)
            e.tensor_add(wstg[:], wstg[:], ys[:])
            e.tensor_tensor(ys[:], hp1[:], w2b[:], ALU.mult)
            if t == NT - 1:
                e.tensor_scalar_mul(ys[:], ys[:], halomR[:])
            e.tensor_add(wstg[:], wstg[:], ys[:])
            e.tensor_add(wstg[:], wstg[:], cbb[:])
            ctx.sig(e, "v2g", 1)
            ctx.sig(e, "va", 1)
            e.wait_ge(ctx.h("sr"), BB["sr"] + 2 * t + 1)
            e.reduce_max(sc[:, 8:9], ys[:], axis=AX.X, apply_absolute_value=True)
            e.tensor_scalar_max(sc[:, 8:9], sc[:, 8:9], 1e-5)
            e.reciprocal(sc[:, 9:10], sc[:, 8:9])
            e.tensor_scalar(wstg[:], ys[:], sc[:, 9:10], 127.0, ALU.mult, ALU.mult)
            e.tensor_scalar(ys[:], wstg[:], RMAGIC, -RMAGIC, ALU.add, ALU.add)
            e.tensor_scalar(rs2[:], sc[:, 8:9], 1.0 / 127.0, mwc2[:, 0:1], ALU.mult, ALU.mult)
            ctx.sig(e, "q", 1)
            for j in range(32):
                e.wait_ge(ctx.h("tp"), TP + 1 + j)
                e.tensor_copy(xq2f[:, j * 128:(j + 1) * 128], ptr[j % 2][:])
                ctx.sig(e, "ev", 1)
            cs = ct[:, (t % 2) * D:((t % 2) + 1) * D]
            xs = xt[:, (t % 2) * D:((t % 2) + 1) * D]
            e.wait_ge(ctx.h("e2"), BB["e2"] + t + 1)
            e.wait_ge(ctx.h("ld16"), BB["ld16"] + 16 * (t + 1))
            ln_stats(e, cs, D, st6, st2, sc[:, 10:11], sc[:, 11:12], "va", ctx)
            e.wait_ge(ctx.h("sr"), BB["sr"] + 2 * t + 2)
            os_ = ot[:, (t % 2) * D:((t % 2) + 1) * D]
            if t >= 2:
                e.wait_ge(ctx.h("st"), BB["st"] + 16 * (t - 1))
            e.tensor_scalar(os_, cs, sc[:, 10:11], r1c[:, 3:4], ALU.subtract, ALU.mult)
            e.tensor_add(os_, os_, xs)
            ctx.sig(e, "gl", 1)

    def scalarB(e):
        for t in range(NT):
            e.wait_ge(ctx.h("va"), BB["va"] + 2 * t + 1)
            if t >= 1:
                e.wait_ge(ctx.h("tp"), BB["tp"] + 32 * t)
            e.activation(ys[:], wstg[:], AF.Silu)
            ctx.sig(e, "sr", 1)
            cs = ct[:, (t % 2) * D:((t % 2) + 1) * D]
            for dg in range(2):
                e.wait_ge(ctx.h("m2"), BB["m2"] + 2 * t + dg + 1)
                e.activation(cs[:, 512 * dg:512 * (dg + 1)], pm2[dg][:],
                             AF.Identity, scale=rs2[:])
                if dg == 1:
                    ctx.sig(e, "e2", 1)
            e.wait_ge(ctx.h("va"), BB["va"] + 2 * t + 2)
            rstd_ops(e, sc[:, 11:12], sc[:, 12:13], r1c[:, 3:4], epsc[:])
            ctx.sig(e, "sr", 1)

    def tensorB(e):
        for t in range(NT):
            TP, EV = BB["tp"] + 32 * t, BB["ev"] + 32 * t
            e.wait_ge(ctx.h("q"), BB["q"] + t + 1)
            e.wait_ge(ctx.h("ev"), EV)
            for j in range(32):
                if j >= 2:
                    e.wait_ge(ctx.h("ev"), EV + j - 1)
                e.transpose(ptr[j % 2][:], ys[:, j * 128:(j + 1) * 128],
                            ident[:]).then_inc(ctx.h("tp"), 1)
            e.wait_ge(ctx.h("ev"), EV + 32)
            if t >= 1:
                e.wait_ge(ctx.h("e2"), BB["e2"] + t)
            for dg in range(2):
                for j in range(32):
                    mm = e.matmul(pm2[dg][:], xq2f[:, j * 128:(j + 1) * 128],
                                  wqbuf[:, j * D + dg * 512: j * D + (dg + 1) * 512],
                                  start=(j == 0), stop=(j == 31))
                mm.then_inc(ctx.h("m2"), 1)

    phase(nc, f"{tag}_B", _ctx=ctx, sync=syncB, gpsimd=gpB, vector=vectorB,
          scalar=scalarB, tensor=tensorB)
    for k, r in dict(ld16=16, hb=48, va=2, sr=2, q=1, tp=32, ev=32, m2=2,
                     e2=1, gl=1, st=16, v2g=1).items():
        ctx.setval(k, BB[k] + r * NT)


def build_l1f_program():
    es = contextlib.ExitStack()
    nc = mk_bass()
    IN = {}
    IN["x"] = nc.dram_tensor("x", [T_LOC, D], F32, kind="ExternalInput").ap()
    IN["w1T"] = nc.dram_tensor("w1T", [D, 4096], F32, kind="ExternalInput").ap()
    IN["w2T"] = nc.dram_tensor("w2T", [4096, D], F32, kind="ExternalInput").ap()
    for nm in ("w0r", "w1r", "w2r", "cbr"):
        IN[nm] = nc.dram_tensor(nm, [128, 4096], F32, kind="ExternalInput").ap()
    for nm in ("halomL", "halomR"):
        IN[nm] = nc.dram_tensor(nm, [128, 1], F32, kind="ExternalInput").ap()
    IN["meta"] = nc.dram_tensor("meta", [1, 48], I32, kind="ExternalInput").ap()
    IN["identC"] = nc.dram_tensor("identC", [128, 128], BF16, kind="ExternalInput").ap()
    IN["x_out"] = nc.dram_tensor("x_out", [T_LOC, D], F32, kind="ExternalOutput").ap()
    IN["h_sh"] = nc.dram_tensor("h_sh", [2 * T_LOC + 2, 4096], BF16, addr_space="Shared").ap()
    IN["flags"] = nc.dram_tensor("flags", [2, 64], I32, addr_space="Shared").ap()
    IN["flag_k"] = 0
    ctx = Ctx(nc, es)
    emit_l1f(ctx, IN)
    es.close()
    return nc


def l1f_in_maps(x_full, ins, nonce=None):
    import ml_dtypes
    identC = np.eye(128, dtype=ml_dtypes.bfloat16)
    f32 = lambda a: np.ascontiguousarray(np.asarray(a, dtype=np.float32))
    w1T = f32(ins["L1_ffn_w1"].T)
    w2T = f32(ins["L1_ffn_w2"].T)
    cw = np.asarray(ins["L1_ffn_conv_w"], dtype=np.float32)[:, 0, :]
    if nonce is None:
        nonce = int(np.random.randint(1, 2 ** 30))
    maps = []
    for c in range(NCORE):
        b, h = c // 2, c % 2
        shard = f32(x_full[b, h * T_LOC:(h + 1) * T_LOC, :])
        hl = np.ones((128, 1), np.float32)
        hr = np.ones((128, 1), np.float32)
        if h == 0:
            hl[0, 0] = 0.0
        if h == 1:
            hr[127, 0] = 0.0
        maps.append(dict(
            x=shard, w1T=w1T, w2T=w2T,
            w0r=rep128(cw[:, 0]), w1r=rep128(cw[:, 1]), w2r=rep128(cw[:, 2]),
            cbr=rep128(ins["L1_ffn_conv_b"]),
            halomL=hl, halomR=hr,
            meta=np.array([[h, nonce] + [nonce + kk + 1 for kk in range(46)]],
                          dtype=np.int32),
            identC=identC,
        ))
    return maps



def emit_l3a(ctx, IN, tag="l3a"):
    nc = ctx.nc
    FI = 2048
    NC2 = FI // 128
    NGRP = 4

    wqbuf = ctx.sb("wqbuf", [128, 32768], BF16)
    wot = ctx.sb(f"{tag}_wot", [128, NC2 * D], BF16)
    ident = ctx.sb("identb", [128, 128], BF16)
    epsc = ctx.sb("epscol", [128, 1], F32)
    onec = ctx.sb("onecolf2", [128, 1], F32)
    meta = ctx.sb("metai", [1, 48], I32)
    acol = ctx.sb(f"{tag}_acol", [128, NC2], F32)
    bcc = ctx.sb(f"{tag}_bcc", [128, NC2], F32)
    bdtc = ctx.sb(f"{tag}_bdtc", [128, NC2], F32)
    cwc = ctx.sb(f"{tag}_cwc", [128, 4 * NC2], F32)
    zcol = ctx.sb(f"{tag}_zcol", [128, 4], BF16)
    pollcell = ctx.sb("pollcell", [1, 2], I32)

    xt = ctx.sb("g_xt", [128, 2 * D], F32)
    xn = ctx.sb(f"{tag}_xn", [128, D], F32)
    xnb = ctx.sb(f"{tag}_xnb", [128, D], BF16)
    wtile = ctx.sb(f"{tag}_wtile", [128, 2 * FI], BF16)
    x1c = wtile[:, 0:1024]
    x1w = ctx.sb(f"{tag}_x1w", [128, 2 * (T_LOC + 3)], BF16)
    yld = ctx.sb(f"{tag}_yld", [128, NC2 * 128], BF16)
    ybuf = ctx.sb(f"{tag}_ybuf", [128, 2 * T_LOC], BF16)
    xnf = ybuf
    dtc = ctx.sb(f"{tag}_dtc", [128, T_LOC], F32)
    btc = ctx.sb(f"{tag}_btc", [128, T_LOC], F32)
    uc = ctx.sb(f"{tag}_uc", [128, T_LOC], F32)
    alc = ctx.sb(f"{tag}_alc", [128, T_LOC], F32)
    ctc = ctx.sb(f"{tag}_ctc", [128, T_LOC], F32)
    hcc = ctx.sb(f"{tag}_hcc", [128, T_LOC], F32)
    szc = ctx.sb(f"{tag}_szc", [128, T_LOC], BF16)
    uedge = ctx.sb(f"{tag}_uedge", [128, 1], F32)
    hinit = ctx.sb(f"{tag}_hinit", [128, 1], F32)
    ct = ctx.sb("g_ct", [128, 2 * D], F32)
    ot = ctx.sb("g_ot", [128, 2 * D], F32)
    sc = ctx.sb("g_sc", [128, 16], F32)
    st6 = ctx.sb("g_st6", [128, 48], F32)
    st2 = ctx.sb("g_st2", [128, 2], F32)
    r1c = ctx.sb("g_r1c", [128, 4], F32)

    pm1 = [ctx.ps(f"ps_pm1_{i}", [128, 512], F32) for i in range(4)]
    pm2 = [ctx.ps(f"ps_pm2_{i}", [128, 512], F32) for i in range(2)]
    ptr = [ctx.ps(f"ps_ptr_{i}", [128, 128], BF16) for i in range(2)]

    x_in, x_out = IN["x"], IN["x_out"]
    x1_sh, ush, hend, flags_t = IN["x1_sh"], IN["u_sh"], IN["hend_sh"], IN["flags"]
    zd = IN["z_d"]
    FB = IN["flag_base"]

    def syncp(e):
        ins = e.dma_start(out=ident[:], in_=IN["identC"])
        ctx.inc(ins, "ld", 16)

    def gpp(e):
        e.memset(epsc[:], LN_EPS)
        e.memset(onec[:], 1.0)
        e.memset(zcol[:], 0.0)
        e.memset(hinit[:], 0.0)
        e.memset(uedge[:], 0.0)
        for dst, src in ((meta, IN["meta"]), (acol, IN["alog2"]), (bcc, IN["bconv2"]),
                         (bdtc, IN["bdt2"]), (cwc, IN["convw2"])):
            ins = e.dma_start(out=dst[:], in_=src)
            ctx.inc(ins, "ldg", 16)
        for c in range(NC2):
            ins = e.dma_start(out=x1_sh[c * 128:(c + 1) * 128, 0:3], in_=zcol[:, 0:3])
            ctx.inc(ins, "ldg", 16)
        for r in range(8):
            ins = e.dma_start(out=wqbuf[:, r * 4096:(r + 1) * 4096],
                              in_=IN["WiT"][r * 128:(r + 1) * 128, :])
            ctx.inc(ins, "ldg", 16)
        for r in range(NC2):
            ins = e.dma_start(out=wot[:, r * D:(r + 1) * D],
                              in_=IN["WoT"][r * 128:(r + 1) * 128, :])
            ctx.inc(ins, "ldg", 16)
        ctx.wait(e, "ldg")
        ctx.wait(e, "ld")
    phase(nc, f"{tag}_pro", _ctx=ctx, sync=syncp, gpsimd=gpp)

    def scp(e):
        e.activation(acol[:], acol[:], AF.Exp)
        ctx.sig(e, "sr0", 1)

    def vcp2(e):
        ctx.wait(e, "sr0")
        e.tensor_scalar_mul(acol[:], acol[:], -1.0)
    phase(nc, f"{tag}_pro2", _ctx=ctx, scalar=scp, vector=vcp2)

    BA = {k: ctx.val(k) for k in ("ld16", "va", "sr", "q", "tp", "ev", "m1", "hg", "hs")}

    def syncA(e):
        for t in range(NT):
            if t >= 2:
                e.wait_ge(ctx.h("q"), BA["q"] + t - 1)
            e.dma_start(out=xt[:, (t % 2) * D:((t % 2) + 1) * D],
                        in_=x_in[t * 128:(t + 1) * 128, :]).then_inc(ctx.h("ld16"), 16)

    def vectorA(e):
        for t in range(NT):
            g, ti = t // 4, t % 4
            TP = BA["tp"] + 8 * t
            xs = xt[:, (t % 2) * D:((t % 2) + 1) * D]
            e.wait_ge(ctx.h("ld16"), BA["ld16"] + 16 * (t + 1))
            ln_stats(e, xs, D, st6, st2, sc[:, 0:1], sc[:, 1:2], "va", ctx)
            e.wait_ge(ctx.h("sr"), BA["sr"] + t + 1)
            e.tensor_scalar(xn[:], xs, sc[:, 0:1], r1c[:, 0:1], ALU.subtract, ALU.mult)
            e.tensor_copy(xnb[:], xn[:])
            ctx.sig(e, "q", 1)
            for j in range(8):
                e.wait_ge(ctx.h("tp"), TP + 1 + j)
                e.tensor_copy(xnf[:, j * 512 + ti * 128:j * 512 + (ti + 1) * 128],
                              ptr[j % 2][:])
                ctx.sig(e, "ev", 1)

    def scalarA(e):
        for g in range(NGRP):
            for ti in range(4):
                t = 4 * g + ti
                e.wait_ge(ctx.h("va"), BA["va"] + t + 1)
                rstd_ops(e, sc[:, 1:2], sc[:, 12:13], r1c[:, 0:1], epsc[:])
                ctx.sig(e, "sr", 1)
            for c in range(32):
                G = 32 * g + c
                e.wait_ge(ctx.h("m1"), BA["m1"] + G + 1)
                if G >= 2:
                    e.wait_ge(ctx.h("hs"), BA["hs"] + 16 * (G - 1))
                e.activation(x1c[:, (G % 2) * 512:(G % 2 + 1) * 512], pm1[G % 4][:],
                             AF.Identity if c < 16 else AF.Silu)
                ctx.sig(e, "hg", 1)

    def gpA(e):
        with e.register("parA3") as par:
            e.reg_load(par, meta[0:1, 0:1])
            for branch in (0, 1):
                off = 3 + branch * T_LOC
                guard = e.If_eq(par, 0) if branch == 0 else e.Else()
                with guard:
                    for g in range(NGRP):
                        for c in range(32):
                            G = 32 * g + c
                            e.wait_ge(ctx.h("hg"), BA["hg"] + G + 1)
                            if c < 16:
                                dst = x1_sh[c * 128:(c + 1) * 128,
                                            off + g * 512:off + (g + 1) * 512]
                            else:
                                dst = zd[(c - 16) * 128:(c - 15) * 128,
                                         g * 512:(g + 1) * 512]
                            e.dma_start(out=dst,
                                        in_=x1c[:, (G % 2) * 512:(G % 2 + 1) * 512]
                                        ).then_inc(ctx.h("hs"), 16)
        e.wait_ge(ctx.h("hs"), BA["hs"] + 16 * 128)

    def tensorA(e):
        for g in range(NGRP):
            for ti in range(4):
                t = 4 * g + ti
                TP, EV = BA["tp"] + 8 * t, BA["ev"] + 8 * t
                e.wait_ge(ctx.h("q"), BA["q"] + t + 1)
                e.wait_ge(ctx.h("ev"), EV)
                for j in range(8):
                    if j >= 2:
                        e.wait_ge(ctx.h("ev"), EV + j - 1)
                    e.transpose(ptr[j % 2][:], xnb[:, j * 128:(j + 1) * 128],
                                ident[:]).then_inc(ctx.h("tp"), 1)
            e.wait_ge(ctx.h("ev"), BA["ev"] + 8 * (4 * g + 4))
            for c in range(32):
                G = 32 * g + c
                if G >= 4:
                    e.wait_ge(ctx.h("hg"), BA["hg"] + G - 3)
                for j in range(8):
                    mm = e.matmul(pm1[G % 4][:],
                                  wqbuf[:, j * 4096 + c * 128:j * 4096 + (c + 1) * 128],
                                  xnf[:, j * 512:(j + 1) * 512],
                                  start=(j == 0), stop=(j == 7))
                mm.then_inc(ctx.h("m1"), 1)

    phase(nc, f"{tag}_A", _ctx=ctx, sync=syncA, gpsimd=gpA, vector=vectorA,
          scalar=scalarA, tensor=tensorA)
    for k, r in dict(ld16=16, va=1, sr=1, q=1, tp=8, ev=8).items():
        ctx.setval(k, BA[k] + r * NT)
    for k, r in dict(m1=128, hg=128, hs=16 * 128).items():
        ctx.setval(k, BA[k] + r)

    emit_pair_sync(ctx, tag, FB + 33, flags_t, meta)

    BC = {k: ctx.val(k) for k in ("lb1", "vcv", "scv")}

    def gpB1(e):
        with e.register("parB31") as par:
            e.reg_load(par, meta[0:1, 0:1])
            for branch in (0, 1):
                off = 3 + branch * T_LOC
                guard = e.If_eq(par, 0) if branch == 0 else e.Else()
                with guard:
                    for c in range(NC2):
                        if c >= 2:
                            e.wait_ge(ctx.h("vcv"), BC["vcv"] + c - 1)
                        e.dma_start(out=x1w[:, (c % 2) * 2051:(c % 2) * 2051 + 2051],
                                    in_=x1_sh[c * 128:(c + 1) * 128, off - 3:off + T_LOC]
                                    ).then_inc(ctx.h("lb1"), 16)

    def vecB1(e):
        for c in range(NC2):
            e.wait_ge(ctx.h("lb1"), BC["lb1"] + 16 * (c + 1))
            if c >= 2:
                e.wait_ge(ctx.h("scv"), BC["scv"] + c - 1)
            w = x1w[:, (c % 2) * 2051:(c % 2) * 2051 + 2051]
            dslot = dtc[:] if c % 2 == 0 else btc[:]
            e.tensor_scalar_mul(dslot, w[:, 0:T_LOC], cwc[:, c:c + 1])
            e.scalar_tensor_tensor(dslot, w[:, 1:T_LOC + 1], cwc[:, NC2 + c:NC2 + c + 1],
                                   dslot, ALU.mult, ALU.add)
            e.scalar_tensor_tensor(dslot, w[:, 2:T_LOC + 2], cwc[:, 2 * NC2 + c:2 * NC2 + c + 1],
                                   dslot, ALU.mult, ALU.add)
            e.scalar_tensor_tensor(dslot, w[:, 3:T_LOC + 3], cwc[:, 3 * NC2 + c:3 * NC2 + c + 1],
                                   dslot, ALU.mult, ALU.add)
            ctx.sig(e, "vcv", 1)

    def scB1(e):
        for c in range(NC2):
            e.wait_ge(ctx.h("vcv"), BC["vcv"] + c + 1)
            dslot = dtc[:] if c % 2 == 0 else btc[:]
            e.activation(wqbuf[:, c * T_LOC:(c + 1) * T_LOC], dslot,
                         AF.Silu, bias=bcc[:, c:c + 1])
            ctx.sig(e, "scv", 1)
    phase(nc, f"{tag}_B1", _ctx=ctx, gpsimd=gpB1, vector=vecB1, scalar=scB1)
    for k, r in dict(lb1=16, vcv=1, scv=1).items():
        ctx.setval(k, BC[k] + r * NC2)

    BD = {k: ctx.val(k) for k in ("wt", "mw", "hgB", "vb2", "sp", "scn", "crr",
                                  "yv", "zl", "yw", "he")}

    def syncB2(e):
        for c2 in range(NC2):
            for wi_ in range(3):
                k3 = 3 * c2 + wi_
                if k3 >= 2:
                    e.wait_ge(ctx.h("mw"), BD["mw"] + 4 * (k3 - 1))
                ins = e.dma_start(
                    out=wtile[:, (k3 % 2) * FI:(k3 % 2 + 1) * FI].rearrange(
                        "p (j n) -> p j n", n=128),
                    in_=IN[("WdtT", "WBT", "WCT")[wi_]][:, c2 * 128:(c2 + 1) * 128]
                    .rearrange("(j p) n -> p j n", p=128))
                ctx.inc(ins, "wt", 16)

    def gpB2(e):
        with e.register("parB32") as par:
            e.reg_load(par, meta[0:1, 0:1])
            b_pol, b_snd, b_pld = ctx.val("fpol2"), ctx.val("fsnd2"), ctx.val("fpld2")
            b = {k: BD[k] for k in ("crr", "he", "zl", "yw")}
            for branch in (0, 1):
                guard = e.If_eq(par, 0) if branch == 0 else e.Else()
                ctx.setval("fpol2", b_pol)
                ctx.setval("fsnd2", b_snd)
                ctx.setval("fpld2", b_pld)
                for k, v in b.items():
                    ctx.setval(k, v)
                with guard:
                    for c2 in range(NC2):
                        if c2 >= 1:
                            e.wait_ge(ctx.h("yv"), BD["yv"] + c2)
                            e.dma_start(out=zd[(c2 - 1) * 128:c2 * 128, :],
                                        in_=ybuf[:, ((c2 - 1) % 2) * T_LOC:((c2 - 1) % 2 + 1) * T_LOC]
                                        ).then_inc(ctx.h("yw"), 16)
                        e.dma_start(out=szc[:], in_=zd[c2 * 128:(c2 + 1) * 128, :]
                                    ).then_inc(ctx.h("zl"), 16)
                        if branch == 0:
                            e.wait_ge(ctx.h("vb2"), BD["vb2"] + 3 * c2 + 1)
                            ins = e.dma_start(out=ush[c2 * 128:(c2 + 1) * 128, 0:1],
                                              in_=uc[:, T_LOC - 1:T_LOC])
                            ctx.inc(ins, "fsnd2", 16)
                            ctx.wait(e, "fsnd2")
                            emit_flag_send(ctx, e, FB + 2 * c2,
                                           flags_t[0:1, FB + 2 * c2:FB + 2 * c2 + 1], meta, snd="fsnd2")
                            ctx.sig(e, "crr", 2)
                            e.wait_ge(ctx.h("scn"), BD["scn"] + c2 + 1)
                            ins = e.dma_start(out=hend[c2 * 128:(c2 + 1) * 128, 0:1],
                                              in_=hcc[:, T_LOC - 1:T_LOC])
                            ctx.inc(ins, "fsnd2", 16)
                            ctx.wait(e, "fsnd2")
                            emit_flag_send(ctx, e, FB + 2 * c2 + 1,
                                           flags_t[0:1, FB + 2 * c2 + 1:FB + 2 * c2 + 2], meta, snd="fsnd2")
                            ctx.sig(e, "he", 1)
                        else:
                            emit_flag_poll(ctx, e, FB + 2 * c2,
                                           flags_t[0:1, FB + 2 * c2:FB + 2 * c2 + 1], meta, pollcell, pol="fpol2")
                            if c2 >= 1:
                                e.wait_ge(ctx.h("scn"), BD["scn"] + c2)
                            ins = e.dma_start(out=uedge[:], in_=ush[c2 * 128:(c2 + 1) * 128, 0:1])
                            ctx.inc(ins, "fpld2", 16)
                            ctx.wait(e, "fpld2")
                            ctx.sig(e, "crr", 1)
                            emit_flag_poll(ctx, e, FB + 2 * c2 + 1,
                                           flags_t[0:1, FB + 2 * c2 + 1:FB + 2 * c2 + 2], meta, pollcell, pol="fpol2")
                            if c2 >= 1:
                                e.wait_ge(ctx.h("scn"), BD["scn"] + c2)
                            ins = e.dma_start(out=hinit[:], in_=hend[c2 * 128:(c2 + 1) * 128, 0:1])
                            ctx.inc(ins, "fpld2", 16)
                            ctx.wait(e, "fpld2")
                            ctx.sig(e, "crr", 1)
                            ctx.sig(e, "he", 1)
                    e.wait_ge(ctx.h("yv"), BD["yv"] + NC2)
                    e.dma_start(out=zd[(NC2 - 1) * 128:NC2 * 128, :],
                                in_=ybuf[:, ((NC2 - 1) % 2) * T_LOC:((NC2 - 1) % 2 + 1) * T_LOC]
                                ).then_inc(ctx.h("yw"), 16)
        e.wait_ge(ctx.h("yw"), BD["yw"] + 16 * NC2)

    def tensorB2(e):
        for c2 in range(NC2):
            for wi_ in range(3):
                k3 = 3 * c2 + wi_
                e.wait_ge(ctx.h("wt"), BD["wt"] + 16 * (k3 + 1))
                for g in range(4):
                    K3 = 4 * k3 + g
                    if K3 >= 4:
                        e.wait_ge(ctx.h("hgB"), BD["hgB"] + K3 - 3)
                    for j in range(NC2):
                        mm = e.matmul(pm1[K3 % 4][:],
                                      wtile[:, (k3 % 2) * FI + j * 128:(k3 % 2) * FI + (j + 1) * 128],
                                      wqbuf[:, j * T_LOC + g * 512:j * T_LOC + (g + 1) * 512],
                                      start=(j == 0), stop=(j == NC2 - 1))
                    mm.then_inc(ctx.h("mw"), 1)

    def scalarB2(e):
        for c2 in range(NC2):
            for wi_ in range(3):
                k3 = 3 * c2 + wi_
                for g in range(4):
                    K3 = 4 * k3 + g
                    e.wait_ge(ctx.h("mw"), BD["mw"] + K3 + 1)
                    if wi_ == 0:
                        if g == 0 and c2 >= 1:
                            e.wait_ge(ctx.h("scn"), BD["scn"] + c2)
                        e.activation(dtc[:, g * 512:(g + 1) * 512], pm1[K3 % 4][:],
                                     AF.Identity, bias=bdtc[:, c2:c2 + 1])
                    elif wi_ == 1:
                        if g == 0 and c2 >= 1:
                            e.wait_ge(ctx.h("scn"), BD["scn"] + c2)
                        e.activation(btc[:, g * 512:(g + 1) * 512], pm1[K3 % 4][:],
                                     AF.Sigmoid)
                    else:
                        if g == 0 and c2 >= 1:
                            e.wait_ge(ctx.h("yv"), BD["yv"] + c2)
                        e.activation(ctc[:, g * 512:(g + 1) * 512], pm1[K3 % 4][:],
                                     AF.Tanh)
                    ctx.sig(e, "hgB", 1)
            e.wait_ge(ctx.h("vb2"), BD["vb2"] + 3 * c2 + 2)
            e.activation(alc[:], alc[:], AF.Exp, scale=-1.0)
            e.activation(alc[:], alc[:], AF.Ln, bias=onec[:])
            ctx.sig(e, "sp", 1)
            e.wait_ge(ctx.h("vb2"), BD["vb2"] + 3 * c2 + 3)
            e.activation(alc[:], hcc[:], AF.Exp)
            ctx.sig(e, "sp", 1)

    def vecB2(e):
        for c2 in range(NC2):
            xcs = wqbuf[:, c2 * T_LOC:(c2 + 1) * T_LOC]
            e.wait_ge(ctx.h("hgB"), BD["hgB"] + 12 * c2 + 8)
            if c2 >= 1:
                e.wait_ge(ctx.h("crr"), BD["crr"] + 2 * c2 - 1)
            e.tensor_mul(uc[:], btc[:], xcs)
            ctx.sig(e, "vb2", 1)
            e.tensor_scalar_mul(alc[:], dtc[:], -1.0)
            e.tensor_max(alc[:], alc[:], dtc[:])
            ctx.sig(e, "vb2", 1)
            e.wait_ge(ctx.h("sp"), BD["sp"] + 2 * c2 + 1)
            e.tensor_scalar_max(dtc[:], dtc[:], 0.0)
            e.tensor_add(dtc[:], dtc[:], alc[:])
            if c2 >= 1:
                e.wait_ge(ctx.h("he"), BD["he"] + c2)
            e.tensor_scalar(hcc[:], dtc[:], acol[:, c2:c2 + 1], -10.0, ALU.mult, ALU.max)
            e.tensor_scalar_min(hcc[:], hcc[:], -1e-4)
            ctx.sig(e, "vb2", 1)
            e.wait_ge(ctx.h("crr"), BD["crr"] + 2 * c2 + 1)
            e.tensor_add(btc[:, 1:T_LOC], uc[:, 0:T_LOC - 1], uc[:, 1:T_LOC])
            e.tensor_add(btc[:, 0:1], uedge[:], uc[:, 0:1])
            e.scalar_tensor_tensor(btc[:], dtc[:], 0.5, btc[:], ALU.mult, ALU.mult)
            e.wait_ge(ctx.h("sp"), BD["sp"] + 2 * c2 + 2)
            e.wait_ge(ctx.h("crr"), BD["crr"] + 2 * c2 + 2)
            e.tensor_tensor_scan(hcc[:], alc[:], btc[:], hinit[:, 0:1], ALU.mult, ALU.add)
            ctx.sig(e, "scn", 1)
            e.wait_ge(ctx.h("zl"), BD["zl"] + 16 * (c2 + 1))
            if c2 >= 2:
                e.wait_ge(ctx.h("yw"), BD["yw"] + 16 * (c2 - 1))
            e.tensor_mul(ctc[:], ctc[:], hcc[:])
            e.tensor_mul(ctc[:], ctc[:], szc[:])
            e.tensor_copy(ybuf[:, (c2 % 2) * T_LOC:(c2 % 2 + 1) * T_LOC], ctc[:])
            ctx.sig(e, "yv", 1)

    phase(nc, f"{tag}_B2", _ctx=ctx, sync=gpB2, gpsimd=syncB2, tensor=tensorB2,
          scalar=scalarB2, vector=vecB2)
    for k, r in dict(wt=48 * 16, mw=4 * 48, hgB=12 * NC2, vb2=3 * NC2, sp=2 * NC2,
                     scn=NC2, crr=2 * NC2, yv=NC2, zl=16 * NC2, yw=16 * NC2,
                     he=NC2).items():
        ctx.setval(k, BD[k] + r)

    BE = {k: ctx.val(k) for k in ("yl", "ld16", "va", "sr", "m2", "e2", "gl", "st")}

    def syncB3(e):
        for t in range(NT):
            if t >= 1:
                e.wait_ge(ctx.h("m2"), BE["m2"] + 2 * t)
            ins = e.dma_start(out=yld[:].rearrange("p (c n) -> p c n", n=128),
                              in_=zd[:, t * 128:(t + 1) * 128]
                              .rearrange("(c p) n -> p c n", p=128))
            ctx.inc(ins, "yl", 16)
            if t >= 2:
                e.wait_ge(ctx.h("gl"), BE["gl"] + t - 1)
            e.dma_start(out=xt[:, (t % 2) * D:((t % 2) + 1) * D],
                        in_=x_in[t * 128:(t + 1) * 128, :]).then_inc(ctx.h("ld16"), 16)

    def gpB3(e):
        for t in range(NT):
            e.wait_ge(ctx.h("gl"), BE["gl"] + t + 1)
            e.dma_start(out=x_out[t * 128:(t + 1) * 128, :],
                        in_=ot[:, (t % 2) * D:((t % 2) + 1) * D]).then_inc(ctx.h("st"), 16)
        e.wait_ge(ctx.h("st"), BE["st"] + 16 * NT)

    def tensorB3(e):
        for t in range(NT):
            e.wait_ge(ctx.h("yl"), BE["yl"] + 16 * (t + 1))
            if t >= 1:
                e.wait_ge(ctx.h("e2"), BE["e2"] + t)
            for dg in range(2):
                for c in range(NC2):
                    mm = e.matmul(pm2[dg][:], yld[:, c * 128:(c + 1) * 128],
                                  wot[:, c * D + dg * 512:c * D + (dg + 1) * 512],
                                  start=(c == 0), stop=(c == NC2 - 1))
                mm.then_inc(ctx.h("m2"), 1)

    def scalarB3(e):
        for t in range(NT):
            cs = ct[:, (t % 2) * D:((t % 2) + 1) * D]
            for dg in range(2):
                e.wait_ge(ctx.h("m2"), BE["m2"] + 2 * t + dg + 1)
                e.activation(cs[:, 512 * dg:512 * (dg + 1)], pm2[dg][:], AF.Identity)
                if dg == 1:
                    ctx.sig(e, "e2", 1)
            e.wait_ge(ctx.h("va"), BE["va"] + t + 1)
            rstd_ops(e, sc[:, 11:12], sc[:, 12:13], r1c[:, 3:4], epsc[:])
            ctx.sig(e, "sr", 1)

    def vecB3(e):
        for t in range(NT):
            cs = ct[:, (t % 2) * D:((t % 2) + 1) * D]
            xs = xt[:, (t % 2) * D:((t % 2) + 1) * D]
            e.wait_ge(ctx.h("e2"), BE["e2"] + t + 1)
            e.wait_ge(ctx.h("ld16"), BE["ld16"] + 16 * (t + 1))
            ln_stats(e, cs, D, st6, st2, sc[:, 10:11], sc[:, 11:12], "va", ctx)
            e.wait_ge(ctx.h("sr"), BE["sr"] + t + 1)
            os_ = ot[:, (t % 2) * D:((t % 2) + 1) * D]
            if t >= 2:
                e.wait_ge(ctx.h("st"), BE["st"] + 16 * (t - 1))
            e.tensor_scalar(os_, cs, sc[:, 10:11], r1c[:, 3:4], ALU.subtract, ALU.mult)
            e.tensor_add(os_, os_, xs)
            ctx.sig(e, "gl", 1)

    phase(nc, f"{tag}_B3", _ctx=ctx, sync=syncB3, gpsimd=gpB3, tensor=tensorB3,
          scalar=scalarB3, vector=vecB3)
    for k, r in dict(yl=16, ld16=16, va=1, sr=1, m2=2, e2=1, gl=1, st=16).items():
        ctx.setval(k, BE[k] + r * NT)


def build_l3a_program():
    es = contextlib.ExitStack()
    nc = mk_bass()
    IN = {}
    IN["x"] = nc.dram_tensor("x", [T_LOC, D], F32, kind="ExternalInput").ap()
    IN["WiT"] = nc.dram_tensor("WiT", [D, 4096], F32, kind="ExternalInput").ap()
    IN["WdtT"] = nc.dram_tensor("WdtT", [2048, 2048], F32, kind="ExternalInput").ap()
    IN["WBT"] = nc.dram_tensor("WBT", [2048, 2048], F32, kind="ExternalInput").ap()
    IN["WCT"] = nc.dram_tensor("WCT", [2048, 2048], F32, kind="ExternalInput").ap()
    IN["WoT"] = nc.dram_tensor("WoT", [2048, D], F32, kind="ExternalInput").ap()
    for nm in ("alog2", "bconv2", "bdt2"):
        IN[nm] = nc.dram_tensor(nm, [128, 16], F32, kind="ExternalInput").ap()
    IN["convw2"] = nc.dram_tensor("convw2", [128, 64], F32, kind="ExternalInput").ap()
    IN["halo0"] = nc.dram_tensor("halo0", [128, 4], F32, kind="ExternalInput").ap()
    IN["meta"] = nc.dram_tensor("meta", [1, 48], I32, kind="ExternalInput").ap()
    IN["identC"] = nc.dram_tensor("identC", [128, 128], BF16, kind="ExternalInput").ap()
    IN["x_out"] = nc.dram_tensor("x_out", [T_LOC, D], F32, kind="ExternalOutput").ap()
    IN["x1_sh"] = nc.dram_tensor("x1_sh", [2048, 3 + 2 * T_LOC], BF16, addr_space="Shared").ap()
    IN["u_sh"] = nc.dram_tensor("u_sh", [2048, 1], F32, addr_space="Shared").ap()
    IN["hend_sh"] = nc.dram_tensor("hend_sh", [2048, 1], F32, addr_space="Shared").ap()
    IN["flags"] = nc.dram_tensor("flags", [2, 64], I32, addr_space="Shared").ap()
    zk = "ExternalOutput" if os.environ.get("L3A_YOUT") else "Internal"
    IN["z_d"] = nc.dram_tensor("z_d", [2048, T_LOC], BF16, kind=zk).ap()
    IN["flag_base"] = 2
    ctx = Ctx(nc, es)
    emit_l3a(ctx, IN)
    es.close()
    return nc


def l3a_in_maps(x_full, ins, nonce=None):
    import ml_dtypes
    identC = np.eye(128, dtype=ml_dtypes.bfloat16)
    f32 = lambda a: np.ascontiguousarray(np.asarray(a, dtype=np.float32))
    WiT = f32(ins["L3_attn_Wi"].T)
    WdtT = f32(ins["L3_attn_Wdt"].T)
    WBT = f32(ins["L3_attn_WB"].T)
    WCT = f32(ins["L3_attn_WC"].T)
    WoT = f32(ins["L3_attn_Wo"].T)
    chunkcols = lambda v: f32(np.asarray(v).reshape(16, 128).T)
    cw = np.asarray(ins["L3_attn_Wconv"], dtype=np.float32)[:, 0, :]
    convw2 = np.concatenate([chunkcols(cw[:, k]) for k in range(4)], axis=1)
    if nonce is None:
        nonce = int(np.random.randint(1, 2 ** 30))
    maps = []
    for c in range(NCORE):
        b, h = c // 2, c % 2
        shard = f32(x_full[b, h * T_LOC:(h + 1) * T_LOC, :])
        maps.append(dict(
            x=shard, WiT=WiT, WdtT=WdtT, WBT=WBT, WCT=WCT, WoT=WoT,
            alog2=chunkcols(ins["L3_attn_A_log"]),
            bconv2=chunkcols(ins["L3_attn_bconv"]),
            bdt2=chunkcols(ins["L3_attn_bdt"]),
            convw2=np.ascontiguousarray(convw2),
            halo0=np.full((128, 4), float(h), np.float32),
            meta=np.array([[h, nonce] + [nonce + kk + 1 for kk in range(46)]],
                          dtype=np.int32),
            identC=identC,
        ))
    return maps


def emit_l0a(ctx, IN, tag="l0a"):
    nc = ctx.nc
    NG = 4

    wxb = ctx.sb(f"{tag}_wxb", [128, 8 * D], BF16)
    wxpb = ctx.sb(f"{tag}_wxp", [128, 8 * 96], BF16)
    wdtb = ctx.sb(f"{tag}_wdt", [64, D], BF16)
    ident = ctx.sb("identb", [128, 128], BF16)
    epsc = ctx.sb("epscol", [128, 1], F32)
    onec = ctx.sb("onecolf2", [128, 1], F32)
    meta = ctx.sb("metai", [1, 48], I32)
    gb = ctx.sb(f"{tag}_gb", [128, D], BF16)
    bb = ctx.sb(f"{tag}_bb", [128, D], BF16)
    bxc = ctx.sb(f"{tag}_bxc", [128, 8], F32)
    cbc = ctx.sb(f"{tag}_cbc", [128, 8], F32)
    cwc = ctx.sb(f"{tag}_cwc", [128, 32], F32)
    bdtc = ctx.sb(f"{tag}_bdtc", [128, 8], F32)
    dpc = ctx.sb(f"{tag}_dpc", [128, 8], F32)
    aabs = ctx.sb(f"{tag}_aabs", [128, 128], F32)
    rmat = ctx.sb(f"{tag}_rmat", [128, 16 * 128], BF16)
    smat = ctx.sb(f"{tag}_smat", [128, 16 * 128], BF16)
    r2mat = ctx.sb(f"{tag}_r2", [128, 2 * 128], BF16)
    sgnb = ctx.sb(f"{tag}_sgn", [128, T_LOC], BF16)
    xedge = ctx.sb(f"{tag}_xedge", [128, 24], F32)
    pollcell = ctx.sb("pollcell", [1, 2], I32)

    xt = ctx.sb("g_xt", [128, 2 * D], F32)
    xnb = ctx.sb(f"{tag}_xnb", [128, D], BF16)
    xf = ctx.sb(f"{tag}_xf", [128, 8 * T_LOC], BF16)
    xcb = ctx.sb(f"{tag}_xcb", [128, 8 * T_LOC], BF16)
    dfb = ctx.sb(f"{tag}_dfb", [128, 8 * T_LOC], BF16)
    xdbl = ctx.sb(f"{tag}_xdbl", [128, T_LOC], BF16)
    bsch = ctx.sb(f"{tag}_bsch", [128, T_LOC], BF16)
    csch = ctx.sb(f"{tag}_csch", [128, T_LOC], BF16)
    atile = ctx.sb(f"{tag}_at", [128, 2 * T_LOC], BF16)
    ustile = ctx.sb(f"{tag}_us", [128, 2 * T_LOC], BF16)
    htile = ctx.sb(f"{tag}_htl", [128, T_LOC], F32)
    wtile2 = ctx.sb(f"{tag}_wt2", [128, 2 * T_LOC], BF16)
    hcar = ctx.sb(f"{tag}_hcar", [128, 1], F32)
    tmpc = ctx.sb(f"{tag}_tmpc", [128, T_LOC + 3], F32)
    tmpd = ctx.sb(f"{tag}_tmpd", [128, T_LOC], F32)
    sc = ctx.sb("g_sc", [128, 16], F32)
    st6 = ctx.sb("g_st6", [128, 48], F32)
    st2 = ctx.sb("g_st2", [128, 2], F32)
    r1c = ctx.sb("g_r1c", [128, 4], F32)
    xn = tmpc[:, 0:D]
    xnf = atile
    ct = tmpd
    ot = tmpc[:, 0:2 * D]

    pm1 = [ctx.ps(f"ps_pm1_{i}", [128, 512], F32) for i in range(4)]
    pm2 = [ctx.ps(f"ps_pm2_{i}", [128, 512], F32) for i in range(2)]
    ptr = [ctx.ps(f"ps_ptr_{i}", [128, 128], BF16) for i in range(2)]

    x_in, x_out = IN["x"], IN["x_out"]
    xe_sh, hc_sh, flags_t = IN["xe_sh"], IN["hc_sh"], IN["flags"]
    FB = IN["flag_base"]

    def syncp(e):
        ins = e.dma_start(out=ident[:], in_=IN["identC"])
        ctx.inc(ins, "ld", 16)

    def gpp(e):
        e.memset(epsc[:], LN_EPS)
        e.memset(onec[:], 1.0)
        e.memset(xedge[:], 0.0)
        e.memset(hcar[:], 0.0)
        e.memset(xdbl[:], 0.0)
        for dst, src in ((meta, IN["meta"]), (gb, IN["lng_r"]), (bb, IN["lnb_r"]),
                         (bxc, IN["bxc"]), (cbc, IN["cbc"]), (cwc, IN["cwc"]),
                         (bdtc, IN["bdtc"]), (dpc, IN["dpc"]), (aabs, IN["alogr"]),
                         (rmat, IN["rmat"]), (smat, IN["smat"]), (r2mat, IN["r2mat"]),
                         (sgnb, IN["sgn"]), (wdtb, IN["Wdt"])):
            ins = e.dma_start(out=dst[:], in_=src)
            ctx.inc(ins, "ldg", 16)
        for r in range(8):
            ins = e.dma_start(out=wxb[:, r * D:(r + 1) * D],
                              in_=IN["Wx"][r * 128:(r + 1) * 128, :])
            ctx.inc(ins, "ldg", 16)
            ins = e.dma_start(out=wxpb[:, r * 96:(r + 1) * 96],
                              in_=IN["Wxp"][r * 128:(r + 1) * 128, :])
            ctx.inc(ins, "ldg", 16)
        ctx.wait(e, "ldg")
        ctx.wait(e, "ld")
    phase(nc, f"{tag}_pro", _ctx=ctx, sync=syncp, gpsimd=gpp)

    def scp(e):
        e.activation(aabs[:], aabs[:], AF.Exp)
    phase(nc, f"{tag}_pro2", _ctx=ctx, scalar=scp)

    BA = {k: ctx.val(k) for k in ("ld16", "va", "sr", "q", "tp", "ev", "m1", "hg")}

    def syncA(e):
        for t in range(NT):
            if t >= 2:
                e.wait_ge(ctx.h("q"), BA["q"] + t - 1)
            e.dma_start(out=xt[:, (t % 2) * D:((t % 2) + 1) * D],
                        in_=x_in[t * 128:(t + 1) * 128, :]).then_inc(ctx.h("ld16"), 16)

    def vectorA(e):
        for t in range(NT):
            ti = t % 4
            TP = BA["tp"] + 8 * t
            xs = xt[:, (t % 2) * D:((t % 2) + 1) * D]
            e.wait_ge(ctx.h("ld16"), BA["ld16"] + 16 * (t + 1))
            ln_stats(e, xs, D, st6, st2, sc[:, 0:1], sc[:, 1:2], "va", ctx)
            e.wait_ge(ctx.h("sr"), BA["sr"] + 2 * t + 1)
            e.tensor_scalar(xn[:], xs, sc[:, 0:1], r1c[:, 0:1], ALU.subtract, ALU.mult)
            ln_stats(e, xn[:], D, st6, st2, sc[:, 2:3], sc[:, 3:4], "va", ctx)
            e.wait_ge(ctx.h("sr"), BA["sr"] + 2 * t + 2)
            e.tensor_scalar(xn[:], xn[:], sc[:, 2:3], r1c[:, 1:2], ALU.subtract, ALU.mult)
            e.tensor_mul(xn[:], xn[:], gb[:])
            e.tensor_add(xn[:], xn[:], bb[:])
            e.tensor_copy(xnb[:], xn[:])
            ctx.sig(e, "q", 1)
            for j in range(8):
                e.wait_ge(ctx.h("tp"), TP + 1 + j)
                e.tensor_copy(xnf[:, j * 512 + ti * 128:j * 512 + (ti + 1) * 128],
                              ptr[j % 2][:])
                ctx.sig(e, "ev", 1)

    def scalarA(e):
        for g in range(NG):
            for ti in range(4):
                t = 4 * g + ti
                e.wait_ge(ctx.h("va"), BA["va"] + 2 * t + 1)
                rstd_ops(e, sc[:, 1:2], sc[:, 12:13], r1c[:, 0:1], epsc[:])
                ctx.sig(e, "sr", 1)
                e.wait_ge(ctx.h("va"), BA["va"] + 2 * t + 2)
                rstd_ops(e, sc[:, 3:4], sc[:, 12:13], r1c[:, 1:2], epsc[:])
                ctx.sig(e, "sr", 1)
            for c in range(8):
                G = 8 * g + c
                e.wait_ge(ctx.h("m1"), BA["m1"] + G + 1)
                e.activation(xf[:, c * T_LOC + g * 512:c * T_LOC + (g + 1) * 512],
                             pm1[G % 4][:], AF.Identity, bias=bxc[:, c:c + 1])
                ctx.sig(e, "hg", 1)

    def tensorA(e):
        for g in range(NG):
            for ti in range(4):
                t = 4 * g + ti
                EV = BA["ev"] + 8 * t
                e.wait_ge(ctx.h("q"), BA["q"] + t + 1)
                e.wait_ge(ctx.h("ev"), EV)
                for j in range(8):
                    if j >= 2:
                        e.wait_ge(ctx.h("ev"), EV + j - 1)
                    e.transpose(ptr[j % 2][:], xnb[:, j * 128:(j + 1) * 128],
                                ident[:]).then_inc(ctx.h("tp"), 1)
            e.wait_ge(ctx.h("ev"), BA["ev"] + 8 * (4 * g + 4))
            for c in range(8):
                G = 8 * g + c
                if G >= 4:
                    e.wait_ge(ctx.h("hg"), BA["hg"] + G - 3)
                for j in range(8):
                    mm = e.matmul(pm1[G % 4][:],
                                  wxb[:, j * D + c * 128:j * D + (c + 1) * 128],
                                  xnf[:, j * 512:(j + 1) * 512],
                                  start=(j == 0), stop=(j == 7))
                mm.then_inc(ctx.h("m1"), 1)

    phase(nc, f"{tag}_A", _ctx=ctx, sync=syncA, vector=vectorA, scalar=scalarA,
          tensor=tensorA)
    for k, r in dict(ld16=16, va=2, sr=2, q=1, tp=8, ev=8).items():
        ctx.setval(k, BA[k] + r * NT)
    for k, r in dict(m1=32, hg=32).items():
        ctx.setval(k, BA[k] + r)

    def gpe(e):
        with e.register("parE0") as par:
            e.reg_load(par, meta[0:1, 0:1])
            b_snd, b_pld = ctx.val("fsnd"), ctx.val("fpld")
            with e.If_eq(par, 0):
                for c in range(8):
                    ins = e.dma_start(out=xe_sh[c * 128:(c + 1) * 128, :],
                                      in_=xf[:, c * T_LOC + T_LOC - 3:c * T_LOC + T_LOC])
                    ctx.inc(ins, "fsnd", 16)
                ctx.wait(e, "fsnd")
                emit_flag_send(ctx, e, FB, flags_t[0:1, FB:FB + 1], meta)
            ctx.setval("fsnd", b_snd)
            ctx.setval("fpld", b_pld)
            with e.Else():
                emit_flag_poll(ctx, e, FB, flags_t[0:1, FB:FB + 1], meta, pollcell)
                for c in range(8):
                    ins = e.dma_start(out=xedge[:, 3 * c:3 * (c + 1)],
                                      in_=xe_sh[c * 128:(c + 1) * 128, :])
                    ctx.inc(ins, "fpld", 16)
                ctx.wait(e, "fpld")
    phase(nc, f"{tag}_edge", _ctx=ctx, gpsimd=gpe)

    BC = {k: ctx.val(k) for k in ("cv", "sv")}

    def vecA2a(e):
        for c in range(8):
            xs_ = xf[:, c * T_LOC:(c + 1) * T_LOC]
            if c >= 1:
                e.wait_ge(ctx.h("sv"), BC["sv"] + c)
            e.tensor_copy(tmpc[:, 0:3], xedge[:, 3 * c:3 * (c + 1)])
            e.tensor_copy(tmpc[:, 3:T_LOC + 3], xs_)
            e.tensor_scalar_mul(tmpd[:], tmpc[:, 0:T_LOC], cwc[:, c:c + 1])
            e.scalar_tensor_tensor(tmpd[:], tmpc[:, 1:T_LOC + 1], cwc[:, 8 + c:9 + c],
                                   tmpd[:], ALU.mult, ALU.add)
            e.scalar_tensor_tensor(tmpd[:], tmpc[:, 2:T_LOC + 2], cwc[:, 16 + c:17 + c],
                                   tmpd[:], ALU.mult, ALU.add)
            e.scalar_tensor_tensor(tmpd[:], tmpc[:, 3:T_LOC + 3], cwc[:, 24 + c:25 + c],
                                   tmpd[:], ALU.mult, ALU.add)
            ctx.sig(e, "cv", 1)

    def scA2a(e):
        for c in range(8):
            e.wait_ge(ctx.h("cv"), BC["cv"] + c + 1)
            e.activation(xcb[:, c * T_LOC:(c + 1) * T_LOC], tmpd[:],
                         AF.Silu, bias=cbc[:, c:c + 1])
            ctx.sig(e, "sv", 1)
    phase(nc, f"{tag}_A2a", _ctx=ctx, vector=vecA2a, scalar=scA2a)
    for k, r in dict(cv=8, sv=8).items():
        ctx.setval(k, BC[k] + r)

    BDl = {k: ctx.val(k) for k in ("m2", "e2", "m3", "vb", "sp2", "dx")}

    def tensorA2b(e):
        for g in range(NG):
            if g >= 2:
                e.wait_ge(ctx.h("e2"), BDl["e2"] + g - 1)
            for j in range(8):
                mm = e.matmul(pm2[g % 2][0:96, :], wxpb[:, j * 96:(j + 1) * 96],
                              xcb[:, j * T_LOC + g * 512:j * T_LOC + (g + 1) * 512],
                              start=(j == 0), stop=(j == 7))
            mm.then_inc(ctx.h("m2"), 1)
        e.wait_ge(ctx.h("e2"), BDl["e2"] + 4)
        for g in range(NG):
            if g >= 1:
                e.wait_ge(ctx.h("vb"), BDl["vb"] + 2 * g)
            mm = e.matmul(pm2[0][:], r2mat[:, 0:128], xdbl[:, g * 512:(g + 1) * 512],
                          start=True, stop=True)
            mm.then_inc(ctx.h("m3"), 1)
            mm = e.matmul(pm2[1][:], r2mat[:, 128:256], xdbl[:, g * 512:(g + 1) * 512],
                          start=True, stop=True)
            mm.then_inc(ctx.h("m3"), 1)
        for c in range(8):
            if c >= 1:
                e.wait_ge(ctx.h("sp2"), BDl["sp2"] + 5 * c - 1)
            for g in range(NG):
                K = 4 * c + g
                mm = e.matmul(pm1[K % 4][:], wdtb[0:64, c * 128:(c + 1) * 128],
                              xdbl[0:64, g * 512:(g + 1) * 512],
                              start=True, stop=True)
                mm.then_inc(ctx.h("m3"), 1)

    def scA2b(e):
        for g in range(NG):
            e.wait_ge(ctx.h("m2"), BDl["m2"] + g + 1)
            e.activation(xdbl[0:96, g * 512:(g + 1) * 512], pm2[g % 2][0:96, :],
                         AF.Identity)
            ctx.sig(e, "e2", 1)
        for c in range(8):
            if c >= 1:
                e.wait_ge(ctx.h("dx"), BDl["dx"] + c)
            for g in range(NG):
                K = 4 * c + g
                e.wait_ge(ctx.h("m3"), BDl["m3"] + 8 + K + 1)
                e.activation(tmpd[:, g * 512:(g + 1) * 512], pm1[K % 4][:],
                             AF.Identity, bias=bdtc[:, c:c + 1])
            ctx.sig(e, "sp2", 4)
            e.wait_ge(ctx.h("vb"), BDl["vb"] + 8 + c + 1)
            e.activation(tmpc[:, 0:T_LOC], tmpc[:, 0:T_LOC], AF.Exp, scale=-1.0)
            e.activation(tmpc[:, 0:T_LOC], tmpc[:, 0:T_LOC], AF.Ln, bias=onec[:])
            ctx.sig(e, "sp2", 1)

    def vecA2b(e):
        for g in range(NG):
            e.wait_ge(ctx.h("m3"), BDl["m3"] + 2 * g + 2)
            e.tensor_mul(bsch[:, g * 512:(g + 1) * 512], pm2[0][:],
                         sgnb[:, g * 512:(g + 1) * 512])
            ctx.sig(e, "vb", 1)
            e.tensor_mul(csch[:, g * 512:(g + 1) * 512], pm2[1][:],
                         sgnb[:, g * 512:(g + 1) * 512])
            ctx.sig(e, "vb", 1)
        for c in range(8):
            e.wait_ge(ctx.h("sp2"), BDl["sp2"] + 5 * c + 4)
            e.tensor_scalar_mul(tmpc[:, 0:T_LOC], tmpd[:], -1.0)
            e.tensor_max(tmpc[:, 0:T_LOC], tmpc[:, 0:T_LOC], tmpd[:])
            ctx.sig(e, "vb", 1)
            e.wait_ge(ctx.h("sp2"), BDl["sp2"] + 5 * c + 5)
            e.tensor_scalar_max(tmpd[:], tmpd[:], 0.0)
            e.tensor_add(tmpd[:], tmpd[:], tmpc[:, 0:T_LOC])
            e.tensor_copy(dfb[:, c * T_LOC:(c + 1) * T_LOC], tmpd[:])
            e.tensor_mul(xf[:, c * T_LOC:(c + 1) * T_LOC],
                         xcb[:, c * T_LOC:(c + 1) * T_LOC],
                         dfb[:, c * T_LOC:(c + 1) * T_LOC])
            ctx.sig(e, "dx", 1)
    phase(nc, f"{tag}_A2b", _ctx=ctx, tensor=tensorA2b, scalar=scA2b, vector=vecA2b)
    for k, r in dict(m2=4, e2=4, m3=40, vb=16, sp2=40, dx=8).items():
        ctx.setval(k, BDl[k] + r)

    BS = {k: ctx.val(k) for k in ("am", "ae", "um", "ue", "scn2", "wv", "ymt",
                                  "yev", "cr2", "hcd")}

    def ymm(e, tp):
        kp, cp = tp % 16, tp // 16
        e.wait_ge(ctx.h("wv"), BS["wv"] + tp + 1)
        if kp == 0 and cp >= 1:
            e.wait_ge(ctx.h("yev"), BS["yev"] + 4 * cp)
        for g in range(NG):
            mm = e.matmul(pm1[g][:], smat[:, kp * 128:(kp + 1) * 128],
                          wtile2[:, (tp % 2) * T_LOC + g * 512:(tp % 2) * T_LOC + (g + 1) * 512],
                          start=(kp == 0), stop=(kp == 15))
        mm.then_inc(ctx.h("ymt"), 1)

    def tensorS(e):
        for c in range(8):
            for k in range(16):
                tau = 16 * c + k
                if tau >= 1:
                    ymm(e, tau - 1)
                for g in range(NG):
                    if tau > 0 or g > 0:
                        e.wait_ge(ctx.h("ae"), BS["ae"] + 4 * tau + g)
                    mm = e.matmul(pm2[0][:], rmat[:, k * 128:(k + 1) * 128],
                                  dfb[:, c * T_LOC + g * 512:c * T_LOC + (g + 1) * 512],
                                  start=True, stop=True)
                    mm.then_inc(ctx.h("am"), 1)
                for g in range(NG):
                    if tau > 0 or g > 0:
                        e.wait_ge(ctx.h("ue"), BS["ue"] + 4 * tau + g)
                    mm = e.matmul(pm2[1][:], rmat[:, k * 128:(k + 1) * 128],
                                  xf[:, c * T_LOC + g * 512:c * T_LOC + (g + 1) * 512],
                                  start=True, stop=True)
                    mm.then_inc(ctx.h("um"), 1)
        ymm(e, 127)

    def scalarS(e):
        for c in range(8):
            for k in range(16):
                tau = 16 * c + k
                for g in range(NG):
                    e.wait_ge(ctx.h("am"), BS["am"] + 4 * tau + g + 1)
                    if tau >= 2:
                        e.wait_ge(ctx.h("scn2"), BS["scn2"] + tau - 1)
                    e.activation(atile[:, (tau % 2) * T_LOC + g * 512:(tau % 2) * T_LOC + (g + 1) * 512],
                                 pm2[0][:], AF.Identity, scale=aabs[:, tau:tau + 1])
                    ctx.sig(e, "ae", 1)

    def vectorS(e):
        for c in range(8):
            for k in range(16):
                tau = 16 * c + k
                for g in range(NG):
                    e.wait_ge(ctx.h("um"), BS["um"] + 4 * tau + g + 1)
                    if tau >= 2:
                        e.wait_ge(ctx.h("scn2"), BS["scn2"] + tau - 1)
                    e.tensor_mul(ustile[:, (tau % 2) * T_LOC + g * 512:(tau % 2) * T_LOC + (g + 1) * 512],
                                 pm2[1][:], bsch[:, g * 512:(g + 1) * 512])
                    ctx.sig(e, "ue", 1)
                e.wait_ge(ctx.h("ae"), BS["ae"] + 4 * tau + 4)
                e.wait_ge(ctx.h("cr2"), BS["cr2"] + tau + 1)
                if tau >= 1:
                    e.wait_ge(ctx.h("wv"), BS["wv"] + tau)
                    e.wait_ge(ctx.h("hcd"), BS["hcd"] + tau)
                e.tensor_tensor_scan(htile[:],
                                     atile[:, (tau % 2) * T_LOC:(tau % 2 + 1) * T_LOC],
                                     ustile[:, (tau % 2) * T_LOC:(tau % 2 + 1) * T_LOC],
                                     hcar[:, 0:1], ALU.mult, ALU.add)
                ctx.sig(e, "scn2", 1)
            e.wait_ge(ctx.h("ymt"), BS["ymt"] + 16 * (c + 1))
            for g in range(NG):
                e.scalar_tensor_tensor(
                    dfb[:, c * T_LOC + g * 512:c * T_LOC + (g + 1) * 512],
                    xcb[:, c * T_LOC + g * 512:c * T_LOC + (g + 1) * 512],
                    dpc[:, c:c + 1], pm1[g][:], ALU.mult, ALU.add)
                ctx.sig(e, "yev", 1)

    def gpS(e):
        for c in range(8):
            for k in range(16):
                tau = 16 * c + k
                e.wait_ge(ctx.h("scn2"), BS["scn2"] + tau + 1)
                if tau >= 2:
                    e.wait_ge(ctx.h("ymt"), BS["ymt"] + tau - 1)
                e.tensor_mul(wtile2[:, (tau % 2) * T_LOC:(tau % 2 + 1) * T_LOC],
                             htile[:], csch[:])
                ctx.sig(e, "wv", 1)

    def syncS(e):
        with e.register("parS0") as par:
            e.reg_load(par, meta[0:1, 0:1])
            b_snd, b_pld = ctx.val("fsnd2"), ctx.val("fpld2")
            b_cr2, b_hcd = BS["cr2"], BS["hcd"]
            with e.If_eq(par, 0):
                for c in range(8):
                    for k in range(16):
                        tau = 16 * c + k
                        ctx.sig(e, "cr2", 1)
                        e.wait_ge(ctx.h("scn2"), BS["scn2"] + tau + 1)
                        ins = e.dma_start(out=hc_sh[tau * 128:(tau + 1) * 128, 0:1],
                                          in_=htile[:, T_LOC - 1:T_LOC])
                        ctx.inc(ins, "fsnd2", 16)
                        ctx.wait(e, "fsnd2")
                        ctx.sig(e, "hcd", 1)
                    emit_flag_send(ctx, e, FB + 1 + c,
                                   flags_t[0:1, FB + 1 + c:FB + 2 + c],
                                   meta, snd="fsnd2")
            ctx.setval("fsnd2", b_snd)
            ctx.setval("fpld2", b_pld)
            ctx.setval("cr2", b_cr2)
            ctx.setval("hcd", b_hcd)
            with e.Else():
                for c in range(8):
                    emit_flag_poll(ctx, e, FB + 1 + c,
                                   flags_t[0:1, FB + 1 + c:FB + 2 + c],
                                   meta, pollcell, pol="fpol2")
                    for k in range(16):
                        tau = 16 * c + k
                        if tau >= 1:
                            e.wait_ge(ctx.h("scn2"), BS["scn2"] + tau)
                        ins = e.dma_start(out=hcar[:],
                                          in_=hc_sh[tau * 128:(tau + 1) * 128, 0:1])
                        ctx.inc(ins, "fpld2", 16)
                        ctx.wait(e, "fpld2")
                        ctx.sig(e, "cr2", 1)
                        ctx.sig(e, "hcd", 1)
    phase(nc, f"{tag}_S", _ctx=ctx, tensor=tensorS, scalar=scalarS, vector=vectorS,
          gpsimd=gpS, sync=syncS)
    for k, r in dict(am=512, ae=512, um=512, ue=512, scn2=128,
                     wv=128, ymt=128, yev=32, cr2=128, hcd=128).items():
        ctx.setval(k, BS[k] + r)

    BE = {k: ctx.val(k) for k in ("ld16", "va", "sr", "tpg", "evg", "gl", "st")}

    def syncG(e):
        for t in range(NT):
            if t >= 2:
                e.wait_ge(ctx.h("gl"), BE["gl"] + t - 1)
            e.dma_start(out=xt[:, (t % 2) * D:((t % 2) + 1) * D],
                        in_=x_in[t * 128:(t + 1) * 128, :]).then_inc(ctx.h("ld16"), 16)

    def gpG(e):
        for t in range(NT):
            e.wait_ge(ctx.h("gl"), BE["gl"] + t + 1)
            e.dma_start(out=x_out[t * 128:(t + 1) * 128, :],
                        in_=ot[:, (t % 2) * D:((t % 2) + 1) * D]).then_inc(ctx.h("st"), 16)
        e.wait_ge(ctx.h("st"), BE["st"] + 16 * NT)

    def tensorG(e):
        for t in range(NT):
            for c in range(8):
                i = 8 * t + c
                if i >= 2:
                    e.wait_ge(ctx.h("evg"), BE["evg"] + i - 1)
                e.transpose(ptr[i % 2][:],
                            dfb[:, c * T_LOC + t * 128:c * T_LOC + (t + 1) * 128],
                            ident[:]).then_inc(ctx.h("tpg"), 1)

    def vecG(e):
        for t in range(NT):
            cs = ct[:, (t % 2) * D:((t % 2) + 1) * D]
            xs = xt[:, (t % 2) * D:((t % 2) + 1) * D]
            if t >= 2:
                e.wait_ge(ctx.h("gl"), BE["gl"] + t - 1)
            for c in range(8):
                i = 8 * t + c
                e.wait_ge(ctx.h("tpg"), BE["tpg"] + i + 1)
                e.tensor_copy(cs[:, c * 128:(c + 1) * 128], ptr[i % 2][:])
                ctx.sig(e, "evg", 1)
            e.wait_ge(ctx.h("ld16"), BE["ld16"] + 16 * (t + 1))
            ln_stats(e, cs, D, st6, st2, sc[:, 10:11], sc[:, 11:12], "va", ctx)
            e.wait_ge(ctx.h("sr"), BE["sr"] + t + 1)
            os_ = ot[:, (t % 2) * D:((t % 2) + 1) * D]
            if t >= 2:
                e.wait_ge(ctx.h("st"), BE["st"] + 16 * (t - 1))
            e.tensor_scalar(os_, cs, sc[:, 10:11], r1c[:, 3:4], ALU.subtract, ALU.mult)
            e.tensor_add(os_, os_, xs)
            ctx.sig(e, "gl", 1)

    def scG(e):
        for t in range(NT):
            e.wait_ge(ctx.h("va"), BE["va"] + t + 1)
            rstd_ops(e, sc[:, 11:12], sc[:, 12:13], r1c[:, 3:4], epsc[:])
            ctx.sig(e, "sr", 1)
    phase(nc, f"{tag}_G", _ctx=ctx, sync=syncG, gpsimd=gpG, tensor=tensorG,
          vector=vecG, scalar=scG)
    for k, r in dict(ld16=16, va=1, sr=1, tpg=8, evg=8, gl=1, st=16).items():
        ctx.setval(k, BE[k] + r * NT)


def build_l0a_program():
    es = contextlib.ExitStack()
    nc = mk_bass()
    IN = {}
    IN["x"] = nc.dram_tensor("x", [T_LOC, D], F32, kind="ExternalInput").ap()
    IN["Wx"] = nc.dram_tensor("Wx", [D, D], F32, kind="ExternalInput").ap()
    IN["Wxp"] = nc.dram_tensor("Wxp", [D, 96], F32, kind="ExternalInput").ap()
    IN["Wdt"] = nc.dram_tensor("Wdt", [64, D], F32, kind="ExternalInput").ap()
    for nm in ("lng_r", "lnb_r"):
        IN[nm] = nc.dram_tensor(nm, [128, D], F32, kind="ExternalInput").ap()
    for nm in ("bxc", "cbc", "bdtc", "dpc"):
        IN[nm] = nc.dram_tensor(nm, [128, 8], F32, kind="ExternalInput").ap()
    IN["cwc"] = nc.dram_tensor("cwc", [128, 32], F32, kind="ExternalInput").ap()
    IN["alogr"] = nc.dram_tensor("alogr", [128, 128], F32, kind="ExternalInput").ap()
    IN["rmat"] = nc.dram_tensor("rmat", [128, 16 * 128], F32, kind="ExternalInput").ap()
    IN["smat"] = nc.dram_tensor("smat", [128, 16 * 128], F32, kind="ExternalInput").ap()
    IN["r2mat"] = nc.dram_tensor("r2mat", [128, 256], F32, kind="ExternalInput").ap()
    IN["sgn"] = nc.dram_tensor("sgn", [128, T_LOC], F32, kind="ExternalInput").ap()
    IN["meta"] = nc.dram_tensor("meta", [1, 48], I32, kind="ExternalInput").ap()
    IN["identC"] = nc.dram_tensor("identC", [128, 128], BF16, kind="ExternalInput").ap()
    IN["x_out"] = nc.dram_tensor("x_out", [T_LOC, D], F32, kind="ExternalOutput").ap()
    IN["xe_sh"] = nc.dram_tensor("xe_sh", [1024, 3], F32, addr_space="Shared").ap()
    IN["hc_sh"] = nc.dram_tensor("hc_sh", [16384, 1], F32, addr_space="Shared").ap()
    IN["flags"] = nc.dram_tensor("flags", [2, 64], I32, addr_space="Shared").ap()
    IN["flag_base"] = 36
    ctx = Ctx(nc, es)
    emit_l0a(ctx, IN)
    es.close()
    return nc


def l0a_consts():
    import ml_dtypes
    identC = np.eye(128, dtype=ml_dtypes.bfloat16)
    rmat = np.zeros((128, 16 * 128), np.float32)
    smat = np.zeros((128, 16 * 128), np.float32)
    for k in range(16):
        for p in range(128):
            rmat[8 * k + p // 16, k * 128 + p] = 1.0
            smat[p, k * 128 + 8 * k + p // 16] = 1.0
    r2mat = np.zeros((128, 256), np.float32)
    for p in range(128):
        r2mat[64 + p % 16, p] = 1.0
        r2mat[80 + p % 16, 128 + p] = 1.0
    sgn = np.tile(np.array([1.0, -1.0], np.float32), T_LOC // 2)[None, :].repeat(128, 0)
    return identC, rmat, smat, r2mat, np.ascontiguousarray(sgn)


def l0a_in_maps(x_full, ins, nonce=None):
    identC, rmat, smat, r2mat, sgn = l0a_consts()
    f32 = lambda a: np.ascontiguousarray(np.asarray(a, dtype=np.float32))
    chunk8 = lambda v: f32(np.asarray(v, dtype=np.float32).reshape(8, 128).T)
    cw = np.asarray(ins["L0_attn_conv_w"], dtype=np.float32)
    cwc = f32(np.concatenate([chunk8(cw[:, k]) for k in range(4)], axis=1))
    al = np.asarray(ins["L0_attn_A_log"], dtype=np.float32)
    alogr = np.zeros((128, 128), np.float32)
    for tau in range(128):
        for p in range(128):
            alogr[p, tau] = al[8 * tau + p // 16, p % 16]
    if nonce is None:
        nonce = int(np.random.randint(1, 2 ** 30))
    maps = []
    for c in range(NCORE):
        b, h = c // 2, c % 2
        shard = f32(x_full[b, h * T_LOC:(h + 1) * T_LOC, :])
        maps.append(dict(
            x=shard, Wx=f32(ins["L0_attn_Wx"]), Wxp=f32(ins["L0_attn_Wxp"]),
            Wdt=f32(ins["L0_attn_Wdt"]),
            lng_r=rep128(ins["L0_attn_ln_g"]), lnb_r=rep128(ins["L0_attn_ln_b"]),
            bxc=chunk8(ins["L0_attn_bx"]), cbc=chunk8(ins["L0_attn_conv_b"]),
            bdtc=chunk8(ins["L0_attn_bdt"]), dpc=chunk8(ins["L0_attn_Dp"]),
            cwc=cwc, alogr=alogr, rmat=rmat, smat=smat, r2mat=r2mat, sgn=sgn,
            meta=np.array([[h, nonce] + [nonce + kk + 1 for kk in range(46)]],
                          dtype=np.int32),
            identC=identC,
        ))
    return maps


def emit_l2a(ctx, IN, tag="l2a"):
    nc = ctx.nc
    NCH = 32
    H, J = 4, 2
    x_in, x_out = IN["x"], IN["x_out"]
    x2_sh, y2_sh, flags_t = IN["x2_sh"], IN["y2_sh"], IN["flags"]
    FB = IN["flag_base"]
    XB, YB = FB, FB + 16

    ident = ctx.sb("identb", [128, 128], BF16)
    identF = ctx.sb("identFb", [128, 128], F32)
    epsc = ctx.sb("epscol", [128, 1], F32)
    meta = ctx.sb("metai", [1, 48], I32)
    pollcell = ctx.sb("pollcell", [1, 2], I32)
    causal = ctx.sb(f"{tag}_causal", [128, 128], BF16)
    onesb = ctx.sb(f"{tag}_onesb", [128, 128], F32)
    onesbb = ctx.sb(f"{tag}_onesbb", [1, 128], BF16)
    zcol = ctx.sb(f"{tag}_zcol", [128, 1], F32)
    rcol = ctx.sb(f"{tag}_rcol", [128, 1], F32)
    bmgbc = ctx.sb(f"{tag}_bmgbc", [128, 1], F32)
    wqb = ctx.sb(f"{tag}_wqb", [128, 8 * 512], BF16)
    wkb = ctx.sb(f"{tag}_wkb", [128, 8 * 512], BF16)
    wvb = ctx.sb(f"{tag}_wvb", [128, 8 * 512], BF16)
    wbmgb = ctx.sb(f"{tag}_wbmgb", [128, 8 * 36], BF16)
    wob = ctx.sb(f"{tag}_wob", [128, 8 * 1024], BF16)
    boutb = ctx.sb(f"{tag}_boutb", [1, 1024], BF16)
    lngr = ctx.sb(f"{tag}_lngr", [128, D], F32)
    lnbr = ctx.sb(f"{tag}_lnbr", [128, D], F32)

    xpt = ctx.sb(f"{tag}_xpt", [128, 2 * D], BF16)
    xps = ctx.sb(f"{tag}_xps", [128, D], BF16)
    xnr = ctx.sb(f"{tag}_xnr", [128, 2 * D], BF16)
    xnf = ctx.sb(f"{tag}_xnf", [128, 2 * D], BF16)
    e1 = ctx.sb(f"{tag}_e1", [128, 2 * 512], F32)
    qsb = ctx.sb(f"{tag}_qsb", [128, 512], BF16)
    ksb = ctx.sb(f"{tag}_ksb", [128, 512], BF16)
    ktm = ctx.sb(f"{tag}_ktm", [128, 512], BF16)
    vsb = ctx.sb(f"{tag}_vsb", [128, 4 * 129], BF16)
    scbt = ctx.sb(f"{tag}_scb", [128, 128], BF16)
    msb = ctx.sb(f"{tag}_msb", [128, 256], F32)
    dmt = ctx.sb(f"{tag}_dmt", [128, 256], BF16)
    fcol = ctx.sb(f"{tag}_fcol", [128, 2], F32)
    abt = ctx.sb(f"{tag}_abt", [128, 256], BF16)
    vtil = ctx.sb(f"{tag}_vtil", [128, 2 * 129], BF16)
    S_sb = ctx.sb(f"{tag}_S", [128, 8 * 129], F32)
    Scpy = ctx.sb(f"{tag}_Scpy", [128, 8 * 129], BF16)
    ndsb = ctx.sb(f"{tag}_nd", [128, 2 * 129], F32)
    ndt2 = ctx.sb(f"{tag}_ndt2", [128, 129], F32)
    ycol = ctx.sb(f"{tag}_ycol", [128, 4], F32)
    yt128 = ctx.sb(f"{tag}_yt", [128, 128], F32)
    dsc = ctx.sb(f"{tag}_dsc", [128, 128], F32)
    bmg_sb = ctx.sb(f"{tag}_bmg", [128, 128], F32)
    dtm = ctx.sb(f"{tag}_dtm", [128, 40], F32)
    Lrows = ctx.sb(f"{tag}_Lr", [1, 8 * 128], F32)
    negL = ctx.sb(f"{tag}_nLr", [1, 8 * 128], F32)
    pcall = ctx.sb(f"{tag}_pc", [128, 8], F32)
    ybuf = ctx.sb(f"{tag}_ybuf", [128, 32 * 512], BF16)
    yown = ctx.sb(f"{tag}_yown", [128, 16 * 512], BF16)
    ybt = ctx.sb(f"{tag}_ybt", [128, 2 * 512], BF16)
    yfm = ctx.sb(f"{tag}_yfm", [128, 2 * D], BF16)
    xt = ctx.sb("g_xt", [128, 2 * D], F32)
    xng = ctx.sb(f"{tag}_xng", [128, D], F32)
    hb = ctx.sb(f"{tag}_hb", [128, D], F32)
    ot = ctx.sb(f"{tag}_ot", [128, 2 * D], F32)
    sc = ctx.sb("g_sc", [128, 16], F32)
    st6 = ctx.sb("g_st6", [128, 48], F32)
    st2 = ctx.sb("g_st2", [128, 2], F32)
    r1c = ctx.sb("g_r1c", [128, 4], F32)

    pm1 = [ctx.ps(f"ps_pm1_{i}", [128, 512], F32) for i in range(4)]
    pm2 = [ctx.ps(f"ps_pm2_{i}", [128, 512], F32) for i in range(2)]
    ptr = [ctx.ps(f"ps_ptr_{i}", [128, 128], BF16) for i in range(2)]

    def syncp(e):
        for dst, src in ((ident, IN["identC"]), (identF, IN["identF"]),
                         (meta, IN["meta"]), (causal, IN["causalC"]),
                         (rcol, IN["rcol"]), (bmgbc, IN["bmgbc"]),
                         (lngr, IN["lngr"]), (lnbr, IN["lnbr"])):
            ins = e.dma_start(out=dst[:], in_=src)
            ctx.inc(ins, "ld", 16)
        ctx.wait(e, "ld")

    def gpp(e):
        e.memset(epsc[:], LN_EPS)
        e.memset(onesb[:], 1.0)
        e.memset(onesbb[:], 1.0)
        e.memset(zcol[:], 0.0)
        e.memset(S_sb[:], 0.0)
        e.memset(bmg_sb[:], 0.0)
        e.memset(vsb[:], 1.0)
        ins = e.dma_start(out=boutb[:], in_=IN["boutr"])
        ctx.inc(ins, "ldg", 16)
        for dst, src in ((wqb, IN["wq"]), (wkb, IN["wk"]), (wvb, IN["wv"]),
                         (wbmgb, IN["wbmg"]), (wob, IN["wo"])):
            w = dst.shape[1] // 8
            for r in range(8):
                ins = e.dma_start(out=dst[:, r * w:(r + 1) * w],
                                  in_=src[r * 128:(r + 1) * 128, :])
                ctx.inc(ins, "ldg", 16)
        ctx.wait(e, "ldg")
    phase(nc, f"{tag}_pro", _ctx=ctx, sync=syncp, gpsimd=gpp)

    B = {k: ctx.val(k) for k in (
        "zxl", "zxn", "zva", "zsr", "ztp", "zev", "zq", "zk", "zv", "zb",
        "zbe", "zqe", "zke", "zve", "zem", "zbet", "zLw", "zlng", "ztb",
        "zdte", "zkte", "zpcm", "zpce", "zlr", "znl", "zsc", "zscv", "zM",
        "zmn", "zdm", "zab", "zvt", "zscp", "zym", "znd", "zinc", "zsup",
        "zyw", "pub", "yow", "ybl", "zxt", "ztpy", "zevy", "zmm", "zoe", "znd2",
        "zhr", "zgl", "zst", "zva2", "zsr2")}

    def syncM(e):
        with e.register("parZ0") as par:
            e.reg_load(par, meta[0:1, 0:1])
            b_snd = ctx.val("fsnd2")
            saved = {}
            for branch in (0, 1):
                cm = e.If_eq(par, 0) if branch == 0 else e.Else()
                with cm:
                    myp, pep = branch, 1 - branch
                    for c in range(NCH):
                        if c // 16 == myp:
                            e.wait_ge(ctx.h("pub"), B["pub"] + 32 * ((c % 16) + 1))
                        else:
                            emit_flag_poll(ctx, e, XB + (c % 16),
                                           flags_t[pep:pep + 1, XB + (c % 16):XB + (c % 16) + 1],
                                           meta, pollcell, pol="fpol2")
                        if c >= 2:
                            e.wait_ge(ctx.h("zxn"), B["zxn"] + c - 1)
                        ins = e.dma_start(out=xpt[:, (c % 2) * D:((c % 2) + 1) * D],
                                          in_=x2_sh[c * 128:(c + 1) * 128, :])
                        ctx.inc(ins, "zxl", 16)
                        e.wait_ge(ctx.h("zLw"), B["zLw"] + c + 1)
                        if c >= 1:
                            e.wait_ge(ctx.h("zsup"), B["zsup"] + 8 * c)
                        ins = e.dma_start(out=Lrows[0:1, :], in_=bmg_sb[64:72, :])
                        ctx.inc(ins, "zlr", 16)
                        if c // 16 == pep:
                            e.wait_ge(ctx.h("zyw"), B["zyw"] + c + 1)
                            ins = e.dma_start(
                                out=y2_sh[myp * 2048 + (c % 16) * 128:myp * 2048 + (c % 16) * 128 + 128, :],
                                in_=ybuf[:, c * 512:(c + 1) * 512])
                            ctx.inc(ins, "fsnd2", 16)
                            ctx.wait(e, "fsnd2")
                            emit_flag_send(ctx, e, YB + (c % 16),
                                           flags_t[myp:myp + 1, YB + (c % 16):YB + (c % 16) + 1],
                                           meta, snd="fsnd2")
                    for c2 in range(16):
                        gc = myp * 16 + c2
                        e.wait_ge(ctx.h("zyw"), B["zyw"] + gc + 1)
                        ins = e.dma_start(out=yown[:, c2 * 512:(c2 + 1) * 512],
                                          in_=ybuf[:, gc * 512:(gc + 1) * 512])
                        ctx.inc(ins, "yow", 16)
                        emit_flag_poll(ctx, e, YB + c2,
                                       flags_t[pep:pep + 1, YB + c2:YB + c2 + 1],
                                       meta, pollcell, pol="fpol2")
                        if c2 >= 2:
                            e.wait_ge(ctx.h("zevy"), B["zevy"] + 8 * (c2 - 1))
                        ins = e.dma_start(out=ybt[:, (c2 % 2) * 512:((c2 % 2) + 1) * 512],
                                          in_=y2_sh[pep * 2048 + c2 * 128:pep * 2048 + c2 * 128 + 128, :])
                        ctx.inc(ins, "ybl", 16)
                        if c2 >= 2:
                            e.wait_ge(ctx.h("zhr"), B["zhr"] + c2 - 1)
                        ins = e.dma_start(out=xt[:, (c2 % 2) * D:((c2 % 2) + 1) * D],
                                          in_=x_in[c2 * 128:(c2 + 1) * 128, :])
                        ctx.inc(ins, "zxt", 16)
                    if branch == 0:
                        for nm in ("fsnd2", "zxl", "zlr", "yow", "ybl", "zxt"):
                            saved[nm] = ctx.val(nm)
                        ctx.setval("fsnd2", b_snd)
                        for nm in ("zxl", "zlr", "yow", "ybl", "zxt"):
                            ctx.setval(nm, B[nm])
            for nm, v in saved.items():
                ctx.setval(nm, v)

    def gpM(e):
        with e.register("parZ1") as par:
            e.reg_load(par, meta[0:1, 0:1])
            b_snd = ctx.val("fsnd")
            saved = {}
            for branch in (0, 1):
                cm = e.If_eq(par, 0) if branch == 0 else e.Else()
                with cm:
                    myp = branch
                    for cl in range(16):
                        ins = e.dma_start(out=xps[:], in_=x_in[cl * 128:(cl + 1) * 128, :])
                        ctx.inc(ins, "pub", 16)
                        ctx.wait(e, "pub")
                        ins = e.dma_start(
                            out=x2_sh[myp * 2048 + cl * 128:myp * 2048 + cl * 128 + 128, :],
                            in_=xps[:])
                        ctx.inc(ins, "pub", 16)
                        ctx.wait(e, "pub")
                        emit_flag_send(ctx, e, XB + cl,
                                       flags_t[myp:myp + 1, XB + cl:XB + cl + 1], meta)
                    if branch == 0:
                        saved["pub"] = ctx.val("pub")
                        saved["fsnd"] = ctx.val("fsnd")
                        ctx.setval("pub", B["pub"])
                        ctx.setval("fsnd", b_snd)
            for nm, v in saved.items():
                ctx.setval(nm, v)
        for c in range(NCH):
            e.wait_ge(ctx.h("zqe"), B["zqe"] + c + 1)
            e.tensor_scalar_min(e1[:, 0:512], e1[:, 0:512], 1.0)
            ctx.sig(e, "zem", 1)
            e.wait_ge(ctx.h("zke"), B["zke"] + c + 1)
            e.tensor_scalar_min(e1[:, 512:1024], e1[:, 512:1024], 1.0)
            ctx.sig(e, "zem", 1)
            e.wait_ge(ctx.h("zbet"), B["zbet"] + 4 * c + 1)
            e.tensor_scalar_mul(dsc[0:8, :], dsc[0:8, :], rcol[0:8, 0:1])
            e.tensor_scalar_min(dsc[0:8, :], dsc[0:8, :], 0.9999)
            ctx.sig(e, "zbet", 1)
            e.wait_ge(ctx.h("zbet"), B["zbet"] + 4 * c + 3)
            e.tensor_scalar_max(dsc[0:8, :], dsc[0:8, :], -30.0)
            ctx.sig(e, "zbet", 1)
            e.wait_ge(ctx.h("zlng"), B["zlng"] + 2 * c + 1)
            e.tensor_scalar_max(bmg_sb[32:36, :], bmg_sb[32:36, :], -30.0)
            ctx.sig(e, "zlng", 1)
            e.wait_ge(ctx.h("zsup"), B["zsup"] + 8 * c)
            for ch in range(8):
                e.tensor_copy(Scpy[:, ch * 129:(ch + 1) * 129],
                              S_sb[:, ch * 129:(ch + 1) * 129])
            ctx.sig(e, "zscp", 1)
            for h in range(H):
                hh = 4 * c + h
                for j in range(J):
                    e.wait_ge(ctx.h("zdm"), B["zdm"] + hh + 1)
                    e.wait_ge(ctx.h("zscv"), B["zscv"] + hh + 1)
                    if hh >= 1 and j == 0:
                        e.wait_ge(ctx.h("zym"), B["zym"] + 2 * hh)
                        e.wait_ge(ctx.h("zinc"), B["zinc"] + 2 * hh)
                    e.tensor_mul(abt[:, j * 128:(j + 1) * 128], scbt[:],
                                 dmt[:, j * 128:(j + 1) * 128])
                    ctx.sig(e, "zab", 1)
                    e.tensor_scalar_mul(vtil[:, j * 129:(j + 1) * 129],
                                        vsb[:, h * 129:(h + 1) * 129],
                                        fcol[:, j:j + 1])
                    ctx.sig(e, "zvt", 1)

    def vecM(e):
        for c in range(NCH):
            xs = xpt[:, (c % 2) * D:((c % 2) + 1) * D]
            e.wait_ge(ctx.h("zxl"), B["zxl"] + 16 * (c + 1))
            ln_stats(e, xs, D, st6, st2, sc[:, 0:1], sc[:, 1:2], "zva", ctx)
            e.wait_ge(ctx.h("zsr"), B["zsr"] + c + 1)
            if c >= 2:
                e.wait_ge(ctx.h("ztp"), B["ztp"] + 12 * (c - 2) + 8)
            e.tensor_scalar(xnr[:, (c % 2) * D:((c % 2) + 1) * D], xs,
                            sc[:, 0:1], r1c[:, 0:1], ALU.subtract, ALU.mult)
            ctx.sig(e, "zxn", 1)
            e.wait_ge(ctx.h("zbet"), B["zbet"] + 4 * c + 4)
            if c >= 1:
                e.wait_ge(ctx.h("zlr"), B["zlr"] + 16 * c)
                e.wait_ge(ctx.h("ztb"), B["ztb"] + c)
            e.tensor_tensor_scan(bmg_sb[64:72, :], onesb[0:8, :], dsc[0:8, :],
                                 zcol[0:8, 0:1], ALU.mult, ALU.add)
            ctx.sig(e, "zLw", 1)
            e.wait_ge(ctx.h("zlr"), B["zlr"] + 16 * (c + 1))
            if c >= 1:
                e.wait_ge(ctx.h("zsup"), B["zsup"] + 8 * c)
            e.tensor_scalar_mul(negL[:], Lrows[:], -1.0)
            ctx.sig(e, "znl", 1)
            e.wait_ge(ctx.h("zem"), B["zem"] + 2 * c + 1)
            if c >= 1:
                e.wait_ge(ctx.h("zym"), B["zym"] + 8 * c)
            e.scalar_tensor_tensor(qsb[:], pm1[0][:], 0.0, e1[:, 0:512],
                                   ALU.max, ALU.add)
            ctx.sig(e, "zq", 1)
            e.wait_ge(ctx.h("zem"), B["zem"] + 2 * c + 2)
            e.scalar_tensor_tensor(ksb[:], pm1[1][:], 0.0, e1[:, 512:1024],
                                   ALU.max, ALU.add)
            ctx.sig(e, "zk", 1)
            e.wait_ge(ctx.h("zdte"), B["zdte"] + 2 * c + 1)
            e.tensor_tensor(ycol[:], dtm[:, 0:8:2], dtm[:, 1:8:2], ALU.subtract)
            ctx.sig(e, "zpcm", 1)
            for h in range(H):
                hh = 4 * c + h
                e.wait_ge(ctx.h("zsc"), B["zsc"] + hh + 1)
                if hh >= 1:
                    e.wait_ge(ctx.h("zab"), B["zab"] + 2 * hh)
                e.tensor_mul(scbt[:], pm2[0][:, 0:128], causal[:])
                ctx.sig(e, "zscv", 1)
                e.wait_ge(ctx.h("zM"), B["zM"] + 4 * hh + 4)
                e.wait_ge(ctx.h("zdm"), B["zdm"] + hh)
                e.tensor_scalar_min(msb[:], pm2[1][:, 0:256], 0.0)
                ctx.sig(e, "zmn", 1)
                for j in range(J):
                    e.wait_ge(ctx.h("znd2"), B["znd2"] + 2 * hh + j + 1)
                    e.tensor_add(ndsb[:, j * 129:(j + 1) * 129],
                                 pm2[0][:, 128:257], ndt2[:])
                    ctx.sig(e, "znd", 1)
                for j in range(J):
                    ch = 2 * h + j
                    e.wait_ge(ctx.h("zinc"), B["zinc"] + 2 * hh + j + 1)
                    e.scalar_tensor_tensor(
                        S_sb[:, ch * 129:(ch + 1) * 129],
                        S_sb[:, ch * 129:(ch + 1) * 129],
                        pcall[:, ch:ch + 1],
                        pm2[1][:, 256:385] if j == 0 else pm1[3][:, 0:129],
                        ALU.mult, ALU.add)
                    ctx.sig(e, "zsup", 1)
                if h == 0:
                    e.wait_ge(ctx.h("zpcm"), B["zpcm"] + 2 * c + 2)
                e.tensor_scalar_add(ycol[:, 0:1], ndsb[:, 128:129], 1e-6)
                e.reciprocal(ycol[:, 0:1], ycol[:, 0:1])
                e.tensor_mul(ycol[:, 0:1], ycol[:, 0:1], dtm[:, 28 + h:29 + h])
                e.tensor_scalar_add(ycol[:, 1:2], ndsb[:, 257:258], 1e-6)
                e.reciprocal(ycol[:, 1:2], ycol[:, 1:2])
                e.tensor_mul(ycol[:, 1:2], ycol[:, 1:2], dtm[:, 32 + h:33 + h])
                e.tensor_scalar_mul(yt128[:], ndsb[:, 0:128], ycol[:, 0:1])
                e.scalar_tensor_tensor(ybuf[:, c * 512 + h * 128:c * 512 + (h + 1) * 128],
                                       ndsb[:, 129:257], ycol[:, 1:2], yt128[:],
                                       ALU.mult, ALU.add)
            ctx.sig(e, "zyw", 1)
        for c2 in range(16):
            xs = xt[:, (c2 % 2) * D:((c2 % 2) + 1) * D]
            e.wait_ge(ctx.h("zxt"), B["zxt"] + 16 * (c2 + 1))
            ln_stats(e, xs, D, st6, st2, sc[:, 2:3], sc[:, 3:4], "zva2", ctx)
            e.wait_ge(ctx.h("zsr2"), B["zsr2"] + 3 * c2 + 1)
            e.tensor_scalar(xng[:], xs, sc[:, 2:3], r1c[:, 1:2],
                            ALU.subtract, ALU.mult)
            e.wait_ge(ctx.h("zoe"), B["zoe"] + 2 * (c2 + 1))
            e.tensor_add(hb[:], xng[:], ot[:, (c2 % 2) * D:((c2 % 2) + 1) * D])
            ln_stats(e, hb[:], D, st6, st2, sc[:, 4:5], sc[:, 5:6], "zva2", ctx)
            e.wait_ge(ctx.h("zsr2"), B["zsr2"] + 3 * c2 + 2)
            e.tensor_scalar(hb[:], hb[:], sc[:, 4:5], r1c[:, 2:3],
                            ALU.subtract, ALU.mult)
            e.tensor_mul(hb[:], hb[:], lngr[:])
            e.tensor_add(hb[:], hb[:], lnbr[:])
            ln_stats(e, hb[:], D, st6, st2, sc[:, 6:7], sc[:, 7:8], "zva2", ctx)
            e.wait_ge(ctx.h("zsr2"), B["zsr2"] + 3 * c2 + 3)
            os_ = ot[:, (c2 % 2) * D:((c2 % 2) + 1) * D]
            e.tensor_scalar(os_, hb[:], sc[:, 6:7], r1c[:, 3:4],
                            ALU.subtract, ALU.mult)
            e.tensor_add(os_, os_, xs)
            ctx.sig(e, "zhr", 1)
            ctx.sig(e, "zgl", 1)

    def scalM(e):
        for c in range(NCH):
            e.wait_ge(ctx.h("zva"), B["zva"] + c + 1)
            rstd_ops(e, sc[:, 1:2], sc[:, 12:13], r1c[:, 0:1], epsc[:])
            ctx.sig(e, "zsr", 1)
            TP = B["ztp"] + 12 * c
            for i in range(8):
                e.wait_ge(ctx.h("ztp"), TP + i + 1)
                if c >= 2:
                    e.wait_ge(ctx.h("zb"), B["zb"] + c - 1)
                e.activation(xnf[:, (c % 2) * D + i * 128:(c % 2) * D + (i + 1) * 128],
                             ptr[i % 2][:], AF.Identity)
                ctx.sig(e, "zev", 1)
            e.wait_ge(ctx.h("zq"), B["zq"] + 2 * c + 1)
            e.activation(e1[:, 0:512], pm1[0][:], AF.Exp)
            ctx.sig(e, "zqe", 1)
            e.wait_ge(ctx.h("zk"), B["zk"] + 2 * c + 1)
            e.activation(e1[:, 512:1024], pm1[1][:], AF.Exp)
            ctx.sig(e, "zke", 1)
            e.wait_ge(ctx.h("zv"), B["zv"] + c + 1)
            if c >= 1:
                e.wait_ge(ctx.h("zym"), B["zym"] + 8 * c)
                e.wait_ge(ctx.h("zvt"), B["zvt"] + 8 * c)
            for h in range(H):
                e.activation(vsb[:, h * 129:h * 129 + 128],
                             pm1[2][:, h * 128:(h + 1) * 128], AF.Identity)
            ctx.sig(e, "zve", 1)
            e.wait_ge(ctx.h("zb"), B["zb"] + c + 1)
            if c >= 1:
                e.wait_ge(ctx.h("ztb"), B["ztb"] + c)
            e.activation(bmg_sb[0:36, :], pm1[3][0:36, 0:128], AF.Identity,
                         bias=bmgbc[0:36, 0:1])
            ctx.sig(e, "zbe", 1)
            e.activation(dsc[0:8, :], bmg_sb[0:8, :], AF.Sigmoid)
            ctx.sig(e, "zbet", 1)
            e.wait_ge(ctx.h("zbet"), B["zbet"] + 4 * c + 2)
            e.activation(dsc[0:8, :], dsc[0:8, :], AF.Ln)
            ctx.sig(e, "zbet", 1)
            e.activation(bmg_sb[32:36, :], bmg_sb[32:36, :], AF.Sigmoid)
            e.activation(bmg_sb[32:36, :], bmg_sb[32:36, :], AF.Ln)
            ctx.sig(e, "zlng", 1)
            for i in range(4):
                e.wait_ge(ctx.h("ztp"), TP + 8 + i + 1)
                if c >= 1:
                    e.wait_ge(ctx.h("zinc"), B["zinc"] + 8 * c)
                e.activation(ktm[:, i * 128:(i + 1) * 128], ptr[i % 2][:], AF.Identity)
                ctx.sig(e, "zkte", 1)
            e.wait_ge(ctx.h("ztb"), B["ztb"] + c + 1)
            e.activation(dtm[:, 0:8], pm2[1][:, 8:16], AF.Identity)
            e.activation(dtm[:, 8:12], pm2[1][:, 32:36], AF.Identity)
            e.activation(dtm[:, 12:20], pm2[1][:, 64:72], AF.Identity)
            ctx.sig(e, "zdte", 1)
            e.activation(dtm[:, 20:28], dtm[:, 12:20], AF.Exp)
            ctx.sig(e, "zdte", 1)
            e.wait_ge(ctx.h("zpcm"), B["zpcm"] + 2 * c + 1)
            e.activation(dtm[:, 28:32], ycol[:], AF.Sigmoid)
            e.activation(dtm[:, 32:36], ycol[:], AF.Sigmoid, scale=-1.0)
            ctx.sig(e, "zpcm", 1)
            e.wait_ge(ctx.h("zpce"), B["zpce"] + 2 * c + 1)
            e.activation(pcall[:], pm2[1][:, 128:136], AF.Exp)
            ctx.sig(e, "zpce", 1)
            for h in range(H):
                hh = 4 * c + h
                e.wait_ge(ctx.h("zmn"), B["zmn"] + hh + 1)
                e.wait_ge(ctx.h("zvt"), B["zvt"] + 2 * hh)
                e.activation(dmt[:], msb[:], AF.Exp, bias=dtm[:, 8 + h:9 + h])
                e.activation(fcol[:, 0:1], msb[:, 127:128], AF.Exp,
                             bias=dtm[:, 8 + h:9 + h])
                e.activation(fcol[:, 1:2], msb[:, 255:256], AF.Exp,
                             bias=dtm[:, 8 + h:9 + h])
                ctx.sig(e, "zdm", 1)
                for j in range(J):
                    ch = 2 * h + j
                    e.wait_ge(ctx.h("zym"), B["zym"] + 2 * hh + j + 1)
                    e.activation(ndt2[:], pm2[0][:, 257:386],
                                 AF.Identity, scale=dtm[:, 20 + ch:21 + ch])
                    ctx.sig(e, "znd2", 1)
        for c2 in range(16):
            e.wait_ge(ctx.h("zva2"), B["zva2"] + 3 * c2 + 1)
            rstd_ops(e, sc[:, 3:4], sc[:, 12:13], r1c[:, 1:2], epsc[:])
            ctx.sig(e, "zsr2", 1)
            TPY = B["ztpy"] + 8 * c2
            for i in range(8):
                e.wait_ge(ctx.h("ztpy"), TPY + i + 1)
                e.activation(yfm[:, (c2 % 2) * D + i * 128:(c2 % 2) * D + (i + 1) * 128],
                             ptr[i % 2][:], AF.Identity)
                ctx.sig(e, "zevy", 1)
            for s in range(2):
                e.wait_ge(ctx.h("zmm"), B["zmm"] + 2 * c2 + s + 1)
                if c2 >= 2:
                    e.wait_ge(ctx.h("zst"), B["zst"] + 16 * (c2 - 1))
                e.activation(ot[:, (c2 % 2) * D + s * 512:(c2 % 2) * D + (s + 1) * 512],
                             pm1[s][:], AF.Identity)
                ctx.sig(e, "zoe", 1)
            e.wait_ge(ctx.h("zva2"), B["zva2"] + 3 * c2 + 2)
            rstd_ops(e, sc[:, 5:6], sc[:, 12:13], r1c[:, 2:3], epsc[:])
            ctx.sig(e, "zsr2", 1)
            e.wait_ge(ctx.h("zva2"), B["zva2"] + 3 * c2 + 3)
            rstd_ops(e, sc[:, 7:8], sc[:, 12:13], r1c[:, 3:4], epsc[:])
            ctx.sig(e, "zsr2", 1)

    def tensM(e):
        for c in range(NCH):
            TP = B["ztp"] + 12 * c
            EV = B["zev"] + 8 * c
            e.wait_ge(ctx.h("zxn"), B["zxn"] + c + 1)
            e.wait_ge(ctx.h("zev"), EV)
            for i in range(8):
                if i >= 2:
                    e.wait_ge(ctx.h("zev"), EV + i - 1)
                e.transpose(ptr[i % 2][:],
                            xnr[:, (c % 2) * D + i * 128:(c % 2) * D + (i + 1) * 128],
                            ident[:]).then_inc(ctx.h("ztp"), 1)
            e.wait_ge(ctx.h("zev"), EV + 8)
            xc = xnf[:, (c % 2) * D:((c % 2) + 1) * D]
            e.wait_ge(ctx.h("zq"), B["zq"] + 2 * c)
            for h in range(H):
                for r in range(8):
                    mm = e.matmul(pm1[0][:, h * 128:(h + 1) * 128],
                                  wqb[:, r * 512 + h * 128:r * 512 + (h + 1) * 128],
                                  xc[:, r * 128:(r + 1) * 128],
                                  start=(r == 0), stop=(r == 7))
            mm.then_inc(ctx.h("zq"), 1)
            e.wait_ge(ctx.h("zk"), B["zk"] + 2 * c)
            for h in range(H):
                for r in range(8):
                    mm = e.matmul(pm1[1][:, h * 128:(h + 1) * 128],
                                  wkb[:, r * 512 + h * 128:r * 512 + (h + 1) * 128],
                                  xc[:, r * 128:(r + 1) * 128],
                                  start=(r == 0), stop=(r == 7))
            mm.then_inc(ctx.h("zk"), 1)
            e.wait_ge(ctx.h("zve"), B["zve"] + c)
            for r in range(8):
                mm = e.matmul(pm1[2][:], xc[:, r * 128:(r + 1) * 128],
                              wvb[:, r * 512:(r + 1) * 512],
                              start=(r == 0), stop=(r == 7))
            mm.then_inc(ctx.h("zv"), 1)
            e.wait_ge(ctx.h("znd"), B["znd"] + 8 * c)
            e.wait_ge(ctx.h("zsup"), B["zsup"] + 8 * c)
            e.wait_ge(ctx.h("zbe"), B["zbe"] + c)
            for r in range(8):
                mm = e.matmul(pm1[3][0:36, 0:128], wbmgb[:, r * 36:(r + 1) * 36],
                              xc[:, r * 128:(r + 1) * 128],
                              start=(r == 0), stop=(r == 7))
            mm.then_inc(ctx.h("zb"), 1)
            e.wait_ge(ctx.h("zk"), B["zk"] + 2 * c + 2)
            for i in range(4):
                if i < 2:
                    e.wait_ge(ctx.h("zev"), EV + 7 + i)
                else:
                    e.wait_ge(ctx.h("zkte"), B["zkte"] + 4 * c + i - 1)
                e.transpose(ptr[i % 2][:], ksb[:, i * 128:(i + 1) * 128],
                            ident[:]).then_inc(ctx.h("ztp"), 1)
            e.wait_ge(ctx.h("zLw"), B["zLw"] + c + 1)
            e.wait_ge(ctx.h("zlng"), B["zlng"] + 2 * (c + 1))
            if c >= 1:
                e.wait_ge(ctx.h("zsup"), B["zsup"] + 8 * c)
                e.wait_ge(ctx.h("zpce"), B["zpce"] + 2 * c)
            e.transpose(pm2[1][:, 0:128], bmg_sb[:], identF[:]).then_inc(ctx.h("ztb"), 1)
            e.wait_ge(ctx.h("zdte"), B["zdte"] + 2 * c + 1)
            e.wait_ge(ctx.h("zlr"), B["zlr"] + 16 * (c + 1))
            mm = e.matmul(pm2[1][:, 128:136], onesb[0:1, :],
                          Lrows[0:1, 127:1024:128], start=True, stop=True)
            mm.then_inc(ctx.h("zpce"), 1)
            for h in range(H):
                hh = 4 * c + h
                e.wait_ge(ctx.h("zq"), B["zq"] + 2 * (c + 1))
                e.wait_ge(ctx.h("zk"), B["zk"] + 2 * (c + 1))
                e.wait_ge(ctx.h("znd"), B["znd"] + 2 * hh)
                e.wait_ge(ctx.h("zscv"), B["zscv"] + hh)
                mm = e.matmul(pm2[0][:, 0:128], ksb[:, h * 128:(h + 1) * 128],
                              qsb[:, h * 128:(h + 1) * 128], start=True, stop=True)
                mm.then_inc(ctx.h("zsc"), 1)
                e.wait_ge(ctx.h("zpce"), B["zpce"] + 2 * c + 2)
                e.wait_ge(ctx.h("znl"), B["znl"] + c + 1)
                if hh >= 1:
                    e.wait_ge(ctx.h("zsup"), B["zsup"] + 2 * hh)
                e.wait_ge(ctx.h("zmn"), B["zmn"] + hh)
                for j in range(J):
                    ch = 2 * h + j
                    mm = e.matmul(pm2[1][:, j * 128:(j + 1) * 128], onesb[0:1, :],
                                  Lrows[0:1, ch * 128:(ch + 1) * 128],
                                  start=True, stop=False)
                    mm.then_inc(ctx.h("zM"), 1)
                    mm = e.matmul(pm2[1][:, j * 128:(j + 1) * 128],
                                  negL[0:1, ch * 128:(ch + 1) * 128], onesb[0:1, 0:128],
                                  start=False, stop=True)
                    mm.then_inc(ctx.h("zM"), 1)
                e.wait_ge(ctx.h("zscp"), B["zscp"] + c + 1)
                for j in range(J):
                    ch = 2 * h + j
                    e.wait_ge(ctx.h("zab"), B["zab"] + 2 * hh + j + 1)
                    if j == 1:
                        e.wait_ge(ctx.h("znd"), B["znd"] + 2 * hh + 1)
                    e.matmul(pm2[0][:, 128:257], abt[:, j * 128:(j + 1) * 128],
                             vsb[:, h * 129:(h + 1) * 129], start=True, stop=True)
                    mm = e.matmul(pm2[0][:, 257:386], qsb[:, h * 128:(h + 1) * 128],
                                  Scpy[:, ch * 129:(ch + 1) * 129],
                                  start=True, stop=True)
                    mm.then_inc(ctx.h("zym"), 1)
                for j in range(J):
                    e.wait_ge(ctx.h("zvt"), B["zvt"] + 2 * hh + j + 1)
                    if j == 0:
                        e.wait_ge(ctx.h("zym"), B["zym"] + 2 * hh + 1)
                        dst = pm2[1][:, 256:385]
                    else:
                        e.wait_ge(ctx.h("zmn"), B["zmn"] + hh + 1)
                        e.wait_ge(ctx.h("zb"), B["zb"] + c + 1)
                        e.wait_ge(ctx.h("zbe"), B["zbe"] + c + 1)
                        dst = pm1[3][:, 0:129]
                    mm = e.matmul(dst, ktm[:, h * 128:(h + 1) * 128],
                                  vtil[:, j * 129:(j + 1) * 129], start=True, stop=True)
                    mm.then_inc(ctx.h("zinc"), 1)
        for c2 in range(16):
            TPY = B["ztpy"] + 8 * c2
            EVY = B["zevy"] + 8 * c2
            e.wait_ge(ctx.h("yow"), B["yow"] + 16 * (c2 + 1))
            e.wait_ge(ctx.h("ybl"), B["ybl"] + 16 * (c2 + 1))
            e.wait_ge(ctx.h("zevy"), EVY)
            if c2 == 0:
                e.wait_ge(ctx.h("zkte"), B["zkte"] + 4 * NCH)
                e.wait_ge(ctx.h("zq"), B["zq"] + 2 * NCH)
                e.wait_ge(ctx.h("zk"), B["zk"] + 2 * NCH)
            for i in range(8):
                if i >= 2:
                    e.wait_ge(ctx.h("zevy"), EVY + i - 1)
                src = yown[:, c2 * 512 + i * 128:c2 * 512 + (i + 1) * 128] if i < 4 \
                    else ybt[:, (c2 % 2) * 512 + (i - 4) * 128:(c2 % 2) * 512 + (i - 3) * 128]
                e.transpose(ptr[i % 2][:], src, ident[:]).then_inc(ctx.h("ztpy"), 1)
            e.wait_ge(ctx.h("zevy"), EVY + 8)
            yc = yfm[:, (c2 % 2) * D:((c2 % 2) + 1) * D]
            for s in range(2):
                e.wait_ge(ctx.h("zoe"), B["zoe"] + 2 * c2 + s)
                for r in range(8):
                    e.matmul(pm1[s][:], yc[:, r * 128:(r + 1) * 128],
                             wob[:, r * 1024 + s * 512:r * 1024 + (s + 1) * 512],
                             start=(r == 0), stop=False)
                mm = e.matmul(pm1[s][:], onesbb[0:1, :],
                              boutb[0:1, s * 512:(s + 1) * 512],
                              start=False, stop=True)
                mm.then_inc(ctx.h("zmm"), 1)

    def gpG(e):
        for c2 in range(16):
            e.wait_ge(ctx.h("zgl"), B["zgl"] + c2 + 1)
            e.dma_start(out=x_out[c2 * 128:(c2 + 1) * 128, :],
                        in_=ot[:, (c2 % 2) * D:((c2 % 2) + 1) * D]).then_inc(ctx.h("zst"), 16)
        e.wait_ge(ctx.h("zst"), B["zst"] + 16 * 16)

    def gpALL(e):
        gpM(e)
        gpG(e)

    phase(nc, f"{tag}_main", _ctx=ctx, sync=syncM, gpsimd=gpALL, vector=vecM,
          scalar=scalM, tensor=tensM)
    for k, r in dict(zxl=16 * 32, zxn=32, zva=32, zsr=32, ztp=12 * 32, zev=8 * 32,
                     zq=2 * 32, zk=2 * 32, zv=32, zb=32, zbe=32, zqe=32, zke=32,
                     zve=32, zem=2 * 32, zbet=4 * 32, zLw=32, zlng=2 * 32, ztb=32,
                     zdte=2 * 32, zkte=4 * 32, zpcm=2 * 32, zpce=2 * 32,
                     zlr=16 * 32, znl=32, zsc=4 * 32, zscv=4 * 32, zM=16 * 32,
                     zmn=4 * 32, zdm=4 * 32, zab=8 * 32, zvt=8 * 32, zscp=32,
                     zym=8 * 32, znd=8 * 32, znd2=8 * 32, zinc=8 * 32, zsup=8 * 32, zyw=32,
                     pub=32 * 16, yow=16 * 16, ybl=16 * 16, zxt=16 * 16,
                     ztpy=8 * 16, zevy=8 * 16, zmm=2 * 16, zoe=2 * 16, zhr=16,
                     zgl=16, zst=16 * 16, zva2=3 * 16, zsr2=3 * 16).items():
        ctx.setval(k, B[k] + r)


def build_l2a_program():
    es = contextlib.ExitStack()
    nc = mk_bass()
    IN = {}
    IN["x"] = nc.dram_tensor("x", [T_LOC, D], F32, kind="ExternalInput").ap()
    for nm in ("wq", "wk", "wv"):
        IN[nm] = nc.dram_tensor(nm, [D, 512], F32, kind="ExternalInput").ap()
    IN["wbmg"] = nc.dram_tensor("wbmg", [D, 36], F32, kind="ExternalInput").ap()
    IN["wo"] = nc.dram_tensor("wo", [D, D], F32, kind="ExternalInput").ap()
    IN["boutr"] = nc.dram_tensor("boutr", [1, D], F32, kind="ExternalInput").ap()
    IN["lngr"] = nc.dram_tensor("lngr", [128, D], F32, kind="ExternalInput").ap()
    IN["lnbr"] = nc.dram_tensor("lnbr", [128, D], F32, kind="ExternalInput").ap()
    IN["rcol"] = nc.dram_tensor("rcol", [128, 1], F32, kind="ExternalInput").ap()
    IN["bmgbc"] = nc.dram_tensor("bmgbc", [128, 1], F32, kind="ExternalInput").ap()
    IN["identC"] = nc.dram_tensor("identC", [128, 128], BF16, kind="ExternalInput").ap()
    IN["identF"] = nc.dram_tensor("identF", [128, 128], F32, kind="ExternalInput").ap()
    IN["causalC"] = nc.dram_tensor("causalC", [128, 128], BF16, kind="ExternalInput").ap()
    IN["meta"] = nc.dram_tensor("meta", [1, 48], I32, kind="ExternalInput").ap()
    IN["x_out"] = nc.dram_tensor("x_out", [T_LOC, D], F32, kind="ExternalOutput").ap()
    IN["x2_sh"] = nc.dram_tensor("x2_sh", [4096, D], BF16, addr_space="Shared").ap()
    IN["y2_sh"] = nc.dram_tensor("y2_sh", [4096, 512], BF16, addr_space="Shared").ap()
    IN["flags"] = nc.dram_tensor("flags", [2, 64], I32, addr_space="Shared").ap()
    IN["flag_base"] = 0
    ctx = Ctx(nc, es)
    emit_l2a(ctx, IN)
    es.close()
    return nc


def _alibi_slopes_np(n):
    import math
    def pow2(m):
        start = 2 ** (-2 ** (-(math.log2(m) - 3)))
        return [start * start ** i for i in range(m)]
    if math.log2(n).is_integer():
        return np.array(pow2(n), dtype=np.float32)
    c = 2 ** int(math.floor(math.log2(n)))
    s = pow2(c) + _alibi_slopes_np(2 * c).tolist()[:n - c]
    return np.array(s, dtype=np.float32)


def l2a_in_maps(x_full, ins, nonce=None):
    import ml_dtypes
    identC = np.eye(128, dtype=ml_dtypes.bfloat16)
    identF = np.eye(128, dtype=np.float32)
    causal = np.triu(np.ones((128, 128), np.float32)).astype(ml_dtypes.bfloat16)
    f32 = lambda a: np.ascontiguousarray(np.asarray(a, dtype=np.float32))
    r_all = np.exp(-_alibi_slopes_np(8))
    if nonce is None:
        nonce = int(np.random.randint(1, 2 ** 30))
    Wq, Wk, Wv = ins["L2_attn_Wq"], ins["L2_attn_Wk"], ins["L2_attn_Wv"]
    Wbeta, bbeta = ins["L2_attn_Wbeta"], ins["L2_attn_bbeta"]
    Wmix, bmix = ins["L2_attn_Wmix"], ins["L2_attn_bmix"]
    Wgate, bgate = ins["L2_attn_Wgate"], ins["L2_attn_bgate"]
    Wout, bout = ins["L2_attn_Wout"], ins["L2_attn_bout"]
    maps = []
    for cidx in range(NCORE):
        b, h = cidx // 2, cidx % 2
        shard = f32(x_full[b, h * T_LOC:(h + 1) * T_LOC, :])
        sl = slice(h * 512, (h + 1) * 512)
        wbmg = np.zeros((36, D), np.float32)
        wbmg[0:8] = np.asarray(Wbeta, np.float32)[h * 8:(h + 1) * 8]
        wbmg[8:16] = np.asarray(Wmix, np.float32)[h * 8:(h + 1) * 8]
        wbmg[32:36] = np.asarray(Wgate, np.float32)[h * 4:(h + 1) * 4]
        bmgb = np.zeros((128, 1), np.float32)
        bmgb[0:8, 0] = np.asarray(bbeta, np.float32)[h * 8:(h + 1) * 8]
        bmgb[8:16, 0] = np.asarray(bmix, np.float32)[h * 8:(h + 1) * 8]
        bmgb[32:36, 0] = np.asarray(bgate, np.float32)[h * 4:(h + 1) * 4]
        rc = np.zeros((128, 1), np.float32)
        for ch in range(8):
            rc[ch, 0] = r_all[h * 4 + ch // 2]
        WoT = np.asarray(Wout, np.float32).T
        perm = np.concatenate([np.arange(h * 512, (h + 1) * 512),
                               np.arange((1 - h) * 512, (2 - h) * 512)])
        maps.append(dict(
            x=shard,
            wq=f32(np.asarray(Wq, np.float32)[sl].T),
            wk=f32(np.asarray(Wk, np.float32)[sl].T),
            wv=f32(np.asarray(Wv, np.float32)[sl].T),
            wbmg=f32(wbmg.T), wo=f32(WoT[perm]),
            boutr=f32(np.asarray(bout, np.float32).reshape(1, -1)),
            lngr=rep128(ins["L2_attn_ln_g"]), lnbr=rep128(ins["L2_attn_ln_b"]),
            rcol=rc, bmgbc=bmgb, identC=identC, identF=identF, causalC=causal,
            meta=np.array([[h, nonce] + [nonce + kk + 1 for kk in range(46)]],
                          dtype=np.int32),
        ))
    return maps


def emit_l1a(ctx, IN, tag="l1a"):
    nc = ctx.nc
    NCH = 32
    NH = 8
    x_in, x_out = IN["x"], IN["x_out"]
    x2_sh, y2_sh, flags_t = IN["x2_sh"], IN["y2_sh"], IN["flags"]
    FB = IN["flag_base"]
    XB, YB = FB, FB + 16

    ident = ctx.sb("identb", [128, 128], BF16)
    epsc = ctx.sb("epscol", [128, 1], F32)
    m30c = ctx.sb(f"{tag}_m30", [128, 1], F32)
    meta = ctx.sb("metai", [1, 48], I32)
    pollcell = ctx.sb("pollcell", [1, 2], I32)
    wqb = ctx.sb(f"{tag}_wqb", [128, 8 * 512], BF16)
    wkb = ctx.sb(f"{tag}_wkb", [128, 8 * 512], BF16)
    wvb = ctx.sb(f"{tag}_wvb", [128, 8 * 512], BF16)
    wpb = ctx.sb(f"{tag}_wpb", [128, 8 * 1024], BF16)
    cosc = ctx.sb(f"{tag}_cosc", [128, 32 * 64], BF16)
    sinc = ctx.sb(f"{tag}_sinc", [128, 32 * 64], BF16)
    cmR = ctx.sb(f"{tag}_cmR", [128, 4 * 512], BF16)

    xt = ctx.sb("g_xt", [128, 2 * D], F32)
    xnp = ctx.sb(f"{tag}_xnp", [128, 2 * D], BF16)
    xpt = ctx.sb(f"{tag}_xpt", [128, 2 * D], BF16)
    xnf = ctx.sb(f"{tag}_xnf", [128, 2 * D], BF16)
    qtm = ctx.sb(f"{tag}_qtm", [128, 512], BF16)
    ktm = ctx.sb(f"{tag}_ktm", [128, 512], BF16)
    qcs = ctx.sb(f"{tag}_qcs", [128, 512], BF16)
    qsw = ctx.sb(f"{tag}_qsw", [128, 512], BF16)
    qfm = ctx.sb(f"{tag}_qfm", [128, 4 * 512], BF16)
    kfm = ctx.sb(f"{tag}_kfm", [128, 4 * 4096], BF16)
    vbuf = ctx.sb(f"{tag}_vbuf", [128, 32 * 8 * 65], BF16)
    ptile = ctx.sb(f"{tag}_pt", [128, 2 * 512], BF16)
    ybuf = ctx.sb(f"{tag}_ybuf", [128, 32 * 512], BF16)
    yown = ctx.sb(f"{tag}_yown", [128, 16 * 512], BF16)
    ybt = ctx.sb(f"{tag}_ybt", [128, 2 * 512], BF16)
    yfm = ctx.sb(f"{tag}_yfm", [128, 2 * D], BF16)
    rdc = ctx.sb(f"{tag}_rdc", [128, 4], F32)
    po = ctx.sb(f"{tag}_po", [128, D], BF16)
    ot = ctx.sb(f"{tag}_ot", [128, D], F32)
    sc = ctx.sb("g_sc", [128, 16], F32)
    st6 = ctx.sb("g_st6", [128, 48], F32)
    st2 = ctx.sb("g_st2", [128, 2], F32)
    r1c = ctx.sb("g_r1c", [128, 4], F32)

    pm1 = [ctx.ps(f"ps_pm1_{i}", [128, 512], F32) for i in range(4)]
    pm2 = [ctx.ps(f"ps_pm2_{i}", [128, 512], F32) for i in range(2)]
    ptr = [ctx.ps(f"ps_ptr_{i}", [128, 128], BF16) for i in range(2)]

    def rep8(base_sb, coff, width):
        return bass.AP(base_sb, coff, [[base_sb.shape[1], 128], [0, 8], [1, width]])

    def syncp(e):
        for dst, src in ((ident, IN["identC"]), (meta, IN["meta"]),
                         (cmR, IN["cmR"])):
            ins = e.dma_start(out=dst[:], in_=src)
            ctx.inc(ins, "ld", 16)
        ctx.wait(e, "ld")

    def gpp(e):
        e.memset(epsc[:], LN_EPS)
        e.memset(m30c[:], -30.0)
        e.memset(vbuf[:], 1.0)
        for dst, src in ((cosc, IN["cosr"]), (sinc, IN["sinr"])):
            ins = e.dma_start(out=dst[:], in_=src)
            ctx.inc(ins, "ldg", 16)
        for dst, src in ((wqb, IN["wq"]), (wkb, IN["wk"]), (wvb, IN["wv"]),
                         (wpb, IN["wp"])):
            w = dst.shape[1] // 8
            for r in range(8):
                ins = e.dma_start(out=dst[:, r * w:(r + 1) * w],
                                  in_=src[r * 128:(r + 1) * 128, :])
                ctx.inc(ins, "ldg", 16)
        ctx.wait(e, "ldg")
    phase(nc, f"{tag}_pro", _ctx=ctx, sync=syncp, gpsimd=gpp)

    B = {k: ctx.val(k) for k in (
        "wxt0", "wva0", "wsr0", "wxn", "pub", "wxl", "wxf", "wtp", "wev",
        "wmq", "wmk", "wmv", "wqe", "wke", "wve", "wrp", "wtq", "wqkev",
        "wqf", "wkf", "wsc", "wpe", "wmsk", "wavp", "wyv", "wyw", "yow",
        "ybl", "wxt", "wtpy", "wevy", "wmm", "woe", "wgl", "wst", "wva2",
        "wsr2", "whr")}

    def kcmax(Q):
        return 4 * Q + 4

    def npairs_before(Q, h):
        return sum(kcmax(q) * NH for q in range(Q)) + kcmax(Q) * h

    NPAIRS = npairs_before(7, 7) + kcmax(7)

    def syncM(e):
        with e.register("parW0") as par:
            e.reg_load(par, meta[0:1, 0:1])
            b_snd = ctx.val("fsnd2")
            saved = {}
            for branch in (0, 1):
                cm = e.If_eq(par, 0) if branch == 0 else e.Else()
                with cm:
                    myp, pep = branch, 1 - branch
                    for cl in range(16):
                        if cl >= 2:
                            e.wait_ge(ctx.h("wxn"), B["wxn"] + cl - 1)
                        ins = e.dma_start(out=xt[:, (cl % 2) * D:((cl % 2) + 1) * D],
                                          in_=x_in[cl * 128:(cl + 1) * 128, :])
                        ctx.inc(ins, "wxt0", 16)
                    for c in range(NCH):
                        if c // 16 == myp:
                            e.wait_ge(ctx.h("pub"), B["pub"] + 16 * ((c % 16) + 1))
                        else:
                            emit_flag_poll(ctx, e, XB + (c % 16),
                                           flags_t[pep:pep + 1, XB + (c % 16):XB + (c % 16) + 1],
                                           meta, pollcell, pol="fpol2")
                        if c >= 2:
                            e.wait_ge(ctx.h("wxf"), B["wxf"] + c - 1)
                        ins = e.dma_start(out=xpt[:, (c % 2) * D:((c % 2) + 1) * D],
                                          in_=x2_sh[c * 128:(c + 1) * 128, :])
                        ctx.inc(ins, "wxl", 16)
                    for c in range(NCH):
                        if c // 16 == pep:
                            e.wait_ge(ctx.h("wyw"), B["wyw"] + (c // 4) + 1)
                            ins = e.dma_start(
                                out=y2_sh[myp * 2048 + (c % 16) * 128:myp * 2048 + (c % 16) * 128 + 128, :],
                                in_=ybuf[:, c * 512:(c + 1) * 512])
                            ctx.inc(ins, "fsnd2", 16)
                            ctx.wait(e, "fsnd2")
                            emit_flag_send(ctx, e, YB + (c % 16),
                                           flags_t[myp:myp + 1, YB + (c % 16):YB + (c % 16) + 1],
                                           meta, snd="fsnd2")
                    for c2 in range(16):
                        gc = myp * 16 + c2
                        e.wait_ge(ctx.h("wyw"), B["wyw"] + (gc // 4) + 1)
                        ins = e.dma_start(out=yown[:, c2 * 512:(c2 + 1) * 512],
                                          in_=ybuf[:, gc * 512:(gc + 1) * 512])
                        ctx.inc(ins, "yow", 16)
                        emit_flag_poll(ctx, e, YB + c2,
                                       flags_t[pep:pep + 1, YB + c2:YB + c2 + 1],
                                       meta, pollcell, pol="fpol2")
                        if c2 >= 2:
                            e.wait_ge(ctx.h("wevy"), B["wevy"] + 8 * (c2 - 1))
                        ins = e.dma_start(out=ybt[:, (c2 % 2) * 512:((c2 % 2) + 1) * 512],
                                          in_=y2_sh[pep * 2048 + c2 * 128:pep * 2048 + c2 * 128 + 128, :])
                        ctx.inc(ins, "ybl", 16)
                        if c2 >= 2:
                            e.wait_ge(ctx.h("whr"), B["whr"] + c2 - 1)
                        ins = e.dma_start(out=xt[:, (c2 % 2) * D:((c2 % 2) + 1) * D],
                                          in_=x_in[c2 * 128:(c2 + 1) * 128, :])
                        ctx.inc(ins, "wxt", 16)
                    if branch == 0:
                        for nm in ("fsnd2", "wxt0", "wxl", "yow", "ybl", "wxt"):
                            saved[nm] = ctx.val(nm)
                        ctx.setval("fsnd2", b_snd)
                        for nm in ("wxt0", "wxl", "yow", "ybl", "wxt"):
                            ctx.setval(nm, B[nm])
            for nm, v in saved.items():
                ctx.setval(nm, v)

    def gpM(e):
        with e.register("parW1") as par:
            e.reg_load(par, meta[0:1, 0:1])
            b_snd = ctx.val("fsnd")
            saved = {}
            for branch in (0, 1):
                cm = e.If_eq(par, 0) if branch == 0 else e.Else()
                with cm:
                    myp = branch
                    for cl in range(16):
                        e.wait_ge(ctx.h("wxn"), B["wxn"] + cl + 1)
                        ins = e.dma_start(
                            out=x2_sh[myp * 2048 + cl * 128:myp * 2048 + cl * 128 + 128, :],
                            in_=xnp[:, (cl % 2) * D:((cl % 2) + 1) * D])
                        ctx.inc(ins, "pub", 16)
                        ctx.wait(e, "pub")
                        emit_flag_send(ctx, e, XB + cl,
                                       flags_t[myp:myp + 1, XB + cl:XB + cl + 1], meta)
                    if branch == 0:
                        saved["pub"] = ctx.val("pub")
                        saved["fsnd"] = ctx.val("fsnd")
                        ctx.setval("pub", B["pub"])
                        ctx.setval("fsnd", b_snd)
            for nm, v in saved.items():
                ctx.setval(nm, v)
        for Q in range(8):
            for ci in range(4):
                c = 4 * Q + ci
                for t_sb, e_sem in ((qtm, "wqe"), (ktm, "wke")):
                    e.wait_ge(ctx.h(e_sem), B[e_sem] + c + 1)
                    e.tensor_mul(bass.AP(qcs, 0, [[512, 128], [64, 8], [1, 64]]),
                                 bass.AP(t_sb, 0, [[512, 128], [64, 8], [1, 64]]),
                                 rep8(cosc, c * 64, 64))
                    e.tensor_mul(bass.AP(qsw, 0, [[512, 128], [64, 8], [1, 32]]),
                                 bass.AP(t_sb, 32, [[512, 128], [64, 8], [1, 32]]),
                                 rep8(sinc, c * 64, 32))
                    e.tensor_mul(bass.AP(qsw, 32, [[512, 128], [64, 8], [1, 32]]),
                                 bass.AP(t_sb, 0, [[512, 128], [64, 8], [1, 32]]),
                                 rep8(sinc, c * 64 + 32, 32))
                    e.tensor_add(t_sb[:], qcs[:], qsw[:])
                    ctx.sig(e, "wrp", 1)
            for h in range(NH):
                for r in range(4):
                    kc = 4 * Q + r
                    i = npairs_before(Q, h) + kc
                    e.wait_ge(ctx.h("wpe"), B["wpe"] + i + 1)
                    e.tensor_mul(ptile[:, (i % 2) * 512:(i % 2 + 1) * 512],
                                 ptile[:, (i % 2) * 512:(i % 2 + 1) * 512],
                                 cmR[:, r * 512:(r + 1) * 512])
                    ctx.sig(e, "wmsk", 1)

    def vecM(e):
        for cl in range(16):
            xs = xt[:, (cl % 2) * D:((cl % 2) + 1) * D]
            e.wait_ge(ctx.h("wxt0"), B["wxt0"] + 16 * (cl + 1))
            ln_stats(e, xs, D, st6, st2, sc[:, 0:1], sc[:, 1:2], "wva0", ctx)
            e.wait_ge(ctx.h("wsr0"), B["wsr0"] + cl + 1)
            if cl >= 2:
                e.wait_ge(ctx.h("pub"), B["pub"] + 16 * (cl - 1))
            e.tensor_scalar(xnp[:, (cl % 2) * D:((cl % 2) + 1) * D], xs,
                            sc[:, 0:1], r1c[:, 0:1], ALU.subtract, ALU.mult)
            ctx.sig(e, "wxn", 1)
        for Q in range(8):
            for h in range(NH):
                e.wait_ge(ctx.h("wavp"), B["wavp"] + npairs_before(Q, h) + kcmax(Q))
                bank = pm1[2] if h % 2 == 0 else pm1[3]
                for sl in range(4):
                    e.tensor_copy(rdc[:, sl:sl + 1], bank[:, sl * 65 + 64:sl * 65 + 65])
                    e.reciprocal(rdc[:, sl:sl + 1], rdc[:, sl:sl + 1])
                    e.tensor_scalar_mul(
                        ybuf[:, (4 * Q + sl) * 512 + h * 64:(4 * Q + sl) * 512 + (h + 1) * 64],
                        bank[:, sl * 65:sl * 65 + 64], rdc[:, sl:sl + 1])
                ctx.sig(e, "wyv", 1)
            ctx.sig(e, "wyw", 1)
        for c2 in range(16):
            xs = xt[:, (c2 % 2) * D:((c2 % 2) + 1) * D]
            e.wait_ge(ctx.h("woe"), B["woe"] + 2 * (c2 + 1))
            ln_stats(e, po, D, st6, st2, sc[:, 2:3], sc[:, 3:4], "wva2", ctx)
            e.wait_ge(ctx.h("wsr2"), B["wsr2"] + c2 + 1)
            e.wait_ge(ctx.h("wxt"), B["wxt"] + 16 * (c2 + 1))
            if c2 >= 1:
                e.wait_ge(ctx.h("wst"), B["wst"] + 16 * c2)
            os_ = ot[:, 0:D]
            e.tensor_scalar(os_, po[:], sc[:, 2:3], r1c[:, 1:2],
                            ALU.subtract, ALU.mult)
            e.tensor_add(os_, os_, xs)
            ctx.sig(e, "wgl", 1)
            ctx.sig(e, "whr", 1)

    def scalM(e):
        for cl in range(16):
            e.wait_ge(ctx.h("wva0"), B["wva0"] + cl + 1)
            rstd_ops(e, sc[:, 1:2], sc[:, 12:13], r1c[:, 0:1], epsc[:])
            ctx.sig(e, "wsr0", 1)
        for Q in range(8):
            for ci in range(4):
                c = 4 * Q + ci
                TP = B["wtp"] + 8 * c
                for i in range(8):
                    e.wait_ge(ctx.h("wtp"), TP + i + 1)
                    if c >= 2:
                        e.wait_ge(ctx.h("wmv"), B["wmv"] + c - 1)
                    e.activation(xnf[:, (c % 2) * D + i * 128:(c % 2) * D + (i + 1) * 128],
                                 ptr[i % 2][:], AF.Identity)
                    ctx.sig(e, "wev", 1)
                ctx.sig(e, "wxf", 1)
                e.wait_ge(ctx.h("wmq"), B["wmq"] + c + 1)
                if c >= 1:
                    e.wait_ge(ctx.h("wtq"), B["wtq"] + 16 * c)
                e.activation(qtm[:], pm2[0][:], AF.Identity)
                ctx.sig(e, "wqe", 1)
                e.wait_ge(ctx.h("wmk"), B["wmk"] + c + 1)
                e.activation(ktm[:], pm2[1][:], AF.Identity)
                ctx.sig(e, "wke", 1)
                e.wait_ge(ctx.h("wmv"), B["wmv"] + c + 1)
                for h in range(NH):
                    e.activation(vbuf[:, (c * 8 + h) * 65:(c * 8 + h) * 65 + 64],
                                 pm1[2][:, h * 64:(h + 1) * 64], AF.Identity)
                ctx.sig(e, "wve", 1)
                for i in range(16):
                    e.wait_ge(ctx.h("wtq"), B["wtq"] + 16 * c + i + 1)
                    h = i % 8
                    pb, cb = 64 * (h % 2), (h // 2)
                    if i < 8:
                        if Q >= 1 and i == 0:
                            e.wait_ge(ctx.h("wsc"), B["wsc"] + npairs_before(Q, 0))
                        dst = qfm[pb:pb + 64, cb * 512 + ci * 128:cb * 512 + (ci + 1) * 128]
                    else:
                        dst = kfm[pb:pb + 64, cb * 4096 + c * 128:cb * 4096 + (c + 1) * 128]
                    e.activation(dst, ptr[i % 2][0:64, 0:128], AF.Identity)
                    ctx.sig(e, "wqf" if i < 8 else "wkf", 1)
                    ctx.sig(e, "wqkev", 1)
            for h in range(NH):
                for kc in range(kcmax(Q)):
                    i = npairs_before(Q, h) + kc
                    e.wait_ge(ctx.h("wsc"), B["wsc"] + i + 1)
                    if i >= 2:
                        e.wait_ge(ctx.h("wavp"), B["wavp"] + i - 1)
                    e.activation(ptile[:, (i % 2) * 512:(i % 2 + 1) * 512],
                                 pm1[i % 2][:], AF.Exp, scale=0.125, bias=m30c[:])
                    ctx.sig(e, "wpe", 1)
        for c2 in range(16):
            TPY = B["wtpy"] + 8 * c2
            for i in range(8):
                e.wait_ge(ctx.h("wtpy"), TPY + i + 1)
                e.activation(yfm[:, (c2 % 2) * D + i * 128:(c2 % 2) * D + (i + 1) * 128],
                             ptr[i % 2][:], AF.Identity)
                ctx.sig(e, "wevy", 1)
            for s in range(2):
                e.wait_ge(ctx.h("wmm"), B["wmm"] + 2 * c2 + s + 1)
                if c2 >= 1:
                    e.wait_ge(ctx.h("wgl"), B["wgl"] + c2)
                e.activation(po[:, s * 512:(s + 1) * 512], pm2[s][:], AF.Identity)
                ctx.sig(e, "woe", 1)
            e.wait_ge(ctx.h("wva2"), B["wva2"] + c2 + 1)
            rstd_ops(e, sc[:, 3:4], sc[:, 12:13], r1c[:, 1:2], epsc[:])
            ctx.sig(e, "wsr2", 1)

    def tensM(e):
        for Q in range(8):
            for ci in range(4):
                c = 4 * Q + ci
                TP = B["wtp"] + 8 * c
                EV = B["wev"] + 8 * c
                e.wait_ge(ctx.h("wxl"), B["wxl"] + 16 * (c + 1))
                e.wait_ge(ctx.h("wev"), EV)
                for i in range(8):
                    if i >= 2:
                        e.wait_ge(ctx.h("wev"), EV + i - 1)
                    elif c >= 1:
                        e.wait_ge(ctx.h("wqkev"), B["wqkev"] + 16 * (c - 1) + 15 + i)
                    e.transpose(ptr[i % 2][:],
                                xpt[:, (c % 2) * D + i * 128:(c % 2) * D + (i + 1) * 128],
                                ident[:]).then_inc(ctx.h("wtp"), 1)
                e.wait_ge(ctx.h("wev"), EV + 8)
                xc = xnf[:, (c % 2) * D:((c % 2) + 1) * D]
                e.wait_ge(ctx.h("wqe"), B["wqe"] + c)
                for r in range(8):
                    mm = e.matmul(pm2[0][:], xc[:, r * 128:(r + 1) * 128],
                                  wqb[:, r * 512:(r + 1) * 512],
                                  start=(r == 0), stop=(r == 7))
                mm.then_inc(ctx.h("wmq"), 1)
                e.wait_ge(ctx.h("wke"), B["wke"] + c)
                for r in range(8):
                    mm = e.matmul(pm2[1][:], xc[:, r * 128:(r + 1) * 128],
                                  wkb[:, r * 512:(r + 1) * 512],
                                  start=(r == 0), stop=(r == 7))
                mm.then_inc(ctx.h("wmk"), 1)
                if ci == 0 and Q >= 1:
                    e.wait_ge(ctx.h("wyv"), B["wyv"] + 8 * Q)
                e.wait_ge(ctx.h("wve"), B["wve"] + c)
                for r in range(8):
                    mm = e.matmul(pm1[2][:], xc[:, r * 128:(r + 1) * 128],
                                  wvb[:, r * 512:(r + 1) * 512],
                                  start=(r == 0), stop=(r == 7))
                mm.then_inc(ctx.h("wmv"), 1)
                e.wait_ge(ctx.h("wrp"), B["wrp"] + 2 * (c + 1))
                for i in range(16):
                    src = qtm if i < 8 else ktm
                    h = i % 8
                    if i < 2:
                        e.wait_ge(ctx.h("wev"), EV + 7 + i)
                    else:
                        e.wait_ge(ctx.h("wqkev"), B["wqkev"] + 16 * c + i - 1)
                    e.transpose(ptr[i % 2][0:64, 0:128], src[:, h * 64:(h + 1) * 64],
                                ident[:]).then_inc(ctx.h("wtq"), 1)
            for h in range(NH):
                i0 = npairs_before(Q, h)
                pb, cb = 64 * (h % 2), (h // 2)
                bank = pm1[2] if h % 2 == 0 else pm1[3]
                for kc in range(kcmax(Q)):
                    i = i0 + kc
                    if i >= 2:
                        e.wait_ge(ctx.h("wpe"), B["wpe"] + i - 1)
                    e.wait_ge(ctx.h("wkf"), B["wkf"] + 8 * (kc + 1))
                    e.wait_ge(ctx.h("wqf"), B["wqf"] + 32 * (Q + 1))
                    lt = kfm[pb:pb + 64, cb * 4096 + kc * 128:cb * 4096 + (kc + 1) * 128]
                    rh = qfm[pb:pb + 64, cb * 512:(cb + 1) * 512]
                    mm = e.matmul(pm1[i % 2][:], lt, rh, start=True, stop=True)
                    mm.then_inc(ctx.h("wsc"), 1)
                    diag = kc >= 4 * Q
                    if diag:
                        e.wait_ge(ctx.h("wmsk"),
                                  B["wmsk"] + 4 * (8 * Q + h) + (kc - 4 * Q) + 1)
                    else:
                        e.wait_ge(ctx.h("wpe"), B["wpe"] + i + 1)
                    if kc == 0:
                        e.wait_ge(ctx.h("wyv"), B["wyv"] + max(0, 8 * Q + h - 1))
                        if h == 0:
                            e.wait_ge(ctx.h("wve"), B["wve"] + 4 * (Q + 1))
                    for sl in range(4):
                        mm = e.matmul(bank[:, sl * 65:(sl + 1) * 65],
                                      ptile[:, (i % 2) * 512 + sl * 128:(i % 2) * 512 + (sl + 1) * 128],
                                      vbuf[:, (kc * 8 + h) * 65:(kc * 8 + h + 1) * 65],
                                      start=(kc == 0 and sl == 0),
                                      stop=(kc == kcmax(Q) - 1 and sl == 3))
                    mm.then_inc(ctx.h("wavp"), 1)
        for c2 in range(16):
            TPY = B["wtpy"] + 8 * c2
            EVY = B["wevy"] + 8 * c2
            e.wait_ge(ctx.h("yow"), B["yow"] + 16 * (c2 + 1))
            e.wait_ge(ctx.h("ybl"), B["ybl"] + 16 * (c2 + 1))
            e.wait_ge(ctx.h("wevy"), EVY)
            if c2 == 0:
                e.wait_ge(ctx.h("wyv"), B["wyv"] + 64)
                e.wait_ge(ctx.h("wqkev"), B["wqkev"] + 16 * 32)
            for i in range(8):
                if i >= 2:
                    e.wait_ge(ctx.h("wevy"), EVY + i - 1)
                src = yown[:, c2 * 512 + i * 128:c2 * 512 + (i + 1) * 128] if i < 4 \
                    else ybt[:, (c2 % 2) * 512 + (i - 4) * 128:(c2 % 2) * 512 + (i - 3) * 128]
                e.transpose(ptr[i % 2][:], src, ident[:]).then_inc(ctx.h("wtpy"), 1)
            e.wait_ge(ctx.h("wevy"), EVY + 8)
            yc = yfm[:, (c2 % 2) * D:((c2 % 2) + 1) * D]
            for s in range(2):
                e.wait_ge(ctx.h("woe"), B["woe"] + 2 * c2 + s)
                for r in range(8):
                    mm = e.matmul(pm2[s][:], yc[:, r * 128:(r + 1) * 128],
                                  wpb[:, r * 1024 + s * 512:r * 1024 + (s + 1) * 512],
                                  start=(r == 0), stop=(r == 7))
                mm.then_inc(ctx.h("wmm"), 1)

    def gpG(e):
        for c2 in range(16):
            e.wait_ge(ctx.h("wgl"), B["wgl"] + c2 + 1)
            e.dma_start(out=x_out[c2 * 128:(c2 + 1) * 128, :],
                        in_=ot[:, 0:D]).then_inc(ctx.h("wst"), 16)
        e.wait_ge(ctx.h("wst"), B["wst"] + 16 * 16)

    def gpALL(e):
        gpM(e)
        gpG(e)

    phase(nc, f"{tag}_main", _ctx=ctx, sync=syncM, gpsimd=gpALL, vector=vecM,
          scalar=scalM, tensor=tensM)
    for k, r in dict(wxt0=16 * 16, wva0=16, wsr0=16, wxn=16, pub=16 * 16,
                     wxl=16 * 32, wxf=32, wtp=8 * 32, wev=8 * 32, wmq=32,
                     wmk=32, wmv=32, wqe=32, wke=32, wve=32, wrp=2 * 32,
                     wtq=16 * 32, wqkev=16 * 32, wqf=8 * 32, wkf=8 * 32,
                     wsc=NPAIRS, wpe=NPAIRS, wmsk=256, wavp=NPAIRS, wyv=64,
                     wyw=8, yow=16 * 16, ybl=16 * 16, wxt=16 * 16,
                     wtpy=8 * 16, wevy=8 * 16, wmm=2 * 16, woe=2 * 16,
                     wgl=16, wst=16 * 16, wva2=16, wsr2=16, whr=16).items():
        ctx.setval(k, B[k] + r)


def build_l1a_program():
    es = contextlib.ExitStack()
    nc = mk_bass()
    IN = {}
    IN["x"] = nc.dram_tensor("x", [T_LOC, D], F32, kind="ExternalInput").ap()
    for nm in ("wq", "wk", "wv"):
        IN[nm] = nc.dram_tensor(nm, [D, 512], F32, kind="ExternalInput").ap()
    IN["wp"] = nc.dram_tensor("wp", [D, D], F32, kind="ExternalInput").ap()
    IN["cosr"] = nc.dram_tensor("cosr", [128, 32 * 64], F32, kind="ExternalInput").ap()
    IN["sinr"] = nc.dram_tensor("sinr", [128, 32 * 64], F32, kind="ExternalInput").ap()
    IN["cmR"] = nc.dram_tensor("cmR", [128, 4 * 512], BF16, kind="ExternalInput").ap()
    IN["identC"] = nc.dram_tensor("identC", [128, 128], BF16, kind="ExternalInput").ap()
    IN["meta"] = nc.dram_tensor("meta", [1, 48], I32, kind="ExternalInput").ap()
    IN["x_out"] = nc.dram_tensor("x_out", [T_LOC, D], F32, kind="ExternalOutput").ap()
    IN["x2_sh"] = nc.dram_tensor("x2_sh", [4096, D], BF16, addr_space="Shared").ap()
    IN["y2_sh"] = nc.dram_tensor("y2_sh", [4096, 512], BF16, addr_space="Shared").ap()
    IN["flags"] = nc.dram_tensor("flags", [2, 64], I32, addr_space="Shared").ap()
    IN["flag_base"] = 0
    ctx = Ctx(nc, es)
    emit_l1a(ctx, IN)
    es.close()
    return nc


def l1a_in_maps(x_full, ins, nonce=None):
    import ml_dtypes
    identC = np.eye(128, dtype=ml_dtypes.bfloat16)
    f32 = lambda a: np.ascontiguousarray(np.asarray(a, dtype=np.float32))
    if nonce is None:
        nonce = int(np.random.randint(1, 2 ** 30))
    wqkv = np.asarray(ins["L1_attn_w_qkv"], np.float32)
    wproj = np.asarray(ins["L1_attn_w_proj"], np.float32)
    half = 32
    theta = 1.0 / 10000.0 ** (np.arange(half, dtype=np.float64) / half)
    pos = np.arange(4096, dtype=np.float64)
    fr = pos[:, None] * theta[None, :]
    cosr = np.repeat(np.cos(fr), 2, axis=1).astype(np.float32)
    sinr = np.repeat(np.sin(fr), 2, axis=1).astype(np.float32)
    sinp = sinr.copy()
    sinp[:, 0:32] *= -1.0
    cosc = np.zeros((128, 32 * 64), np.float32)
    sinc = np.zeros((128, 32 * 64), np.float32)
    for c in range(32):
        cosc[:, c * 64:(c + 1) * 64] = cosr[c * 128:(c + 1) * 128]
        sinc[:, c * 64:(c + 1) * 64] = sinp[c * 128:(c + 1) * 128]
    s_idx = np.arange(128)[:, None]
    t_idx = np.arange(512)[None, :]
    cmR = np.concatenate(
        [(128 * r + s_idx <= t_idx).astype(np.float32) for r in range(4)],
        axis=1).astype(ml_dtypes.bfloat16)
    maps = []
    for cidx in range(NCORE):
        b, h = cidx // 2, cidx % 2
        shard = f32(x_full[b, h * T_LOC:(h + 1) * T_LOC, :])
        sl = slice(h * 512, (h + 1) * 512)
        perm = np.concatenate([np.arange(h * 512, (h + 1) * 512),
                               np.arange((1 - h) * 512, (2 - h) * 512)])
        maps.append(dict(
            x=shard,
            wq=f32(wqkv[:, 0:1024][:, sl]),
            wk=f32(wqkv[:, 1024:2048][:, sl]),
            wv=f32(wqkv[:, 2048:3072][:, sl]),
            wp=f32(wproj[perm]),
            cosr=cosc, sinr=sinc, cmR=np.ascontiguousarray(cmR), identC=identC,
            meta=np.array([[h, nonce] + [nonce + kk + 1 for kk in range(46)]],
                          dtype=np.int32),
        ))
    return maps


def emit_flag_send(ctx, g, k, flags, meta_sb, snd="fsnd"):
    ins = g.dma_start(out=flags, in_=meta_sb[0:1, 2 + k:3 + k])
    ctx.inc(ins, snd, 16)
    ctx.wait(g, snd)


_uid = [0]


def emit_flag_poll(ctx, g, k, flags, meta_sb, pollcell, pol="fpol"):
    _uid[0] += 1
    u = _uid[0]
    g.sem_clear(ctx.h(pol))
    with (g.register(f"tg{u}") as tgt, g.register(f"nf{u}") as nf,
          g.register(f"cn{u}") as cnt):
        g.reg_load(tgt, meta_sb[0:1, 2 + k:3 + k])
        g.reg_mov(nf, 1)
        g.reg_mov(cnt, 0)
        with g.While(nf):
            ins = g.dma_start(out=pollcell[0:1, 0:1], in_=flags)
            ins.then_inc(ctx.h(pol), 16)
            g.reg_add(cnt, cnt, 16)
            g.wait_ge(ctx.h(pol), cnt)
            with g.register(f"rv{u}") as rv:
                g.reg_load(rv, pollcell[0:1, 0:1])
                g.reg_alu(nf, rv, tgt, ALU.not_equal)


def emit_pair_sync(ctx, tag, k, flags_t, meta_sb):
    nc = ctx.nc
    pollcell = ctx.sb("pollcell", [1, 2], I32)

    def g(e):
        with e.register("par") as par:
            e.reg_load(par, meta_sb[0:1, 0:1])
            b_snd, b_pol = ctx.val("fsnd"), ctx.val("fpol")
            with e.If_eq(par, 0):
                emit_flag_send(ctx, e, k, flags_t[0:1, k:k + 1], meta_sb)
                emit_flag_poll(ctx, e, k, flags_t[1:2, k:k + 1], meta_sb, pollcell)
            ctx.setval("fsnd", b_snd)
            ctx.setval("fpol", b_pol)
            with e.Else():
                emit_flag_send(ctx, e, k, flags_t[1:2, k:k + 1], meta_sb)
                emit_flag_poll(ctx, e, k, flags_t[0:1, k:k + 1], meta_sb, pollcell)
    phase(nc, f"{tag}_sync{k}", _ctx=ctx, gpsimd=g)


def row2d(a):
    return np.ascontiguousarray(np.asarray(a, dtype=np.float32).reshape(1, -1))


def rep128(a):
    a = np.asarray(a, dtype=np.float32).reshape(1, -1)
    return np.ascontiguousarray(np.broadcast_to(a, (128, a.shape[1])))


def mk_bass():
    return bass.Bass("TRN2", target_bir_lowering=False,
                     detect_race_conditions=not bool(os.environ.get("NORACE")))


def build_l2f_program():
    es = contextlib.ExitStack()
    nc = mk_bass()
    IN = {}
    IN["x"] = nc.dram_tensor("x", [T_LOC, D], F32, kind="ExternalInput").ap()
    IN["w1T"] = nc.dram_tensor("w1T", [D, 4096], F32, kind="ExternalInput").ap()
    IN["w2T"] = nc.dram_tensor("w2T", [4096, D], F32, kind="ExternalInput").ap()
    for nm, n in (("ln1_g", D), ("ln1_b", D), ("ln2_g", 4096), ("ln2_b", 4096)):
        IN[nm] = nc.dram_tensor(nm, [128, n], F32, kind="ExternalInput").ap()
    for nm, n in (("b1", 4096), ("b2", D)):
        IN[nm] = nc.dram_tensor(nm, [1, n], F32, kind="ExternalInput").ap()
    IN["identC"] = nc.dram_tensor("identC", [128, 128], BF16, kind="ExternalInput").ap()
    IN["x_out"] = nc.dram_tensor("x_out", [T_LOC, D], F32, kind="ExternalOutput").ap()
    import os
    hk = "ExternalOutput" if os.environ.get("L2F_HOUT") else "Internal"
    IN["h_spill"] = nc.dram_tensor("h_spill", [T_LOC, 4096], BF16, kind=hk).ap()
    ctx = Ctx(nc, es)
    emit_l2f(ctx, IN)
    dump = [t for t in os.environ.get("L2F_DUMP", "").split(",") if t]
    if dump:
        outs = {}
        for nm in dump:
            th = ctx._sb[nm]
            shp = list(th.shape) if hasattr(th, "shape") else None
            outs[nm] = nc.dram_tensor("dbg_" + nm, [int(shp[0]), int(np.prod(shp[1:]))],
                                      F32, kind="ExternalOutput").ap()

        def dumper(e):
            for nm in dump:
                th = ctx._sb[nm]
                ins_ = e.dma_start(out=outs[nm], in_=th[:])
                ctx.inc(ins_, "dumps", 16)
            ctx.wait(e, "dumps")
        phase(nc, "dbgdump", _ctx=ctx, gpsimd=dumper)
    es.close()
    return nc


def l2f_in_maps(x_full, ins):
    import ml_dtypes
    identC = np.eye(128, dtype=ml_dtypes.bfloat16)
    w1T = np.ascontiguousarray(ins["L2_ffn_w1"].T.astype(np.float32))
    w2T = np.ascontiguousarray(ins["L2_ffn_w2"].T.astype(np.float32))
    maps = []
    for c in range(NCORE):
        b, h = c // 2, c % 2
        shard = np.ascontiguousarray(x_full[b, h * T_LOC:(h + 1) * T_LOC, :].astype(np.float32))
        maps.append(dict(
            x=shard, w1T=w1T, w2T=w2T,
            ln1_g=rep128(ins["L2_ffn_ln1_g"]), ln1_b=rep128(ins["L2_ffn_ln1_b"]),
            b1=row2d(ins["L2_ffn_b1"]), ln2_g=rep128(ins["L2_ffn_ln2_g"]),
            ln2_b=rep128(ins["L2_ffn_ln2_b"]), b2=row2d(ins["L2_ffn_b2"]),
            identC=identC,
        ))
    return maps


def gather_stream(res, key="x_out"):
    out = np.zeros((4, 4096, D), dtype=np.float32)
    for c in range(NCORE):
        b, h = c // 2, c % 2
        out[b, h * T_LOC:(h + 1) * T_LOC, :] = res.results[c][key]
    return out


def run_l2f(x_full, ins):
    nc = build_l2f_program()
    res = run_bass_kernel_spmd(nc, l2f_in_maps(x_full, ins), core_ids=list(range(NCORE)))
    return gather_stream(res)


_BLOCK_SEQ = None


def _block_seq():
    global _BLOCK_SEQ
    if _BLOCK_SEQ is None:
        _BLOCK_SEQ = [
            (build_l0a_program(), l0a_in_maps),
            (build_l0f_program(), l0f_in_maps),
            (build_l1a_program(), l1a_in_maps),
            (build_l1f_program(), l1f_in_maps),
            (build_l2a_program(), l2a_in_maps),
            (build_l2f_program(), l2f_in_maps),
            (build_l3a_program(), l3a_in_maps),
            (build_l3f_program(), l3f_in_maps),
        ]
    return _BLOCK_SEQ


def kernel(**inputs):
    x = np.asarray(inputs["L0_attn_f"], dtype=np.float32)
    for nc, in_maps in _block_seq():
        res = run_bass_kernel_spmd(nc, in_maps(x, inputs),
                                   core_ids=list(range(NCORE)))
        x = gather_stream(res)
    return x


# version 05 (final)
# speedup vs baseline: 1.1559x; 1.1559x over previous
import contextlib
import os as _os
BRANCH_CLR = not _os.environ.get("NOCLR")
import numpy as np
import concourse.bass as bass
import concourse.mybir as mybir
from concourse.bass_utils import run_bass_kernel_spmd

F32 = mybir.dt.float32
BF16 = mybir.dt.bfloat16
I32 = mybir.dt.int32
AF = mybir.ActivationFunctionType
ALU = mybir.AluOpType
AX = mybir.AxisListType

NCORE = 8
T_LOC = 2048
NT = T_LOC // 128
D = 1024
ND = D // 128
LN_EPS = 1e-5
RMAGIC = 12582912.0


_PERSIST_SB = {"identb", "epscol", "onecolf2", "metai", "pollcell"}


class Ctx:
    def __init__(self, nc, es):
        self.nc = nc
        self.es = es
        self._sems = {}
        self._sb = {}
        self.block_es = None

    def begin_block(self):
        self.block_es = contextlib.ExitStack()
        self.block_id = getattr(self, "block_id", -1) + 1

    def end_block(self):
        if self.block_es is not None:
            self.block_es.close()
            self.block_es = None
        for n in [n for n in self._sb
                  if n not in _PERSIST_SB and not n.startswith("ps_")]:
            del self._sb[n]

    def reset_sems_phase(self, name):
        def s(e):
            for nm in list(self._sems):
                e.sem_clear(self._sems[nm][0])
                self._sems[nm][1] = 0
        phase(self.nc, name, sync=s)

    def sem(self, name):
        name = getattr(self, "alias", {}).get(name, name)
        if name not in self._sems:
            h = self.es.enter_context(self.nc.semaphore(name))
            self._sems[name] = [h, 0]
        return self._sems[name]

    def inc(self, instr, name, n=1):
        s = self.sem(name)
        s[1] += n
        instr.then_inc(s[0], n)
        return s[1]

    def h(self, name):
        return self.sem(name)[0]

    def val(self, name):
        return self.sem(name)[1]

    def setval(self, name, v):
        self.sem(name)[1] = v

    def sig(self, eng, name, n=1):
        sm = self.sem(name)
        sm[1] += n
        eng.sem_inc(sm[0], n)
        return sm[1]

    def wait(self, eng, name, val=None):
        s = self.sem(name)
        eng.wait_ge(s[0], s[1] if val is None else val)

    def sb(self, name, shape, dtype):
        if name not in self._sb:
            persist = name in _PERSIST_SB or self.block_es is None
            stack = self.es if persist else self.block_es
            tname = name if persist else f"{name}_b{self.block_id}"
            self._sb[name] = stack.enter_context(self.nc.sbuf_tensor(tname, shape, dtype))
        return self._sb[name]

    def ps(self, name, shape, dtype=F32):
        if name not in self._sb:
            self._sb[name] = self.es.enter_context(self.nc.psum_tensor(name, shape, dtype))
        return self._sb[name]


_EDGE_OPS = {
    "tensor_tensor", "tensor_add", "tensor_sub", "tensor_mul", "tensor_max",
    "tensor_scalar", "tensor_single_scalar", "tensor_scalar_add", "tensor_scalar_sub",
    "tensor_scalar_mul", "tensor_scalar_max", "tensor_scalar_min", "tensor_relu",
    "tensor_copy", "tensor_reduce", "reduce_sum", "reduce_max", "reciprocal",
    "bn_stats", "bn_aggr", "select", "copy_predicated", "memset", "memzero",
    "activation", "add", "mul", "copy", "sqrt", "square", "sign", "iota",
    "tensor_tensor_scan", "scalar_tensor_tensor", "transpose", "stream_shuffle",
}


class SafeEng:

    def __init__(self, ctx, e, nm):
        self._ctx, self._e, self._nm = ctx, e, nm

    def __getattr__(self, attr):
        f = getattr(self._e, attr)
        if attr not in _EDGE_OPS:
            return f
        ctx, e, sem = self._ctx, self._e, "se_" + self._nm

        def wrapped(*a, **k):
            ins = f(*a, **k)
            n = ctx.inc(ins, sem, 1)
            e.wait_ge(ctx.h(sem), n)
            return ins
        return wrapped


def phase(nc, name=None, _ctx=None, **bodies):
    with nc.Block(name, no_gpsimd_drain=True) as blk:
        for engname, fn in bodies.items():
            if _ctx is not None and engname in ("vector", "scalar", "gpsimd"):
                getattr(blk, engname)(lambda e, fn=fn, n=engname: fn(SafeEng(_ctx, e, n)))
            else:
                getattr(blk, engname)(fn)


def bcast_ap(dram_handle, n, parts=128):
    return bass.AP(dram_handle, 0, [[0, parts], [1, n]])


def row_ap(dram_handle, n):
    return bass.AP(dram_handle, 0, [[0, 1], [1, n]])


def rstd_ops(e_s, var_ap, tmp_ap, out_ap, epsc):
    e_s.activation(tmp_ap, var_ap, AF.Ln, bias=epsc)
    return e_s.activation(out_ap, tmp_ap, AF.Exp, scale=-0.5)


def ln_stats(e_v, src_ap, width, st6, st2, mcol, vcol, inc=None, ctx=None):
    nk = (width + 511) // 512
    for k in range(nk):
        w0, w1 = 512 * k, min(512 * (k + 1), width)
        e_v.bn_stats(st6[:, 6 * k:6 * (k + 1)], src_ap[:, w0:w1])
    e_v.bn_aggr(st2[:], st6[:, 0:6 * nk])
    e_v.tensor_copy(mcol, st2[:, 0:1])
    ins = e_v.tensor_copy(vcol, st2[:, 1:2])
    if inc is not None:
        ctx.sig(e_v, inc, 1)
    return ins


def emit_weight_quant(ctx, wT, rows, cols, wq, mwc, tag, stg, wi):
    import os
    """Ternary-quantize host-transposed weight wT ([rows, cols] f32 DRAM) into
    wq (SBUF bf16 [128, (rows//128)*cols], chunk r at offset r*cols).
    mwc (SBUF f32 [128, 2]): col0 = clip(mean|w|,1e-5); col1 = 1/col0. Two DMA passes.
    stg: f32 scratch [128, >=cols]; wi: i32 scratch [128, >=cols] (borrowed)."""
    nc = ctx.nc
    nch = rows // 128
    acc = ctx.sb(f"wacc_{tag}", [128, 33], F32)
    onescol = ctx.sb("onescolf", [128, 1], F32)
    onesrow = ctx.sb("onesrowf", [1, 128], F32)
    sca = ctx.sb("wsca", [1, 2], F32)
    psm = ctx.ps("ps_pm1_0", [128, 512], F32)

    ld, vq = "wld", "wvq"

    def sync(e):
        base_vq = ctx.val(vq)
        for r in range(nch):
            if r >= 1:
                e.wait_ge(ctx.h(vq), base_vq + r)
            ins = e.dma_start(out=stg[:, 0:cols], in_=wT[r * 128:(r + 1) * 128, :])
            ctx.inc(ins, ld, 16)

    def vector(e):
        e.memset(onescol[:], 1.0)
        e.memset(onesrow[:], 1.0)
        base = ctx.val(ld) - 16 * nch
        for r in range(nch):
            e.wait_ge(ctx.h(ld), base + 16 * (r + 1))
            e.reduce_sum(acc[:, r:r + 1], stg[:, 0:cols],
                         axis=AX.X, apply_absolute_value=True)
            ctx.sig(e, vq, 1)
        e.reduce_sum(acc[:, 32:33], acc[:, 0:nch], axis=AX.X)
    phase(nc, f"wqa_{tag}", _ctx=ctx, sync=sync, vector=vector)
    if int(os.environ.get("WQ_STAGE", "9")) < 2 and tag.endswith("w2"):
        return

    def tensor(e):
        ins = e.matmul(psm[0:1, 0:1], acc[:, 32:33], onescol[:], start=True, stop=True)
        ctx.inc(ins, vq, 1)

    def vector2(e):
        ctx.wait(e, vq)
        e.tensor_scalar(sca[0:1, 0:1], psm[0:1, 0:1], 1.0 / (rows * cols), 1e-5,
                        ALU.mult, ALU.max)
        ctx.sig(e, vq, 1)
    phase(nc, f"wqb_{tag}", _ctx=ctx, tensor=tensor, vector=vector2)
    if int(os.environ.get("WQ_STAGE", "9")) < 3 and tag.endswith("w2"):
        return

    def tensor2(e):
        ins = e.matmul(psm[:, 0:1], onesrow[:], sca[0:1, 0:1], start=True, stop=True)
        ctx.inc(ins, vq, 1)

    def vector3(e):
        ctx.wait(e, vq)
        e.tensor_copy(mwc[:, 0:1], psm[:, 0:1])
        e.reciprocal(mwc[:, 1:2], mwc[:, 0:1])
    phase(nc, f"wqc_{tag}", _ctx=ctx, tensor=tensor2, vector=vector3)
    if int(os.environ.get("WQ_STAGE", "9")) < 4 and tag.endswith("w2"):
        return

    def sync2(e):
        base_vq = ctx.val(vq)
        for r in range(nch):
            if r >= 1:
                e.wait_ge(ctx.h(vq), base_vq + r)
            ins = e.dma_start(out=stg[:, 0:cols], in_=wT[r * 128:(r + 1) * 128, :])
            ctx.inc(ins, ld, 16)

    def vector4(e):
        base_ld = ctx.val(ld) - 16 * nch
        for r in range(nch):
            e.wait_ge(ctx.h(ld), base_ld + 16 * (r + 1))
            e.tensor_scalar(stg[:, 0:cols], stg[:, 0:cols], mwc[:, 1:2], RMAGIC,
                            ALU.mult, ALU.add)
            e.tensor_scalar(stg[:, 0:cols], stg[:, 0:cols], -RMAGIC, 1.0,
                            ALU.add, ALU.min)
            e.tensor_scalar_max(wq[:, r * cols:(r + 1) * cols], stg[:, 0:cols], -1.0)
            ctx.sig(e, vq, 1)
    phase(nc, f"wqd_{tag}", _ctx=ctx, sync=sync2, vector=vector4)


import os
L2F_STAGE = int(os.environ.get("L2F_STAGE", "9"))


def emit_l2f(ctx, IN, tag="l2f"):
    nc = ctx.nc
    FF = 4096
    NF = FF // 128

    wqbuf = ctx.sb("wqbuf", [128, 32768], BF16)
    mwc1 = ctx.sb(f"{tag}_mwc1", [128, 2], F32)
    mwc2 = ctx.sb(f"{tag}_mwc2", [128, 2], F32)
    g1b = ctx.sb(f"{tag}_g1b", [128, D], BF16)
    b1lb = ctx.sb(f"{tag}_b1lb", [128, D], BF16)
    g2b = ctx.sb(f"{tag}_g2b", [128, FF], BF16)
    b2lb = ctx.sb(f"{tag}_b2lb", [128, FF], BF16)
    b1row = ctx.sb(f"{tag}_b1row", [1, FF], BF16)
    b2row = ctx.sb(f"{tag}_b2row", [1, D], BF16)
    ident = ctx.sb("identb", [128, 128], BF16)
    epsc = ctx.sb("epscol", [128, 1], F32)

    xt = ctx.sb("g_xt", [128, 2 * D], F32)
    xq1t = ctx.sb(f"{tag}_xq1t", [128, D], BF16)
    xq1f = ctx.sb(f"{tag}_xq1f", [128, D], BF16)
    ht = ctx.sb(f"{tag}_ht", [128, 2 * FF], BF16)
    wstg = ctx.sb("wstg", [128, FF], F32)
    hbt = ctx.sb(f"{tag}_hbt", [128, 2 * FF], BF16)
    xq2t = ctx.sb(f"{tag}_xq2t", [128, FF], BF16)
    xq2f = ctx.sb(f"{tag}_xq2f", [128, FF], BF16)
    un2 = ht[:, 0:FF]
    ut = wstg[:, 0:D]
    ct = ctx.sb("g_ct", [128, 2 * D], F32)
    ot = ctx.sb("g_ot", [128, 2 * D], F32)
    sc = ctx.sb("g_sc", [128, 16], F32)
    st6 = ctx.sb("g_st6", [128, 48], F32)
    st2 = ctx.sb("g_st2", [128, 2], F32)
    r1c = ctx.sb("g_r1c", [128, 4], F32)
    rs1 = ctx.sb(f"{tag}_rs1", [128, 1], F32)
    rs2 = ctx.sb(f"{tag}_rs2", [128, 1], F32)
    iv1c = ctx.sb(f"{tag}_iv1c", [128, 1], BF16)
    iv2c = ctx.sb(f"{tag}_iv2c", [128, 1], BF16)
    iv1r = ctx.sb(f"{tag}_iv1r", [1, 128], BF16)
    iv2r = ctx.sb(f"{tag}_iv2r", [1, 128], BF16)

    pm1 = [ctx.ps(f"ps_pm1_{i}", [128, 512], F32) for i in range(4)]
    pm2 = [ctx.ps(f"ps_pm2_{i}", [128, 512], F32) for i in range(2)]
    ptr = [ctx.ps(f"ps_ptr_{i}", [128, 128], BF16) for i in range(2)]

    x_in, x_out, h_dram = IN["x"], IN["x_out"], IN["h_spill"]

    def syncp(e):
        ins = e.dma_start(out=ident[:], in_=IN["identC"])
        ctx.inc(ins, "ld", 16)

    def gpp(e):
        e.memset(epsc[:], LN_EPS)
        for dst, src in ((g1b, IN["ln1_g"]), (b1lb, IN["ln1_b"]),
                         (g2b, IN["ln2_g"]), (b2lb, IN["ln2_b"])):
            ins = e.dma_start(out=dst[:], in_=src)
            ctx.inc(ins, "ldg", 16)
        for dst, src in ((b1row, IN["b1"]), (b2row, IN["b2"])):
            ins = e.dma_start(out=dst[:], in_=src)
            ctx.inc(ins, "ldg", 16)
        ctx.wait(e, "ldg")
        ctx.wait(e, "ld")
    phase(nc, f"{tag}_pro", _ctx=ctx, sync=syncp, gpsimd=gpp)

    if L2F_STAGE < 1:
        return
    emit_weight_quant(ctx, IN["w1T"], D, FF, wqbuf, mwc1, tag + "w1", wstg, None)
    if L2F_STAGE < 2:
        return

    BA = {k: ctx.val(k) for k in ("ld16", "va", "sr", "q", "tp", "ev", "m1", "hg", "hs")}

    def syncA(e):
        for t in range(NT):
            if t >= 2:
                e.wait_ge(ctx.h("va"), BA["va"] + 2 * (t - 2) + 2)
            e.dma_start(out=xt[:, (t % 2) * D:((t % 2) + 1) * D],
                        in_=x_in[t * 128:(t + 1) * 128, :]).then_inc(ctx.h("ld16"), 16)

    def gpA(e):
        for t in range(NT):
            e.wait_ge(ctx.h("hg"), BA["hg"] + 8 * (t + 1))
            e.dma_start(out=h_dram[t * 128:(t + 1) * 128, :],
                        in_=ht[:, (t % 2) * FF:((t % 2) + 1) * FF]).then_inc(ctx.h("hs"), 16)
        e.wait_ge(ctx.h("hs"), BA["hs"] + 16 * NT)

    def vectorA(e):
        for t in range(NT):
            TP, EV = BA["tp"] + 9 * t, BA["ev"] + 9 * t
            xs = xt[:, (t % 2) * D:((t % 2) + 1) * D]
            e.wait_ge(ctx.h("ld16"), BA["ld16"] + 16 * (t + 1))
            ln_stats(e, xs, D, st6, st2, sc[:, 0:1], sc[:, 1:2], "va", ctx)
            e.wait_ge(ctx.h("sr"), BA["sr"] + 2 * t + 1)
            e.tensor_scalar(ut[:], xs, sc[:, 0:1], r1c[:, 0:1], ALU.subtract, ALU.mult)
            ln_stats(e, ut[:], D, st6, st2, sc[:, 2:3], sc[:, 3:4], "va", ctx)
            e.wait_ge(ctx.h("sr"), BA["sr"] + 2 * t + 2)
            e.tensor_scalar(ut[:], ut[:], sc[:, 2:3], r1c[:, 1:2], ALU.subtract, ALU.mult)
            e.tensor_mul(ut[:], ut[:], g1b[:])
            e.tensor_add(ut[:], ut[:], b1lb[:])
            e.reduce_max(sc[:, 4:5], ut[:], axis=AX.X, apply_absolute_value=True)
            e.tensor_scalar_max(sc[:, 4:5], sc[:, 4:5], 1e-5)
            e.reciprocal(sc[:, 5:6], sc[:, 4:5])
            e.tensor_scalar(ut[:], ut[:], sc[:, 5:6], 127.0, ALU.mult, ALU.mult)
            e.tensor_scalar(xq1t[:], ut[:], RMAGIC, -RMAGIC, ALU.add, ALU.add)
            e.tensor_scalar(rs1[:], sc[:, 4:5], 1.0 / 127.0, mwc1[:, 0:1], ALU.mult, ALU.mult)
            e.tensor_scalar(iv1c[:], sc[:, 5:6], 127.0, mwc1[:, 1:2],
                            ALU.mult, ALU.mult)
            ctx.sig(e, "q", 1)
            e.wait_ge(ctx.h("tp"), TP + 1)
            e.tensor_copy(iv1r[:], ptr[0][0:1, 0:128])
            ctx.sig(e, "ev", 1)
            for j in range(8):
                e.wait_ge(ctx.h("tp"), TP + 2 + j)
                e.tensor_copy(xq1f[:, j * 128:(j + 1) * 128], ptr[(1 + j) % 2][:])
                ctx.sig(e, "ev", 1)

    def scalarA(e):
        for t in range(NT):
            e.wait_ge(ctx.h("va"), BA["va"] + 2 * t + 1)
            rstd_ops(e, sc[:, 1:2], sc[:, 12:13], r1c[:, 0:1], epsc[:])
            ctx.sig(e, "sr", 1)
            e.wait_ge(ctx.h("va"), BA["va"] + 2 * t + 2)
            rstd_ops(e, sc[:, 3:4], sc[:, 12:13], r1c[:, 1:2], epsc[:])
            ctx.sig(e, "sr", 1)
            hs = ht[:, (t % 2) * FF:((t % 2) + 1) * FF]
            if t >= 2:
                e.wait_ge(ctx.h("hs"), BA["hs"] + 16 * (t - 1))
            for g in range(8):
                e.wait_ge(ctx.h("m1"), BA["m1"] + 8 * t + g + 1)
                e.activation(hs[:, 512 * g:512 * (g + 1)], pm1[(8 * t + g) % 4][:],
                             AF.Gelu, scale=rs1[:])
                ctx.sig(e, "hg", 1)

    def tensorA(e):
        for t in range(NT):
            TP, EV = BA["tp"] + 9 * t, BA["ev"] + 9 * t
            e.wait_ge(ctx.h("q"), BA["q"] + t + 1)
            e.wait_ge(ctx.h("ev"), EV)
            e.transpose(ptr[0][0:1, 0:128], iv1c[:], ident[:]).then_inc(ctx.h("tp"), 1)
            for j in range(8):
                i = 1 + j
                if i >= 2:
                    e.wait_ge(ctx.h("ev"), EV + i - 1)
                e.transpose(ptr[(1 + j) % 2][:], xq1t[:, j * 128:(j + 1) * 128],
                            ident[:]).then_inc(ctx.h("tp"), 1)
            e.wait_ge(ctx.h("ev"), EV + 9)
            for g in range(8):
                G = 8 * t + g
                if G >= 4:
                    e.wait_ge(ctx.h("hg"), BA["hg"] + G - 3)
                for j in range(8):
                    e.matmul(pm1[G % 4][:], xq1f[:, j * 128:(j + 1) * 128],
                             wqbuf[:, j * FF + g * 512: j * FF + (g + 1) * 512],
                             start=(j == 0), stop=False)
                e.matmul(pm1[G % 4][:], iv1r[:], b1row[0:1, g * 512:(g + 1) * 512],
                         start=False, stop=True).then_inc(ctx.h("m1"), 1)

    phase(nc, f"{tag}_A", _ctx=ctx, sync=syncA, gpsimd=gpA, vector=vectorA, scalar=scalarA, tensor=tensorA)
    ctx.setval("ld16", BA["ld16"] + 16 * NT)
    ctx.setval("va", BA["va"] + 2 * NT)
    ctx.setval("sr", BA["sr"] + 2 * NT)
    ctx.setval("q", BA["q"] + NT)
    ctx.setval("tp", BA["tp"] + 9 * NT)
    ctx.setval("ev", BA["ev"] + 9 * NT)
    ctx.setval("m1", BA["m1"] + 8 * NT)
    ctx.setval("hg", BA["hg"] + 8 * NT)
    ctx.setval("hs", BA["hs"] + 16 * NT)

    if L2F_STAGE < 3:
        return
    emit_weight_quant(ctx, IN["w2T"], FF, D, wqbuf, mwc2, tag + "w2", wstg, None)
    if L2F_STAGE < 4:
        return

    BB = {k: ctx.val(k) for k in ("ld16", "hb", "va", "sr", "q", "tp", "ev",
                                  "m2", "e2", "gl", "st", "v2g", "g2v")}

    def syncB(e):
        for t in range(NT):
            if t >= 2:
                e.wait_ge(ctx.h("v2g"), BB["v2g"] + t - 1)
            e.dma_start(out=hbt[:, (t % 2) * FF:((t % 2) + 1) * FF],
                        in_=h_dram[t * 128:(t + 1) * 128, :]).then_inc(ctx.h("hb"), 16)
            if t >= 2:
                e.wait_ge(ctx.h("gl"), BB["gl"] + t - 1)
            e.dma_start(out=xt[:, (t % 2) * D:((t % 2) + 1) * D],
                        in_=x_in[t * 128:(t + 1) * 128, :]).then_inc(ctx.h("ld16"), 16)

    def gpB(e):
        for t in range(NT):
            e.wait_ge(ctx.h("v2g"), BB["v2g"] + t + 1)
            e.tensor_mul(un2[:], un2[:], g2b[:])
            e.tensor_add(un2[:], un2[:], b2lb[:])
            ctx.sig(e, "g2v", 1)
            e.wait_ge(ctx.h("gl"), BB["gl"] + t + 1)
            e.dma_start(out=x_out[t * 128:(t + 1) * 128, :],
                        in_=ot[:, (t % 2) * D:((t % 2) + 1) * D]).then_inc(ctx.h("st"), 16)
        e.wait_ge(ctx.h("st"), BB["st"] + 16 * NT)

    def vectorB(e):
        for t in range(NT):
            TP, EV = BB["tp"] + 33 * t, BB["ev"] + 33 * t
            hbs = hbt[:, (t % 2) * FF:((t % 2) + 1) * FF]
            e.wait_ge(ctx.h("hb"), BB["hb"] + 16 * (t + 1))
            ln_stats(e, hbs, FF, st6, st2, sc[:, 6:7], sc[:, 7:8], "va", ctx)
            e.wait_ge(ctx.h("sr"), BB["sr"] + 2 * t + 1)
            e.tensor_scalar(un2[:], hbs, sc[:, 6:7], r1c[:, 2:3],
                            ALU.subtract, ALU.mult)
            ctx.sig(e, "v2g", 1)
            e.wait_ge(ctx.h("g2v"), BB["g2v"] + t + 1)
            e.reduce_max(sc[:, 8:9], un2[:], axis=AX.X, apply_absolute_value=True)
            e.tensor_scalar_max(sc[:, 8:9], sc[:, 8:9], 1e-5)
            e.reciprocal(sc[:, 9:10], sc[:, 8:9])
            e.tensor_scalar(wstg[:], un2[:], sc[:, 9:10], 127.0, ALU.mult, ALU.mult)
            e.tensor_scalar(xq2t[:], wstg[:], RMAGIC, -RMAGIC, ALU.add, ALU.add)
            e.tensor_scalar(rs2[:], sc[:, 8:9], 1.0 / 127.0, mwc2[:, 0:1], ALU.mult, ALU.mult)
            e.tensor_scalar(iv2c[:], sc[:, 9:10], 127.0, mwc2[:, 1:2],
                            ALU.mult, ALU.mult)
            ctx.sig(e, "q", 1)
            e.wait_ge(ctx.h("tp"), TP + 1)
            e.tensor_copy(iv2r[:], ptr[0][0:1, 0:128])
            ctx.sig(e, "ev", 1)
            for j in range(32):
                e.wait_ge(ctx.h("tp"), TP + 2 + j)
                e.tensor_copy(xq2f[:, j * 128:(j + 1) * 128], ptr[(1 + j) % 2][:])
                ctx.sig(e, "ev", 1)
            cs = ct[:, (t % 2) * D:((t % 2) + 1) * D]
            xs = xt[:, (t % 2) * D:((t % 2) + 1) * D]
            e.wait_ge(ctx.h("e2"), BB["e2"] + t + 1)
            e.wait_ge(ctx.h("ld16"), BB["ld16"] + 16 * (t + 1))
            ln_stats(e, cs, D, st6, st2, sc[:, 10:11], sc[:, 11:12], "va", ctx)
            e.wait_ge(ctx.h("sr"), BB["sr"] + 2 * t + 2)
            os_ = ot[:, (t % 2) * D:((t % 2) + 1) * D]
            if t >= 2:
                e.wait_ge(ctx.h("st"), BB["st"] + 16 * (t - 1))
            e.tensor_scalar(os_, cs, sc[:, 10:11], r1c[:, 3:4], ALU.subtract, ALU.mult)
            e.tensor_add(os_, os_, xs)
            ctx.sig(e, "gl", 1)

    def scalarB(e):
        for t in range(NT):
            e.wait_ge(ctx.h("va"), BB["va"] + 2 * t + 1)
            rstd_ops(e, sc[:, 7:8], sc[:, 12:13], r1c[:, 2:3], epsc[:])
            ctx.sig(e, "sr", 1)
            cs = ct[:, (t % 2) * D:((t % 2) + 1) * D]
            for dg in range(2):
                e.wait_ge(ctx.h("m2"), BB["m2"] + 2 * t + dg + 1)
                e.activation(cs[:, 512 * dg:512 * (dg + 1)], pm2[dg][:],
                             AF.Identity, scale=rs2[:])
                if dg == 1:
                    ctx.sig(e, "e2", 1)
            e.wait_ge(ctx.h("va"), BB["va"] + 2 * t + 2)
            rstd_ops(e, sc[:, 11:12], sc[:, 12:13], r1c[:, 3:4], epsc[:])
            ctx.sig(e, "sr", 1)

    def tensorB(e):
        for t in range(NT):
            TP, EV = BB["tp"] + 33 * t, BB["ev"] + 33 * t
            e.wait_ge(ctx.h("q"), BB["q"] + t + 1)
            e.wait_ge(ctx.h("ev"), EV)
            e.transpose(ptr[0][0:1, 0:128], iv2c[:], ident[:]).then_inc(ctx.h("tp"), 1)
            for j in range(32):
                i = 1 + j
                if i >= 2:
                    e.wait_ge(ctx.h("ev"), EV + i - 1)
                e.transpose(ptr[(1 + j) % 2][:], xq2t[:, j * 128:(j + 1) * 128],
                            ident[:]).then_inc(ctx.h("tp"), 1)
            e.wait_ge(ctx.h("ev"), EV + 33)
            if t >= 1:
                e.wait_ge(ctx.h("e2"), BB["e2"] + t)
            for dg in range(2):
                for j in range(32):
                    e.matmul(pm2[dg][:], xq2f[:, j * 128:(j + 1) * 128],
                             wqbuf[:, j * D + dg * 512: j * D + (dg + 1) * 512],
                             start=(j == 0), stop=False)
                e.matmul(pm2[dg][:], iv2r[:], b2row[0:1, dg * 512:(dg + 1) * 512],
                         start=False, stop=True).then_inc(ctx.h("m2"), 1)

    phase(nc, f"{tag}_B", _ctx=ctx, sync=syncB, gpsimd=gpB, vector=vectorB, scalar=scalarB, tensor=tensorB)
    ctx.setval("ld16", BB["ld16"] + 16 * NT)
    ctx.setval("hb", BB["hb"] + 16 * NT)
    ctx.setval("va", BB["va"] + 2 * NT)
    ctx.setval("sr", BB["sr"] + 2 * NT)
    ctx.setval("q", BB["q"] + NT)
    ctx.setval("tp", BB["tp"] + 33 * NT)
    ctx.setval("ev", BB["ev"] + 33 * NT)
    ctx.setval("m2", BB["m2"] + 2 * NT)
    ctx.setval("e2", BB["e2"] + NT)
    ctx.setval("gl", BB["gl"] + NT)
    ctx.setval("st", BB["st"] + 16 * NT)
    ctx.setval("v2g", BB["v2g"] + NT)
    ctx.setval("g2v", BB["g2v"] + NT)



def emit_l3f(ctx, IN, tag="l3f"):
    nc = ctx.nc
    E = 8

    ew = ctx.sb(f"{tag}_ew", [128, E * 8 * D], BF16)
    ident = ctx.sb("identb", [128, 128], BF16)
    epsc = ctx.sb("epscol", [128, 1], F32)
    gws = ctx.sb(f"{tag}_gw", [128, 8 * E], BF16)
    gbrow = ctx.sb(f"{tag}_gb", [1, E], BF16)
    onesrb = ctx.sb(f"{tag}_ones", [1, 128], BF16)
    ebs = ctx.sb(f"{tag}_eb", [E, D], BF16)

    xt = ctx.sb("g_xt", [128, 2 * D], F32)
    xn = ctx.sb(f"{tag}_xn", [128, D], F32)
    xnb = ctx.sb(f"{tag}_xnb", [128, D], BF16)
    xnf = ctx.sb(f"{tag}_xnf", [128, D], BF16)
    acc = ctx.sb(f"{tag}_acc", [128, 2 * D], F32)
    ot = ctx.sb("g_ot", [128, 2 * D], F32)
    sc = ctx.sb("g_sc", [128, 16], F32)
    st6 = ctx.sb("g_st6", [128, 48], F32)
    st2 = ctx.sb("g_st2", [128, 2], F32)
    r1c = ctx.sb("g_r1c", [128, 4], F32)
    gE = ctx.sb(f"{tag}_gE", [128, E], F32)
    wts = ctx.sb(f"{tag}_wts", [128, E], F32)
    wtsb = ctx.sb(f"{tag}_wtsb", [128, E], BF16)
    wtsf = ctx.sb(f"{tag}_wtsf", [8, 128], BF16)

    pg = ctx.ps("ps_pm1_0", [128, 512], F32)
    pb = [ctx.ps(f"ps_pm2_{i}", [128, 512], F32) for i in range(2)]
    pe = [ctx.ps(f"ps_pm1_{i}", [128, 512], F32) for i in (1, 2, 3)]
    ptr = [ctx.ps(f"ps_ptr_{i}", [128, 128], BF16) for i in range(2)]

    def syncp(e):
        ins = e.dma_start(out=ident[:], in_=IN["identC"])
        ctx.inc(ins, "ld", 16)

    def gpp(e):
        e.memset(epsc[:], LN_EPS)
        e.memset(onesrb[:], 1.0)
        ins = e.dma_start(out=gbrow[:], in_=IN["gate_b"])
        ctx.inc(ins, "ldg", 16)
        ins = e.dma_start(out=ew[:].rearrange("p (c n) -> p c n", n=D),
                          in_=IN["expert_w"].rearrange("(c p) n -> p c n", p=128))
        ctx.inc(ins, "ldg", 16)
        ins = e.dma_start(out=gws[:].rearrange("p (c n) -> p c n", n=E),
                          in_=IN["gate_w"].rearrange("(c p) n -> p c n", p=128))
        ctx.inc(ins, "ldg", 16)
        ins = e.dma_start(out=ebs[:], in_=IN["expert_b"])
        ctx.inc(ins, "ldg", 16)
        ctx.wait(e, "ldg")
        ctx.wait(e, "ld")
    phase(nc, f"{tag}_pro", _ctx=ctx, sync=syncp, gpsimd=gpp)

    B = {k: ctx.val(k) for k in ("ld16", "va", "sr", "q", "tp", "ev", "mg",
                                 "sx", "m2", "me", "ea", "gl", "st")}
    x_in, x_out = IN["x"], IN["x_out"]

    def sync(e):
        for t in range(NT):
            if t >= 2:
                e.wait_ge(ctx.h("gl"), B["gl"] + t - 1)
            e.dma_start(out=xt[:, (t % 2) * D:((t % 2) + 1) * D],
                        in_=x_in[t * 128:(t + 1) * 128, :]).then_inc(ctx.h("ld16"), 16)

    def gp(e):
        for t in range(NT):
            e.wait_ge(ctx.h("gl"), B["gl"] + t + 1)
            e.dma_start(out=x_out[t * 128:(t + 1) * 128, :],
                        in_=ot[:, (t % 2) * D:((t % 2) + 1) * D]).then_inc(ctx.h("st"), 16)
        e.wait_ge(ctx.h("st"), B["st"] + 16 * NT)

    def vector(e):
        for t in range(NT):
            xs = xt[:, (t % 2) * D:((t % 2) + 1) * D]
            e.wait_ge(ctx.h("ld16"), B["ld16"] + 16 * (t + 1))
            ln_stats(e, xs, D, st6, st2, sc[:, 0:1], sc[:, 1:2], "va", ctx)
            e.wait_ge(ctx.h("sr"), B["sr"] + 2 * t + 1)
            e.tensor_scalar(xn[:], xs, sc[:, 0:1], r1c[:, 0:1], ALU.subtract, ALU.mult)
            e.tensor_copy(xnb[:], xn[:])
            ctx.sig(e, "q", 1)
            for j in range(8):
                e.wait_ge(ctx.h("tp"), B["tp"] + 9 * t + j + 1)
                e.tensor_copy(xnf[:, j * 128:(j + 1) * 128], ptr[j % 2][:])
                ctx.sig(e, "ev", 1)
            e.wait_ge(ctx.h("mg"), B["mg"] + t + 1)
            e.reduce_max(sc[:, 2:3], pg[:, 0:E], axis=AX.X)
            e.tensor_scalar_mul(sc[:, 3:4], sc[:, 2:3], -1.0)
            ctx.sig(e, "va", 1)
            e.wait_ge(ctx.h("sx"), B["sx"] + t + 1)
            e.reduce_sum(sc[:, 4:5], gE[:, 0:E], axis=AX.X)
            e.tensor_scalar_mul(sc[:, 5:6], sc[:, 4:5], 0.1)
            e.tensor_scalar(wts[:], gE[:], sc[:, 5:6], None, ALU.is_ge)
            e.tensor_mul(wts[:], wts[:], gE[:])
            e.reduce_sum(sc[:, 6:7], wts[:], axis=AX.X)
            e.reciprocal(sc[:, 7:8], sc[:, 6:7])
            e.tensor_scalar_mul(wts[:], wts[:], sc[:, 7:8])
            e.tensor_copy(wtsb[:], wts[:])
            ctx.sig(e, "q", 1)
            e.wait_ge(ctx.h("tp"), B["tp"] + 9 * t + 9)
            e.tensor_copy(wtsf[:], ptr[0][0:8, 0:128])
            ctx.sig(e, "ev", 1)
            ac = acc[:, (t % 2) * D:((t % 2) + 1) * D]
            e.wait_ge(ctx.h("m2"), B["m2"] + 2 * (t + 1))
            e.tensor_copy(ac[:, 0:512], pb[0][:])
            e.tensor_copy(ac[:, 512:1024], pb[1][:])
            ctx.sig(e, "ea", 1)
            for ei in range(E):
                e.wait_ge(ctx.h("me"), B["me"] + 16 * t + 2 * ei + 2)
                e.scalar_tensor_tensor(ac[:, 0:512], pe[0][:], wts[:, ei:ei + 1],
                                       ac[:, 0:512], ALU.mult, ALU.add)
                e.scalar_tensor_tensor(ac[:, 512:1024], pe[1][:], wts[:, ei:ei + 1],
                                       ac[:, 512:1024], ALU.mult, ALU.add)
                ctx.sig(e, "ea", 2)
            ln_stats(e, ac, D, st6, st2, sc[:, 10:11], sc[:, 11:12], "va", ctx)
            e.wait_ge(ctx.h("sr"), B["sr"] + 2 * t + 2)
            os_ = ot[:, (t % 2) * D:((t % 2) + 1) * D]
            if t >= 2:
                e.wait_ge(ctx.h("st"), B["st"] + 16 * (t - 1))
            e.tensor_scalar(os_, ac, sc[:, 10:11], r1c[:, 3:4], ALU.subtract, ALU.mult)
            e.tensor_add(os_, os_, xs)
            ctx.sig(e, "gl", 1)

    def scalar(e):
        for t in range(NT):
            e.wait_ge(ctx.h("va"), B["va"] + 3 * t + 1)
            rstd_ops(e, sc[:, 1:2], sc[:, 12:13], r1c[:, 0:1], epsc[:])
            ctx.sig(e, "sr", 1)
            e.wait_ge(ctx.h("va"), B["va"] + 3 * t + 2)
            e.activation(gE[:, 0:E], pg[:, 0:E], AF.Exp, bias=sc[:, 3:4])
            ctx.sig(e, "sx", 1)
            e.wait_ge(ctx.h("va"), B["va"] + 3 * t + 3)
            rstd_ops(e, sc[:, 11:12], sc[:, 12:13], r1c[:, 3:4], epsc[:])
            ctx.sig(e, "sr", 1)

    def tensor(e):
        for t in range(NT):
            TP, EV = B["tp"] + 9 * t, B["ev"] + 9 * t
            e.wait_ge(ctx.h("q"), B["q"] + 2 * t + 1)
            e.wait_ge(ctx.h("ev"), EV)
            for j in range(8):
                if j >= 2:
                    e.wait_ge(ctx.h("ev"), EV + j - 1)
                e.transpose(ptr[j % 2][:], xnb[:, j * 128:(j + 1) * 128],
                            ident[:]).then_inc(ctx.h("tp"), 1)
            e.wait_ge(ctx.h("ev"), EV + 8)
            if t >= 1:
                e.wait_ge(ctx.h("sx"), B["sx"] + t)
            for j in range(8):
                e.matmul(pg[:, 0:E], xnf[:, j * 128:(j + 1) * 128], gws[:, j * E:(j + 1) * E],
                         start=(j == 0), stop=False)
            mm = e.matmul(pg[:, 0:E], onesrb[:], gbrow[:], start=False, stop=True)
            mm.then_inc(ctx.h("mg"), 1)
            e.wait_ge(ctx.h("q"), B["q"] + 2 * t + 2)
            e.transpose(ptr[0][0:8, 0:128], wtsb[:, 0:E], ident[:]).then_inc(ctx.h("tp"), 1)
            e.wait_ge(ctx.h("ev"), EV + 9)
            if t >= 1:
                e.wait_ge(ctx.h("ea"), B["ea"] + 17 * t)
            for dg in range(2):
                e.matmul(pb[dg][:], wtsf[0:8, :], ebs[0:8, dg * 512:(dg + 1) * 512],
                         start=True, stop=True).then_inc(ctx.h("m2"), 1)
            for ei in range(E):
                if ei >= 1:
                    e.wait_ge(ctx.h("ea"), B["ea"] + 17 * t + 1 + 2 * ei)
                for dg in range(2):
                    pdst = pe[0] if dg == 0 else pe[1]
                    for j in range(8):
                        mm = e.matmul(pdst[:], xnf[:, j * 128:(j + 1) * 128],
                                 ew[:, (ei * 8 + j) * D + dg * 512:(ei * 8 + j) * D + (dg + 1) * 512],
                                 start=(j == 0), stop=(j == 7))
                    mm.then_inc(ctx.h("me"), 1)

    phase(nc, f"{tag}_main", _ctx=ctx, sync=sync, gpsimd=gp, vector=vector, scalar=scalar, tensor=tensor)
    for k, r in dict(ld16=16, va=3, sr=2, q=2, tp=9, ev=9, mg=1, sx=1, m2=2,
                     me=16, ea=17, gl=1, st=16).items():
        ctx.setval(k, B[k] + r * NT)


def build_l3f_program():
    es = contextlib.ExitStack()
    nc = mk_bass()
    IN = {}
    IN["x"] = nc.dram_tensor("x", [T_LOC, D], F32, kind="ExternalInput").ap()
    IN["gate_w"] = nc.dram_tensor("gate_w", [D, 8], F32, kind="ExternalInput").ap()
    IN["gate_b"] = nc.dram_tensor("gate_b", [1, 8], F32, kind="ExternalInput").ap()
    IN["expert_w"] = nc.dram_tensor("expert_w", [8 * D, D], F32, kind="ExternalInput").ap()
    IN["expert_b"] = nc.dram_tensor("expert_b", [8, D], F32, kind="ExternalInput").ap()
    IN["identC"] = nc.dram_tensor("identC", [128, 128], BF16, kind="ExternalInput").ap()
    IN["x_out"] = nc.dram_tensor("x_out", [T_LOC, D], F32, kind="ExternalOutput").ap()
    ctx = Ctx(nc, es)
    emit_l3f(ctx, IN)
    es.close()
    return nc


def l3f_in_maps(x_full, ins):
    import ml_dtypes
    identC = np.eye(128, dtype=ml_dtypes.bfloat16)
    ew = np.ascontiguousarray(ins["L3_ffn_expert_w"].astype(np.float32).reshape(8 * D, D))
    maps = []
    for c in range(NCORE):
        b, h = c // 2, c % 2
        shard = np.ascontiguousarray(x_full[b, h * T_LOC:(h + 1) * T_LOC, :].astype(np.float32))
        maps.append(dict(
            x=shard, gate_w=np.ascontiguousarray(ins["L3_ffn_gate_w"].astype(np.float32)),
            gate_b=row2d(ins["L3_ffn_gate_b"]), expert_w=ew,
            expert_b=np.ascontiguousarray(ins["L3_ffn_expert_b"].astype(np.float32)),
            identC=identC,
        ))
    return maps



def emit_l0f(ctx, IN, tag="l0f"):
    nc = ctx.nc
    FF = 4096
    E, R = 8, 4

    wqbuf = ctx.sb("wqbuf", [128, 32768], BF16)
    ident = ctx.sb("identb", [128, 128], BF16)
    epsc = ctx.sb("epscol", [128, 1], F32)
    gws = ctx.sb(f"{tag}_gw", [128, 8 * E], BF16)
    gbrow = ctx.sb(f"{tag}_gb", [1, E], BF16)
    onesrb = ctx.sb(f"{tag}_ones", [1, 128], BF16)
    a2 = ctx.sb(f"{tag}_a2", [128, 8 * 32], BF16)
    bcat = ctx.sb(f"{tag}_bcat", [32, FF], BF16)
    bi_row = ctx.sb(f"{tag}_birow", [1, FF], BF16)
    bo_row = ctx.sb(f"{tag}_borow", [1, D], BF16)

    xt = ctx.sb("g_xt", [128, 2 * D], F32)
    xn = ctx.sb(f"{tag}_xn", [128, D], F32)
    xnb = ctx.sb(f"{tag}_xnb", [128, D], BF16)
    xnf = ctx.sb(f"{tag}_xnf", [128, D], BF16)
    mrep = ctx.sb(f"{tag}_mrep", [128, 64], F32)
    cm = ctx.sb(f"{tag}_cm", [128, 32], BF16)
    cmf = ctx.sb(f"{tag}_cmf", [32, 128], BF16)
    inter = ctx.sb(f"{tag}_inter", [128, 2 * FF], BF16)
    intf = ctx.sb(f"{tag}_intf", [128, FF], BF16)
    ct = ctx.sb("g_ct", [128, 2 * D], F32)
    ot = ctx.sb("g_ot", [128, 2 * D], F32)
    sc = ctx.sb("g_sc", [128, 16], F32)
    st6 = ctx.sb("g_st6", [128, 48], F32)
    st2 = ctx.sb("g_st2", [128, 2], F32)
    r1c = ctx.sb("g_r1c", [128, 4], F32)

    pm1 = [ctx.ps(f"ps_pm1_{i}", [128, 512], F32) for i in range(4)]
    pm2 = [ctx.ps(f"ps_pm2_{i}", [128, 512], F32) for i in range(2)]
    ptr = [ctx.ps(f"ps_ptr_{i}", [128, 128], BF16) for i in range(2)]

    x_in, x_out, h_dram = IN["x"], IN["x_out"], IN["h_spill"]

    def syncp(e):
        ins = e.dma_start(out=ident[:], in_=IN["identC"])
        ctx.inc(ins, "ld", 16)

    def gpp(e):
        e.memset(epsc[:], LN_EPS)
        e.memset(onesrb[:], 1.0)
        for r in range(8):
            ins = e.dma_start(out=wqbuf[:, r * FF:(r + 1) * FF],
                              in_=IN["wiT"][r * 128:(r + 1) * 128, :])
            ctx.inc(ins, "ldg", 16)
        for dst, src in ((gbrow, IN["gate_b"]), (bi_row, IN["wi_b"]), (bo_row, IN["wo_b"]),
                         (bcat, IN["Bcat"])):
            ins = e.dma_start(out=dst[:], in_=src)
            ctx.inc(ins, "ldg", 16)
        ins = e.dma_start(out=gws[:].rearrange("p (c n) -> p c n", n=E),
                          in_=IN["gateWT"].rearrange("(c p) n -> p c n", p=128))
        ctx.inc(ins, "ldg", 16)
        ins = e.dma_start(out=a2[:].rearrange("p (c n) -> p c n", n=32),
                          in_=IN["AcatT"].rearrange("(c p) n -> p c n", p=128))
        ctx.inc(ins, "ldg", 16)
        ctx.wait(e, "ldg")
        ctx.wait(e, "ld")
    phase(nc, f"{tag}_pro", _ctx=ctx, sync=syncp, gpsimd=gpp)
    if int(os.environ.get("L0F_STAGE", "9")) < 2:
        return

    BA = {k: ctx.val(k) for k in ("ld16", "va", "sr", "q", "tp", "ev", "mg",
                                  "sx", "mc", "m1", "hg", "hs")}

    def syncA(e):
        for t in range(NT):
            if t >= 2:
                e.wait_ge(ctx.h("q"), BA["q"] + 2 * (t - 2) + 1)
            e.dma_start(out=xt[:, (t % 2) * D:((t % 2) + 1) * D],
                        in_=x_in[t * 128:(t + 1) * 128, :]).then_inc(ctx.h("ld16"), 16)

    def gpA(e):
        for t in range(NT):
            e.wait_ge(ctx.h("hg"), BA["hg"] + 8 * (t + 1))
            e.dma_start(out=h_dram[t * 128:(t + 1) * 128, :],
                        in_=inter[:, (t % 2) * FF:((t % 2) + 1) * FF]).then_inc(ctx.h("hs"), 16)
        e.wait_ge(ctx.h("hs"), BA["hs"] + 16 * NT)

    def vectorA(e):
        for t in range(NT):
            TP, EV = BA["tp"] + 9 * t, BA["ev"] + 9 * t
            xs = xt[:, (t % 2) * D:((t % 2) + 1) * D]
            e.wait_ge(ctx.h("ld16"), BA["ld16"] + 16 * (t + 1))
            ln_stats(e, xs, D, st6, st2, sc[:, 0:1], sc[:, 1:2], "va", ctx)
            e.wait_ge(ctx.h("sr"), BA["sr"] + t + 1)
            e.tensor_scalar(xn[:], xs, sc[:, 0:1], r1c[:, 0:1], ALU.subtract, ALU.mult)
            e.tensor_copy(xnb[:], xn[:])
            ctx.sig(e, "q", 1)
            for j in range(8):
                e.wait_ge(ctx.h("tp"), TP + j + 1)
                e.tensor_copy(xnf[:, j * 128:(j + 1) * 128], ptr[j % 2][:])
                ctx.sig(e, "ev", 1)
            e.wait_ge(ctx.h("mg"), BA["mg"] + t + 1)
            m3 = os.environ.get("L0F_M3", "c") if os.environ.get("L0F_NOGATE", "0") in ("0", "3") else ""
            if m3 in ("a", "b", "c"):
                e.tensor_copy(mrep[:, 8:16], pm1[3][:, 0:E])
            if m3 in ("b", "c"):
                e.reduce_max(sc[:, 2:3], mrep[:, 8:16], axis=AX.X)
            if m3 == "c":
                e.tensor_scalar(mrep[:, 0:E], mrep[:, 8:16], sc[:, 2:3], None, ALU.is_ge)
            else:
                e.memset(mrep[:, 0:E], 1.0)
            ctx.sig(e, "va", 1)
            e.wait_ge(ctx.h("sx"), BA["sx"] + t + 1)
            e.wait_ge(ctx.h("mc"), BA["mc"] + t + 1)
            if os.environ.get("L0F_NOGATE", "0") == "0":
                e.tensor_copy(mrep[:, 0:32], pm2[0][:, 0:32])
                e.tensor_mul(mrep[:, 32:64], mrep[:, 32:64], mrep[:, 0:32])
                e.tensor_copy(cm[:], mrep[:, 32:64])
            else:
                e.memset(cm[:], 0.0)
            ctx.sig(e, "q", 1)
            e.wait_ge(ctx.h("tp"), TP + 9)
            e.tensor_copy(cmf[:], ptr[1][0:32, 0:128])
            ctx.sig(e, "ev", 1)

    def scalarA(e):
        for t in range(NT):
            e.wait_ge(ctx.h("va"), BA["va"] + 2 * t + 1)
            rstd_ops(e, sc[:, 1:2], sc[:, 12:13], r1c[:, 0:1], epsc[:])
            ctx.sig(e, "sr", 1)
            e.wait_ge(ctx.h("va"), BA["va"] + 2 * t + 2)
            if os.environ.get("L0F_NOGATE", "0") in ("0", "4"):
                src = bass.AP(ctx._sb[f"{tag}_mrep"], 0, [[64, 128], [1, E], [0, R]])
                e.activation(mrep[:, 32:64], src, AF.Identity)
            else:
                e.copy(sc[:, 14:15], sc[:, 0:1])
            ctx.sig(e, "sx", 1)
            hs = inter[:, (t % 2) * FF:((t % 2) + 1) * FF]
            if t >= 2:
                e.wait_ge(ctx.h("hs"), BA["hs"] + 16 * (t - 1))
            for g in range(8):
                e.wait_ge(ctx.h("m1"), BA["m1"] + 8 * t + g + 1)
                e.activation(hs[:, 512 * g:512 * (g + 1)], pm1[(8 * t + g) % 3][:], AF.Relu)
                ctx.sig(e, "hg", 1)

    def tensorA(e):
        for t in range(NT):
            TP, EV = BA["tp"] + 9 * t, BA["ev"] + 9 * t
            e.wait_ge(ctx.h("q"), BA["q"] + 2 * t + 1)
            e.wait_ge(ctx.h("ev"), EV)
            for j in range(8):
                if j >= 2:
                    e.wait_ge(ctx.h("ev"), EV + j - 1)
                e.transpose(ptr[j % 2][:], xnb[:, j * 128:(j + 1) * 128],
                            ident[:]).then_inc(ctx.h("tp"), 1)
            e.wait_ge(ctx.h("ev"), EV + 8)
            if t >= 1:
                e.wait_ge(ctx.h("q"), BA["q"] + 2 * t)
            if os.environ.get("L0F_NOGATE", "0") in ("0", "2", "3", "4"):
                for j in range(8):
                    e.matmul(pm1[3][:, 0:E], xnf[:, j * 128:(j + 1) * 128],
                             gws[:, j * E:(j + 1) * E], start=(j == 0), stop=False)
                e.matmul(pm1[3][:, 0:E], onesrb[:], gbrow[:], start=False, stop=True
                         ).then_inc(ctx.h("mg"), 1)
                for j in range(8):
                    mm = e.matmul(pm2[0][:, 0:32], xnf[:, j * 128:(j + 1) * 128],
                                  a2[:, j * 32:(j + 1) * 32], start=(j == 0), stop=(j == 7))
                mm.then_inc(ctx.h("mc"), 1)
            else:
                e.matmul(pm1[3][:, 0:E], onesrb[:], gbrow[:], start=True, stop=True
                         ).then_inc(ctx.h("mg"), 1)
                e.matmul(pm2[0][:, 0:32], onesrb[:], bcat[0:1, 0:32],
                         start=True, stop=True).then_inc(ctx.h("mc"), 1)
            e.wait_ge(ctx.h("q"), BA["q"] + 2 * t + 2)
            e.transpose(ptr[1][0:32, 0:128], cm[:, 0:32], ident[:]).then_inc(ctx.h("tp"), 1)
            e.wait_ge(ctx.h("ev"), EV + 9)
            for g in range(8):
                G = 8 * t + g
                if G >= 3:
                    e.wait_ge(ctx.h("hg"), BA["hg"] + G - 2)
                for j in range(8):
                    e.matmul(pm1[G % 3][:], xnf[:, j * 128:(j + 1) * 128],
                             wqbuf[:, j * FF + g * 512: j * FF + (g + 1) * 512],
                             start=(j == 0), stop=False)
                e.matmul(pm1[G % 3][:], cmf[:], bcat[0:32, g * 512:(g + 1) * 512],
                         start=False, stop=False)
                e.matmul(pm1[G % 3][:], onesrb[:], bi_row[0:1, g * 512:(g + 1) * 512],
                         start=False, stop=True).then_inc(ctx.h("m1"), 1)

    phase(nc, f"{tag}_A", _ctx=ctx, sync=syncA, gpsimd=gpA, vector=vectorA,
          scalar=scalarA, tensor=tensorA)
    for k, r in dict(ld16=16, va=2, sr=1, q=2, tp=9, ev=9, mg=1, sx=1, mc=1,
                     m1=8, hg=8, hs=16).items():
        ctx.setval(k, BA[k] + r * NT)

    if int(os.environ.get("L0F_STAGE", "9")) < 3:
        return
    def syncw(e):
        for r in range(32):
            ins = e.dma_start(out=wqbuf[:, r * D:(r + 1) * D],
                              in_=IN["woT"][r * 128:(r + 1) * 128, :])
            ctx.inc(ins, "ldg", 16)
        ctx.wait(e, "ldg")
    phase(nc, f"{tag}_wo", _ctx=ctx, gpsimd=syncw)

    BB = {k: ctx.val(k) for k in ("ld16", "hb", "va", "sr", "tp", "ev",
                                  "m2", "e2", "gl", "st")}

    def syncB(e):
        for t in range(NT):
            if t >= 2:
                e.wait_ge(ctx.h("tp"), BB["tp"] + 32 * (t - 1))
            e.dma_start(out=inter[:, (t % 2) * FF:((t % 2) + 1) * FF],
                        in_=h_dram[t * 128:(t + 1) * 128, :]).then_inc(ctx.h("hb"), 16)
            if t >= 2:
                e.wait_ge(ctx.h("gl"), BB["gl"] + t - 1)
            e.dma_start(out=xt[:, (t % 2) * D:((t % 2) + 1) * D],
                        in_=x_in[t * 128:(t + 1) * 128, :]).then_inc(ctx.h("ld16"), 16)

    def gpB(e):
        for t in range(NT):
            e.wait_ge(ctx.h("gl"), BB["gl"] + t + 1)
            e.dma_start(out=x_out[t * 128:(t + 1) * 128, :],
                        in_=ot[:, (t % 2) * D:((t % 2) + 1) * D]).then_inc(ctx.h("st"), 16)
        e.wait_ge(ctx.h("st"), BB["st"] + 16 * NT)

    def vectorB(e):
        for t in range(NT):
            TP, EV = BB["tp"] + 32 * t, BB["ev"] + 32 * t
            hbs = inter[:, (t % 2) * FF:((t % 2) + 1) * FF]
            for j in range(32):
                e.wait_ge(ctx.h("tp"), TP + j + 1)
                e.tensor_copy(intf[:, j * 128:(j + 1) * 128], ptr[j % 2][:])
                ctx.sig(e, "ev", 1)
            cs = ct[:, (t % 2) * D:((t % 2) + 1) * D]
            xs = xt[:, (t % 2) * D:((t % 2) + 1) * D]
            e.wait_ge(ctx.h("e2"), BB["e2"] + t + 1)
            e.wait_ge(ctx.h("ld16"), BB["ld16"] + 16 * (t + 1))
            ln_stats(e, cs, D, st6, st2, sc[:, 10:11], sc[:, 11:12], "va", ctx)
            e.wait_ge(ctx.h("sr"), BB["sr"] + t + 1)
            os_ = ot[:, (t % 2) * D:((t % 2) + 1) * D]
            if t >= 2:
                e.wait_ge(ctx.h("st"), BB["st"] + 16 * (t - 1))
            e.tensor_scalar(os_, cs, sc[:, 10:11], r1c[:, 3:4], ALU.subtract, ALU.mult)
            e.tensor_add(os_, os_, xs)
            ctx.sig(e, "gl", 1)

    def scalarB(e):
        for t in range(NT):
            cs = ct[:, (t % 2) * D:((t % 2) + 1) * D]
            for dg in range(2):
                e.wait_ge(ctx.h("m2"), BB["m2"] + 2 * t + dg + 1)
                e.activation(cs[:, 512 * dg:512 * (dg + 1)], pm2[dg][:], AF.Identity)
                if dg == 1:
                    ctx.sig(e, "e2", 1)
            e.wait_ge(ctx.h("va"), BB["va"] + t + 1)
            rstd_ops(e, sc[:, 11:12], sc[:, 12:13], r1c[:, 3:4], epsc[:])
            ctx.sig(e, "sr", 1)

    def tensorB(e):
        for t in range(NT):
            TP, EV = BB["tp"] + 32 * t, BB["ev"] + 32 * t
            e.wait_ge(ctx.h("hb"), BB["hb"] + 16 * (t + 1))
            e.wait_ge(ctx.h("ev"), EV)
            for j in range(32):
                if j >= 2:
                    e.wait_ge(ctx.h("ev"), EV + j - 1)
                e.transpose(ptr[j % 2][:], hbs_ap(inter, t, j),
                            ident[:]).then_inc(ctx.h("tp"), 1)
            e.wait_ge(ctx.h("ev"), EV + 32)
            if t >= 1:
                e.wait_ge(ctx.h("e2"), BB["e2"] + t)
            for dg in range(2):
                for j in range(32):
                    e.matmul(pm2[dg][:], intf[:, j * 128:(j + 1) * 128],
                             wqbuf[:, j * D + dg * 512: j * D + (dg + 1) * 512],
                             start=(j == 0), stop=False)
                e.matmul(pm2[dg][:], onesrb[:], bo_row[0:1, dg * 512:(dg + 1) * 512],
                         start=False, stop=True).then_inc(ctx.h("m2"), 1)

    phase(nc, f"{tag}_B", _ctx=ctx, sync=syncB, gpsimd=gpB, vector=vectorB,
          scalar=scalarB, tensor=tensorB)
    for k, r in dict(ld16=16, hb=16, va=1, sr=1, tp=32, ev=32, m2=2,
                     e2=1, gl=1, st=16).items():
        ctx.setval(k, BB[k] + r * NT)


def hbs_ap(inter, t, j):
    FF = 4096
    return inter[:, (t % 2) * FF + j * 128:(t % 2) * FF + (j + 1) * 128]


def build_l0f_program():
    es = contextlib.ExitStack()
    nc = mk_bass()
    IN = {}
    IN["x"] = nc.dram_tensor("x", [T_LOC, D], F32, kind="ExternalInput").ap()
    IN["wiT"] = nc.dram_tensor("wiT", [D, 4096], F32, kind="ExternalInput").ap()
    IN["woT"] = nc.dram_tensor("woT", [4096, D], F32, kind="ExternalInput").ap()
    IN["gateWT"] = nc.dram_tensor("gateWT", [D, 8], F32, kind="ExternalInput").ap()
    IN["gate_b"] = nc.dram_tensor("gate_b", [1, 8], F32, kind="ExternalInput").ap()
    IN["wi_b"] = nc.dram_tensor("wi_b", [1, 4096], F32, kind="ExternalInput").ap()
    IN["wo_b"] = nc.dram_tensor("wo_b", [1, D], F32, kind="ExternalInput").ap()
    IN["AcatT"] = nc.dram_tensor("AcatT", [D, 32], F32, kind="ExternalInput").ap()
    IN["Bcat"] = nc.dram_tensor("Bcat", [32, 4096], F32, kind="ExternalInput").ap()
    IN["identC"] = nc.dram_tensor("identC", [128, 128], BF16, kind="ExternalInput").ap()
    IN["x_out"] = nc.dram_tensor("x_out", [T_LOC, D], F32, kind="ExternalOutput").ap()
    IN["h_spill"] = nc.dram_tensor("h_spill", [T_LOC, 4096], BF16).ap()
    ctx = Ctx(nc, es)
    emit_l0f(ctx, IN)
    es.close()
    return nc


def l0f_in_maps(x_full, ins):
    import ml_dtypes
    identC = np.eye(128, dtype=ml_dtypes.bfloat16)
    f32 = lambda a: np.ascontiguousarray(np.asarray(a, dtype=np.float32))
    wiT = f32(ins["L0_ffn_wi_W"].T)
    woT = f32(ins["L0_ffn_wo_W"].T)
    gateWT = f32(ins["L0_ffn_gate_W"].T)
    AcatT = f32(ins["L0_ffn_lora_A"].reshape(32, D).T)
    Bcat = f32(ins["L0_ffn_lora_B"].transpose(0, 2, 1).reshape(32, 4096))
    maps = []
    for c in range(NCORE):
        b, h = c // 2, c % 2
        shard = f32(x_full[b, h * T_LOC:(h + 1) * T_LOC, :])
        maps.append(dict(
            x=shard, wiT=wiT, woT=woT, gateWT=gateWT,
            gate_b=row2d(ins["L0_ffn_gate_b"]), wi_b=row2d(ins["L0_ffn_wi_b"]),
            wo_b=row2d(ins["L0_ffn_wo_b"]), AcatT=AcatT, Bcat=Bcat, identC=identC,
        ))
    return maps




def emit_l1f(ctx, IN, tag="l1f"):
    nc = ctx.nc
    FF = 4096

    wqbuf = ctx.sb("wqbuf", [128, 32768], BF16)
    ident = ctx.sb("identb", [128, 128], BF16)
    epsc = ctx.sb("epscol", [128, 1], F32)
    mwc1 = ctx.sb(f"{tag}_mwc1", [128, 2], F32)
    mwc2 = ctx.sb(f"{tag}_mwc2", [128, 2], F32)
    w0b = ctx.sb(f"{tag}_w0b", [128, FF], BF16)
    w1b = ctx.sb(f"{tag}_w1b", [128, FF], BF16)
    w2b = ctx.sb(f"{tag}_w2b", [128, FF], BF16)
    cbb = ctx.sb(f"{tag}_cbb", [128, FF], BF16)
    zrow = ctx.sb(f"{tag}_zrow", [1, 4096], BF16)
    halomL = ctx.sb(f"{tag}_hL", [128, 1], F32)
    halomR = ctx.sb(f"{tag}_hR", [128, 1], F32)
    meta = ctx.sb("metai", [1, 132], I32)

    xt = ctx.sb("g_xt", [128, 2 * D], F32)
    wstg = ctx.sb("wstg", [128, FF], F32)
    ut = wstg[:, 0:D]
    xq1t = ctx.sb(f"{tag}_xq1t", [128, D], BF16)
    xq1f = ctx.sb(f"{tag}_xq1f", [128, D], BF16)
    ht = ctx.sb(f"{tag}_ht", [128, 2 * FF], BF16)
    h0 = ctx.sb(f"{tag}_h0", [128, FF], BF16)
    hm1 = ctx.sb(f"{tag}_hm1", [128, FF], BF16)
    hp1 = ctx.sb(f"{tag}_hp1", [128, FF], BF16)
    ys = ctx.sb(f"{tag}_ys", [128, FF], BF16)
    xq2f = ctx.sb(f"{tag}_xq2f", [128, FF], BF16)
    ct = ctx.sb("g_ct", [128, 2 * D], F32)
    ot = ctx.sb("g_ot", [128, 2 * D], F32)
    sc = ctx.sb("g_sc", [128, 16], F32)
    st6 = ctx.sb("g_st6", [128, 48], F32)
    st2 = ctx.sb("g_st2", [128, 2], F32)
    r1c = ctx.sb("g_r1c", [128, 4], F32)
    rs1 = ctx.sb(f"{tag}_rs1", [128, 1], F32)
    rs2 = ctx.sb(f"{tag}_rs2", [128, 1], F32)

    pm1 = [ctx.ps(f"ps_pm1_{i}", [128, 512], F32) for i in range(4)]
    pm2 = [ctx.ps(f"ps_pm2_{i}", [128, 512], F32) for i in range(2)]
    ptr = [ctx.ps(f"ps_ptr_{i}", [128, 128], BF16) for i in range(2)]

    x_in, x_out = IN["x"], IN["x_out"]
    h_sh, flags_t = IN["h_sh"], IN["flags"]

    def syncp(e):
        ins = e.dma_start(out=ident[:], in_=IN["identC"])
        ctx.inc(ins, "ld", 16)

    def gpp(e):
        e.memset(epsc[:], LN_EPS)
        e.memset(zrow[:], 0.0)
        ins = e.dma_start(out=IN["h_sh"][0:1, :], in_=zrow[:])
        ctx.inc(ins, "ldg", 16)
        ins = e.dma_start(out=IN["h_sh"][2 * T_LOC + 1:2 * T_LOC + 2, :], in_=zrow[:])
        ctx.inc(ins, "ldg", 16)
        for dst, src in ((w0b, IN["w0r"]), (w1b, IN["w1r"]), (w2b, IN["w2r"]),
                         (cbb, IN["cbr"])):
            ins = e.dma_start(out=dst[:], in_=src)
            ctx.inc(ins, "ldg", 16)
        for dst, src in ((halomL, IN["halomL"]), (halomR, IN["halomR"]),
                         (meta, IN["meta"])):
            ins = e.dma_start(out=dst[:], in_=src)
            ctx.inc(ins, "ldg", 16)
        ctx.wait(e, "ldg")
        ctx.wait(e, "ld")
    phase(nc, f"{tag}_pro", _ctx=ctx, sync=syncp, gpsimd=gpp)

    emit_weight_quant(ctx, IN["w1T"], D, FF, wqbuf, mwc1, tag + "w1", wstg, None)

    BA = {k: ctx.val(k) for k in ("ld16", "va", "sr", "q", "tp", "ev", "m1", "hg", "hs")}

    def syncA(e):
        for t in range(NT):
            if t >= 2:
                e.wait_ge(ctx.h("q"), BA["q"] + t - 1)
            e.dma_start(out=xt[:, (t % 2) * D:((t % 2) + 1) * D],
                        in_=x_in[t * 128:(t + 1) * 128, :]).then_inc(ctx.h("ld16"), 16)

    def gpA(e):
        with e.register("parA") as par:
            e.reg_load(par, meta[0:1, 0:1])
            for branch, off in ((0, 1), (1, 1 + T_LOC)):
                guard = e.If_eq(par, 0) if branch == 0 else e.Else()
                with guard:
                    for t in range(NT):
                        e.wait_ge(ctx.h("hg"), BA["hg"] + 8 * (t + 1))
                        e.dma_start(out=h_sh[off + t * 128:off + (t + 1) * 128, :],
                                    in_=ht[:, (t % 2) * FF:((t % 2) + 1) * FF]
                                    ).then_inc(ctx.h("hs"), 16)
        e.wait_ge(ctx.h("hs"), BA["hs"] + 16 * NT)

    def vectorA(e):
        for t in range(NT):
            TP, EV = BA["tp"] + 8 * t, BA["ev"] + 8 * t
            xs = xt[:, (t % 2) * D:((t % 2) + 1) * D]
            e.wait_ge(ctx.h("ld16"), BA["ld16"] + 16 * (t + 1))
            ln_stats(e, xs, D, st6, st2, sc[:, 0:1], sc[:, 1:2], "va", ctx)
            e.wait_ge(ctx.h("sr"), BA["sr"] + t + 1)
            e.tensor_scalar(ut[:], xs, sc[:, 0:1], r1c[:, 0:1], ALU.subtract, ALU.mult)
            e.reduce_max(sc[:, 4:5], ut[:], axis=AX.X, apply_absolute_value=True)
            e.tensor_scalar_max(sc[:, 4:5], sc[:, 4:5], 1e-5)
            e.reciprocal(sc[:, 5:6], sc[:, 4:5])
            e.tensor_scalar(ut[:], ut[:], sc[:, 5:6], 127.0, ALU.mult, ALU.mult)
            e.tensor_scalar(xq1t[:], ut[:], RMAGIC, -RMAGIC, ALU.add, ALU.add)
            e.tensor_scalar(rs1[:], sc[:, 4:5], 1.0 / 127.0, mwc1[:, 0:1], ALU.mult, ALU.mult)
            ctx.sig(e, "q", 1)
            for j in range(8):
                e.wait_ge(ctx.h("tp"), TP + 1 + j)
                e.tensor_copy(xq1f[:, j * 128:(j + 1) * 128], ptr[j % 2][:])
                ctx.sig(e, "ev", 1)

    def scalarA(e):
        for t in range(NT):
            e.wait_ge(ctx.h("va"), BA["va"] + t + 1)
            rstd_ops(e, sc[:, 1:2], sc[:, 12:13], r1c[:, 0:1], epsc[:])
            ctx.sig(e, "sr", 1)
            hs = ht[:, (t % 2) * FF:((t % 2) + 1) * FF]
            if t >= 2:
                e.wait_ge(ctx.h("hs"), BA["hs"] + 16 * (t - 1))
            for g in range(8):
                e.wait_ge(ctx.h("m1"), BA["m1"] + 8 * t + g + 1)
                e.activation(hs[:, 512 * g:512 * (g + 1)], pm1[(8 * t + g) % 4][:],
                             AF.Identity, scale=rs1[:])
                ctx.sig(e, "hg", 1)

    def tensorA(e):
        for t in range(NT):
            TP, EV = BA["tp"] + 8 * t, BA["ev"] + 8 * t
            e.wait_ge(ctx.h("q"), BA["q"] + t + 1)
            e.wait_ge(ctx.h("ev"), EV)
            for j in range(8):
                if j >= 2:
                    e.wait_ge(ctx.h("ev"), EV + j - 1)
                e.transpose(ptr[j % 2][:], xq1t[:, j * 128:(j + 1) * 128],
                            ident[:]).then_inc(ctx.h("tp"), 1)
            e.wait_ge(ctx.h("ev"), EV + 8)
            for g in range(8):
                G = 8 * t + g
                if G >= 4:
                    e.wait_ge(ctx.h("hg"), BA["hg"] + G - 3)
                for j in range(8):
                    mm = e.matmul(pm1[G % 4][:], xq1f[:, j * 128:(j + 1) * 128],
                                  wqbuf[:, j * FF + g * 512: j * FF + (g + 1) * 512],
                                  start=(j == 0), stop=(j == 7))
                mm.then_inc(ctx.h("m1"), 1)

    phase(nc, f"{tag}_A", _ctx=ctx, sync=syncA, gpsimd=gpA, vector=vectorA,
          scalar=scalarA, tensor=tensorA)
    for k, r in dict(ld16=16, va=1, sr=1, q=1, tp=8, ev=8, m1=8, hg=8, hs=16).items():
        ctx.setval(k, BA[k] + r * NT)

    emit_pair_sync(ctx, tag, IN["flag_k"], flags_t, meta)

    emit_weight_quant(ctx, IN["w2T"], FF, D, wqbuf, mwc2, tag + "w2", wstg, None)

    BB = {k: ctx.val(k) for k in ("ld16", "hb", "va", "sr", "q", "tp", "ev",
                                  "m2", "e2", "gl", "st", "v2g")}

    def syncB(e):
        with e.register("parB") as par:
            e.reg_load(par, meta[0:1, 0:1])
            for branch, off in ((0, 1), (1, 1 + T_LOC)):
                guard = e.If_eq(par, 0) if branch == 0 else e.Else()
                with guard:
                    for t in range(NT):
                        if t >= 1:
                            e.wait_ge(ctx.h("v2g"), BB["v2g"] + t)
                        r0 = off + t * 128
                        e.dma_start(out=hm1[:], in_=h_sh[r0 - 1:r0 + 127, :]).then_inc(ctx.h("hb"), 16)
                        e.dma_start(out=h0[:], in_=h_sh[r0:r0 + 128, :]).then_inc(ctx.h("hb"), 16)
                        e.dma_start(out=hp1[:], in_=h_sh[r0 + 1:r0 + 129, :]).then_inc(ctx.h("hb"), 16)
                        if t >= 2:
                            e.wait_ge(ctx.h("gl"), BB["gl"] + t - 1)
                        e.dma_start(out=xt[:, (t % 2) * D:((t % 2) + 1) * D],
                                    in_=x_in[t * 128:(t + 1) * 128, :]).then_inc(ctx.h("ld16"), 16)

    def gpB(e):
        for t in range(NT):
            e.wait_ge(ctx.h("gl"), BB["gl"] + t + 1)
            e.dma_start(out=x_out[t * 128:(t + 1) * 128, :],
                        in_=ot[:, (t % 2) * D:((t % 2) + 1) * D]).then_inc(ctx.h("st"), 16)
        e.wait_ge(ctx.h("st"), BB["st"] + 16 * NT)

    def vectorB(e):
        for t in range(NT):
            TP, EV = BB["tp"] + 32 * t, BB["ev"] + 32 * t
            e.wait_ge(ctx.h("hb"), BB["hb"] + 48 * (t + 1))
            if t >= 1:
                e.wait_ge(ctx.h("sr"), BB["sr"] + 2 * t - 1)
                e.wait_ge(ctx.h("tp"), BB["tp"] + 32 * t)
            e.tensor_tensor(wstg[:], hm1[:], w0b[:], ALU.mult)
            if t == 0:
                e.tensor_scalar_mul(wstg[:], wstg[:], halomL[:])
            e.tensor_tensor(ys[:], h0[:], w1b[:], ALU.mult)
            e.tensor_add(wstg[:], wstg[:], ys[:])
            e.tensor_tensor(ys[:], hp1[:], w2b[:], ALU.mult)
            if t == NT - 1:
                e.tensor_scalar_mul(ys[:], ys[:], halomR[:])
            e.tensor_add(wstg[:], wstg[:], ys[:])
            e.tensor_add(wstg[:], wstg[:], cbb[:])
            ctx.sig(e, "v2g", 1)
            ctx.sig(e, "va", 1)
            e.wait_ge(ctx.h("sr"), BB["sr"] + 2 * t + 1)
            e.reduce_max(sc[:, 8:9], ys[:], axis=AX.X, apply_absolute_value=True)
            e.tensor_scalar_max(sc[:, 8:9], sc[:, 8:9], 1e-5)
            e.reciprocal(sc[:, 9:10], sc[:, 8:9])
            e.tensor_scalar(wstg[:], ys[:], sc[:, 9:10], 127.0, ALU.mult, ALU.mult)
            e.tensor_scalar(ys[:], wstg[:], RMAGIC, -RMAGIC, ALU.add, ALU.add)
            e.tensor_scalar(rs2[:], sc[:, 8:9], 1.0 / 127.0, mwc2[:, 0:1], ALU.mult, ALU.mult)
            ctx.sig(e, "q", 1)
            for j in range(32):
                e.wait_ge(ctx.h("tp"), TP + 1 + j)
                e.tensor_copy(xq2f[:, j * 128:(j + 1) * 128], ptr[j % 2][:])
                ctx.sig(e, "ev", 1)
            cs = ct[:, (t % 2) * D:((t % 2) + 1) * D]
            xs = xt[:, (t % 2) * D:((t % 2) + 1) * D]
            e.wait_ge(ctx.h("e2"), BB["e2"] + t + 1)
            e.wait_ge(ctx.h("ld16"), BB["ld16"] + 16 * (t + 1))
            ln_stats(e, cs, D, st6, st2, sc[:, 10:11], sc[:, 11:12], "va", ctx)
            e.wait_ge(ctx.h("sr"), BB["sr"] + 2 * t + 2)
            os_ = ot[:, (t % 2) * D:((t % 2) + 1) * D]
            if t >= 2:
                e.wait_ge(ctx.h("st"), BB["st"] + 16 * (t - 1))
            e.tensor_scalar(os_, cs, sc[:, 10:11], r1c[:, 3:4], ALU.subtract, ALU.mult)
            e.tensor_add(os_, os_, xs)
            ctx.sig(e, "gl", 1)

    def scalarB(e):
        for t in range(NT):
            e.wait_ge(ctx.h("va"), BB["va"] + 2 * t + 1)
            if t >= 1:
                e.wait_ge(ctx.h("tp"), BB["tp"] + 32 * t)
            e.activation(ys[:], wstg[:], AF.Silu)
            ctx.sig(e, "sr", 1)
            cs = ct[:, (t % 2) * D:((t % 2) + 1) * D]
            for dg in range(2):
                e.wait_ge(ctx.h("m2"), BB["m2"] + 2 * t + dg + 1)
                e.activation(cs[:, 512 * dg:512 * (dg + 1)], pm2[dg][:],
                             AF.Identity, scale=rs2[:])
                if dg == 1:
                    ctx.sig(e, "e2", 1)
            e.wait_ge(ctx.h("va"), BB["va"] + 2 * t + 2)
            rstd_ops(e, sc[:, 11:12], sc[:, 12:13], r1c[:, 3:4], epsc[:])
            ctx.sig(e, "sr", 1)

    def tensorB(e):
        for t in range(NT):
            TP, EV = BB["tp"] + 32 * t, BB["ev"] + 32 * t
            e.wait_ge(ctx.h("q"), BB["q"] + t + 1)
            e.wait_ge(ctx.h("ev"), EV)
            for j in range(32):
                if j >= 2:
                    e.wait_ge(ctx.h("ev"), EV + j - 1)
                e.transpose(ptr[j % 2][:], ys[:, j * 128:(j + 1) * 128],
                            ident[:]).then_inc(ctx.h("tp"), 1)
            e.wait_ge(ctx.h("ev"), EV + 32)
            if t >= 1:
                e.wait_ge(ctx.h("e2"), BB["e2"] + t)
            for dg in range(2):
                for j in range(32):
                    mm = e.matmul(pm2[dg][:], xq2f[:, j * 128:(j + 1) * 128],
                                  wqbuf[:, j * D + dg * 512: j * D + (dg + 1) * 512],
                                  start=(j == 0), stop=(j == 31))
                mm.then_inc(ctx.h("m2"), 1)

    phase(nc, f"{tag}_B", _ctx=ctx, sync=syncB, gpsimd=gpB, vector=vectorB,
          scalar=scalarB, tensor=tensorB)
    for k, r in dict(ld16=16, hb=48, va=2, sr=2, q=1, tp=32, ev=32, m2=2,
                     e2=1, gl=1, st=16, v2g=1).items():
        ctx.setval(k, BB[k] + r * NT)


def build_l1f_program():
    es = contextlib.ExitStack()
    nc = mk_bass()
    IN = {}
    IN["x"] = nc.dram_tensor("x", [T_LOC, D], F32, kind="ExternalInput").ap()
    IN["w1T"] = nc.dram_tensor("w1T", [D, 4096], F32, kind="ExternalInput").ap()
    IN["w2T"] = nc.dram_tensor("w2T", [4096, D], F32, kind="ExternalInput").ap()
    for nm in ("w0r", "w1r", "w2r", "cbr"):
        IN[nm] = nc.dram_tensor(nm, [128, 4096], F32, kind="ExternalInput").ap()
    for nm in ("halomL", "halomR"):
        IN[nm] = nc.dram_tensor(nm, [128, 1], F32, kind="ExternalInput").ap()
    IN["meta"] = nc.dram_tensor("meta", [1, 132], I32, kind="ExternalInput").ap()
    IN["identC"] = nc.dram_tensor("identC", [128, 128], BF16, kind="ExternalInput").ap()
    IN["x_out"] = nc.dram_tensor("x_out", [T_LOC, D], F32, kind="ExternalOutput").ap()
    IN["h_sh"] = nc.dram_tensor("h_sh", [2 * T_LOC + 2, 4096], BF16, addr_space="Shared").ap()
    IN["flags"] = nc.dram_tensor("flags", [2, 128], I32, addr_space="Shared").ap()
    IN["flag_k"] = 0
    ctx = Ctx(nc, es)
    emit_l1f(ctx, IN)
    es.close()
    return nc


def l1f_in_maps(x_full, ins, nonce=None):
    import ml_dtypes
    identC = np.eye(128, dtype=ml_dtypes.bfloat16)
    f32 = lambda a: np.ascontiguousarray(np.asarray(a, dtype=np.float32))
    w1T = f32(ins["L1_ffn_w1"].T)
    w2T = f32(ins["L1_ffn_w2"].T)
    cw = np.asarray(ins["L1_ffn_conv_w"], dtype=np.float32)[:, 0, :]
    if nonce is None:
        nonce = int(np.random.randint(1, 2 ** 30))
    maps = []
    for c in range(NCORE):
        b, h = c // 2, c % 2
        shard = f32(x_full[b, h * T_LOC:(h + 1) * T_LOC, :])
        hl = np.ones((128, 1), np.float32)
        hr = np.ones((128, 1), np.float32)
        if h == 0:
            hl[0, 0] = 0.0
        if h == 1:
            hr[127, 0] = 0.0
        maps.append(dict(
            x=shard, w1T=w1T, w2T=w2T,
            w0r=rep128(cw[:, 0]), w1r=rep128(cw[:, 1]), w2r=rep128(cw[:, 2]),
            cbr=rep128(ins["L1_ffn_conv_b"]),
            halomL=hl, halomR=hr,
            meta=np.array([[h, nonce] + [nonce + kk + 1 for kk in range(130)]],
                          dtype=np.int32),
            identC=identC,
        ))
    return maps



def emit_l3a(ctx, IN, tag="l3a"):
    nc = ctx.nc
    FI = 2048
    NC2 = FI // 128
    NGRP = 4

    wqbuf = ctx.sb("wqbuf", [128, 32768], BF16)
    wot = ctx.sb(f"{tag}_wot", [128, NC2 * D], BF16)
    ident = ctx.sb("identb", [128, 128], BF16)
    epsc = ctx.sb("epscol", [128, 1], F32)
    onec = ctx.sb("onecolf2", [128, 1], F32)
    meta = ctx.sb("metai", [1, 132], I32)
    acol = ctx.sb(f"{tag}_acol", [128, NC2], F32)
    bcc = ctx.sb(f"{tag}_bcc", [128, NC2], F32)
    bdtc = ctx.sb(f"{tag}_bdtc", [128, NC2], F32)
    cwc = ctx.sb(f"{tag}_cwc", [128, 4 * NC2], F32)
    zcol = ctx.sb(f"{tag}_zcol", [128, 4], BF16)
    pollcell = ctx.sb("pollcell", [1, 2], I32)

    xt = ctx.sb("g_xt", [128, 2 * D], F32)
    xn = ctx.sb(f"{tag}_xn", [128, D], F32)
    xnb = ctx.sb(f"{tag}_xnb", [128, D], BF16)
    wtile = ctx.sb(f"{tag}_wtile", [128, 2 * FI], BF16)
    x1c = wtile[:, 0:1024]
    x1w = ctx.sb(f"{tag}_x1w", [128, 2 * (T_LOC + 3)], BF16)
    yld = ctx.sb(f"{tag}_yld", [128, NC2 * 128], BF16)
    ybuf = ctx.sb(f"{tag}_ybuf", [128, 2 * T_LOC], BF16)
    xnf = ybuf
    dtc = ctx.sb(f"{tag}_dtc", [128, T_LOC], F32)
    btc = ctx.sb(f"{tag}_btc", [128, T_LOC], F32)
    uc = ctx.sb(f"{tag}_uc", [128, T_LOC], F32)
    alc = ctx.sb(f"{tag}_alc", [128, T_LOC], F32)
    ctc = ctx.sb(f"{tag}_ctc", [128, T_LOC], F32)
    hcc = ctx.sb(f"{tag}_hcc", [128, T_LOC], F32)
    szc = ctx.sb(f"{tag}_szc", [128, T_LOC], BF16)
    uedge = ctx.sb(f"{tag}_uedge", [128, 1], F32)
    hinit = ctx.sb(f"{tag}_hinit", [128, 1], F32)
    ct = ctx.sb("g_ct", [128, 2 * D], F32)
    ot = ctx.sb("g_ot", [128, 2 * D], F32)
    sc = ctx.sb("g_sc", [128, 16], F32)
    st6 = ctx.sb("g_st6", [128, 48], F32)
    st2 = ctx.sb("g_st2", [128, 2], F32)
    r1c = ctx.sb("g_r1c", [128, 4], F32)

    pm1 = [ctx.ps(f"ps_pm1_{i}", [128, 512], F32) for i in range(4)]
    pm2 = [ctx.ps(f"ps_pm2_{i}", [128, 512], F32) for i in range(2)]
    ptr = [ctx.ps(f"ps_ptr_{i}", [128, 128], BF16) for i in range(2)]

    x_in, x_out = IN["x"], IN["x_out"]
    x1_sh, ush, hend, flags_t = IN["x1_sh"], IN["u_sh"], IN["hend_sh"], IN["flags"]
    zd = IN["z_d"]
    FB = IN["flag_base"]

    def syncp(e):
        ins = e.dma_start(out=ident[:], in_=IN["identC"])
        ctx.inc(ins, "ld", 16)

    def gpp(e):
        e.memset(epsc[:], LN_EPS)
        e.memset(onec[:], 1.0)
        e.memset(zcol[:], 0.0)
        e.memset(hinit[:], 0.0)
        e.memset(uedge[:], 0.0)
        for dst, src in ((meta, IN["meta"]), (acol, IN["alog2"]), (bcc, IN["bconv2"]),
                         (bdtc, IN["bdt2"]), (cwc, IN["convw2"])):
            ins = e.dma_start(out=dst[:], in_=src)
            ctx.inc(ins, "ldg", 16)
        for c in range(NC2):
            ins = e.dma_start(out=x1_sh[c * 128:(c + 1) * 128, 0:3], in_=zcol[:, 0:3])
            ctx.inc(ins, "ldg", 16)
        for r in range(8):
            ins = e.dma_start(out=wqbuf[:, r * 4096:(r + 1) * 4096],
                              in_=IN["WiT"][r * 128:(r + 1) * 128, :])
            ctx.inc(ins, "ldg", 16)
        for r in range(NC2):
            ins = e.dma_start(out=wot[:, r * D:(r + 1) * D],
                              in_=IN["WoT"][r * 128:(r + 1) * 128, :])
            ctx.inc(ins, "ldg", 16)
        ctx.wait(e, "ldg")
        ctx.wait(e, "ld")
    phase(nc, f"{tag}_pro", _ctx=ctx, sync=syncp, gpsimd=gpp)

    def scp(e):
        e.activation(acol[:], acol[:], AF.Exp)
        ctx.sig(e, "sr0", 1)

    def vcp2(e):
        ctx.wait(e, "sr0")
        e.tensor_scalar_mul(acol[:], acol[:], -1.0)
    phase(nc, f"{tag}_pro2", _ctx=ctx, scalar=scp, vector=vcp2)

    BA = {k: ctx.val(k) for k in ("ld16", "va", "sr", "q", "tp", "ev", "m1", "hg", "hs")}

    def syncA(e):
        for t in range(NT):
            if t >= 2:
                e.wait_ge(ctx.h("q"), BA["q"] + t - 1)
            e.dma_start(out=xt[:, (t % 2) * D:((t % 2) + 1) * D],
                        in_=x_in[t * 128:(t + 1) * 128, :]).then_inc(ctx.h("ld16"), 16)

    def vectorA(e):
        for t in range(NT):
            g, ti = t // 4, t % 4
            TP = BA["tp"] + 8 * t
            xs = xt[:, (t % 2) * D:((t % 2) + 1) * D]
            e.wait_ge(ctx.h("ld16"), BA["ld16"] + 16 * (t + 1))
            ln_stats(e, xs, D, st6, st2, sc[:, 0:1], sc[:, 1:2], "va", ctx)
            e.wait_ge(ctx.h("sr"), BA["sr"] + t + 1)
            e.tensor_scalar(xn[:], xs, sc[:, 0:1], r1c[:, 0:1], ALU.subtract, ALU.mult)
            e.tensor_copy(xnb[:], xn[:])
            ctx.sig(e, "q", 1)
            for j in range(8):
                e.wait_ge(ctx.h("tp"), TP + 1 + j)
                e.tensor_copy(xnf[:, j * 512 + ti * 128:j * 512 + (ti + 1) * 128],
                              ptr[j % 2][:])
                ctx.sig(e, "ev", 1)

    def scalarA(e):
        for g in range(NGRP):
            for ti in range(4):
                t = 4 * g + ti
                e.wait_ge(ctx.h("va"), BA["va"] + t + 1)
                rstd_ops(e, sc[:, 1:2], sc[:, 12:13], r1c[:, 0:1], epsc[:])
                ctx.sig(e, "sr", 1)
            for c in range(32):
                G = 32 * g + c
                e.wait_ge(ctx.h("m1"), BA["m1"] + G + 1)
                if G >= 2:
                    e.wait_ge(ctx.h("hs"), BA["hs"] + 16 * (G - 1))
                e.activation(x1c[:, (G % 2) * 512:(G % 2 + 1) * 512], pm1[G % 4][:],
                             AF.Identity if c < 16 else AF.Silu)
                ctx.sig(e, "hg", 1)

    def gpA(e):
        with e.register("parA3") as par:
            e.reg_load(par, meta[0:1, 0:1])
            for branch in (0, 1):
                off = 3 + branch * T_LOC
                guard = e.If_eq(par, 0) if branch == 0 else e.Else()
                with guard:
                    for g in range(NGRP):
                        for c in range(32):
                            G = 32 * g + c
                            e.wait_ge(ctx.h("hg"), BA["hg"] + G + 1)
                            if c < 16:
                                dst = x1_sh[c * 128:(c + 1) * 128,
                                            off + g * 512:off + (g + 1) * 512]
                            else:
                                dst = zd[(c - 16) * 128:(c - 15) * 128,
                                         g * 512:(g + 1) * 512]
                            e.dma_start(out=dst,
                                        in_=x1c[:, (G % 2) * 512:(G % 2 + 1) * 512]
                                        ).then_inc(ctx.h("hs"), 16)
        e.wait_ge(ctx.h("hs"), BA["hs"] + 16 * 128)

    def tensorA(e):
        for g in range(NGRP):
            for ti in range(4):
                t = 4 * g + ti
                TP, EV = BA["tp"] + 8 * t, BA["ev"] + 8 * t
                e.wait_ge(ctx.h("q"), BA["q"] + t + 1)
                e.wait_ge(ctx.h("ev"), EV)
                for j in range(8):
                    if j >= 2:
                        e.wait_ge(ctx.h("ev"), EV + j - 1)
                    e.transpose(ptr[j % 2][:], xnb[:, j * 128:(j + 1) * 128],
                                ident[:]).then_inc(ctx.h("tp"), 1)
            e.wait_ge(ctx.h("ev"), BA["ev"] + 8 * (4 * g + 4))
            for c in range(32):
                G = 32 * g + c
                if G >= 4:
                    e.wait_ge(ctx.h("hg"), BA["hg"] + G - 3)
                for j in range(8):
                    mm = e.matmul(pm1[G % 4][:],
                                  wqbuf[:, j * 4096 + c * 128:j * 4096 + (c + 1) * 128],
                                  xnf[:, j * 512:(j + 1) * 512],
                                  start=(j == 0), stop=(j == 7))
                mm.then_inc(ctx.h("m1"), 1)

    phase(nc, f"{tag}_A", _ctx=ctx, sync=syncA, gpsimd=gpA, vector=vectorA,
          scalar=scalarA, tensor=tensorA)
    for k, r in dict(ld16=16, va=1, sr=1, q=1, tp=8, ev=8).items():
        ctx.setval(k, BA[k] + r * NT)
    for k, r in dict(m1=128, hg=128, hs=16 * 128).items():
        ctx.setval(k, BA[k] + r)

    emit_pair_sync(ctx, tag, FB + 33, flags_t, meta)

    BC = {k: ctx.val(k) for k in ("lb1", "vcv", "scv")}

    def gpB1(e):
        with e.register("parB31") as par:
            e.reg_load(par, meta[0:1, 0:1])
            for branch in (0, 1):
                off = 3 + branch * T_LOC
                guard = e.If_eq(par, 0) if branch == 0 else e.Else()
                with guard:
                    for c in range(NC2):
                        if c >= 2:
                            e.wait_ge(ctx.h("vcv"), BC["vcv"] + c - 1)
                        e.dma_start(out=x1w[:, (c % 2) * 2051:(c % 2) * 2051 + 2051],
                                    in_=x1_sh[c * 128:(c + 1) * 128, off - 3:off + T_LOC]
                                    ).then_inc(ctx.h("lb1"), 16)

    def vecB1(e):
        for c in range(NC2):
            e.wait_ge(ctx.h("lb1"), BC["lb1"] + 16 * (c + 1))
            if c >= 2:
                e.wait_ge(ctx.h("scv"), BC["scv"] + c - 1)
            w = x1w[:, (c % 2) * 2051:(c % 2) * 2051 + 2051]
            dslot = dtc[:] if c % 2 == 0 else btc[:]
            e.tensor_scalar_mul(dslot, w[:, 0:T_LOC], cwc[:, c:c + 1])
            e.scalar_tensor_tensor(dslot, w[:, 1:T_LOC + 1], cwc[:, NC2 + c:NC2 + c + 1],
                                   dslot, ALU.mult, ALU.add)
            e.scalar_tensor_tensor(dslot, w[:, 2:T_LOC + 2], cwc[:, 2 * NC2 + c:2 * NC2 + c + 1],
                                   dslot, ALU.mult, ALU.add)
            e.scalar_tensor_tensor(dslot, w[:, 3:T_LOC + 3], cwc[:, 3 * NC2 + c:3 * NC2 + c + 1],
                                   dslot, ALU.mult, ALU.add)
            ctx.sig(e, "vcv", 1)

    def scB1(e):
        for c in range(NC2):
            e.wait_ge(ctx.h("vcv"), BC["vcv"] + c + 1)
            dslot = dtc[:] if c % 2 == 0 else btc[:]
            e.activation(wqbuf[:, c * T_LOC:(c + 1) * T_LOC], dslot,
                         AF.Silu, bias=bcc[:, c:c + 1])
            ctx.sig(e, "scv", 1)
    phase(nc, f"{tag}_B1", _ctx=ctx, gpsimd=gpB1, vector=vecB1, scalar=scB1)
    for k, r in dict(lb1=16, vcv=1, scv=1).items():
        ctx.setval(k, BC[k] + r * NC2)

    BD = {k: ctx.val(k) for k in ("wt", "mw", "hgB", "vb2", "sp", "scn", "crr",
                                  "yv", "zl", "yw", "he")}

    def syncB2(e):
        for c2 in range(NC2):
            for wi_ in range(3):
                k3 = 3 * c2 + wi_
                if k3 >= 2:
                    e.wait_ge(ctx.h("mw"), BD["mw"] + 4 * (k3 - 1))
                ins = e.dma_start(
                    out=wtile[:, (k3 % 2) * FI:(k3 % 2 + 1) * FI].rearrange(
                        "p (j n) -> p j n", n=128),
                    in_=IN[("WdtT", "WBT", "WCT")[wi_]][:, c2 * 128:(c2 + 1) * 128]
                    .rearrange("(j p) n -> p j n", p=128))
                ctx.inc(ins, "wt", 16)

    def gpB2(e):
        with e.register("parB32") as par:
            e.reg_load(par, meta[0:1, 0:1])
            b_pol = ctx.val("fpol2")
            b = {k: BD[k] for k in ("crr", "he", "zl", "yw")}
            for branch in (0, 1):
                guard = e.If_eq(par, 0) if branch == 0 else e.Else()
                ctx.setval("fpol2", b_pol)
                for k, v in b.items():
                    ctx.setval(k, v)
                with guard:
                    for c2 in range(NC2):
                        if c2 >= 1:
                            e.wait_ge(ctx.h("yv"), BD["yv"] + c2)
                            e.dma_start(out=zd[(c2 - 1) * 128:c2 * 128, :],
                                        in_=ybuf[:, ((c2 - 1) % 2) * T_LOC:((c2 - 1) % 2 + 1) * T_LOC]
                                        ).then_inc(ctx.h("yw"), 16)
                        e.dma_start(out=szc[:], in_=zd[c2 * 128:(c2 + 1) * 128, :]
                                    ).then_inc(ctx.h("zl"), 16)
                        if branch == 0:
                            e.wait_ge(ctx.h("vb2"), BD["vb2"] + 3 * c2 + 1)
                            ins = e.dma_start(out=ush[c2 * 128:(c2 + 1) * 128, 0:1],
                                              in_=uc[:, T_LOC - 1:T_LOC])
                            ctx.inc(ins, "fsa2", 16)
                            ctx.wait(e, "fsa2")
                            emit_flag_send(ctx, e, FB + 2 * c2,
                                           flags_t[0:1, FB + 2 * c2:FB + 2 * c2 + 1], meta, snd="fsa2")
                            ctx.sig(e, "crr", 2)
                            e.wait_ge(ctx.h("scn"), BD["scn"] + c2 + 1)
                            ins = e.dma_start(out=hend[c2 * 128:(c2 + 1) * 128, 0:1],
                                              in_=hcc[:, T_LOC - 1:T_LOC])
                            ctx.inc(ins, "fsa2", 16)
                            ctx.wait(e, "fsa2")
                            emit_flag_send(ctx, e, FB + 2 * c2 + 1,
                                           flags_t[0:1, FB + 2 * c2 + 1:FB + 2 * c2 + 2], meta, snd="fsa2")
                            ctx.sig(e, "he", 1)
                        else:
                            emit_flag_poll(ctx, e, FB + 2 * c2,
                                           flags_t[0:1, FB + 2 * c2:FB + 2 * c2 + 1], meta, pollcell, pol="fpol2")
                            if c2 >= 1:
                                e.wait_ge(ctx.h("scn"), BD["scn"] + c2)
                            ins = e.dma_start(out=uedge[:], in_=ush[c2 * 128:(c2 + 1) * 128, 0:1])
                            ctx.inc(ins, "fpa2", 16)
                            ctx.wait(e, "fpa2")
                            ctx.sig(e, "crr", 1)
                            emit_flag_poll(ctx, e, FB + 2 * c2 + 1,
                                           flags_t[0:1, FB + 2 * c2 + 1:FB + 2 * c2 + 2], meta, pollcell, pol="fpol2")
                            if c2 >= 1:
                                e.wait_ge(ctx.h("scn"), BD["scn"] + c2)
                            ins = e.dma_start(out=hinit[:], in_=hend[c2 * 128:(c2 + 1) * 128, 0:1])
                            ctx.inc(ins, "fpa2", 16)
                            ctx.wait(e, "fpa2")
                            ctx.sig(e, "crr", 1)
                            ctx.sig(e, "he", 1)
                    e.wait_ge(ctx.h("yv"), BD["yv"] + NC2)
                    e.dma_start(out=zd[(NC2 - 1) * 128:NC2 * 128, :],
                                in_=ybuf[:, ((NC2 - 1) % 2) * T_LOC:((NC2 - 1) % 2 + 1) * T_LOC]
                                ).then_inc(ctx.h("yw"), 16)
        e.wait_ge(ctx.h("yw"), BD["yw"] + 16 * NC2)

    def tensorB2(e):
        for c2 in range(NC2):
            for wi_ in range(3):
                k3 = 3 * c2 + wi_
                e.wait_ge(ctx.h("wt"), BD["wt"] + 16 * (k3 + 1))
                for g in range(4):
                    K3 = 4 * k3 + g
                    if K3 >= 4:
                        e.wait_ge(ctx.h("hgB"), BD["hgB"] + K3 - 3)
                    for j in range(NC2):
                        mm = e.matmul(pm1[K3 % 4][:],
                                      wtile[:, (k3 % 2) * FI + j * 128:(k3 % 2) * FI + (j + 1) * 128],
                                      wqbuf[:, j * T_LOC + g * 512:j * T_LOC + (g + 1) * 512],
                                      start=(j == 0), stop=(j == NC2 - 1))
                    mm.then_inc(ctx.h("mw"), 1)

    def scalarB2(e):
        for c2 in range(NC2):
            for wi_ in range(3):
                k3 = 3 * c2 + wi_
                for g in range(4):
                    K3 = 4 * k3 + g
                    e.wait_ge(ctx.h("mw"), BD["mw"] + K3 + 1)
                    if wi_ == 0:
                        if g == 0 and c2 >= 1:
                            e.wait_ge(ctx.h("scn"), BD["scn"] + c2)
                        e.activation(dtc[:, g * 512:(g + 1) * 512], pm1[K3 % 4][:],
                                     AF.Identity, bias=bdtc[:, c2:c2 + 1])
                    elif wi_ == 1:
                        if g == 0 and c2 >= 1:
                            e.wait_ge(ctx.h("scn"), BD["scn"] + c2)
                        e.activation(btc[:, g * 512:(g + 1) * 512], pm1[K3 % 4][:],
                                     AF.Sigmoid)
                    else:
                        if g == 0 and c2 >= 1:
                            e.wait_ge(ctx.h("yv"), BD["yv"] + c2)
                        e.activation(ctc[:, g * 512:(g + 1) * 512], pm1[K3 % 4][:],
                                     AF.Tanh)
                    ctx.sig(e, "hgB", 1)
            e.wait_ge(ctx.h("vb2"), BD["vb2"] + 3 * c2 + 2)
            e.activation(alc[:], alc[:], AF.Exp, scale=-1.0)
            e.activation(alc[:], alc[:], AF.Ln, bias=onec[:])
            ctx.sig(e, "sp", 1)
            e.wait_ge(ctx.h("vb2"), BD["vb2"] + 3 * c2 + 3)
            e.activation(alc[:], hcc[:], AF.Exp)
            ctx.sig(e, "sp", 1)

    def vecB2(e):
        for c2 in range(NC2):
            xcs = wqbuf[:, c2 * T_LOC:(c2 + 1) * T_LOC]
            e.wait_ge(ctx.h("hgB"), BD["hgB"] + 12 * c2 + 8)
            if c2 >= 1:
                e.wait_ge(ctx.h("crr"), BD["crr"] + 2 * c2 - 1)
            e.tensor_mul(uc[:], btc[:], xcs)
            ctx.sig(e, "vb2", 1)
            e.tensor_scalar_mul(alc[:], dtc[:], -1.0)
            e.tensor_max(alc[:], alc[:], dtc[:])
            ctx.sig(e, "vb2", 1)
            e.wait_ge(ctx.h("sp"), BD["sp"] + 2 * c2 + 1)
            e.tensor_scalar_max(dtc[:], dtc[:], 0.0)
            e.tensor_add(dtc[:], dtc[:], alc[:])
            if c2 >= 1:
                e.wait_ge(ctx.h("he"), BD["he"] + c2)
            e.tensor_scalar(hcc[:], dtc[:], acol[:, c2:c2 + 1], -10.0, ALU.mult, ALU.max)
            e.tensor_scalar_min(hcc[:], hcc[:], -1e-4)
            ctx.sig(e, "vb2", 1)
            e.wait_ge(ctx.h("crr"), BD["crr"] + 2 * c2 + 1)
            e.tensor_add(btc[:, 1:T_LOC], uc[:, 0:T_LOC - 1], uc[:, 1:T_LOC])
            e.tensor_add(btc[:, 0:1], uedge[:], uc[:, 0:1])
            e.scalar_tensor_tensor(btc[:], dtc[:], 0.5, btc[:], ALU.mult, ALU.mult)
            e.wait_ge(ctx.h("sp"), BD["sp"] + 2 * c2 + 2)
            e.wait_ge(ctx.h("crr"), BD["crr"] + 2 * c2 + 2)
            e.tensor_tensor_scan(hcc[:], alc[:], btc[:], hinit[:, 0:1], ALU.mult, ALU.add)
            ctx.sig(e, "scn", 1)
            e.wait_ge(ctx.h("zl"), BD["zl"] + 16 * (c2 + 1))
            if c2 >= 2:
                e.wait_ge(ctx.h("yw"), BD["yw"] + 16 * (c2 - 1))
            e.tensor_mul(ctc[:], ctc[:], hcc[:])
            e.tensor_mul(ctc[:], ctc[:], szc[:])
            e.tensor_copy(ybuf[:, (c2 % 2) * T_LOC:(c2 % 2 + 1) * T_LOC], ctc[:])
            ctx.sig(e, "yv", 1)

    phase(nc, f"{tag}_B2", _ctx=ctx, sync=gpB2, gpsimd=syncB2, tensor=tensorB2,
          scalar=scalarB2, vector=vecB2)
    for k, r in dict(wt=48 * 16, mw=4 * 48, hgB=12 * NC2, vb2=3 * NC2, sp=2 * NC2,
                     scn=NC2, crr=2 * NC2, yv=NC2, zl=16 * NC2, yw=16 * NC2,
                     he=NC2).items():
        ctx.setval(k, BD[k] + r)

    BE = {k: ctx.val(k) for k in ("yl", "ld16", "va", "sr", "m2", "e2", "gl", "st")}

    def syncB3(e):
        for t in range(NT):
            if t >= 1:
                e.wait_ge(ctx.h("m2"), BE["m2"] + 2 * t)
            ins = e.dma_start(out=yld[:].rearrange("p (c n) -> p c n", n=128),
                              in_=zd[:, t * 128:(t + 1) * 128]
                              .rearrange("(c p) n -> p c n", p=128))
            ctx.inc(ins, "yl", 16)
            if t >= 2:
                e.wait_ge(ctx.h("gl"), BE["gl"] + t - 1)
            e.dma_start(out=xt[:, (t % 2) * D:((t % 2) + 1) * D],
                        in_=x_in[t * 128:(t + 1) * 128, :]).then_inc(ctx.h("ld16"), 16)

    def gpB3(e):
        for t in range(NT):
            e.wait_ge(ctx.h("gl"), BE["gl"] + t + 1)
            e.dma_start(out=x_out[t * 128:(t + 1) * 128, :],
                        in_=ot[:, (t % 2) * D:((t % 2) + 1) * D]).then_inc(ctx.h("st"), 16)
        e.wait_ge(ctx.h("st"), BE["st"] + 16 * NT)

    def tensorB3(e):
        for t in range(NT):
            e.wait_ge(ctx.h("yl"), BE["yl"] + 16 * (t + 1))
            if t >= 1:
                e.wait_ge(ctx.h("e2"), BE["e2"] + t)
            for dg in range(2):
                for c in range(NC2):
                    mm = e.matmul(pm2[dg][:], yld[:, c * 128:(c + 1) * 128],
                                  wot[:, c * D + dg * 512:c * D + (dg + 1) * 512],
                                  start=(c == 0), stop=(c == NC2 - 1))
                mm.then_inc(ctx.h("m2"), 1)

    def scalarB3(e):
        for t in range(NT):
            cs = ct[:, (t % 2) * D:((t % 2) + 1) * D]
            for dg in range(2):
                e.wait_ge(ctx.h("m2"), BE["m2"] + 2 * t + dg + 1)
                e.activation(cs[:, 512 * dg:512 * (dg + 1)], pm2[dg][:], AF.Identity)
                if dg == 1:
                    ctx.sig(e, "e2", 1)
            e.wait_ge(ctx.h("va"), BE["va"] + t + 1)
            rstd_ops(e, sc[:, 11:12], sc[:, 12:13], r1c[:, 3:4], epsc[:])
            ctx.sig(e, "sr", 1)

    def vecB3(e):
        for t in range(NT):
            cs = ct[:, (t % 2) * D:((t % 2) + 1) * D]
            xs = xt[:, (t % 2) * D:((t % 2) + 1) * D]
            e.wait_ge(ctx.h("e2"), BE["e2"] + t + 1)
            e.wait_ge(ctx.h("ld16"), BE["ld16"] + 16 * (t + 1))
            ln_stats(e, cs, D, st6, st2, sc[:, 10:11], sc[:, 11:12], "va", ctx)
            e.wait_ge(ctx.h("sr"), BE["sr"] + t + 1)
            os_ = ot[:, (t % 2) * D:((t % 2) + 1) * D]
            if t >= 2:
                e.wait_ge(ctx.h("st"), BE["st"] + 16 * (t - 1))
            e.tensor_scalar(os_, cs, sc[:, 10:11], r1c[:, 3:4], ALU.subtract, ALU.mult)
            e.tensor_add(os_, os_, xs)
            ctx.sig(e, "gl", 1)

    phase(nc, f"{tag}_B3", _ctx=ctx, sync=syncB3, gpsimd=gpB3, tensor=tensorB3,
          scalar=scalarB3, vector=vecB3)
    for k, r in dict(yl=16, ld16=16, va=1, sr=1, m2=2, e2=1, gl=1, st=16).items():
        ctx.setval(k, BE[k] + r * NT)


def build_l3a_program():
    es = contextlib.ExitStack()
    nc = mk_bass()
    IN = {}
    IN["x"] = nc.dram_tensor("x", [T_LOC, D], F32, kind="ExternalInput").ap()
    IN["WiT"] = nc.dram_tensor("WiT", [D, 4096], F32, kind="ExternalInput").ap()
    IN["WdtT"] = nc.dram_tensor("WdtT", [2048, 2048], F32, kind="ExternalInput").ap()
    IN["WBT"] = nc.dram_tensor("WBT", [2048, 2048], F32, kind="ExternalInput").ap()
    IN["WCT"] = nc.dram_tensor("WCT", [2048, 2048], F32, kind="ExternalInput").ap()
    IN["WoT"] = nc.dram_tensor("WoT", [2048, D], F32, kind="ExternalInput").ap()
    for nm in ("alog2", "bconv2", "bdt2"):
        IN[nm] = nc.dram_tensor(nm, [128, 16], F32, kind="ExternalInput").ap()
    IN["convw2"] = nc.dram_tensor("convw2", [128, 64], F32, kind="ExternalInput").ap()
    IN["halo0"] = nc.dram_tensor("halo0", [128, 4], F32, kind="ExternalInput").ap()
    IN["meta"] = nc.dram_tensor("meta", [1, 132], I32, kind="ExternalInput").ap()
    IN["identC"] = nc.dram_tensor("identC", [128, 128], BF16, kind="ExternalInput").ap()
    IN["x_out"] = nc.dram_tensor("x_out", [T_LOC, D], F32, kind="ExternalOutput").ap()
    IN["x1_sh"] = nc.dram_tensor("x1_sh", [2048, 3 + 2 * T_LOC], BF16, addr_space="Shared").ap()
    IN["u_sh"] = nc.dram_tensor("u_sh", [2048, 1], F32, addr_space="Shared").ap()
    IN["hend_sh"] = nc.dram_tensor("hend_sh", [2048, 1], F32, addr_space="Shared").ap()
    IN["flags"] = nc.dram_tensor("flags", [2, 128], I32, addr_space="Shared").ap()
    zk = "ExternalOutput" if os.environ.get("L3A_YOUT") else "Internal"
    IN["z_d"] = nc.dram_tensor("z_d", [2048, T_LOC], BF16, kind=zk).ap()
    IN["flag_base"] = 2
    ctx = Ctx(nc, es)
    emit_l3a(ctx, IN)
    es.close()
    return nc


def l3a_in_maps(x_full, ins, nonce=None):
    import ml_dtypes
    identC = np.eye(128, dtype=ml_dtypes.bfloat16)
    f32 = lambda a: np.ascontiguousarray(np.asarray(a, dtype=np.float32))
    WiT = f32(ins["L3_attn_Wi"].T)
    WdtT = f32(ins["L3_attn_Wdt"].T)
    WBT = f32(ins["L3_attn_WB"].T)
    WCT = f32(ins["L3_attn_WC"].T)
    WoT = f32(ins["L3_attn_Wo"].T)
    chunkcols = lambda v: f32(np.asarray(v).reshape(16, 128).T)
    cw = np.asarray(ins["L3_attn_Wconv"], dtype=np.float32)[:, 0, :]
    convw2 = np.concatenate([chunkcols(cw[:, k]) for k in range(4)], axis=1)
    if nonce is None:
        nonce = int(np.random.randint(1, 2 ** 30))
    maps = []
    for c in range(NCORE):
        b, h = c // 2, c % 2
        shard = f32(x_full[b, h * T_LOC:(h + 1) * T_LOC, :])
        maps.append(dict(
            x=shard, WiT=WiT, WdtT=WdtT, WBT=WBT, WCT=WCT, WoT=WoT,
            alog2=chunkcols(ins["L3_attn_A_log"]),
            bconv2=chunkcols(ins["L3_attn_bconv"]),
            bdt2=chunkcols(ins["L3_attn_bdt"]),
            convw2=np.ascontiguousarray(convw2),
            halo0=np.full((128, 4), float(h), np.float32),
            meta=np.array([[h, nonce] + [nonce + kk + 1 for kk in range(130)]],
                          dtype=np.int32),
            identC=identC,
        ))
    return maps


def emit_l0a(ctx, IN, tag="l0a"):
    nc = ctx.nc
    NG = 4

    wxb = ctx.sb(f"{tag}_wxb", [128, 8 * D], BF16)
    wxpb = ctx.sb(f"{tag}_wxp", [128, 8 * 96], BF16)
    wdtb = ctx.sb(f"{tag}_wdt", [64, D], BF16)
    ident = ctx.sb("identb", [128, 128], BF16)
    epsc = ctx.sb("epscol", [128, 1], F32)
    onec = ctx.sb("onecolf2", [128, 1], F32)
    meta = ctx.sb("metai", [1, 132], I32)
    gb = ctx.sb(f"{tag}_gb", [128, D], BF16)
    bb = ctx.sb(f"{tag}_bb", [128, D], BF16)
    bxc = ctx.sb(f"{tag}_bxc", [128, 8], F32)
    cbc = ctx.sb(f"{tag}_cbc", [128, 8], F32)
    cwc = ctx.sb(f"{tag}_cwc", [128, 32], F32)
    bdtc = ctx.sb(f"{tag}_bdtc", [128, 8], F32)
    dpc = ctx.sb(f"{tag}_dpc", [128, 8], F32)
    aabs = ctx.sb(f"{tag}_aabs", [128, 128], F32)
    rmat = ctx.sb(f"{tag}_rmat", [128, 16 * 128], BF16)
    smat = ctx.sb(f"{tag}_smat", [128, 16 * 128], BF16)
    r2mat = ctx.sb(f"{tag}_r2", [128, 2 * 128], BF16)
    sgnb = ctx.sb(f"{tag}_sgn", [128, T_LOC], BF16)
    xedge = ctx.sb(f"{tag}_xedge", [128, 24], F32)
    pollcell = ctx.sb("pollcell", [1, 2], I32)

    xt = ctx.sb("g_xt", [128, 2 * D], F32)
    xnb = ctx.sb(f"{tag}_xnb", [128, D], BF16)
    xf = ctx.sb(f"{tag}_xf", [128, 8 * T_LOC], BF16)
    xcb = ctx.sb(f"{tag}_xcb", [128, 8 * T_LOC], BF16)
    dfb = ctx.sb(f"{tag}_dfb", [128, 8 * T_LOC], BF16)
    xdbl = ctx.sb(f"{tag}_xdbl", [128, T_LOC], BF16)
    bsch = ctx.sb(f"{tag}_bsch", [128, T_LOC], BF16)
    csch = ctx.sb(f"{tag}_csch", [128, T_LOC], BF16)
    atile = ctx.sb(f"{tag}_at", [128, 2 * T_LOC], BF16)
    ustile = ctx.sb(f"{tag}_us", [128, 2 * T_LOC], BF16)
    htile = ctx.sb(f"{tag}_htl", [128, T_LOC], F32)
    wtile2 = ctx.sb(f"{tag}_wt2", [128, 2 * T_LOC], BF16)
    hcar = ctx.sb(f"{tag}_hcar", [128, 1], F32)
    tmpc = ctx.sb(f"{tag}_tmpc", [128, T_LOC + 3], F32)
    tmpd = ctx.sb(f"{tag}_tmpd", [128, T_LOC], F32)
    sc = ctx.sb("g_sc", [128, 16], F32)
    st6 = ctx.sb("g_st6", [128, 48], F32)
    st2 = ctx.sb("g_st2", [128, 2], F32)
    r1c = ctx.sb("g_r1c", [128, 4], F32)
    xn = tmpc[:, 0:D]
    xnf = atile
    ct = tmpd
    ot = tmpc[:, 0:2 * D]

    pm1 = [ctx.ps(f"ps_pm1_{i}", [128, 512], F32) for i in range(4)]
    pm2 = [ctx.ps(f"ps_pm2_{i}", [128, 512], F32) for i in range(2)]
    ptr = [ctx.ps(f"ps_ptr_{i}", [128, 128], BF16) for i in range(2)]

    x_in, x_out = IN["x"], IN["x_out"]
    xe_sh, hc_sh, flags_t = IN["xe_sh"], IN["hc_sh"], IN["flags"]
    FB = IN["flag_base"]

    def syncp(e):
        ins = e.dma_start(out=ident[:], in_=IN["identC"])
        ctx.inc(ins, "ld", 16)

    def gpp(e):
        e.memset(epsc[:], LN_EPS)
        e.memset(onec[:], 1.0)
        e.memset(xedge[:], 0.0)
        e.memset(hcar[:], 0.0)
        e.memset(xdbl[:], 0.0)
        for dst, src in ((meta, IN["meta"]), (gb, IN["lng_r"]), (bb, IN["lnb_r"]),
                         (bxc, IN["bxc"]), (cbc, IN["cbc"]), (cwc, IN["cwc"]),
                         (bdtc, IN["bdtc"]), (dpc, IN["dpc"]), (aabs, IN["alogr"]),
                         (rmat, IN["rmat"]), (smat, IN["smat"]), (r2mat, IN["r2mat"]),
                         (sgnb, IN["sgn"]), (wdtb, IN["Wdt"])):
            ins = e.dma_start(out=dst[:], in_=src)
            ctx.inc(ins, "ldg", 16)
        for r in range(8):
            ins = e.dma_start(out=wxb[:, r * D:(r + 1) * D],
                              in_=IN["Wx"][r * 128:(r + 1) * 128, :])
            ctx.inc(ins, "ldg", 16)
            ins = e.dma_start(out=wxpb[:, r * 96:(r + 1) * 96],
                              in_=IN["Wxp"][r * 128:(r + 1) * 128, :])
            ctx.inc(ins, "ldg", 16)
        ctx.wait(e, "ldg")
        ctx.wait(e, "ld")
    phase(nc, f"{tag}_pro", _ctx=ctx, sync=syncp, gpsimd=gpp)

    def scp(e):
        e.activation(aabs[:], aabs[:], AF.Exp)
    phase(nc, f"{tag}_pro2", _ctx=ctx, scalar=scp)

    BA = {k: ctx.val(k) for k in ("ld16", "va", "sr", "q", "tp", "ev", "m1", "hg")}

    def syncA(e):
        for t in range(NT):
            if t >= 2:
                e.wait_ge(ctx.h("q"), BA["q"] + t - 1)
            e.dma_start(out=xt[:, (t % 2) * D:((t % 2) + 1) * D],
                        in_=x_in[t * 128:(t + 1) * 128, :]).then_inc(ctx.h("ld16"), 16)

    def vectorA(e):
        for t in range(NT):
            ti = t % 4
            TP = BA["tp"] + 8 * t
            xs = xt[:, (t % 2) * D:((t % 2) + 1) * D]
            e.wait_ge(ctx.h("ld16"), BA["ld16"] + 16 * (t + 1))
            ln_stats(e, xs, D, st6, st2, sc[:, 0:1], sc[:, 1:2], "va", ctx)
            e.wait_ge(ctx.h("sr"), BA["sr"] + 2 * t + 1)
            e.tensor_scalar(xn[:], xs, sc[:, 0:1], r1c[:, 0:1], ALU.subtract, ALU.mult)
            ln_stats(e, xn[:], D, st6, st2, sc[:, 2:3], sc[:, 3:4], "va", ctx)
            e.wait_ge(ctx.h("sr"), BA["sr"] + 2 * t + 2)
            e.tensor_scalar(xn[:], xn[:], sc[:, 2:3], r1c[:, 1:2], ALU.subtract, ALU.mult)
            e.tensor_mul(xn[:], xn[:], gb[:])
            e.tensor_add(xn[:], xn[:], bb[:])
            e.tensor_copy(xnb[:], xn[:])
            ctx.sig(e, "q", 1)
            for j in range(8):
                e.wait_ge(ctx.h("tp"), TP + 1 + j)
                e.tensor_copy(xnf[:, j * 512 + ti * 128:j * 512 + (ti + 1) * 128],
                              ptr[j % 2][:])
                ctx.sig(e, "ev", 1)

    def scalarA(e):
        for g in range(NG):
            for ti in range(4):
                t = 4 * g + ti
                e.wait_ge(ctx.h("va"), BA["va"] + 2 * t + 1)
                rstd_ops(e, sc[:, 1:2], sc[:, 12:13], r1c[:, 0:1], epsc[:])
                ctx.sig(e, "sr", 1)
                e.wait_ge(ctx.h("va"), BA["va"] + 2 * t + 2)
                rstd_ops(e, sc[:, 3:4], sc[:, 12:13], r1c[:, 1:2], epsc[:])
                ctx.sig(e, "sr", 1)
            for c in range(8):
                G = 8 * g + c
                e.wait_ge(ctx.h("m1"), BA["m1"] + G + 1)
                e.activation(xf[:, c * T_LOC + g * 512:c * T_LOC + (g + 1) * 512],
                             pm1[G % 4][:], AF.Identity, bias=bxc[:, c:c + 1])
                ctx.sig(e, "hg", 1)

    def tensorA(e):
        for g in range(NG):
            for ti in range(4):
                t = 4 * g + ti
                EV = BA["ev"] + 8 * t
                e.wait_ge(ctx.h("q"), BA["q"] + t + 1)
                e.wait_ge(ctx.h("ev"), EV)
                for j in range(8):
                    if j >= 2:
                        e.wait_ge(ctx.h("ev"), EV + j - 1)
                    e.transpose(ptr[j % 2][:], xnb[:, j * 128:(j + 1) * 128],
                                ident[:]).then_inc(ctx.h("tp"), 1)
            e.wait_ge(ctx.h("ev"), BA["ev"] + 8 * (4 * g + 4))
            for c in range(8):
                G = 8 * g + c
                if G >= 4:
                    e.wait_ge(ctx.h("hg"), BA["hg"] + G - 3)
                for j in range(8):
                    mm = e.matmul(pm1[G % 4][:],
                                  wxb[:, j * D + c * 128:j * D + (c + 1) * 128],
                                  xnf[:, j * 512:(j + 1) * 512],
                                  start=(j == 0), stop=(j == 7))
                mm.then_inc(ctx.h("m1"), 1)

    phase(nc, f"{tag}_A", _ctx=ctx, sync=syncA, vector=vectorA, scalar=scalarA,
          tensor=tensorA)
    for k, r in dict(ld16=16, va=2, sr=2, q=1, tp=8, ev=8).items():
        ctx.setval(k, BA[k] + r * NT)
    for k, r in dict(m1=32, hg=32).items():
        ctx.setval(k, BA[k] + r)

    def gpe(e):
        with e.register("parE0") as par:
            e.reg_load(par, meta[0:1, 0:1])
            with e.If_eq(par, 0):
                for c in range(8):
                    ins = e.dma_start(out=xe_sh[c * 128:(c + 1) * 128, :],
                                      in_=xf[:, c * T_LOC + T_LOC - 3:c * T_LOC + T_LOC])
                    ctx.inc(ins, "fsa", 16)
                ctx.wait(e, "fsa")
                emit_flag_send(ctx, e, FB, flags_t[0:1, FB:FB + 1], meta, snd="fsa")
            with e.Else():
                emit_flag_poll(ctx, e, FB, flags_t[0:1, FB:FB + 1], meta, pollcell)
                for c in range(8):
                    ins = e.dma_start(out=xedge[:, 3 * c:3 * (c + 1)],
                                      in_=xe_sh[c * 128:(c + 1) * 128, :])
                    ctx.inc(ins, "fpa", 16)
                ctx.wait(e, "fpa")
    phase(nc, f"{tag}_edge", _ctx=ctx, gpsimd=gpe)

    BC = {k: ctx.val(k) for k in ("cv", "sv")}

    def vecA2a(e):
        for c in range(8):
            xs_ = xf[:, c * T_LOC:(c + 1) * T_LOC]
            if c >= 1:
                e.wait_ge(ctx.h("sv"), BC["sv"] + c)
            e.tensor_copy(tmpc[:, 0:3], xedge[:, 3 * c:3 * (c + 1)])
            e.tensor_copy(tmpc[:, 3:T_LOC + 3], xs_)
            e.tensor_scalar_mul(tmpd[:], tmpc[:, 0:T_LOC], cwc[:, c:c + 1])
            e.scalar_tensor_tensor(tmpd[:], tmpc[:, 1:T_LOC + 1], cwc[:, 8 + c:9 + c],
                                   tmpd[:], ALU.mult, ALU.add)
            e.scalar_tensor_tensor(tmpd[:], tmpc[:, 2:T_LOC + 2], cwc[:, 16 + c:17 + c],
                                   tmpd[:], ALU.mult, ALU.add)
            e.scalar_tensor_tensor(tmpd[:], tmpc[:, 3:T_LOC + 3], cwc[:, 24 + c:25 + c],
                                   tmpd[:], ALU.mult, ALU.add)
            ctx.sig(e, "cv", 1)

    def scA2a(e):
        for c in range(8):
            e.wait_ge(ctx.h("cv"), BC["cv"] + c + 1)
            e.activation(xcb[:, c * T_LOC:(c + 1) * T_LOC], tmpd[:],
                         AF.Silu, bias=cbc[:, c:c + 1])
            ctx.sig(e, "sv", 1)
    phase(nc, f"{tag}_A2a", _ctx=ctx, vector=vecA2a, scalar=scA2a)
    for k, r in dict(cv=8, sv=8).items():
        ctx.setval(k, BC[k] + r)

    BDl = {k: ctx.val(k) for k in ("m2", "e2", "m3", "vb", "sp2", "dx")}

    def tensorA2b(e):
        for g in range(NG):
            if g >= 2:
                e.wait_ge(ctx.h("e2"), BDl["e2"] + g - 1)
            for j in range(8):
                mm = e.matmul(pm2[g % 2][0:96, :], wxpb[:, j * 96:(j + 1) * 96],
                              xcb[:, j * T_LOC + g * 512:j * T_LOC + (g + 1) * 512],
                              start=(j == 0), stop=(j == 7))
            mm.then_inc(ctx.h("m2"), 1)
        e.wait_ge(ctx.h("e2"), BDl["e2"] + 4)
        for g in range(NG):
            if g >= 1:
                e.wait_ge(ctx.h("vb"), BDl["vb"] + 2 * g)
            mm = e.matmul(pm2[0][:], r2mat[:, 0:128], xdbl[:, g * 512:(g + 1) * 512],
                          start=True, stop=True)
            mm.then_inc(ctx.h("m3"), 1)
            mm = e.matmul(pm2[1][:], r2mat[:, 128:256], xdbl[:, g * 512:(g + 1) * 512],
                          start=True, stop=True)
            mm.then_inc(ctx.h("m3"), 1)
        for c in range(8):
            if c >= 1:
                e.wait_ge(ctx.h("sp2"), BDl["sp2"] + 5 * c - 1)
            for g in range(NG):
                K = 4 * c + g
                mm = e.matmul(pm1[K % 4][:], wdtb[0:64, c * 128:(c + 1) * 128],
                              xdbl[0:64, g * 512:(g + 1) * 512],
                              start=True, stop=True)
                mm.then_inc(ctx.h("m3"), 1)

    def scA2b(e):
        for g in range(NG):
            e.wait_ge(ctx.h("m2"), BDl["m2"] + g + 1)
            e.activation(xdbl[0:96, g * 512:(g + 1) * 512], pm2[g % 2][0:96, :],
                         AF.Identity)
            ctx.sig(e, "e2", 1)
        for c in range(8):
            if c >= 1:
                e.wait_ge(ctx.h("dx"), BDl["dx"] + c)
            for g in range(NG):
                K = 4 * c + g
                e.wait_ge(ctx.h("m3"), BDl["m3"] + 8 + K + 1)
                e.activation(tmpd[:, g * 512:(g + 1) * 512], pm1[K % 4][:],
                             AF.Identity, bias=bdtc[:, c:c + 1])
            ctx.sig(e, "sp2", 4)
            e.wait_ge(ctx.h("vb"), BDl["vb"] + 8 + c + 1)
            e.activation(tmpc[:, 0:T_LOC], tmpc[:, 0:T_LOC], AF.Exp, scale=-1.0)
            e.activation(tmpc[:, 0:T_LOC], tmpc[:, 0:T_LOC], AF.Ln, bias=onec[:])
            ctx.sig(e, "sp2", 1)

    def vecA2b(e):
        for g in range(NG):
            e.wait_ge(ctx.h("m3"), BDl["m3"] + 2 * g + 2)
            e.tensor_mul(bsch[:, g * 512:(g + 1) * 512], pm2[0][:],
                         sgnb[:, g * 512:(g + 1) * 512])
            ctx.sig(e, "vb", 1)
            e.tensor_mul(csch[:, g * 512:(g + 1) * 512], pm2[1][:],
                         sgnb[:, g * 512:(g + 1) * 512])
            ctx.sig(e, "vb", 1)
        for c in range(8):
            e.wait_ge(ctx.h("sp2"), BDl["sp2"] + 5 * c + 4)
            e.tensor_scalar_mul(tmpc[:, 0:T_LOC], tmpd[:], -1.0)
            e.tensor_max(tmpc[:, 0:T_LOC], tmpc[:, 0:T_LOC], tmpd[:])
            ctx.sig(e, "vb", 1)
            e.wait_ge(ctx.h("sp2"), BDl["sp2"] + 5 * c + 5)
            e.tensor_scalar_max(tmpd[:], tmpd[:], 0.0)
            e.tensor_add(tmpd[:], tmpd[:], tmpc[:, 0:T_LOC])
            e.tensor_copy(dfb[:, c * T_LOC:(c + 1) * T_LOC], tmpd[:])
            e.tensor_mul(xf[:, c * T_LOC:(c + 1) * T_LOC],
                         xcb[:, c * T_LOC:(c + 1) * T_LOC],
                         dfb[:, c * T_LOC:(c + 1) * T_LOC])
            ctx.sig(e, "dx", 1)
    phase(nc, f"{tag}_A2b", _ctx=ctx, tensor=tensorA2b, scalar=scA2b, vector=vecA2b)
    for k, r in dict(m2=4, e2=4, m3=40, vb=16, sp2=40, dx=8).items():
        ctx.setval(k, BDl[k] + r)

    BS = {k: ctx.val(k) for k in ("am", "ae", "um", "ue", "scn2", "wv", "ymt",
                                  "yev", "cr2", "hcd")}

    def ymm(e, tp):
        kp, cp = tp % 16, tp // 16
        e.wait_ge(ctx.h("wv"), BS["wv"] + tp + 1)
        if kp == 0 and cp >= 1:
            e.wait_ge(ctx.h("yev"), BS["yev"] + 4 * cp)
        for g in range(NG):
            mm = e.matmul(pm1[g][:], smat[:, kp * 128:(kp + 1) * 128],
                          wtile2[:, (tp % 2) * T_LOC + g * 512:(tp % 2) * T_LOC + (g + 1) * 512],
                          start=(kp == 0), stop=(kp == 15))
        mm.then_inc(ctx.h("ymt"), 1)

    def tensorS(e):
        for c in range(8):
            for k in range(16):
                tau = 16 * c + k
                if tau >= 1:
                    ymm(e, tau - 1)
                for g in range(NG):
                    if tau > 0 or g > 0:
                        e.wait_ge(ctx.h("ae"), BS["ae"] + 4 * tau + g)
                    mm = e.matmul(pm2[0][:], rmat[:, k * 128:(k + 1) * 128],
                                  dfb[:, c * T_LOC + g * 512:c * T_LOC + (g + 1) * 512],
                                  start=True, stop=True)
                    mm.then_inc(ctx.h("am"), 1)
                for g in range(NG):
                    if tau > 0 or g > 0:
                        e.wait_ge(ctx.h("ue"), BS["ue"] + 4 * tau + g)
                    mm = e.matmul(pm2[1][:], rmat[:, k * 128:(k + 1) * 128],
                                  xf[:, c * T_LOC + g * 512:c * T_LOC + (g + 1) * 512],
                                  start=True, stop=True)
                    mm.then_inc(ctx.h("um"), 1)
        ymm(e, 127)

    def scalarS(e):
        for c in range(8):
            for k in range(16):
                tau = 16 * c + k
                for g in range(NG):
                    e.wait_ge(ctx.h("am"), BS["am"] + 4 * tau + g + 1)
                    if tau >= 2:
                        e.wait_ge(ctx.h("scn2"), BS["scn2"] + tau - 1)
                    e.activation(atile[:, (tau % 2) * T_LOC + g * 512:(tau % 2) * T_LOC + (g + 1) * 512],
                                 pm2[0][:], AF.Identity, scale=aabs[:, tau:tau + 1])
                    ctx.sig(e, "ae", 1)

    def vectorS(e):
        for c in range(8):
            for k in range(16):
                tau = 16 * c + k
                for g in range(NG):
                    e.wait_ge(ctx.h("um"), BS["um"] + 4 * tau + g + 1)
                    if tau >= 2:
                        e.wait_ge(ctx.h("scn2"), BS["scn2"] + tau - 1)
                    e.tensor_mul(ustile[:, (tau % 2) * T_LOC + g * 512:(tau % 2) * T_LOC + (g + 1) * 512],
                                 pm2[1][:], bsch[:, g * 512:(g + 1) * 512])
                    ctx.sig(e, "ue", 1)
                e.wait_ge(ctx.h("ae"), BS["ae"] + 4 * tau + 4)
                e.wait_ge(ctx.h("cr2"), BS["cr2"] + tau + 1)
                if tau >= 1:
                    e.wait_ge(ctx.h("wv"), BS["wv"] + tau)
                    e.wait_ge(ctx.h("hcd"), BS["hcd"] + tau)
                e.tensor_tensor_scan(htile[:],
                                     atile[:, (tau % 2) * T_LOC:(tau % 2 + 1) * T_LOC],
                                     ustile[:, (tau % 2) * T_LOC:(tau % 2 + 1) * T_LOC],
                                     hcar[:, 0:1], ALU.mult, ALU.add)
                ctx.sig(e, "scn2", 1)
            e.wait_ge(ctx.h("ymt"), BS["ymt"] + 16 * (c + 1))
            for g in range(NG):
                e.scalar_tensor_tensor(
                    dfb[:, c * T_LOC + g * 512:c * T_LOC + (g + 1) * 512],
                    xcb[:, c * T_LOC + g * 512:c * T_LOC + (g + 1) * 512],
                    dpc[:, c:c + 1], pm1[g][:], ALU.mult, ALU.add)
                ctx.sig(e, "yev", 1)

    def gpS(e):
        for c in range(8):
            for k in range(16):
                tau = 16 * c + k
                e.wait_ge(ctx.h("scn2"), BS["scn2"] + tau + 1)
                if tau >= 2:
                    e.wait_ge(ctx.h("ymt"), BS["ymt"] + tau - 1)
                e.tensor_mul(wtile2[:, (tau % 2) * T_LOC:(tau % 2 + 1) * T_LOC],
                             htile[:], csch[:])
                ctx.sig(e, "wv", 1)

    def syncS(e):
        with e.register("parS0") as par:
            e.reg_load(par, meta[0:1, 0:1])
            b_cr2, b_hcd = BS["cr2"], BS["hcd"]
            with e.If_eq(par, 0):
                for c in range(8):
                    for k in range(16):
                        tau = 16 * c + k
                        ctx.sig(e, "cr2", 1)
                        e.wait_ge(ctx.h("scn2"), BS["scn2"] + tau + 1)
                        ins = e.dma_start(out=hc_sh[tau * 128:(tau + 1) * 128, 0:1],
                                          in_=htile[:, T_LOC - 1:T_LOC])
                        ctx.inc(ins, "fsa2", 16)
                        ctx.wait(e, "fsa2")
                        ctx.sig(e, "hcd", 1)
                    emit_flag_send(ctx, e, FB + 1 + c,
                                   flags_t[0:1, FB + 1 + c:FB + 2 + c],
                                   meta, snd="fsa2")
            ctx.setval("cr2", b_cr2)
            ctx.setval("hcd", b_hcd)
            with e.Else():
                for c in range(8):
                    emit_flag_poll(ctx, e, FB + 1 + c,
                                   flags_t[0:1, FB + 1 + c:FB + 2 + c],
                                   meta, pollcell, pol="fpol2")
                    for k in range(16):
                        tau = 16 * c + k
                        if tau >= 1:
                            e.wait_ge(ctx.h("scn2"), BS["scn2"] + tau)
                        ins = e.dma_start(out=hcar[:],
                                          in_=hc_sh[tau * 128:(tau + 1) * 128, 0:1])
                        ctx.inc(ins, "fpa2", 16)
                        ctx.wait(e, "fpa2")
                        ctx.sig(e, "cr2", 1)
                        ctx.sig(e, "hcd", 1)
    phase(nc, f"{tag}_S", _ctx=ctx, tensor=tensorS, scalar=scalarS, vector=vectorS,
          gpsimd=gpS, sync=syncS)
    for k, r in dict(am=512, ae=512, um=512, ue=512, scn2=128,
                     wv=128, ymt=128, yev=32, cr2=128, hcd=128).items():
        ctx.setval(k, BS[k] + r)

    BE = {k: ctx.val(k) for k in ("ld16", "va", "sr", "tpg", "evg", "gl", "st")}

    def syncG(e):
        for t in range(NT):
            if t >= 2:
                e.wait_ge(ctx.h("gl"), BE["gl"] + t - 1)
            e.dma_start(out=xt[:, (t % 2) * D:((t % 2) + 1) * D],
                        in_=x_in[t * 128:(t + 1) * 128, :]).then_inc(ctx.h("ld16"), 16)

    def gpG(e):
        for t in range(NT):
            e.wait_ge(ctx.h("gl"), BE["gl"] + t + 1)
            e.dma_start(out=x_out[t * 128:(t + 1) * 128, :],
                        in_=ot[:, (t % 2) * D:((t % 2) + 1) * D]).then_inc(ctx.h("st"), 16)
        e.wait_ge(ctx.h("st"), BE["st"] + 16 * NT)

    def tensorG(e):
        for t in range(NT):
            for c in range(8):
                i = 8 * t + c
                if i >= 2:
                    e.wait_ge(ctx.h("evg"), BE["evg"] + i - 1)
                e.transpose(ptr[i % 2][:],
                            dfb[:, c * T_LOC + t * 128:c * T_LOC + (t + 1) * 128],
                            ident[:]).then_inc(ctx.h("tpg"), 1)

    def vecG(e):
        for t in range(NT):
            cs = ct[:, (t % 2) * D:((t % 2) + 1) * D]
            xs = xt[:, (t % 2) * D:((t % 2) + 1) * D]
            if t >= 2:
                e.wait_ge(ctx.h("gl"), BE["gl"] + t - 1)
            for c in range(8):
                i = 8 * t + c
                e.wait_ge(ctx.h("tpg"), BE["tpg"] + i + 1)
                e.tensor_copy(cs[:, c * 128:(c + 1) * 128], ptr[i % 2][:])
                ctx.sig(e, "evg", 1)
            e.wait_ge(ctx.h("ld16"), BE["ld16"] + 16 * (t + 1))
            ln_stats(e, cs, D, st6, st2, sc[:, 10:11], sc[:, 11:12], "va", ctx)
            e.wait_ge(ctx.h("sr"), BE["sr"] + t + 1)
            os_ = ot[:, (t % 2) * D:((t % 2) + 1) * D]
            if t >= 2:
                e.wait_ge(ctx.h("st"), BE["st"] + 16 * (t - 1))
            e.tensor_scalar(os_, cs, sc[:, 10:11], r1c[:, 3:4], ALU.subtract, ALU.mult)
            e.tensor_add(os_, os_, xs)
            ctx.sig(e, "gl", 1)

    def scG(e):
        for t in range(NT):
            e.wait_ge(ctx.h("va"), BE["va"] + t + 1)
            rstd_ops(e, sc[:, 11:12], sc[:, 12:13], r1c[:, 3:4], epsc[:])
            ctx.sig(e, "sr", 1)
    phase(nc, f"{tag}_G", _ctx=ctx, sync=syncG, gpsimd=gpG, tensor=tensorG,
          vector=vecG, scalar=scG)
    for k, r in dict(ld16=16, va=1, sr=1, tpg=8, evg=8, gl=1, st=16).items():
        ctx.setval(k, BE[k] + r * NT)


def build_l0a_program():
    es = contextlib.ExitStack()
    nc = mk_bass()
    IN = {}
    IN["x"] = nc.dram_tensor("x", [T_LOC, D], F32, kind="ExternalInput").ap()
    IN["Wx"] = nc.dram_tensor("Wx", [D, D], F32, kind="ExternalInput").ap()
    IN["Wxp"] = nc.dram_tensor("Wxp", [D, 96], F32, kind="ExternalInput").ap()
    IN["Wdt"] = nc.dram_tensor("Wdt", [64, D], F32, kind="ExternalInput").ap()
    for nm in ("lng_r", "lnb_r"):
        IN[nm] = nc.dram_tensor(nm, [128, D], F32, kind="ExternalInput").ap()
    for nm in ("bxc", "cbc", "bdtc", "dpc"):
        IN[nm] = nc.dram_tensor(nm, [128, 8], F32, kind="ExternalInput").ap()
    IN["cwc"] = nc.dram_tensor("cwc", [128, 32], F32, kind="ExternalInput").ap()
    IN["alogr"] = nc.dram_tensor("alogr", [128, 128], F32, kind="ExternalInput").ap()
    IN["rmat"] = nc.dram_tensor("rmat", [128, 16 * 128], F32, kind="ExternalInput").ap()
    IN["smat"] = nc.dram_tensor("smat", [128, 16 * 128], F32, kind="ExternalInput").ap()
    IN["r2mat"] = nc.dram_tensor("r2mat", [128, 256], F32, kind="ExternalInput").ap()
    IN["sgn"] = nc.dram_tensor("sgn", [128, T_LOC], F32, kind="ExternalInput").ap()
    IN["meta"] = nc.dram_tensor("meta", [1, 132], I32, kind="ExternalInput").ap()
    IN["identC"] = nc.dram_tensor("identC", [128, 128], BF16, kind="ExternalInput").ap()
    IN["x_out"] = nc.dram_tensor("x_out", [T_LOC, D], F32, kind="ExternalOutput").ap()
    IN["xe_sh"] = nc.dram_tensor("xe_sh", [1024, 3], F32, addr_space="Shared").ap()
    IN["hc_sh"] = nc.dram_tensor("hc_sh", [16384, 1], F32, addr_space="Shared").ap()
    IN["flags"] = nc.dram_tensor("flags", [2, 128], I32, addr_space="Shared").ap()
    IN["flag_base"] = 36
    ctx = Ctx(nc, es)
    emit_l0a(ctx, IN)
    es.close()
    return nc


def l0a_consts():
    import ml_dtypes
    identC = np.eye(128, dtype=ml_dtypes.bfloat16)
    rmat = np.zeros((128, 16 * 128), np.float32)
    smat = np.zeros((128, 16 * 128), np.float32)
    for k in range(16):
        for p in range(128):
            rmat[8 * k + p // 16, k * 128 + p] = 1.0
            smat[p, k * 128 + 8 * k + p // 16] = 1.0
    r2mat = np.zeros((128, 256), np.float32)
    for p in range(128):
        r2mat[64 + p % 16, p] = 1.0
        r2mat[80 + p % 16, 128 + p] = 1.0
    sgn = np.tile(np.array([1.0, -1.0], np.float32), T_LOC // 2)[None, :].repeat(128, 0)
    return identC, rmat, smat, r2mat, np.ascontiguousarray(sgn)


def l0a_in_maps(x_full, ins, nonce=None):
    identC, rmat, smat, r2mat, sgn = l0a_consts()
    f32 = lambda a: np.ascontiguousarray(np.asarray(a, dtype=np.float32))
    chunk8 = lambda v: f32(np.asarray(v, dtype=np.float32).reshape(8, 128).T)
    cw = np.asarray(ins["L0_attn_conv_w"], dtype=np.float32)
    cwc = f32(np.concatenate([chunk8(cw[:, k]) for k in range(4)], axis=1))
    al = np.asarray(ins["L0_attn_A_log"], dtype=np.float32)
    alogr = np.zeros((128, 128), np.float32)
    for tau in range(128):
        for p in range(128):
            alogr[p, tau] = al[8 * tau + p // 16, p % 16]
    if nonce is None:
        nonce = int(np.random.randint(1, 2 ** 30))
    maps = []
    for c in range(NCORE):
        b, h = c // 2, c % 2
        shard = f32(x_full[b, h * T_LOC:(h + 1) * T_LOC, :])
        maps.append(dict(
            x=shard, Wx=f32(ins["L0_attn_Wx"]), Wxp=f32(ins["L0_attn_Wxp"]),
            Wdt=f32(ins["L0_attn_Wdt"]),
            lng_r=rep128(ins["L0_attn_ln_g"]), lnb_r=rep128(ins["L0_attn_ln_b"]),
            bxc=chunk8(ins["L0_attn_bx"]), cbc=chunk8(ins["L0_attn_conv_b"]),
            bdtc=chunk8(ins["L0_attn_bdt"]), dpc=chunk8(ins["L0_attn_Dp"]),
            cwc=cwc, alogr=alogr, rmat=rmat, smat=smat, r2mat=r2mat, sgn=sgn,
            meta=np.array([[h, nonce] + [nonce + kk + 1 for kk in range(130)]],
                          dtype=np.int32),
            identC=identC,
        ))
    return maps


def emit_l2a(ctx, IN, tag="l2a"):
    nc = ctx.nc
    NCH = 32
    H, J = 4, 2
    x_in, x_out = IN["x"], IN["x_out"]
    x2_sh, y2_sh, flags_t = IN["x2_sh"], IN["y2_sh"], IN["flags"]
    FB = IN["flag_base"]
    XB, YB = FB, FB + 16

    ident = ctx.sb("identb", [128, 128], BF16)
    identF = ctx.sb("identFb", [128, 128], F32)
    epsc = ctx.sb("epscol", [128, 1], F32)
    meta = ctx.sb("metai", [1, 132], I32)
    pollcell = ctx.sb("pollcell", [1, 2], I32)
    causal = ctx.sb(f"{tag}_causal", [128, 128], BF16)
    onesb = ctx.sb(f"{tag}_onesb", [128, 128], F32)
    onesbb = ctx.sb(f"{tag}_onesbb", [1, 128], BF16)
    zcol = ctx.sb(f"{tag}_zcol", [128, 1], F32)
    rcol = ctx.sb(f"{tag}_rcol", [128, 1], F32)
    bmgbc = ctx.sb(f"{tag}_bmgbc", [128, 1], F32)
    wqb = ctx.sb(f"{tag}_wqb", [128, 8 * 512], BF16)
    wkb = ctx.sb(f"{tag}_wkb", [128, 8 * 512], BF16)
    wvb = ctx.sb(f"{tag}_wvb", [128, 8 * 512], BF16)
    wbmgb = ctx.sb(f"{tag}_wbmgb", [128, 8 * 36], BF16)
    wob = ctx.sb(f"{tag}_wob", [128, 8 * 1024], BF16)
    boutb = ctx.sb(f"{tag}_boutb", [1, 1024], BF16)
    lngr = ctx.sb(f"{tag}_lngr", [128, D], F32)
    lnbr = ctx.sb(f"{tag}_lnbr", [128, D], F32)

    xpt = ctx.sb(f"{tag}_xpt", [128, 2 * D], BF16)
    xps = ctx.sb(f"{tag}_xps", [128, D], BF16)
    xnr = ctx.sb(f"{tag}_xnr", [128, 2 * D], BF16)
    xnf = ctx.sb(f"{tag}_xnf", [128, 2 * D], BF16)
    e1 = ctx.sb(f"{tag}_e1", [128, 2 * 512], F32)
    qsb = ctx.sb(f"{tag}_qsb", [128, 512], BF16)
    ksb = ctx.sb(f"{tag}_ksb", [128, 512], BF16)
    ktm = ctx.sb(f"{tag}_ktm", [128, 512], BF16)
    vsb = ctx.sb(f"{tag}_vsb", [128, 4 * 129], BF16)
    scbt = ctx.sb(f"{tag}_scb", [128, 128], BF16)
    msb = ctx.sb(f"{tag}_msb", [128, 256], F32)
    dmt = ctx.sb(f"{tag}_dmt", [128, 256], BF16)
    fcol = ctx.sb(f"{tag}_fcol", [128, 2], F32)
    abt = ctx.sb(f"{tag}_abt", [128, 256], BF16)
    vtil = ctx.sb(f"{tag}_vtil", [128, 2 * 129], BF16)
    S_sb = ctx.sb(f"{tag}_S", [128, 8 * 129], F32)
    Scpy = ctx.sb(f"{tag}_Scpy", [128, 8 * 129], BF16)
    ndsb = ctx.sb(f"{tag}_nd", [128, 2 * 129], F32)
    ndt2 = ctx.sb(f"{tag}_ndt2", [128, 129], F32)
    ycol = ctx.sb(f"{tag}_ycol", [128, 4], F32)
    yt128 = ctx.sb(f"{tag}_yt", [128, 128], F32)
    dsc = ctx.sb(f"{tag}_dsc", [128, 128], F32)
    bmg_sb = ctx.sb(f"{tag}_bmg", [128, 128], F32)
    dtm = ctx.sb(f"{tag}_dtm", [128, 40], F32)
    Lrows = ctx.sb(f"{tag}_Lr", [1, 8 * 128], F32)
    negL = ctx.sb(f"{tag}_nLr", [1, 8 * 128], F32)
    pcall = ctx.sb(f"{tag}_pc", [128, 8], F32)
    ybuf = ctx.sb(f"{tag}_ybuf", [128, 32 * 512], BF16)
    yown = ctx.sb(f"{tag}_yown", [128, 16 * 512], BF16)
    ybt = ctx.sb(f"{tag}_ybt", [128, 2 * 512], BF16)
    yfm = ctx.sb(f"{tag}_yfm", [128, 2 * D], BF16)
    xt = ctx.sb("g_xt", [128, 2 * D], F32)
    xng = ctx.sb(f"{tag}_xng", [128, D], F32)
    hb = ctx.sb(f"{tag}_hb", [128, D], F32)
    ot = ctx.sb(f"{tag}_ot", [128, 2 * D], F32)
    sc = ctx.sb("g_sc", [128, 16], F32)
    st6 = ctx.sb("g_st6", [128, 48], F32)
    st2 = ctx.sb("g_st2", [128, 2], F32)
    r1c = ctx.sb("g_r1c", [128, 4], F32)

    pm1 = [ctx.ps(f"ps_pm1_{i}", [128, 512], F32) for i in range(4)]
    pm2 = [ctx.ps(f"ps_pm2_{i}", [128, 512], F32) for i in range(2)]
    ptr = [ctx.ps(f"ps_ptr_{i}", [128, 128], BF16) for i in range(2)]

    def syncp(e):
        for dst, src in ((ident, IN["identC"]), (identF, IN["identF"]),
                         (meta, IN["meta"]), (causal, IN["causalC"]),
                         (rcol, IN["rcol"]), (bmgbc, IN["bmgbc"]),
                         (lngr, IN["lngr"]), (lnbr, IN["lnbr"])):
            ins = e.dma_start(out=dst[:], in_=src)
            ctx.inc(ins, "ld", 16)
        ctx.wait(e, "ld")

    def gpp(e):
        e.memset(epsc[:], LN_EPS)
        e.memset(onesb[:], 1.0)
        e.memset(onesbb[:], 1.0)
        e.memset(zcol[:], 0.0)
        e.memset(S_sb[:], 0.0)
        e.memset(bmg_sb[:], 0.0)
        e.memset(vsb[:], 1.0)
        ins = e.dma_start(out=boutb[:], in_=IN["boutr"])
        ctx.inc(ins, "ldg", 16)
        for dst, src in ((wqb, IN["wq"]), (wkb, IN["wk"]), (wvb, IN["wv"]),
                         (wbmgb, IN["wbmg"]), (wob, IN["wo"])):
            w = dst.shape[1] // 8
            for r in range(8):
                ins = e.dma_start(out=dst[:, r * w:(r + 1) * w],
                                  in_=src[r * 128:(r + 1) * 128, :])
                ctx.inc(ins, "ldg", 16)
        ctx.wait(e, "ldg")
    phase(nc, f"{tag}_pro", _ctx=ctx, sync=syncp, gpsimd=gpp)

    B = {k: ctx.val(k) for k in (
        "zxl", "zxn", "zva", "zsr", "ztp", "zev", "zq", "zk", "zv", "zb",
        "zbe", "zqe", "zke", "zve", "zem", "zbet", "zLw", "zlng", "ztb",
        "zdte", "zkte", "zpcm", "zpce", "zlr", "znl", "zsc", "zscv", "zM",
        "zmn", "zdm", "zab", "zvt", "zscp", "zym", "znd", "zinc", "zsup",
        "zyw", "pub", "yow", "ybl", "zxt", "ztpy", "zevy", "zmm", "zoe", "znd2",
        "zhr", "zgl", "zst", "zva2", "zsr2")}

    def syncM(e):
        with e.register("parZ0") as par:
            e.reg_load(par, meta[0:1, 0:1])
            b_snd = ctx.val("fsnd2")
            saved = {}
            for branch in (0, 1):
                cm = e.If_eq(par, 0) if branch == 0 else e.Else()
                with cm:
                    myp, pep = branch, 1 - branch
                    for c in range(NCH):
                        if c // 16 == myp:
                            e.wait_ge(ctx.h("pub"), B["pub"] + 32 * ((c % 16) + 1))
                        else:
                            emit_flag_poll(ctx, e, XB + (c % 16),
                                           flags_t[pep:pep + 1, XB + (c % 16):XB + (c % 16) + 1],
                                           meta, pollcell, pol="fpol2")
                        if c >= 2:
                            e.wait_ge(ctx.h("zxn"), B["zxn"] + c - 1)
                        ins = e.dma_start(out=xpt[:, (c % 2) * D:((c % 2) + 1) * D],
                                          in_=x2_sh[c * 128:(c + 1) * 128, :])
                        ctx.inc(ins, "zxl", 16)
                        e.wait_ge(ctx.h("zLw"), B["zLw"] + c + 1)
                        if c >= 1:
                            e.wait_ge(ctx.h("zsup"), B["zsup"] + 8 * c)
                        ins = e.dma_start(out=Lrows[0:1, :], in_=bmg_sb[64:72, :])
                        ctx.inc(ins, "zlr", 16)
                        if c // 16 == pep:
                            e.wait_ge(ctx.h("zyw"), B["zyw"] + c + 1)
                            ins = e.dma_start(
                                out=y2_sh[myp * 2048 + (c % 16) * 128:myp * 2048 + (c % 16) * 128 + 128, :],
                                in_=ybuf[:, c * 512:(c + 1) * 512])
                            ctx.inc(ins, "fsnd2", 16)
                            ctx.wait(e, "fsnd2")
                            emit_flag_send(ctx, e, YB + (c % 16),
                                           flags_t[myp:myp + 1, YB + (c % 16):YB + (c % 16) + 1],
                                           meta, snd="fsnd2")
                    for c2 in range(16):
                        gc = myp * 16 + c2
                        e.wait_ge(ctx.h("zyw"), B["zyw"] + gc + 1)
                        ins = e.dma_start(out=yown[:, c2 * 512:(c2 + 1) * 512],
                                          in_=ybuf[:, gc * 512:(gc + 1) * 512])
                        ctx.inc(ins, "yow", 16)
                        emit_flag_poll(ctx, e, YB + c2,
                                       flags_t[pep:pep + 1, YB + c2:YB + c2 + 1],
                                       meta, pollcell, pol="fpol2")
                        if c2 >= 2:
                            e.wait_ge(ctx.h("zevy"), B["zevy"] + 8 * (c2 - 1))
                        ins = e.dma_start(out=ybt[:, (c2 % 2) * 512:((c2 % 2) + 1) * 512],
                                          in_=y2_sh[pep * 2048 + c2 * 128:pep * 2048 + c2 * 128 + 128, :])
                        ctx.inc(ins, "ybl", 16)
                        if c2 >= 2:
                            e.wait_ge(ctx.h("zhr"), B["zhr"] + c2 - 1)
                        ins = e.dma_start(out=xt[:, (c2 % 2) * D:((c2 % 2) + 1) * D],
                                          in_=x_in[c2 * 128:(c2 + 1) * 128, :])
                        ctx.inc(ins, "zxt", 16)
                    if branch == 0:
                        for nm in ("fsnd2", "zxl", "zlr", "yow", "ybl", "zxt"):
                            saved[nm] = ctx.val(nm)
                        ctx.setval("fsnd2", b_snd)
                        for nm in ("zxl", "zlr", "yow", "ybl", "zxt"):
                            ctx.setval(nm, B[nm])
            for nm, v in saved.items():
                ctx.setval(nm, v)

    def gpM(e):
        with e.register("parZ1") as par:
            e.reg_load(par, meta[0:1, 0:1])
            b_snd = ctx.val("fsnd")
            saved = {}
            for branch in (0, 1):
                cm = e.If_eq(par, 0) if branch == 0 else e.Else()
                with cm:
                    myp = branch
                    for cl in range(16):
                        ins = e.dma_start(out=xps[:], in_=x_in[cl * 128:(cl + 1) * 128, :])
                        ctx.inc(ins, "pub", 16)
                        ctx.wait(e, "pub")
                        ins = e.dma_start(
                            out=x2_sh[myp * 2048 + cl * 128:myp * 2048 + cl * 128 + 128, :],
                            in_=xps[:])
                        ctx.inc(ins, "pub", 16)
                        ctx.wait(e, "pub")
                        emit_flag_send(ctx, e, XB + cl,
                                       flags_t[myp:myp + 1, XB + cl:XB + cl + 1], meta)
                    if branch == 0:
                        saved["pub"] = ctx.val("pub")
                        saved["fsnd"] = ctx.val("fsnd")
                        ctx.setval("pub", B["pub"])
                        ctx.setval("fsnd", b_snd)
            for nm, v in saved.items():
                ctx.setval(nm, v)
        for c in range(NCH):
            e.wait_ge(ctx.h("zqe"), B["zqe"] + c + 1)
            e.tensor_scalar_min(e1[:, 0:512], e1[:, 0:512], 1.0)
            ctx.sig(e, "zem", 1)
            e.wait_ge(ctx.h("zke"), B["zke"] + c + 1)
            e.tensor_scalar_min(e1[:, 512:1024], e1[:, 512:1024], 1.0)
            ctx.sig(e, "zem", 1)
            e.wait_ge(ctx.h("zbet"), B["zbet"] + 4 * c + 1)
            e.tensor_scalar_mul(dsc[0:8, :], dsc[0:8, :], rcol[0:8, 0:1])
            e.tensor_scalar_min(dsc[0:8, :], dsc[0:8, :], 0.9999)
            ctx.sig(e, "zbet", 1)
            e.wait_ge(ctx.h("zbet"), B["zbet"] + 4 * c + 3)
            e.tensor_scalar_max(dsc[0:8, :], dsc[0:8, :], -30.0)
            ctx.sig(e, "zbet", 1)
            e.wait_ge(ctx.h("zlng"), B["zlng"] + 2 * c + 1)
            e.tensor_scalar_max(bmg_sb[32:36, :], bmg_sb[32:36, :], -30.0)
            ctx.sig(e, "zlng", 1)
            e.wait_ge(ctx.h("zsup"), B["zsup"] + 8 * c)
            for ch in range(8):
                e.tensor_copy(Scpy[:, ch * 129:(ch + 1) * 129],
                              S_sb[:, ch * 129:(ch + 1) * 129])
            ctx.sig(e, "zscp", 1)
            for h in range(H):
                hh = 4 * c + h
                for j in range(J):
                    e.wait_ge(ctx.h("zdm"), B["zdm"] + hh + 1)
                    e.wait_ge(ctx.h("zscv"), B["zscv"] + hh + 1)
                    if hh >= 1 and j == 0:
                        e.wait_ge(ctx.h("zym"), B["zym"] + 2 * hh)
                        e.wait_ge(ctx.h("zinc"), B["zinc"] + 2 * hh)
                    e.tensor_mul(abt[:, j * 128:(j + 1) * 128], scbt[:],
                                 dmt[:, j * 128:(j + 1) * 128])
                    ctx.sig(e, "zab", 1)
                    e.tensor_scalar_mul(vtil[:, j * 129:(j + 1) * 129],
                                        vsb[:, h * 129:(h + 1) * 129],
                                        fcol[:, j:j + 1])
                    ctx.sig(e, "zvt", 1)

    def vecM(e):
        for c in range(NCH):
            xs = xpt[:, (c % 2) * D:((c % 2) + 1) * D]
            e.wait_ge(ctx.h("zxl"), B["zxl"] + 16 * (c + 1))
            ln_stats(e, xs, D, st6, st2, sc[:, 0:1], sc[:, 1:2], "zva", ctx)
            e.wait_ge(ctx.h("zsr"), B["zsr"] + c + 1)
            if c >= 2:
                e.wait_ge(ctx.h("ztp"), B["ztp"] + 12 * (c - 2) + 8)
            e.tensor_scalar(xnr[:, (c % 2) * D:((c % 2) + 1) * D], xs,
                            sc[:, 0:1], r1c[:, 0:1], ALU.subtract, ALU.mult)
            ctx.sig(e, "zxn", 1)
            e.wait_ge(ctx.h("zbet"), B["zbet"] + 4 * c + 4)
            if c >= 1:
                e.wait_ge(ctx.h("zlr"), B["zlr"] + 16 * c)
                e.wait_ge(ctx.h("ztb"), B["ztb"] + c)
            e.tensor_tensor_scan(bmg_sb[64:72, :], onesb[0:8, :], dsc[0:8, :],
                                 zcol[0:8, 0:1], ALU.mult, ALU.add)
            ctx.sig(e, "zLw", 1)
            e.wait_ge(ctx.h("zlr"), B["zlr"] + 16 * (c + 1))
            if c >= 1:
                e.wait_ge(ctx.h("zsup"), B["zsup"] + 8 * c)
            e.tensor_scalar_mul(negL[:], Lrows[:], -1.0)
            ctx.sig(e, "znl", 1)
            e.wait_ge(ctx.h("zem"), B["zem"] + 2 * c + 1)
            if c >= 1:
                e.wait_ge(ctx.h("zym"), B["zym"] + 8 * c)
            e.scalar_tensor_tensor(qsb[:], pm1[0][:], 0.0, e1[:, 0:512],
                                   ALU.max, ALU.add)
            ctx.sig(e, "zq", 1)
            e.wait_ge(ctx.h("zem"), B["zem"] + 2 * c + 2)
            e.scalar_tensor_tensor(ksb[:], pm1[1][:], 0.0, e1[:, 512:1024],
                                   ALU.max, ALU.add)
            ctx.sig(e, "zk", 1)
            e.wait_ge(ctx.h("zdte"), B["zdte"] + 2 * c + 1)
            e.tensor_tensor(ycol[:], dtm[:, 0:8:2], dtm[:, 1:8:2], ALU.subtract)
            ctx.sig(e, "zpcm", 1)
            for h in range(H):
                hh = 4 * c + h
                e.wait_ge(ctx.h("zsc"), B["zsc"] + hh + 1)
                if hh >= 1:
                    e.wait_ge(ctx.h("zab"), B["zab"] + 2 * hh)
                e.tensor_mul(scbt[:], pm2[0][:, 0:128], causal[:])
                ctx.sig(e, "zscv", 1)
                e.wait_ge(ctx.h("zM"), B["zM"] + 4 * hh + 4)
                e.wait_ge(ctx.h("zdm"), B["zdm"] + hh)
                e.tensor_scalar_min(msb[:], pm2[1][:, 0:256], 0.0)
                ctx.sig(e, "zmn", 1)
                for j in range(J):
                    e.wait_ge(ctx.h("znd2"), B["znd2"] + 2 * hh + j + 1)
                    e.tensor_add(ndsb[:, j * 129:(j + 1) * 129],
                                 pm2[0][:, 128:257], ndt2[:])
                    ctx.sig(e, "znd", 1)
                for j in range(J):
                    ch = 2 * h + j
                    e.wait_ge(ctx.h("zinc"), B["zinc"] + 2 * hh + j + 1)
                    e.scalar_tensor_tensor(
                        S_sb[:, ch * 129:(ch + 1) * 129],
                        S_sb[:, ch * 129:(ch + 1) * 129],
                        pcall[:, ch:ch + 1],
                        pm2[1][:, 256:385] if j == 0 else pm1[3][:, 0:129],
                        ALU.mult, ALU.add)
                    ctx.sig(e, "zsup", 1)
                if h == 0:
                    e.wait_ge(ctx.h("zpcm"), B["zpcm"] + 2 * c + 2)
                e.tensor_scalar_add(ycol[:, 0:1], ndsb[:, 128:129], 1e-6)
                e.reciprocal(ycol[:, 0:1], ycol[:, 0:1])
                e.tensor_mul(ycol[:, 0:1], ycol[:, 0:1], dtm[:, 28 + h:29 + h])
                e.tensor_scalar_add(ycol[:, 1:2], ndsb[:, 257:258], 1e-6)
                e.reciprocal(ycol[:, 1:2], ycol[:, 1:2])
                e.tensor_mul(ycol[:, 1:2], ycol[:, 1:2], dtm[:, 32 + h:33 + h])
                e.tensor_scalar_mul(yt128[:], ndsb[:, 0:128], ycol[:, 0:1])
                e.scalar_tensor_tensor(ybuf[:, c * 512 + h * 128:c * 512 + (h + 1) * 128],
                                       ndsb[:, 129:257], ycol[:, 1:2], yt128[:],
                                       ALU.mult, ALU.add)
            ctx.sig(e, "zyw", 1)
        for c2 in range(16):
            xs = xt[:, (c2 % 2) * D:((c2 % 2) + 1) * D]
            e.wait_ge(ctx.h("zxt"), B["zxt"] + 16 * (c2 + 1))
            ln_stats(e, xs, D, st6, st2, sc[:, 2:3], sc[:, 3:4], "zva2", ctx)
            e.wait_ge(ctx.h("zsr2"), B["zsr2"] + 3 * c2 + 1)
            e.tensor_scalar(xng[:], xs, sc[:, 2:3], r1c[:, 1:2],
                            ALU.subtract, ALU.mult)
            e.wait_ge(ctx.h("zoe"), B["zoe"] + 2 * (c2 + 1))
            e.tensor_add(hb[:], xng[:], ot[:, (c2 % 2) * D:((c2 % 2) + 1) * D])
            ln_stats(e, hb[:], D, st6, st2, sc[:, 4:5], sc[:, 5:6], "zva2", ctx)
            e.wait_ge(ctx.h("zsr2"), B["zsr2"] + 3 * c2 + 2)
            e.tensor_scalar(hb[:], hb[:], sc[:, 4:5], r1c[:, 2:3],
                            ALU.subtract, ALU.mult)
            e.tensor_mul(hb[:], hb[:], lngr[:])
            e.tensor_add(hb[:], hb[:], lnbr[:])
            ln_stats(e, hb[:], D, st6, st2, sc[:, 6:7], sc[:, 7:8], "zva2", ctx)
            e.wait_ge(ctx.h("zsr2"), B["zsr2"] + 3 * c2 + 3)
            os_ = ot[:, (c2 % 2) * D:((c2 % 2) + 1) * D]
            e.tensor_scalar(os_, hb[:], sc[:, 6:7], r1c[:, 3:4],
                            ALU.subtract, ALU.mult)
            e.tensor_add(os_, os_, xs)
            ctx.sig(e, "zhr", 1)
            ctx.sig(e, "zgl", 1)

    def scalM(e):
        for c in range(NCH):
            e.wait_ge(ctx.h("zva"), B["zva"] + c + 1)
            rstd_ops(e, sc[:, 1:2], sc[:, 12:13], r1c[:, 0:1], epsc[:])
            ctx.sig(e, "zsr", 1)
            TP = B["ztp"] + 12 * c
            for i in range(8):
                e.wait_ge(ctx.h("ztp"), TP + i + 1)
                if c >= 2:
                    e.wait_ge(ctx.h("zb"), B["zb"] + c - 1)
                e.activation(xnf[:, (c % 2) * D + i * 128:(c % 2) * D + (i + 1) * 128],
                             ptr[i % 2][:], AF.Identity)
                ctx.sig(e, "zev", 1)
            e.wait_ge(ctx.h("zq"), B["zq"] + 2 * c + 1)
            e.activation(e1[:, 0:512], pm1[0][:], AF.Exp)
            ctx.sig(e, "zqe", 1)
            e.wait_ge(ctx.h("zk"), B["zk"] + 2 * c + 1)
            e.activation(e1[:, 512:1024], pm1[1][:], AF.Exp)
            ctx.sig(e, "zke", 1)
            e.wait_ge(ctx.h("zv"), B["zv"] + c + 1)
            if c >= 1:
                e.wait_ge(ctx.h("zym"), B["zym"] + 8 * c)
                e.wait_ge(ctx.h("zvt"), B["zvt"] + 8 * c)
            for h in range(H):
                e.activation(vsb[:, h * 129:h * 129 + 128],
                             pm1[2][:, h * 128:(h + 1) * 128], AF.Identity)
            ctx.sig(e, "zve", 1)
            e.wait_ge(ctx.h("zb"), B["zb"] + c + 1)
            if c >= 1:
                e.wait_ge(ctx.h("ztb"), B["ztb"] + c)
            e.activation(bmg_sb[0:36, :], pm1[3][0:36, 0:128], AF.Identity,
                         bias=bmgbc[0:36, 0:1])
            ctx.sig(e, "zbe", 1)
            e.activation(dsc[0:8, :], bmg_sb[0:8, :], AF.Sigmoid)
            ctx.sig(e, "zbet", 1)
            e.wait_ge(ctx.h("zbet"), B["zbet"] + 4 * c + 2)
            e.activation(dsc[0:8, :], dsc[0:8, :], AF.Ln)
            ctx.sig(e, "zbet", 1)
            e.activation(bmg_sb[32:36, :], bmg_sb[32:36, :], AF.Sigmoid)
            e.activation(bmg_sb[32:36, :], bmg_sb[32:36, :], AF.Ln)
            ctx.sig(e, "zlng", 1)
            for i in range(4):
                e.wait_ge(ctx.h("ztp"), TP + 8 + i + 1)
                if c >= 1:
                    e.wait_ge(ctx.h("zinc"), B["zinc"] + 8 * c)
                e.activation(ktm[:, i * 128:(i + 1) * 128], ptr[i % 2][:], AF.Identity)
                ctx.sig(e, "zkte", 1)
            e.wait_ge(ctx.h("ztb"), B["ztb"] + c + 1)
            e.activation(dtm[:, 0:8], pm2[1][:, 8:16], AF.Identity)
            e.activation(dtm[:, 8:12], pm2[1][:, 32:36], AF.Identity)
            e.activation(dtm[:, 12:20], pm2[1][:, 64:72], AF.Identity)
            ctx.sig(e, "zdte", 1)
            e.activation(dtm[:, 20:28], dtm[:, 12:20], AF.Exp)
            ctx.sig(e, "zdte", 1)
            e.wait_ge(ctx.h("zpcm"), B["zpcm"] + 2 * c + 1)
            e.activation(dtm[:, 28:32], ycol[:], AF.Sigmoid)
            e.activation(dtm[:, 32:36], ycol[:], AF.Sigmoid, scale=-1.0)
            ctx.sig(e, "zpcm", 1)
            e.wait_ge(ctx.h("zpce"), B["zpce"] + 2 * c + 1)
            e.activation(pcall[:], pm2[1][:, 128:136], AF.Exp)
            ctx.sig(e, "zpce", 1)
            for h in range(H):
                hh = 4 * c + h
                e.wait_ge(ctx.h("zmn"), B["zmn"] + hh + 1)
                e.wait_ge(ctx.h("zvt"), B["zvt"] + 2 * hh)
                e.activation(dmt[:], msb[:], AF.Exp, bias=dtm[:, 8 + h:9 + h])
                e.activation(fcol[:, 0:1], msb[:, 127:128], AF.Exp,
                             bias=dtm[:, 8 + h:9 + h])
                e.activation(fcol[:, 1:2], msb[:, 255:256], AF.Exp,
                             bias=dtm[:, 8 + h:9 + h])
                ctx.sig(e, "zdm", 1)
                for j in range(J):
                    ch = 2 * h + j
                    e.wait_ge(ctx.h("zym"), B["zym"] + 2 * hh + j + 1)
                    e.activation(ndt2[:], pm2[0][:, 257:386],
                                 AF.Identity, scale=dtm[:, 20 + ch:21 + ch])
                    ctx.sig(e, "znd2", 1)
        for c2 in range(16):
            e.wait_ge(ctx.h("zva2"), B["zva2"] + 3 * c2 + 1)
            rstd_ops(e, sc[:, 3:4], sc[:, 12:13], r1c[:, 1:2], epsc[:])
            ctx.sig(e, "zsr2", 1)
            TPY = B["ztpy"] + 8 * c2
            for i in range(8):
                e.wait_ge(ctx.h("ztpy"), TPY + i + 1)
                e.activation(yfm[:, (c2 % 2) * D + i * 128:(c2 % 2) * D + (i + 1) * 128],
                             ptr[i % 2][:], AF.Identity)
                ctx.sig(e, "zevy", 1)
            for s in range(2):
                e.wait_ge(ctx.h("zmm"), B["zmm"] + 2 * c2 + s + 1)
                if c2 >= 2:
                    e.wait_ge(ctx.h("zst"), B["zst"] + 16 * (c2 - 1))
                e.activation(ot[:, (c2 % 2) * D + s * 512:(c2 % 2) * D + (s + 1) * 512],
                             pm1[s][:], AF.Identity)
                ctx.sig(e, "zoe", 1)
            e.wait_ge(ctx.h("zva2"), B["zva2"] + 3 * c2 + 2)
            rstd_ops(e, sc[:, 5:6], sc[:, 12:13], r1c[:, 2:3], epsc[:])
            ctx.sig(e, "zsr2", 1)
            e.wait_ge(ctx.h("zva2"), B["zva2"] + 3 * c2 + 3)
            rstd_ops(e, sc[:, 7:8], sc[:, 12:13], r1c[:, 3:4], epsc[:])
            ctx.sig(e, "zsr2", 1)

    def tensM(e):
        for c in range(NCH):
            TP = B["ztp"] + 12 * c
            EV = B["zev"] + 8 * c
            e.wait_ge(ctx.h("zxn"), B["zxn"] + c + 1)
            e.wait_ge(ctx.h("zev"), EV)
            for i in range(8):
                if i >= 2:
                    e.wait_ge(ctx.h("zev"), EV + i - 1)
                e.transpose(ptr[i % 2][:],
                            xnr[:, (c % 2) * D + i * 128:(c % 2) * D + (i + 1) * 128],
                            ident[:]).then_inc(ctx.h("ztp"), 1)
            e.wait_ge(ctx.h("zev"), EV + 8)
            xc = xnf[:, (c % 2) * D:((c % 2) + 1) * D]
            e.wait_ge(ctx.h("zq"), B["zq"] + 2 * c)
            for h in range(H):
                for r in range(8):
                    mm = e.matmul(pm1[0][:, h * 128:(h + 1) * 128],
                                  wqb[:, r * 512 + h * 128:r * 512 + (h + 1) * 128],
                                  xc[:, r * 128:(r + 1) * 128],
                                  start=(r == 0), stop=(r == 7))
            mm.then_inc(ctx.h("zq"), 1)
            e.wait_ge(ctx.h("zk"), B["zk"] + 2 * c)
            for h in range(H):
                for r in range(8):
                    mm = e.matmul(pm1[1][:, h * 128:(h + 1) * 128],
                                  wkb[:, r * 512 + h * 128:r * 512 + (h + 1) * 128],
                                  xc[:, r * 128:(r + 1) * 128],
                                  start=(r == 0), stop=(r == 7))
            mm.then_inc(ctx.h("zk"), 1)
            e.wait_ge(ctx.h("zve"), B["zve"] + c)
            for r in range(8):
                mm = e.matmul(pm1[2][:], xc[:, r * 128:(r + 1) * 128],
                              wvb[:, r * 512:(r + 1) * 512],
                              start=(r == 0), stop=(r == 7))
            mm.then_inc(ctx.h("zv"), 1)
            e.wait_ge(ctx.h("znd"), B["znd"] + 8 * c)
            e.wait_ge(ctx.h("zsup"), B["zsup"] + 8 * c)
            e.wait_ge(ctx.h("zbe"), B["zbe"] + c)
            for r in range(8):
                mm = e.matmul(pm1[3][0:36, 0:128], wbmgb[:, r * 36:(r + 1) * 36],
                              xc[:, r * 128:(r + 1) * 128],
                              start=(r == 0), stop=(r == 7))
            mm.then_inc(ctx.h("zb"), 1)
            e.wait_ge(ctx.h("zk"), B["zk"] + 2 * c + 2)
            for i in range(4):
                if i < 2:
                    e.wait_ge(ctx.h("zev"), EV + 7 + i)
                else:
                    e.wait_ge(ctx.h("zkte"), B["zkte"] + 4 * c + i - 1)
                e.transpose(ptr[i % 2][:], ksb[:, i * 128:(i + 1) * 128],
                            ident[:]).then_inc(ctx.h("ztp"), 1)
            e.wait_ge(ctx.h("zLw"), B["zLw"] + c + 1)
            e.wait_ge(ctx.h("zlng"), B["zlng"] + 2 * (c + 1))
            if c >= 1:
                e.wait_ge(ctx.h("zsup"), B["zsup"] + 8 * c)
                e.wait_ge(ctx.h("zpce"), B["zpce"] + 2 * c)
            e.transpose(pm2[1][:, 0:128], bmg_sb[:], identF[:]).then_inc(ctx.h("ztb"), 1)
            e.wait_ge(ctx.h("zdte"), B["zdte"] + 2 * c + 1)
            e.wait_ge(ctx.h("zlr"), B["zlr"] + 16 * (c + 1))
            mm = e.matmul(pm2[1][:, 128:136], onesb[0:1, :],
                          Lrows[0:1, 127:1024:128], start=True, stop=True)
            mm.then_inc(ctx.h("zpce"), 1)
            for h in range(H):
                hh = 4 * c + h
                e.wait_ge(ctx.h("zq"), B["zq"] + 2 * (c + 1))
                e.wait_ge(ctx.h("zk"), B["zk"] + 2 * (c + 1))
                e.wait_ge(ctx.h("znd"), B["znd"] + 2 * hh)
                e.wait_ge(ctx.h("zscv"), B["zscv"] + hh)
                mm = e.matmul(pm2[0][:, 0:128], ksb[:, h * 128:(h + 1) * 128],
                              qsb[:, h * 128:(h + 1) * 128], start=True, stop=True)
                mm.then_inc(ctx.h("zsc"), 1)
                e.wait_ge(ctx.h("zpce"), B["zpce"] + 2 * c + 2)
                e.wait_ge(ctx.h("znl"), B["znl"] + c + 1)
                if hh >= 1:
                    e.wait_ge(ctx.h("zsup"), B["zsup"] + 2 * hh)
                e.wait_ge(ctx.h("zmn"), B["zmn"] + hh)
                for j in range(J):
                    ch = 2 * h + j
                    mm = e.matmul(pm2[1][:, j * 128:(j + 1) * 128], onesb[0:1, :],
                                  Lrows[0:1, ch * 128:(ch + 1) * 128],
                                  start=True, stop=False)
                    mm.then_inc(ctx.h("zM"), 1)
                    mm = e.matmul(pm2[1][:, j * 128:(j + 1) * 128],
                                  negL[0:1, ch * 128:(ch + 1) * 128], onesb[0:1, 0:128],
                                  start=False, stop=True)
                    mm.then_inc(ctx.h("zM"), 1)
                e.wait_ge(ctx.h("zscp"), B["zscp"] + c + 1)
                for j in range(J):
                    ch = 2 * h + j
                    e.wait_ge(ctx.h("zab"), B["zab"] + 2 * hh + j + 1)
                    if j == 1:
                        e.wait_ge(ctx.h("znd"), B["znd"] + 2 * hh + 1)
                    e.matmul(pm2[0][:, 128:257], abt[:, j * 128:(j + 1) * 128],
                             vsb[:, h * 129:(h + 1) * 129], start=True, stop=True)
                    mm = e.matmul(pm2[0][:, 257:386], qsb[:, h * 128:(h + 1) * 128],
                                  Scpy[:, ch * 129:(ch + 1) * 129],
                                  start=True, stop=True)
                    mm.then_inc(ctx.h("zym"), 1)
                for j in range(J):
                    e.wait_ge(ctx.h("zvt"), B["zvt"] + 2 * hh + j + 1)
                    if j == 0:
                        e.wait_ge(ctx.h("zym"), B["zym"] + 2 * hh + 1)
                        dst = pm2[1][:, 256:385]
                    else:
                        e.wait_ge(ctx.h("zmn"), B["zmn"] + hh + 1)
                        e.wait_ge(ctx.h("zb"), B["zb"] + c + 1)
                        e.wait_ge(ctx.h("zbe"), B["zbe"] + c + 1)
                        dst = pm1[3][:, 0:129]
                    mm = e.matmul(dst, ktm[:, h * 128:(h + 1) * 128],
                                  vtil[:, j * 129:(j + 1) * 129], start=True, stop=True)
                    mm.then_inc(ctx.h("zinc"), 1)
        for c2 in range(16):
            TPY = B["ztpy"] + 8 * c2
            EVY = B["zevy"] + 8 * c2
            e.wait_ge(ctx.h("yow"), B["yow"] + 16 * (c2 + 1))
            e.wait_ge(ctx.h("ybl"), B["ybl"] + 16 * (c2 + 1))
            e.wait_ge(ctx.h("zevy"), EVY)
            if c2 == 0:
                e.wait_ge(ctx.h("zkte"), B["zkte"] + 4 * NCH)
                e.wait_ge(ctx.h("zq"), B["zq"] + 2 * NCH)
                e.wait_ge(ctx.h("zk"), B["zk"] + 2 * NCH)
            for i in range(8):
                if i >= 2:
                    e.wait_ge(ctx.h("zevy"), EVY + i - 1)
                src = yown[:, c2 * 512 + i * 128:c2 * 512 + (i + 1) * 128] if i < 4 \
                    else ybt[:, (c2 % 2) * 512 + (i - 4) * 128:(c2 % 2) * 512 + (i - 3) * 128]
                e.transpose(ptr[i % 2][:], src, ident[:]).then_inc(ctx.h("ztpy"), 1)
            e.wait_ge(ctx.h("zevy"), EVY + 8)
            yc = yfm[:, (c2 % 2) * D:((c2 % 2) + 1) * D]
            for s in range(2):
                e.wait_ge(ctx.h("zoe"), B["zoe"] + 2 * c2 + s)
                for r in range(8):
                    e.matmul(pm1[s][:], yc[:, r * 128:(r + 1) * 128],
                             wob[:, r * 1024 + s * 512:r * 1024 + (s + 1) * 512],
                             start=(r == 0), stop=False)
                mm = e.matmul(pm1[s][:], onesbb[0:1, :],
                              boutb[0:1, s * 512:(s + 1) * 512],
                              start=False, stop=True)
                mm.then_inc(ctx.h("zmm"), 1)

    def gpG(e):
        for c2 in range(16):
            e.wait_ge(ctx.h("zgl"), B["zgl"] + c2 + 1)
            e.dma_start(out=x_out[c2 * 128:(c2 + 1) * 128, :],
                        in_=ot[:, (c2 % 2) * D:((c2 % 2) + 1) * D]).then_inc(ctx.h("zst"), 16)
        e.wait_ge(ctx.h("zst"), B["zst"] + 16 * 16)

    def gpALL(e):
        gpM(e)
        gpG(e)

    phase(nc, f"{tag}_main", _ctx=ctx, sync=syncM, gpsimd=gpALL, vector=vecM,
          scalar=scalM, tensor=tensM)
    for k, r in dict(zxl=16 * 32, zxn=32, zva=32, zsr=32, ztp=12 * 32, zev=8 * 32,
                     zq=2 * 32, zk=2 * 32, zv=32, zb=32, zbe=32, zqe=32, zke=32,
                     zve=32, zem=2 * 32, zbet=4 * 32, zLw=32, zlng=2 * 32, ztb=32,
                     zdte=2 * 32, zkte=4 * 32, zpcm=2 * 32, zpce=2 * 32,
                     zlr=16 * 32, znl=32, zsc=4 * 32, zscv=4 * 32, zM=16 * 32,
                     zmn=4 * 32, zdm=4 * 32, zab=8 * 32, zvt=8 * 32, zscp=32,
                     zym=8 * 32, znd=8 * 32, znd2=8 * 32, zinc=8 * 32, zsup=8 * 32, zyw=32,
                     pub=32 * 16, yow=16 * 16, ybl=16 * 16, zxt=16 * 16,
                     ztpy=8 * 16, zevy=8 * 16, zmm=2 * 16, zoe=2 * 16, zhr=16,
                     zgl=16, zst=16 * 16, zva2=3 * 16, zsr2=3 * 16).items():
        ctx.setval(k, B[k] + r)


def build_l2a_program():
    es = contextlib.ExitStack()
    nc = mk_bass()
    IN = {}
    IN["x"] = nc.dram_tensor("x", [T_LOC, D], F32, kind="ExternalInput").ap()
    for nm in ("wq", "wk", "wv"):
        IN[nm] = nc.dram_tensor(nm, [D, 512], F32, kind="ExternalInput").ap()
    IN["wbmg"] = nc.dram_tensor("wbmg", [D, 36], F32, kind="ExternalInput").ap()
    IN["wo"] = nc.dram_tensor("wo", [D, D], F32, kind="ExternalInput").ap()
    IN["boutr"] = nc.dram_tensor("boutr", [1, D], F32, kind="ExternalInput").ap()
    IN["lngr"] = nc.dram_tensor("lngr", [128, D], F32, kind="ExternalInput").ap()
    IN["lnbr"] = nc.dram_tensor("lnbr", [128, D], F32, kind="ExternalInput").ap()
    IN["rcol"] = nc.dram_tensor("rcol", [128, 1], F32, kind="ExternalInput").ap()
    IN["bmgbc"] = nc.dram_tensor("bmgbc", [128, 1], F32, kind="ExternalInput").ap()
    IN["identC"] = nc.dram_tensor("identC", [128, 128], BF16, kind="ExternalInput").ap()
    IN["identF"] = nc.dram_tensor("identF", [128, 128], F32, kind="ExternalInput").ap()
    IN["causalC"] = nc.dram_tensor("causalC", [128, 128], BF16, kind="ExternalInput").ap()
    IN["meta"] = nc.dram_tensor("meta", [1, 132], I32, kind="ExternalInput").ap()
    IN["x_out"] = nc.dram_tensor("x_out", [T_LOC, D], F32, kind="ExternalOutput").ap()
    IN["x2_sh"] = nc.dram_tensor("x2_sh", [4096, D], BF16, addr_space="Shared").ap()
    IN["y2_sh"] = nc.dram_tensor("y2_sh", [4096, 512], BF16, addr_space="Shared").ap()
    IN["flags"] = nc.dram_tensor("flags", [2, 128], I32, addr_space="Shared").ap()
    IN["flag_base"] = 0
    ctx = Ctx(nc, es)
    emit_l2a(ctx, IN)
    es.close()
    return nc


def _alibi_slopes_np(n):
    import math
    def pow2(m):
        start = 2 ** (-2 ** (-(math.log2(m) - 3)))
        return [start * start ** i for i in range(m)]
    if math.log2(n).is_integer():
        return np.array(pow2(n), dtype=np.float32)
    c = 2 ** int(math.floor(math.log2(n)))
    s = pow2(c) + _alibi_slopes_np(2 * c).tolist()[:n - c]
    return np.array(s, dtype=np.float32)


def l2a_in_maps(x_full, ins, nonce=None):
    import ml_dtypes
    identC = np.eye(128, dtype=ml_dtypes.bfloat16)
    identF = np.eye(128, dtype=np.float32)
    causal = np.triu(np.ones((128, 128), np.float32)).astype(ml_dtypes.bfloat16)
    f32 = lambda a: np.ascontiguousarray(np.asarray(a, dtype=np.float32))
    r_all = np.exp(-_alibi_slopes_np(8))
    if nonce is None:
        nonce = int(np.random.randint(1, 2 ** 30))
    Wq, Wk, Wv = ins["L2_attn_Wq"], ins["L2_attn_Wk"], ins["L2_attn_Wv"]
    Wbeta, bbeta = ins["L2_attn_Wbeta"], ins["L2_attn_bbeta"]
    Wmix, bmix = ins["L2_attn_Wmix"], ins["L2_attn_bmix"]
    Wgate, bgate = ins["L2_attn_Wgate"], ins["L2_attn_bgate"]
    Wout, bout = ins["L2_attn_Wout"], ins["L2_attn_bout"]
    maps = []
    for cidx in range(NCORE):
        b, h = cidx // 2, cidx % 2
        shard = f32(x_full[b, h * T_LOC:(h + 1) * T_LOC, :])
        sl = slice(h * 512, (h + 1) * 512)
        wbmg = np.zeros((36, D), np.float32)
        wbmg[0:8] = np.asarray(Wbeta, np.float32)[h * 8:(h + 1) * 8]
        wbmg[8:16] = np.asarray(Wmix, np.float32)[h * 8:(h + 1) * 8]
        wbmg[32:36] = np.asarray(Wgate, np.float32)[h * 4:(h + 1) * 4]
        bmgb = np.zeros((128, 1), np.float32)
        bmgb[0:8, 0] = np.asarray(bbeta, np.float32)[h * 8:(h + 1) * 8]
        bmgb[8:16, 0] = np.asarray(bmix, np.float32)[h * 8:(h + 1) * 8]
        bmgb[32:36, 0] = np.asarray(bgate, np.float32)[h * 4:(h + 1) * 4]
        rc = np.zeros((128, 1), np.float32)
        for ch in range(8):
            rc[ch, 0] = r_all[h * 4 + ch // 2]
        WoT = np.asarray(Wout, np.float32).T
        perm = np.concatenate([np.arange(h * 512, (h + 1) * 512),
                               np.arange((1 - h) * 512, (2 - h) * 512)])
        maps.append(dict(
            x=shard,
            wq=f32(np.asarray(Wq, np.float32)[sl].T),
            wk=f32(np.asarray(Wk, np.float32)[sl].T),
            wv=f32(np.asarray(Wv, np.float32)[sl].T),
            wbmg=f32(wbmg.T), wo=f32(WoT[perm]),
            boutr=f32(np.asarray(bout, np.float32).reshape(1, -1)),
            lngr=rep128(ins["L2_attn_ln_g"]), lnbr=rep128(ins["L2_attn_ln_b"]),
            rcol=rc, bmgbc=bmgb, identC=identC, identF=identF, causalC=causal,
            meta=np.array([[h, nonce] + [nonce + kk + 1 for kk in range(130)]],
                          dtype=np.int32),
        ))
    return maps


def emit_l1a(ctx, IN, tag="l1a"):
    nc = ctx.nc
    NCH = 32
    NH = 8
    x_in, x_out = IN["x"], IN["x_out"]
    x2_sh, y2_sh, flags_t = IN["x2_sh"], IN["y2_sh"], IN["flags"]
    FB = IN["flag_base"]
    XB, YB = FB, FB + 16

    ident = ctx.sb("identb", [128, 128], BF16)
    epsc = ctx.sb("epscol", [128, 1], F32)
    m30c = ctx.sb(f"{tag}_m30", [128, 1], F32)
    meta = ctx.sb("metai", [1, 132], I32)
    pollcell = ctx.sb("pollcell", [1, 2], I32)
    wqb = ctx.sb(f"{tag}_wqb", [128, 8 * 512], BF16)
    wkb = ctx.sb(f"{tag}_wkb", [128, 8 * 512], BF16)
    wvb = ctx.sb(f"{tag}_wvb", [128, 8 * 512], BF16)
    wpb = ctx.sb(f"{tag}_wpb", [128, 8 * 1024], BF16)
    cosc = ctx.sb(f"{tag}_cosc", [128, 32 * 64], BF16)
    sinc = ctx.sb(f"{tag}_sinc", [128, 32 * 64], BF16)
    cmR = ctx.sb(f"{tag}_cmR", [128, 4 * 512], BF16)

    xt = ctx.sb("g_xt", [128, 2 * D], F32)
    xnp = ctx.sb(f"{tag}_xnp", [128, D], BF16)
    xpt = ctx.sb(f"{tag}_xpt", [128, 2 * D], BF16)
    xnf = ctx.sb(f"{tag}_xnf", [128, 2 * D], BF16)
    qtm = ctx.sb(f"{tag}_qtm", [128, 512], BF16)
    ktm = ctx.sb(f"{tag}_ktm", [128, 512], BF16)
    qcs = ctx.sb(f"{tag}_qcs", [128, 512], BF16)
    qsw = ctx.sb(f"{tag}_qsw", [128, 512], BF16)
    qfm = ctx.sb(f"{tag}_qfm", [128, 4 * 512], BF16)
    kfm = ctx.sb(f"{tag}_kfm", [128, 4 * 4096], BF16)
    vbuf = ctx.sb(f"{tag}_vbuf", [128, 32 * 8 * 65], BF16)
    ptile = ctx.sb(f"{tag}_pt", [128, 2 * 512], BF16)
    ybuf = ctx.sb(f"{tag}_ybuf", [128, 32 * 512], BF16)
    yown = ctx.sb(f"{tag}_yown", [128, 16 * 512], BF16)
    ybt = ctx.sb(f"{tag}_ybt", [128, 2 * 512], BF16)
    yfm = ctx.sb(f"{tag}_yfm", [128, 2 * D], BF16)
    rdc = ctx.sb(f"{tag}_rdc", [128, 4], F32)
    po = ctx.sb(f"{tag}_po", [128, D], BF16)
    ot = ctx.sb(f"{tag}_ot", [128, D], F32)
    sc = ctx.sb("g_sc", [128, 16], F32)
    st6 = ctx.sb("g_st6", [128, 48], F32)
    st2 = ctx.sb("g_st2", [128, 2], F32)
    r1c = ctx.sb("g_r1c", [128, 4], F32)

    pm1 = [ctx.ps(f"ps_pm1_{i}", [128, 512], F32) for i in range(4)]
    pm2 = [ctx.ps(f"ps_pm2_{i}", [128, 512], F32) for i in range(2)]
    ptr = [ctx.ps(f"ps_ptr_{i}", [128, 128], BF16) for i in range(2)]

    def rep8(base_sb, coff, width):
        return bass.AP(base_sb, coff, [[base_sb.shape[1], 128], [0, 8], [1, width]])

    def syncp(e):
        for dst, src in ((ident, IN["identC"]), (meta, IN["meta"]),
                         (cmR, IN["cmR"])):
            ins = e.dma_start(out=dst[:], in_=src)
            ctx.inc(ins, "ld", 16)
        ctx.wait(e, "ld")

    def gpp(e):
        e.memset(epsc[:], LN_EPS)
        e.memset(m30c[:], -30.0)
        e.memset(vbuf[:], 1.0)
        for dst, src in ((cosc, IN["cosr"]), (sinc, IN["sinr"])):
            ins = e.dma_start(out=dst[:], in_=src)
            ctx.inc(ins, "ldg", 16)
        for dst, src in ((wqb, IN["wq"]), (wkb, IN["wk"]), (wvb, IN["wv"]),
                         (wpb, IN["wp"])):
            w = dst.shape[1] // 8
            for r in range(8):
                ins = e.dma_start(out=dst[:, r * w:(r + 1) * w],
                                  in_=src[r * 128:(r + 1) * 128, :])
                ctx.inc(ins, "ldg", 16)
        ctx.wait(e, "ldg")
    phase(nc, f"{tag}_pro", _ctx=ctx, sync=syncp, gpsimd=gpp)

    B = {k: ctx.val(k) for k in (
        "wxt0", "wva0", "wsr0", "wxn", "pub", "wxl", "wxf", "wtp", "wev",
        "wmq", "wmk", "wmv", "wqe", "wke", "wve", "wrp", "wtq", "wqkev",
        "wqf", "wkf", "wsc", "wpe", "wmsk", "wavp", "wyv", "wyw", "yow",
        "ybl", "wxt", "wtpy", "wevy", "wmm", "woe", "wgl", "wst", "wva2",
        "wsr2", "whr")}

    def kcmax(Q):
        return 4 * Q + 4

    def npairs_before(Q, h):
        return sum(kcmax(q) * NH for q in range(Q)) + kcmax(Q) * h

    NPAIRS = npairs_before(7, 7) + kcmax(7)

    def syncM(e):
        with e.register("parW0") as par:
            e.reg_load(par, meta[0:1, 0:1])
            b_snd = ctx.val("fsnd2")
            saved = {}
            for branch in (0, 1):
                cm = e.If_eq(par, 0) if branch == 0 else e.Else()
                with cm:
                    myp, pep = branch, 1 - branch
                    for cl in range(16):
                        if cl >= 2:
                            e.wait_ge(ctx.h("wxn"), B["wxn"] + cl - 1)
                        ins = e.dma_start(out=xt[:, (cl % 2) * D:((cl % 2) + 1) * D],
                                          in_=x_in[cl * 128:(cl + 1) * 128, :])
                        ctx.inc(ins, "wxt0", 16)
                    for c in range(NCH):
                        if c // 16 == myp:
                            e.wait_ge(ctx.h("pub"), B["pub"] + 16 * ((c % 16) + 1))
                        else:
                            emit_flag_poll(ctx, e, XB + (c % 16),
                                           flags_t[pep:pep + 1, XB + (c % 16):XB + (c % 16) + 1],
                                           meta, pollcell, pol="fpol2")
                        if c >= 2:
                            e.wait_ge(ctx.h("wxf"), B["wxf"] + c - 1)
                        ins = e.dma_start(out=xpt[:, (c % 2) * D:((c % 2) + 1) * D],
                                          in_=x2_sh[c * 128:(c + 1) * 128, :])
                        ctx.inc(ins, "wxl", 16)
                    for c in range(NCH):
                        if c // 16 == pep:
                            e.wait_ge(ctx.h("wyw"), B["wyw"] + (c // 4) + 1)
                            ins = e.dma_start(
                                out=y2_sh[myp * 2048 + (c % 16) * 128:myp * 2048 + (c % 16) * 128 + 128, :],
                                in_=ybuf[:, c * 512:(c + 1) * 512])
                            ctx.inc(ins, "fsnd2", 16)
                            ctx.wait(e, "fsnd2")
                            emit_flag_send(ctx, e, YB + (c % 16),
                                           flags_t[myp:myp + 1, YB + (c % 16):YB + (c % 16) + 1],
                                           meta, snd="fsnd2")
                    for c2 in range(16):
                        gc = myp * 16 + c2
                        e.wait_ge(ctx.h("wyw"), B["wyw"] + (gc // 4) + 1)
                        ins = e.dma_start(out=yown[:, c2 * 512:(c2 + 1) * 512],
                                          in_=ybuf[:, gc * 512:(gc + 1) * 512])
                        ctx.inc(ins, "yow", 16)
                        emit_flag_poll(ctx, e, YB + c2,
                                       flags_t[pep:pep + 1, YB + c2:YB + c2 + 1],
                                       meta, pollcell, pol="fpol2")
                        if c2 >= 2:
                            e.wait_ge(ctx.h("wevy"), B["wevy"] + 8 * (c2 - 1))
                        ins = e.dma_start(out=ybt[:, (c2 % 2) * 512:((c2 % 2) + 1) * 512],
                                          in_=y2_sh[pep * 2048 + c2 * 128:pep * 2048 + c2 * 128 + 128, :])
                        ctx.inc(ins, "ybl", 16)
                        if c2 >= 2:
                            e.wait_ge(ctx.h("whr"), B["whr"] + c2 - 1)
                        ins = e.dma_start(out=xt[:, (c2 % 2) * D:((c2 % 2) + 1) * D],
                                          in_=x_in[c2 * 128:(c2 + 1) * 128, :])
                        ctx.inc(ins, "wxt", 16)
                    if branch == 0:
                        for nm in ("fsnd2", "wxt0", "wxl", "yow", "ybl", "wxt"):
                            saved[nm] = ctx.val(nm)
                        ctx.setval("fsnd2", b_snd)
                        for nm in ("wxt0", "wxl", "yow", "ybl", "wxt"):
                            ctx.setval(nm, B[nm])
            for nm, v in saved.items():
                ctx.setval(nm, v)

    def gpM(e):
        with e.register("parW1") as par:
            e.reg_load(par, meta[0:1, 0:1])
            b_snd = ctx.val("fsnd")
            saved = {}
            for branch in (0, 1):
                cm = e.If_eq(par, 0) if branch == 0 else e.Else()
                with cm:
                    myp = branch
                    for cl in range(16):
                        e.wait_ge(ctx.h("wxn"), B["wxn"] + cl + 1)
                        ins = e.dma_start(
                            out=x2_sh[myp * 2048 + cl * 128:myp * 2048 + cl * 128 + 128, :],
                            in_=xnp[:, 0:D])
                        ctx.inc(ins, "pub", 16)
                        ctx.wait(e, "pub")
                        emit_flag_send(ctx, e, XB + cl,
                                       flags_t[myp:myp + 1, XB + cl:XB + cl + 1], meta)
                    if branch == 0:
                        saved["pub"] = ctx.val("pub")
                        saved["fsnd"] = ctx.val("fsnd")
                        ctx.setval("pub", B["pub"])
                        ctx.setval("fsnd", b_snd)
            for nm, v in saved.items():
                ctx.setval(nm, v)
        for Q in range(8):
            for ci in range(4):
                c = 4 * Q + ci
                for t_sb, e_sem in ((qtm, "wqe"), (ktm, "wke")):
                    e.wait_ge(ctx.h(e_sem), B[e_sem] + c + 1)
                    e.tensor_mul(bass.AP(qcs, 0, [[512, 128], [64, 8], [1, 64]]),
                                 bass.AP(t_sb, 0, [[512, 128], [64, 8], [1, 64]]),
                                 rep8(cosc, c * 64, 64))
                    e.tensor_mul(bass.AP(qsw, 0, [[512, 128], [64, 8], [1, 32]]),
                                 bass.AP(t_sb, 32, [[512, 128], [64, 8], [1, 32]]),
                                 rep8(sinc, c * 64, 32))
                    e.tensor_mul(bass.AP(qsw, 32, [[512, 128], [64, 8], [1, 32]]),
                                 bass.AP(t_sb, 0, [[512, 128], [64, 8], [1, 32]]),
                                 rep8(sinc, c * 64 + 32, 32))
                    e.tensor_add(t_sb[:], qcs[:], qsw[:])
                    ctx.sig(e, "wrp", 1)
            for h in range(NH):
                for r in range(4):
                    kc = 4 * Q + r
                    i = npairs_before(Q, h) + kc
                    e.wait_ge(ctx.h("wpe"), B["wpe"] + i + 1)
                    e.tensor_mul(ptile[:, (i % 2) * 512:(i % 2 + 1) * 512],
                                 ptile[:, (i % 2) * 512:(i % 2 + 1) * 512],
                                 cmR[:, r * 512:(r + 1) * 512])
                    ctx.sig(e, "wmsk", 1)

    def vecM(e):
        for cl in range(16):
            xs = xt[:, (cl % 2) * D:((cl % 2) + 1) * D]
            e.wait_ge(ctx.h("wxt0"), B["wxt0"] + 16 * (cl + 1))
            ln_stats(e, xs, D, st6, st2, sc[:, 0:1], sc[:, 1:2], "wva0", ctx)
            e.wait_ge(ctx.h("wsr0"), B["wsr0"] + cl + 1)
            if cl >= 1:
                e.wait_ge(ctx.h("pub"), B["pub"] + 16 * cl)
            e.tensor_scalar(xnp[:, 0:D], xs,
                            sc[:, 0:1], r1c[:, 0:1], ALU.subtract, ALU.mult)
            ctx.sig(e, "wxn", 1)
        for Q in range(8):
            for h in range(NH):
                e.wait_ge(ctx.h("wavp"), B["wavp"] + npairs_before(Q, h) + kcmax(Q))
                bank = pm1[2] if h % 2 == 0 else pm1[3]
                for sl in range(4):
                    e.tensor_copy(rdc[:, sl:sl + 1], bank[:, sl * 65 + 64:sl * 65 + 65])
                    e.reciprocal(rdc[:, sl:sl + 1], rdc[:, sl:sl + 1])
                    e.tensor_scalar_mul(
                        ybuf[:, (4 * Q + sl) * 512 + h * 64:(4 * Q + sl) * 512 + (h + 1) * 64],
                        bank[:, sl * 65:sl * 65 + 64], rdc[:, sl:sl + 1])
                ctx.sig(e, "wyv", 1)
            ctx.sig(e, "wyw", 1)
        for c2 in range(16):
            xs = xt[:, (c2 % 2) * D:((c2 % 2) + 1) * D]
            e.wait_ge(ctx.h("woe"), B["woe"] + 2 * (c2 + 1))
            ln_stats(e, po, D, st6, st2, sc[:, 2:3], sc[:, 3:4], "wva2", ctx)
            e.wait_ge(ctx.h("wsr2"), B["wsr2"] + c2 + 1)
            e.wait_ge(ctx.h("wxt"), B["wxt"] + 16 * (c2 + 1))
            if c2 >= 1:
                e.wait_ge(ctx.h("wst"), B["wst"] + 16 * c2)
            os_ = ot[:, 0:D]
            e.tensor_scalar(os_, po[:], sc[:, 2:3], r1c[:, 1:2],
                            ALU.subtract, ALU.mult)
            e.tensor_add(os_, os_, xs)
            ctx.sig(e, "wgl", 1)
            ctx.sig(e, "whr", 1)

    def scalM(e):
        for cl in range(16):
            e.wait_ge(ctx.h("wva0"), B["wva0"] + cl + 1)
            rstd_ops(e, sc[:, 1:2], sc[:, 12:13], r1c[:, 0:1], epsc[:])
            ctx.sig(e, "wsr0", 1)
        for Q in range(8):
            for ci in range(4):
                c = 4 * Q + ci
                TP = B["wtp"] + 8 * c
                for i in range(8):
                    e.wait_ge(ctx.h("wtp"), TP + i + 1)
                    if c >= 2:
                        e.wait_ge(ctx.h("wmv"), B["wmv"] + c - 1)
                    e.activation(xnf[:, (c % 2) * D + i * 128:(c % 2) * D + (i + 1) * 128],
                                 ptr[i % 2][:], AF.Identity)
                    ctx.sig(e, "wev", 1)
                ctx.sig(e, "wxf", 1)
                e.wait_ge(ctx.h("wmq"), B["wmq"] + c + 1)
                if c >= 1:
                    e.wait_ge(ctx.h("wtq"), B["wtq"] + 16 * c)
                e.activation(qtm[:], pm2[0][:], AF.Identity)
                ctx.sig(e, "wqe", 1)
                e.wait_ge(ctx.h("wmk"), B["wmk"] + c + 1)
                e.activation(ktm[:], pm2[1][:], AF.Identity)
                ctx.sig(e, "wke", 1)
                e.wait_ge(ctx.h("wmv"), B["wmv"] + c + 1)
                for h in range(NH):
                    e.activation(vbuf[:, (c * 8 + h) * 65:(c * 8 + h) * 65 + 64],
                                 pm1[2][:, h * 64:(h + 1) * 64], AF.Identity)
                ctx.sig(e, "wve", 1)
                for i in range(16):
                    e.wait_ge(ctx.h("wtq"), B["wtq"] + 16 * c + i + 1)
                    h = i % 8
                    pb, cb = 64 * (h % 2), (h // 2)
                    if i < 8:
                        if Q >= 1 and i == 0:
                            e.wait_ge(ctx.h("wsc"), B["wsc"] + npairs_before(Q, 0))
                        dst = qfm[pb:pb + 64, cb * 512 + ci * 128:cb * 512 + (ci + 1) * 128]
                    else:
                        dst = kfm[pb:pb + 64, cb * 4096 + c * 128:cb * 4096 + (c + 1) * 128]
                    e.activation(dst, ptr[i % 2][0:64, 0:128], AF.Identity)
                    ctx.sig(e, "wqf" if i < 8 else "wkf", 1)
                    ctx.sig(e, "wqkev", 1)
            for h in range(NH):
                for kc in range(kcmax(Q)):
                    i = npairs_before(Q, h) + kc
                    e.wait_ge(ctx.h("wsc"), B["wsc"] + i + 1)
                    if i >= 2:
                        e.wait_ge(ctx.h("wavp"), B["wavp"] + i - 1)
                    e.activation(ptile[:, (i % 2) * 512:(i % 2 + 1) * 512],
                                 pm1[i % 2][:], AF.Exp, scale=0.125, bias=m30c[:])
                    ctx.sig(e, "wpe", 1)
        for c2 in range(16):
            TPY = B["wtpy"] + 8 * c2
            for i in range(8):
                e.wait_ge(ctx.h("wtpy"), TPY + i + 1)
                e.activation(yfm[:, (c2 % 2) * D + i * 128:(c2 % 2) * D + (i + 1) * 128],
                             ptr[i % 2][:], AF.Identity)
                ctx.sig(e, "wevy", 1)
            for s in range(2):
                e.wait_ge(ctx.h("wmm"), B["wmm"] + 2 * c2 + s + 1)
                if c2 >= 1:
                    e.wait_ge(ctx.h("wgl"), B["wgl"] + c2)
                e.activation(po[:, s * 512:(s + 1) * 512], pm2[s][:], AF.Identity)
                ctx.sig(e, "woe", 1)
            e.wait_ge(ctx.h("wva2"), B["wva2"] + c2 + 1)
            rstd_ops(e, sc[:, 3:4], sc[:, 12:13], r1c[:, 1:2], epsc[:])
            ctx.sig(e, "wsr2", 1)

    def tensM(e):
        for Q in range(8):
            for ci in range(4):
                c = 4 * Q + ci
                TP = B["wtp"] + 8 * c
                EV = B["wev"] + 8 * c
                e.wait_ge(ctx.h("wxl"), B["wxl"] + 16 * (c + 1))
                e.wait_ge(ctx.h("wev"), EV)
                for i in range(8):
                    if i >= 2:
                        e.wait_ge(ctx.h("wev"), EV + i - 1)
                    elif c >= 1:
                        e.wait_ge(ctx.h("wqkev"), B["wqkev"] + 16 * (c - 1) + 15 + i)
                    e.transpose(ptr[i % 2][:],
                                xpt[:, (c % 2) * D + i * 128:(c % 2) * D + (i + 1) * 128],
                                ident[:]).then_inc(ctx.h("wtp"), 1)
                e.wait_ge(ctx.h("wev"), EV + 8)
                xc = xnf[:, (c % 2) * D:((c % 2) + 1) * D]
                e.wait_ge(ctx.h("wqe"), B["wqe"] + c)
                for r in range(8):
                    mm = e.matmul(pm2[0][:], xc[:, r * 128:(r + 1) * 128],
                                  wqb[:, r * 512:(r + 1) * 512],
                                  start=(r == 0), stop=(r == 7))
                mm.then_inc(ctx.h("wmq"), 1)
                e.wait_ge(ctx.h("wke"), B["wke"] + c)
                for r in range(8):
                    mm = e.matmul(pm2[1][:], xc[:, r * 128:(r + 1) * 128],
                                  wkb[:, r * 512:(r + 1) * 512],
                                  start=(r == 0), stop=(r == 7))
                mm.then_inc(ctx.h("wmk"), 1)
                if ci == 0 and Q >= 1:
                    e.wait_ge(ctx.h("wyv"), B["wyv"] + 8 * Q)
                e.wait_ge(ctx.h("wve"), B["wve"] + c)
                for r in range(8):
                    mm = e.matmul(pm1[2][:], xc[:, r * 128:(r + 1) * 128],
                                  wvb[:, r * 512:(r + 1) * 512],
                                  start=(r == 0), stop=(r == 7))
                mm.then_inc(ctx.h("wmv"), 1)
                e.wait_ge(ctx.h("wrp"), B["wrp"] + 2 * (c + 1))
                for i in range(16):
                    src = qtm if i < 8 else ktm
                    h = i % 8
                    if i < 2:
                        e.wait_ge(ctx.h("wev"), EV + 7 + i)
                    else:
                        e.wait_ge(ctx.h("wqkev"), B["wqkev"] + 16 * c + i - 1)
                    e.transpose(ptr[i % 2][0:64, 0:128], src[:, h * 64:(h + 1) * 64],
                                ident[:]).then_inc(ctx.h("wtq"), 1)
            for h in range(NH):
                i0 = npairs_before(Q, h)
                pb, cb = 64 * (h % 2), (h // 2)
                bank = pm1[2] if h % 2 == 0 else pm1[3]
                for kc in range(kcmax(Q)):
                    i = i0 + kc
                    if i >= 2:
                        e.wait_ge(ctx.h("wpe"), B["wpe"] + i - 1)
                    e.wait_ge(ctx.h("wkf"), B["wkf"] + 8 * (kc + 1))
                    e.wait_ge(ctx.h("wqf"), B["wqf"] + 32 * (Q + 1))
                    lt = kfm[pb:pb + 64, cb * 4096 + kc * 128:cb * 4096 + (kc + 1) * 128]
                    rh = qfm[pb:pb + 64, cb * 512:(cb + 1) * 512]
                    mm = e.matmul(pm1[i % 2][:], lt, rh, start=True, stop=True)
                    mm.then_inc(ctx.h("wsc"), 1)
                    diag = kc >= 4 * Q
                    if diag:
                        e.wait_ge(ctx.h("wmsk"),
                                  B["wmsk"] + 4 * (8 * Q + h) + (kc - 4 * Q) + 1)
                    else:
                        e.wait_ge(ctx.h("wpe"), B["wpe"] + i + 1)
                    if kc == 0:
                        e.wait_ge(ctx.h("wyv"), B["wyv"] + max(0, 8 * Q + h - 1))
                        if h == 0:
                            e.wait_ge(ctx.h("wve"), B["wve"] + 4 * (Q + 1))
                    for sl in range(4):
                        mm = e.matmul(bank[:, sl * 65:(sl + 1) * 65],
                                      ptile[:, (i % 2) * 512 + sl * 128:(i % 2) * 512 + (sl + 1) * 128],
                                      vbuf[:, (kc * 8 + h) * 65:(kc * 8 + h + 1) * 65],
                                      start=(kc == 0 and sl == 0),
                                      stop=(kc == kcmax(Q) - 1 and sl == 3))
                    mm.then_inc(ctx.h("wavp"), 1)
        for c2 in range(16):
            TPY = B["wtpy"] + 8 * c2
            EVY = B["wevy"] + 8 * c2
            e.wait_ge(ctx.h("yow"), B["yow"] + 16 * (c2 + 1))
            e.wait_ge(ctx.h("ybl"), B["ybl"] + 16 * (c2 + 1))
            e.wait_ge(ctx.h("wevy"), EVY)
            if c2 == 0:
                e.wait_ge(ctx.h("wyv"), B["wyv"] + 64)
                e.wait_ge(ctx.h("wqkev"), B["wqkev"] + 16 * 32)
            for i in range(8):
                if i >= 2:
                    e.wait_ge(ctx.h("wevy"), EVY + i - 1)
                src = yown[:, c2 * 512 + i * 128:c2 * 512 + (i + 1) * 128] if i < 4 \
                    else ybt[:, (c2 % 2) * 512 + (i - 4) * 128:(c2 % 2) * 512 + (i - 3) * 128]
                e.transpose(ptr[i % 2][:], src, ident[:]).then_inc(ctx.h("wtpy"), 1)
            e.wait_ge(ctx.h("wevy"), EVY + 8)
            yc = yfm[:, (c2 % 2) * D:((c2 % 2) + 1) * D]
            for s in range(2):
                e.wait_ge(ctx.h("woe"), B["woe"] + 2 * c2 + s)
                for r in range(8):
                    mm = e.matmul(pm2[s][:], yc[:, r * 128:(r + 1) * 128],
                                  wpb[:, r * 1024 + s * 512:r * 1024 + (s + 1) * 512],
                                  start=(r == 0), stop=(r == 7))
                mm.then_inc(ctx.h("wmm"), 1)

    def gpG(e):
        for c2 in range(16):
            e.wait_ge(ctx.h("wgl"), B["wgl"] + c2 + 1)
            e.dma_start(out=x_out[c2 * 128:(c2 + 1) * 128, :],
                        in_=ot[:, 0:D]).then_inc(ctx.h("wst"), 16)
        e.wait_ge(ctx.h("wst"), B["wst"] + 16 * 16)

    def gpALL(e):
        gpM(e)
        gpG(e)

    phase(nc, f"{tag}_main", _ctx=ctx, sync=syncM, gpsimd=gpALL, vector=vecM,
          scalar=scalM, tensor=tensM)
    for k, r in dict(wxt0=16 * 16, wva0=16, wsr0=16, wxn=16, pub=16 * 16,
                     wxl=16 * 32, wxf=32, wtp=8 * 32, wev=8 * 32, wmq=32,
                     wmk=32, wmv=32, wqe=32, wke=32, wve=32, wrp=2 * 32,
                     wtq=16 * 32, wqkev=16 * 32, wqf=8 * 32, wkf=8 * 32,
                     wsc=NPAIRS, wpe=NPAIRS, wmsk=256, wavp=NPAIRS, wyv=64,
                     wyw=8, yow=16 * 16, ybl=16 * 16, wxt=16 * 16,
                     wtpy=8 * 16, wevy=8 * 16, wmm=2 * 16, woe=2 * 16,
                     wgl=16, wst=16 * 16, wva2=16, wsr2=16, whr=16).items():
        ctx.setval(k, B[k] + r)


def build_l1a_program():
    es = contextlib.ExitStack()
    nc = mk_bass()
    IN = {}
    IN["x"] = nc.dram_tensor("x", [T_LOC, D], F32, kind="ExternalInput").ap()
    for nm in ("wq", "wk", "wv"):
        IN[nm] = nc.dram_tensor(nm, [D, 512], F32, kind="ExternalInput").ap()
    IN["wp"] = nc.dram_tensor("wp", [D, D], F32, kind="ExternalInput").ap()
    IN["cosr"] = nc.dram_tensor("cosr", [128, 32 * 64], F32, kind="ExternalInput").ap()
    IN["sinr"] = nc.dram_tensor("sinr", [128, 32 * 64], F32, kind="ExternalInput").ap()
    IN["cmR"] = nc.dram_tensor("cmR", [128, 4 * 512], BF16, kind="ExternalInput").ap()
    IN["identC"] = nc.dram_tensor("identC", [128, 128], BF16, kind="ExternalInput").ap()
    IN["meta"] = nc.dram_tensor("meta", [1, 132], I32, kind="ExternalInput").ap()
    IN["x_out"] = nc.dram_tensor("x_out", [T_LOC, D], F32, kind="ExternalOutput").ap()
    IN["x2_sh"] = nc.dram_tensor("x2_sh", [4096, D], BF16, addr_space="Shared").ap()
    IN["y2_sh"] = nc.dram_tensor("y2_sh", [4096, 512], BF16, addr_space="Shared").ap()
    IN["flags"] = nc.dram_tensor("flags", [2, 128], I32, addr_space="Shared").ap()
    IN["flag_base"] = 0
    ctx = Ctx(nc, es)
    emit_l1a(ctx, IN)
    es.close()
    return nc


def l1a_in_maps(x_full, ins, nonce=None):
    import ml_dtypes
    identC = np.eye(128, dtype=ml_dtypes.bfloat16)
    f32 = lambda a: np.ascontiguousarray(np.asarray(a, dtype=np.float32))
    if nonce is None:
        nonce = int(np.random.randint(1, 2 ** 30))
    wqkv = np.asarray(ins["L1_attn_w_qkv"], np.float32)
    wproj = np.asarray(ins["L1_attn_w_proj"], np.float32)
    half = 32
    theta = 1.0 / 10000.0 ** (np.arange(half, dtype=np.float64) / half)
    pos = np.arange(4096, dtype=np.float64)
    fr = pos[:, None] * theta[None, :]
    cosr = np.repeat(np.cos(fr), 2, axis=1).astype(np.float32)
    sinr = np.repeat(np.sin(fr), 2, axis=1).astype(np.float32)
    sinp = sinr.copy()
    sinp[:, 0:32] *= -1.0
    cosc = np.zeros((128, 32 * 64), np.float32)
    sinc = np.zeros((128, 32 * 64), np.float32)
    for c in range(32):
        cosc[:, c * 64:(c + 1) * 64] = cosr[c * 128:(c + 1) * 128]
        sinc[:, c * 64:(c + 1) * 64] = sinp[c * 128:(c + 1) * 128]
    s_idx = np.arange(128)[:, None]
    t_idx = np.arange(512)[None, :]
    cmR = np.concatenate(
        [(128 * r + s_idx <= t_idx).astype(np.float32) for r in range(4)],
        axis=1).astype(ml_dtypes.bfloat16)
    maps = []
    for cidx in range(NCORE):
        b, h = cidx // 2, cidx % 2
        shard = f32(x_full[b, h * T_LOC:(h + 1) * T_LOC, :])
        sl = slice(h * 512, (h + 1) * 512)
        perm = np.concatenate([np.arange(h * 512, (h + 1) * 512),
                               np.arange((1 - h) * 512, (2 - h) * 512)])
        maps.append(dict(
            x=shard,
            wq=f32(wqkv[:, 0:1024][:, sl]),
            wk=f32(wqkv[:, 1024:2048][:, sl]),
            wv=f32(wqkv[:, 2048:3072][:, sl]),
            wp=f32(wproj[perm]),
            cosr=cosc, sinr=sinc, cmR=np.ascontiguousarray(cmR), identC=identC,
            meta=np.array([[h, nonce] + [nonce + kk + 1 for kk in range(130)]],
                          dtype=np.int32),
        ))
    return maps


def emit_flag_send(ctx, g, k, flags, meta_sb, snd="fsnd"):
    ins = g.dma_start(out=flags, in_=meta_sb[0:1, 2 + k:3 + k])
    ctx.inc(ins, snd, 16)
    ctx.wait(g, snd)


_uid = [0]


def emit_flag_poll(ctx, g, k, flags, meta_sb, pollcell, pol="fpol"):
    _uid[0] += 1
    u = _uid[0]
    g.sem_clear(ctx.h(pol))
    with (g.register(f"tg{u}") as tgt, g.register(f"nf{u}") as nf,
          g.register(f"cn{u}") as cnt):
        g.reg_load(tgt, meta_sb[0:1, 2 + k:3 + k])
        g.reg_mov(nf, 1)
        g.reg_mov(cnt, 0)
        with g.While(nf):
            ins = g.dma_start(out=pollcell[0:1, 0:1], in_=flags)
            ins.then_inc(ctx.h(pol), 16)
            g.reg_add(cnt, cnt, 16)
            g.wait_ge(ctx.h(pol), cnt)
            with g.register(f"rv{u}") as rv:
                g.reg_load(rv, pollcell[0:1, 0:1])
                g.reg_alu(nf, rv, tgt, ALU.not_equal)


def emit_pair_sync(ctx, tag, k, flags_t, meta_sb):
    nc = ctx.nc
    pollcell = ctx.sb("pollcell", [1, 2], I32)

    def g(e):
        _uid[0] += 1
        with e.register(f"par_ps{_uid[0]}") as par:
            e.reg_load(par, meta_sb[0:1, 0:1])
            b_snd, b_pol = ctx.val("fsnd"), ctx.val("fpol")
            with e.If_eq(par, 0):
                emit_flag_send(ctx, e, k, flags_t[0:1, k:k + 1], meta_sb)
                emit_flag_poll(ctx, e, k, flags_t[1:2, k:k + 1], meta_sb, pollcell)
            ctx.setval("fsnd", b_snd)
            ctx.setval("fpol", b_pol)
            with e.Else():
                emit_flag_send(ctx, e, k, flags_t[1:2, k:k + 1], meta_sb)
                emit_flag_poll(ctx, e, k, flags_t[0:1, k:k + 1], meta_sb, pollcell)
    phase(nc, f"{tag}_sync{k}", _ctx=ctx, gpsimd=g)


def row2d(a):
    return np.ascontiguousarray(np.asarray(a, dtype=np.float32).reshape(1, -1))


def rep128(a):
    a = np.asarray(a, dtype=np.float32).reshape(1, -1)
    return np.ascontiguousarray(np.broadcast_to(a, (128, a.shape[1])))


def mk_bass():
    return bass.Bass("TRN2", target_bir_lowering=False,
                     detect_race_conditions=not bool(os.environ.get("NORACE")))


def build_l2f_program():
    es = contextlib.ExitStack()
    nc = mk_bass()
    IN = {}
    IN["x"] = nc.dram_tensor("x", [T_LOC, D], F32, kind="ExternalInput").ap()
    IN["w1T"] = nc.dram_tensor("w1T", [D, 4096], F32, kind="ExternalInput").ap()
    IN["w2T"] = nc.dram_tensor("w2T", [4096, D], F32, kind="ExternalInput").ap()
    for nm, n in (("ln1_g", D), ("ln1_b", D), ("ln2_g", 4096), ("ln2_b", 4096)):
        IN[nm] = nc.dram_tensor(nm, [128, n], F32, kind="ExternalInput").ap()
    for nm, n in (("b1", 4096), ("b2", D)):
        IN[nm] = nc.dram_tensor(nm, [1, n], F32, kind="ExternalInput").ap()
    IN["identC"] = nc.dram_tensor("identC", [128, 128], BF16, kind="ExternalInput").ap()
    IN["x_out"] = nc.dram_tensor("x_out", [T_LOC, D], F32, kind="ExternalOutput").ap()
    import os
    hk = "ExternalOutput" if os.environ.get("L2F_HOUT") else "Internal"
    IN["h_spill"] = nc.dram_tensor("h_spill", [T_LOC, 4096], BF16, kind=hk).ap()
    ctx = Ctx(nc, es)
    emit_l2f(ctx, IN)
    dump = [t for t in os.environ.get("L2F_DUMP", "").split(",") if t]
    if dump:
        outs = {}
        for nm in dump:
            th = ctx._sb[nm]
            shp = list(th.shape) if hasattr(th, "shape") else None
            outs[nm] = nc.dram_tensor("dbg_" + nm, [int(shp[0]), int(np.prod(shp[1:]))],
                                      F32, kind="ExternalOutput").ap()

        def dumper(e):
            for nm in dump:
                th = ctx._sb[nm]
                ins_ = e.dma_start(out=outs[nm], in_=th[:])
                ctx.inc(ins_, "dumps", 16)
            ctx.wait(e, "dumps")
        phase(nc, "dbgdump", _ctx=ctx, gpsimd=dumper)
    es.close()
    return nc


def l2f_in_maps(x_full, ins):
    import ml_dtypes
    identC = np.eye(128, dtype=ml_dtypes.bfloat16)
    w1T = np.ascontiguousarray(ins["L2_ffn_w1"].T.astype(np.float32))
    w2T = np.ascontiguousarray(ins["L2_ffn_w2"].T.astype(np.float32))
    maps = []
    for c in range(NCORE):
        b, h = c // 2, c % 2
        shard = np.ascontiguousarray(x_full[b, h * T_LOC:(h + 1) * T_LOC, :].astype(np.float32))
        maps.append(dict(
            x=shard, w1T=w1T, w2T=w2T,
            ln1_g=rep128(ins["L2_ffn_ln1_g"]), ln1_b=rep128(ins["L2_ffn_ln1_b"]),
            b1=row2d(ins["L2_ffn_b1"]), ln2_g=rep128(ins["L2_ffn_ln2_g"]),
            ln2_b=rep128(ins["L2_ffn_ln2_b"]), b2=row2d(ins["L2_ffn_b2"]),
            identC=identC,
        ))
    return maps


def gather_stream(res, key="x_out"):
    out = np.zeros((4, 4096, D), dtype=np.float32)
    for c in range(NCORE):
        b, h = c // 2, c % 2
        out[b, h * T_LOC:(h + 1) * T_LOC, :] = res.results[c][key]
    return out


def run_l2f(x_full, ins):
    nc = build_l2f_program()
    res = run_bass_kernel_spmd(nc, l2f_in_maps(x_full, ins), core_ids=list(range(NCORE)))
    return gather_stream(res)


_FUSED_SPECS = [
    ("l0a", dict(
        ext=dict(Wx=([1024, 1024], "F32"), Wxp=([1024, 96], "F32"),
                 Wdt=([64, 1024], "F32"), lng_r=([128, 1024], "F32"),
                 lnb_r=([128, 1024], "F32"), bxc=([128, 8], "F32"),
                 cbc=([128, 8], "F32"), bdtc=([128, 8], "F32"),
                 dpc=([128, 8], "F32"), cwc=([128, 32], "F32"),
                 alogr=([128, 128], "F32"), rmat=([128, 2048], "F32"),
                 smat=([128, 2048], "F32"), r2mat=([128, 256], "F32"),
                 sgn=([128, 2048], "F32")),
        internal=dict(),
        shared=dict(xe_sh=([1024, 3], "F32"), hc_sh=([16384, 1], "F32")),
        flag=("flag_base", 0))),
    ("l0f", dict(
        ext=dict(wiT=([1024, 4096], "F32"), woT=([4096, 1024], "F32"),
                 gateWT=([1024, 8], "F32"), gate_b=([1, 8], "F32"),
                 wi_b=([1, 4096], "F32"), wo_b=([1, 1024], "F32"),
                 AcatT=([1024, 32], "F32"), Bcat=([32, 4096], "F32")),
        internal=dict(h_spill=([2048, 4096], "BF16")),
        shared=dict(), flag=None)),
    ("l1a", dict(
        ext=dict(wq=([1024, 512], "F32"), wk=([1024, 512], "F32"),
                 wv=([1024, 512], "F32"), wp=([1024, 1024], "F32"),
                 cosr=([128, 2048], "F32"), sinr=([128, 2048], "F32"),
                 cmR=([128, 2048], "BF16")),
        internal=dict(),
        shared=dict(x2_sh=([4096, 1024], "BF16"), y2_sh=([4096, 512], "BF16")),
        flag=("flag_base", 10))),
    ("l1f", dict(
        ext=dict(w1T=([1024, 4096], "F32"), w2T=([4096, 1024], "F32"),
                 w0r=([128, 4096], "F32"), w1r=([128, 4096], "F32"),
                 w2r=([128, 4096], "F32"), cbr=([128, 4096], "F32"),
                 halomL=([128, 1], "F32"), halomR=([128, 1], "F32")),
        internal=dict(),
        shared=dict(h_sh=([2 * 2048 + 2, 4096], "BF16")),
        flag=("flag_k", 42))),
    ("l2a", dict(
        ext=dict(wq=([1024, 512], "F32"), wk=([1024, 512], "F32"),
                 wv=([1024, 512], "F32"), wbmg=([1024, 36], "F32"),
                 wo=([1024, 1024], "F32"), boutr=([1, 1024], "F32"),
                 lngr=([128, 1024], "F32"), lnbr=([128, 1024], "F32"),
                 rcol=([128, 1], "F32"), bmgbc=([128, 1], "F32"),
                 identF=([128, 128], "F32"), causalC=([128, 128], "BF16")),
        internal=dict(),
        shared=dict(x2_sh=([4096, 1024], "BF16"), y2_sh=([4096, 512], "BF16")),
        flag=("flag_base", 44))),
    ("l2f", dict(
        ext=dict(w1T=([1024, 4096], "F32"), w2T=([4096, 1024], "F32"),
                 ln1_g=([128, 1024], "F32"), ln1_b=([128, 1024], "F32"),
                 ln2_g=([128, 4096], "F32"), ln2_b=([128, 4096], "F32"),
                 b1=([1, 4096], "F32"), b2=([1, 1024], "F32")),
        internal=dict(h_spill=([2048, 4096], "BF16")),
        shared=dict(), flag=None)),
    ("l3a", dict(
        ext=dict(WiT=([1024, 4096], "F32"), WdtT=([2048, 2048], "F32"),
                 WBT=([2048, 2048], "F32"), WCT=([2048, 2048], "F32"),
                 WoT=([2048, 1024], "F32"), alog2=([128, 16], "F32"),
                 bconv2=([128, 16], "F32"), bdt2=([128, 16], "F32"),
                 convw2=([128, 64], "F32"), halo0=([128, 4], "F32")),
        internal=dict(z_d=([2048, 2048], "BF16")),
        shared=dict(x1_sh=([2048, 3 + 2 * 2048], "BF16"),
                    u_sh=([2048, 1], "F32"), hend_sh=([2048, 1], "F32")),
        flag=("flag_base", 80))),
    ("l3f", dict(
        ext=dict(gate_w=([1024, 8], "F32"), gate_b=([1, 8], "F32"),
                 expert_w=([8 * 1024, 1024], "F32"), expert_b=([8, 1024], "F32")),
        internal=dict(), shared=dict(), flag=None)),
]
_EMITTERS = None
_DT = {"F32": F32, "BF16": BF16, "I32": I32}


def build_fused_program(nblocks=None, tags=None):
    import os
    if nblocks is None:
        nblocks = int(os.environ.get("FUSED_NBLOCKS", "8"))
    global _EMITTERS
    if _EMITTERS is None:
        _EMITTERS = dict(l0a=emit_l0a, l0f=emit_l0f, l1a=emit_l1a, l1f=emit_l1f,
                         l2a=emit_l2a, l2f=emit_l2f, l3a=emit_l3a, l3f=emit_l3f)
    es = contextlib.ExitStack()
    nc = mk_bass()
    meta_ap = nc.dram_tensor("meta", [1, 132], I32, kind="ExternalInput").ap()
    identC_ap = nc.dram_tensor("identC", [128, 128], BF16, kind="ExternalInput").ap()
    flags_ap = nc.dram_tensor("flags", [2, 128], I32, addr_space="Shared").ap()
    nstream = len(tags) if tags else nblocks
    stream = [nc.dram_tensor("l0a_x", [T_LOC, D], F32, kind="ExternalInput").ap()]
    for i in range(1, nstream):
        stream.append(nc.dram_tensor(f"xm{i}", [T_LOC, D], F32).ap())
    stream.append(nc.dram_tensor("x_out", [T_LOC, D], F32, kind="ExternalOutput").ap())
    ctx = Ctx(nc, es)
    l1a_comp = ("wva0 wsr0 wxn wxf wtp wev wmq wmk wmv wqe wke wve "
                "wrp wtq wqkev wqf wkf wsc wpe wmsk wavp wyv wyw wtpy "
                "wevy wmm woe wgl wva2 wsr2 whr").split()
    l0a_comp = ("cv sv m3 vb sp2 dx am ae um ue scn2 wv ymt yev cr2 hcd "
                "tpg evg").split()
    l3a_comp = "crr he hgB mw scn scv sp sr0 vb2 vcv yl yv yw zl".split()
    misc_comp = "ea me mc".split()
    zcomp = ("zxn zva zsr ztp zev zq zk zv zb zbe zqe zke zve zem zbet "
             "zLw zlng ztb zdte zkte zpcm zpce znl zsc zscv zM zmn zdm "
             "zab zvt zscp zym znd znd2 zinc zsup zyw ztpy zevy zmm zoe "
             "zhr zgl zva2 zsr2").split()
    ctx.alias = {}
    if os.environ.get("NOALIAS"):
        l1a_comp = l0a_comp = l3a_comp = misc_comp = []
    for names in (l1a_comp, l0a_comp, l3a_comp, misc_comp):
        assert len(names) <= len(zcomp)
        ctx.alias.update(dict(zip(names, zcomp)))
    if not os.environ.get("NOALIAS"):
        ctx.alias.update(dict(wxt0="zxl", wxl="zlr", wxt="zxt",
                              wst="st", zst="st", wt="st2g",
                              lb1="hs"))
    for nm, shp, dt in (("identb", [128, 128], BF16),
                        ("epscol", [128, 1], F32), ("onecolf2", [128, 1], F32),
                        ("metai", [1, 132], I32), ("pollcell", [1, 2], I32)):
        ctx.sb(nm, shp, dt)
    for i in range(4):
        ctx.ps(f"ps_pm1_{i}", [128, 512], F32)
    for i in range(2):
        ctx.ps(f"ps_pm2_{i}", [128, 512], F32)
    for i in range(2):
        ctx.ps(f"ps_ptr_{i}", [128, 128], BF16)
    specs = [s for s in _FUSED_SPECS if s[0] in tags] if tags \
        else _FUSED_SPECS[:nblocks]
    for bi, (tag, spec) in enumerate(specs):
        IN = {"x": stream[bi], "x_out": stream[bi + 1], "meta": meta_ap,
              "identC": identC_ap, "flags": flags_ap}
        for nm, (shp, dt) in spec["ext"].items():
            IN[nm] = nc.dram_tensor(f"{tag}i_{nm}", shp, _DT[dt],
                                    kind="ExternalInput").ap()
        for nm, (shp, dt) in spec["internal"].items():
            IN[nm] = nc.dram_tensor(f"{tag}i_{nm}", shp, _DT[dt]).ap()
        for nm, (shp, dt) in spec["shared"].items():
            IN[nm] = nc.dram_tensor(f"{tag}i_{nm}", shp, _DT[dt],
                                    addr_space="Shared").ap()
        if spec["flag"] is not None:
            IN[spec["flag"][0]] = spec["flag"][1]
        ctx.begin_block()
        _EMITTERS[tag](ctx, IN)
        ctx.end_block()
    es.close()
    return nc


_IN_MAPS = dict(l0a=("n", "l0a_in_maps"), l0f=("p", "l0f_in_maps"),
                l1a=("n", "l1a_in_maps"), l1f=("n", "l1f_in_maps"),
                l2a=("n", "l2a_in_maps"), l2f=("p", "l2f_in_maps"),
                l3a=("n", "l3a_in_maps"), l3f=("p", "l3f_in_maps"))


def fused_in_maps(x_full, ins, nonce=None):
    import ml_dtypes
    if nonce is None:
        nonce = int(np.random.randint(1, 2 ** 30))
    identC = np.eye(128, dtype=ml_dtypes.bfloat16)
    maps = [dict(identC=identC,
                 meta=np.array([[c % 2, nonce] + [nonce + kk + 1
                                                  for kk in range(130)]],
                               dtype=np.int32))
            for c in range(NCORE)]
    g = globals()
    for bi, (tag, spec) in enumerate(_FUSED_SPECS):
        kind, fn_name = _IN_MAPS[tag]
        fn = g[fn_name]
        bm = fn(x_full, ins, nonce=nonce) if kind == "n" else fn(x_full, ins)
        for c in range(NCORE):
            for k, v in bm[c].items():
                if k in ("meta", "identC"):
                    continue
                if k == "x":
                    if bi == 0:
                        maps[c]["l0a_x"] = v
                    continue
                maps[c][f"{tag}i_{k}"] = v
    return maps


_BLOCK_SEQ = None


def _block_seq():
    global _BLOCK_SEQ
    if _BLOCK_SEQ is None:
        _BLOCK_SEQ = [
            (build_l0a_program(), l0a_in_maps),
            (build_l0f_program(), l0f_in_maps),
            (build_l1a_program(), l1a_in_maps),
            (build_l1f_program(), l1f_in_maps),
            (build_l2a_program(), l2a_in_maps),
            (build_l2f_program(), l2f_in_maps),
            (build_l3a_program(), l3a_in_maps),
            (build_l3f_program(), l3f_in_maps),
        ]
    return _BLOCK_SEQ


_FUSED_NC = None


def kernel(**inputs):
    global _FUSED_NC
    if _FUSED_NC is None:
        _FUSED_NC = (build_fused_program(tags=["l0a", "l0f", "l1a", "l1f", "l2a"]),
                     build_fused_program(tags=["l2f", "l3a", "l3f"]))
    x = np.asarray(inputs["L0_attn_f"], dtype=np.float32)
    for nc_ in _FUSED_NC:
        res = run_bass_kernel_spmd(nc_, fused_in_maps(x, inputs),
                                   core_ids=list(range(NCORE)))
        x = gather_stream(res)
    return x
